# version 11
# speedup vs baseline: 1.1535x; 1.0395x over previous
import math
import numpy as np
import concourse.bass as bass
import concourse.mybir as mybir
from concourse.bass_utils import run_bass_kernel_spmd
from concourse.bass_core_rust import IndirectOffsetOnAxis

F32 = mybir.dt.float32
BF16 = mybir.dt.bfloat16
I32 = mybir.dt.int32
U32 = mybir.dt.uint32
AF = mybir.ActivationFunctionType
ALU = mybir.AluOpType
AX = mybir.AxisListType

T = 4096
D = 1024
NT = T // 128
HC = D // 128
F = 512
NFC = F // 128
E = 32
CAP = 1024
NH = 16
HD = 64
EPS_LN = 1e-5
TWO_PI = 2.0 * math.pi
BIGIDX = float(1 << 22)
ENGS = ("sync", "tensor", "vector", "scalar", "gpsimd")
CHAINED = ("vector", "scalar", "gpsimd")


class Sem:
    __slots__ = ("h", "cnt", "name")

    def __init__(self, nc, stack, name):
        self.h = stack.enter_context(nc.semaphore(name))
        self.cnt = 0
        self.name = name

    def bump(self, n=1):
        self.cnt += n
        return self.cnt


class Prog:
    def __init__(self, nc, block, stack):
        self.nc = nc
        self.block = block
        self.stack = stack
        self.ops = {k: [] for k in ENGS}
        self.sems = {}
        self.chain = {k: self.sem("ch_" + k) for k in CHAINED}
        self.dsem = {k: self.sem("d_" + k) for k in ("sync", "scalar", "gpsimd")}
        self.mmsem = self.sem("mmg")
        self.waited = {k: {} for k in ENGS}

    def sem(self, name):
        if name not in self.sems:
            self.sems[name] = Sem(self.nc, self.stack, name)
        return self.sems[name]

    def _record(self, eng, emit, waits, inc, inc_n):
        wl = []
        for s, c in waits:
            if c <= 0:
                continue
            w = self.waited[eng]
            if w.get(s.name, 0) >= c:
                continue
            w[s.name] = c
            wl.append((s.h, c))
        self.ops[eng].append((wl, emit, None if inc is None else inc.h, inc_n))
        return inc.bump(inc_n) if inc is not None else None

    def cop(self, eng, emit, waits=()):
        ch = self.chain[eng]
        w = list(waits) + [(ch, ch.cnt)]
        return self._record(eng, emit, w, ch, 1)

    def dma(self, eng, emit, waits=()):
        return self._record(eng, emit, waits, self.dsem[eng], 16)

    def mmg(self, emit, waits=()):
        return self._record("tensor", emit, waits, self.mmsem, 1)

    def wait_only(self, eng, waits):
        self._record(eng, lambda e: None, waits, None, 0)

    def emit(self):
        blk = self.block
        decs = {"sync": blk.sync, "tensor": blk.tensor, "vector": blk.vector,
                "scalar": blk.scalar, "gpsimd": blk.gpsimd}
        for name in ENGS:
            entries = self.ops[name]
            if not entries:
                continue

            def make(entries):
                def _(e):
                    for wl, emit, inch, incn in entries:
                        for h, c in wl:
                            e.wait_ge(h, c)
                        ins = emit(e)
                        if inch is not None:
                            if isinstance(ins, (list, tuple)):
                                ins = ins[-1]
                            ins.then_inc(inch, incn)
                return _
            decs[name](make(entries))
            self.ops[name] = []

    def barrier(self):
        waits = [(s, s.cnt) for s in self.sems.values() if s.cnt > 0]
        for eng in ENGS:
            self.wait_only(eng, waits)
        self.emit()


def declare_io(nc):
    t = {}
    di = lambda n, sh, dt=F32: nc.dram_tensor(n, sh, dt, kind="ExternalInput").ap()
    dint = lambda n, sh, dt: nc.dram_tensor(n, sh, dt).ap()
    t["x_tm"] = di("x_tm", [T, D])
    t["x_fm"] = di("x_fm", [8, D, T])
    t["x_fm_own"] = di("x_fm_own", [D, T])
    t["sel"] = di("sel", [8, 1])
    t["W_spec"] = di("W_spec", [D, 2 * F])
    t["W_decay"] = di("W_decay", [D, F])
    t["b_decay"] = di("b_decay", [F])
    t["freqs"] = di("freqs", [F])
    t["W_out"] = di("W_out", [2 * F, D])
    t["gate_wT"] = di("gate_wT", [D, E])
    t["w_gate_up"] = di("w_gate_up", [E, D, 2 * F])
    t["w_down"] = di("w_down", [E, F, D])
    t["shared_w_gate_up"] = di("shared_w_gate_up", [D, 2 * F])
    t["shared_w_down"] = di("shared_w_down", [F, D])
    t["curv"] = di("curv", [1])
    for w in ("wq", "wk", "wv", "wo"):
        t[w + "T"] = di(w + "T", [D, D])
        t[w + "b"] = di(w + "b", [D])
    t["WrT"] = di("WrT", [D, D])
    t["WkT"] = di("WkT", [D, D])
    t["WvT"] = di("WvT", [D, D])
    t["mu_r"] = di("mu_r", [D])
    t["mu_k"] = di("mu_k", [D])
    t["out"] = nc.dram_tensor("out", [T, D], F32, kind="ExternalOutput").ap()

    t["xn_tm"] = dint("xn_tm", [T, D], BF16)
    t["y_fm"] = dint("y_fm", [2 * F, T], BF16)
    t["attn0"] = dint("attn0", [T, D], F32)
    t["buf"] = dint("buf", [E * CAP, D], BF16)
    t["eo"] = dint("eo", [E * CAP, D], BF16)
    t["x1_tm"] = dint("x1_tm", [T, D], F32)
    t["xn1_tm"] = dint("xn1_tm", [T, D], BF16)
    t["htan_tm"] = dint("htan_tm", [T, D], BF16)
    t["ffn1"] = dint("ffn1", [T, D], F32)
    t["qh_tm"] = dint("qh_tm", [T, D], BF16)
    t["kh_tm"] = dint("kh_tm", [T, D], BF16)
    t["v_tm"] = dint("v_tm", [T, D], BF16)
    t["qtkt"] = dint("qtkt", [T, 128], BF16)
    t["misc_d"] = dint("misc_d", [512], F32)
    t["act_sh"] = dint("act_sh", [4 * 128, T], BF16)
    return t


def phase_ln(p, t, x_dram, xn_dram, rstd_dst, m_dst, mk_alloc, tag="ln0",
             htan_dram=None, sb=None):
    from contextlib import ExitStack
    st = ExitStack()
    alloc = mk_alloc(st)
    vch, ach = p.chain["vector"], p.chain["scalar"]
    dsy = p.dsem["sync"]
    dgp = p.dsem["gpsimd"]
    xt = alloc(tag + "_x", [128, 2 * D], F32)
    sqt = alloc(tag + "_sq", [128, D], F32)
    xnf = alloc(tag + "_xnf", [128, D], F32)
    xnt = alloc(tag + "_xn", [128, 2 * D], BF16)
    htt = alloc(tag + "_ht", [128, 2 * D], BF16) if htan_dram is not None else None
    cols = alloc(tag + "_cols", [128, 4 * NT], F32)
    CSS, CV, CT0, CT1 = (k * NT for k in range(4))
    xn_c = [0] * NT
    st_c = [0] * NT
    if htan_dram is not None:
        sc32, sc32_rdy = sb["sc32"], sb["sc32_rdy"]

    for i in range(NT):
        bx = (i % 2) * D
        ld = p.dma("sync", lambda e, i=i, bx=bx: e.dma_start(
            out=xt[:, bx:bx + D], in_=x_dram[i * 128:(i + 1) * 128, :]),
            waits=[(vch, xn_c[i - 2])] if i >= 2 else [])
        sq = p.cop("scalar", lambda e, i=i, bx=bx: e.activation(
            out=sqt[:], in_=xt[:, bx:bx + D], func=AF.Square,
            accum_out=cols[:, CSS + i:CSS + i + 1]), waits=[(dsy, ld)])
        p.cop("vector", lambda e, i=i, bx=bx: e.reduce_sum(
            m_dst[:, i:i + 1], xt[:, bx:bx + D], axis=AX.X), waits=[(dsy, ld)])
        p.cop("vector", lambda e, i=i: e.tensor_scalar_mul(
            m_dst[:, i:i + 1], m_dst[:, i:i + 1], 1.0 / D))
        p.cop("vector", lambda e, i=i: e.tensor_scalar_mul(
            cols[:, CT0 + i:CT0 + i + 1], cols[:, CSS + i:CSS + i + 1], 1.0 / D),
            waits=[(ach, sq)])
        p.cop("vector", lambda e, i=i: e.tensor_tensor(
            cols[:, CV + i:CV + i + 1], m_dst[:, i:i + 1], m_dst[:, i:i + 1], ALU.mult))
        var = p.cop("vector", lambda e, i=i: e.tensor_tensor(
            cols[:, CV + i:CV + i + 1], cols[:, CT0 + i:CT0 + i + 1],
            cols[:, CV + i:CV + i + 1], ALU.subtract))
        lnv = p.cop("scalar", lambda e, i=i: e.activation(
            out=cols[:, CT0 + i:CT0 + i + 1], in_=cols[:, CV + i:CV + i + 1],
            func=AF.Ln, bias=p.cst[:, 0:1]), waits=[(vch, var), (vch, p.cst_rdy)])
        rstd = p.cop("scalar", lambda e, i=i: e.activation(
            out=rstd_dst[:, i:i + 1], in_=cols[:, CT0 + i:CT0 + i + 1],
            func=AF.Exp, scale=-0.5))
        xnf_c = p.cop("vector", lambda e, i=i, bx=bx: e.tensor_scalar(
            xnf[:], xt[:, bx:bx + D],
            m_dst[:, i:i + 1], rstd_dst[:, i:i + 1], ALU.subtract, ALU.mult),
            waits=[(ach, rstd)])
        xn_c[i] = p.cop("vector", lambda e, i=i, bx=bx: e.tensor_copy(
            xnt[:, bx:bx + D], xnf[:]),
            waits=([(dgp, st_c[i - 2])] if i >= 2 else []))
        st_c[i] = p.dma("gpsimd", lambda e, i=i, bx=bx: e.dma_start(
            out=xn_dram[i * 128:(i + 1) * 128, :], in_=xnt[:, bx:bx + D]),
            waits=[(vch, xn_c[i])])
        if htan_dram is not None:
            lnv2 = p.cop("scalar", lambda e, i=i: e.activation(
                out=cols[:, CT1 + i:CT1 + i + 1], in_=cols[:, CV + i:CV + i + 1],
                func=AF.Ln))
            sb_ = p.cop("vector", lambda e, i=i: e.tensor_tensor(
                cols[:, CT1 + i:CT1 + i + 1], cols[:, CT1 + i:CT1 + i + 1],
                cols[:, CT0 + i:CT0 + i + 1], ALU.subtract), waits=[(ach, lnv2)])
            rt = p.cop("scalar", lambda e, i=i: e.activation(
                out=cols[:, CT1 + i:CT1 + i + 1], in_=cols[:, CT1 + i:CT1 + i + 1],
                func=AF.Exp, scale=0.5), waits=[(vch, sb_)])
            z = p.cop("vector", lambda e, i=i: e.tensor_tensor(
                cols[:, CT1 + i:CT1 + i + 1], cols[:, CT1 + i:CT1 + i + 1],
                sc32[:, 0:1], ALU.mult), waits=[(ach, rt), (vch, sc32_rdy)])
            z = p.cop("vector", lambda e, i=i: e.tensor_scalar_max(
                cols[:, CT1 + i:CT1 + i + 1], cols[:, CT1 + i:CT1 + i + 1], 1e-7))
            z2 = p.cop("vector", lambda e, i=i: e.tensor_tensor(
                cols[:, CSS + i:CSS + i + 1], cols[:, CT1 + i:CT1 + i + 1],
                cols[:, CT1 + i:CT1 + i + 1], ALU.mult))
            l1 = p.cop("scalar", lambda e, i=i: e.activation(
                out=cols[:, CSS + i:CSS + i + 1], in_=cols[:, CSS + i:CSS + i + 1],
                func=AF.Ln, bias=p.cst[:, 1:2]), waits=[(vch, z2)])
            s1 = p.cop("scalar", lambda e, i=i: e.activation(
                out=cols[:, CSS + i:CSS + i + 1], in_=cols[:, CSS + i:CSS + i + 1],
                func=AF.Exp, scale=0.5))
            w_ = p.cop("vector", lambda e, i=i: e.tensor_tensor(
                cols[:, CSS + i:CSS + i + 1], cols[:, CSS + i:CSS + i + 1],
                cols[:, CT1 + i:CT1 + i + 1], ALU.add), waits=[(ach, s1)])
            asinh = p.cop("scalar", lambda e, i=i: e.activation(
                out=cols[:, CSS + i:CSS + i + 1], in_=cols[:, CSS + i:CSS + i + 1],
                func=AF.Ln), waits=[(vch, w_)])
            p.cop("vector", lambda e, i=i: e.reciprocal(
                cols[:, CT1 + i:CT1 + i + 1], cols[:, CT1 + i:CT1 + i + 1]),
                waits=[(ach, asinh)])
            cf = p.cop("vector", lambda e, i=i: e.tensor_tensor(
                cols[:, CSS + i:CSS + i + 1], cols[:, CSS + i:CSS + i + 1],
                cols[:, CT1 + i:CT1 + i + 1], ALU.mult))
            ht_c = p.cop("vector", lambda e, i=i, bx=bx: e.tensor_scalar_mul(
                htt[:, bx:bx + D], xnf[:], cols[:, CSS + i:CSS + i + 1]),
                waits=[(vch, cf)] + ([(dgp, st_c[i - 2])] if i >= 2 else []))
            st_c[i] = p.dma("gpsimd", lambda e, i=i, bx=bx: e.dma_start(
                out=htan_dram[i * 128:(i + 1) * 128, :], in_=htt[:, bx:bx + D]),
                waits=[(vch, ht_c)])
    p.barrier()
    st.close()


def phase_l0_attn(p, t, mk_alloc):
    from contextlib import ExitStack
    TB = 2048
    NTB = T // TB
    NQ = TB // 512
    vch, ach, pch = p.chain["vector"], p.chain["scalar"], p.chain["gpsimd"]
    dsy, dgp, dsc = p.dsem["sync"], p.dsem["gpsimd"], p.dsem["scalar"]
    mms = p.mmsem

    stA = ExitStack()
    alloc = mk_alloc(stA)
    xnT = alloc("p4_xnT", [128, HC * T], BF16)
    wdec = alloc("p4_wdec", [128, HC * F], BF16)
    wspec = alloc("p4_wspec", [128, HC * 2 * F], BF16)
    bdec = alloc("p4_bdec", [128, NFC], F32)
    om = alloc("p4_om", [128, NFC], F32)
    trow = alloc("p4_trow", [128, TB], F32)
    dt_ = alloc("p4_d", [128, TB], F32)
    ur = alloc("p4_ur", [128, TB], F32)
    ui = alloc("p4_ui", [128, TB], F32)
    cc = alloc("p4_c", [128, TB], F32)
    ss = alloc("p4_s", [128, TB], F32)
    t1 = alloc("p4_t1", [128, TB], F32)
    t2 = alloc("p4_t2", [128, TB], F32)
    gi_ = alloc("p4_gi", [128, TB], F32)
    carry = alloc("p4_carry", [128, 2 * NFC], F32)
    ybf = alloc("p4_ybf", [128, 2 * TB], BF16)
    ps = alloc("p4_ps", [128, 2 * 2048], F32, psum=True)

    w1 = p.dma("gpsimd", lambda e: e.dma_start(
        out=wdec[:].rearrange("p (c f) -> p c f", c=HC),
        in_=t["W_decay"].rearrange("(c p) f -> p c f", p=128)))
    w2 = p.dma("gpsimd", lambda e: e.dma_start(
        out=wspec[:].rearrange("p (c f) -> p c f", c=HC),
        in_=t["W_spec"].rearrange("(c p) f -> p c f", p=128)))
    w3 = p.dma("sync", lambda e: e.dma_start(
        out=bdec[:], in_=t["b_decay"].rearrange("(c p) -> p c", p=128),
        allow_slow_non_contiguous=True))
    w4 = p.dma("sync", lambda e: e.dma_start(
        out=om[:], in_=t["freqs"].rearrange("(c p) -> p c", p=128),
        allow_slow_non_contiguous=True))
    xt_c = 0
    for c in range(HC):
        xt_c = p.dma("sync", lambda e, c=c: e.dma_start(
            out=xnT[:, c * T:(c + 1) * T], in_=t["xn_tm"][:, c * 128:(c + 1) * 128],
            transpose=True))
    om_c = p.cop("vector", lambda e: e.tensor_scalar_mul(om[:], om[:], 0.1),
                 waits=[(dsy, w4)])

    ev_c = {}
    mm_c = {}
    yst_c = [0] * (NFC * NTB)
    blk_v_done = 0
    g = 0
    for fc in range(NFC):
        for tb in range(NTB):
            k = fc * NTB + tb
            for kind in range(3):
                waits = [(dgp, w2), (dsy, xt_c)]
                if g - 2 in ev_c:
                    waits.append((ach, ev_c[g - 2]))

                def em(e, fc=fc, tb=tb, kind=kind, g=g):
                    pp = g % 2
                    t0 = tb * TB
                    out = []
                    for hc in range(HC):
                        if kind == 0:
                            lhsT = wdec[:, hc * F + fc * 128: hc * F + fc * 128 + 128]
                        elif kind == 1:
                            lhsT = wspec[:, hc * 2 * F + fc * 128: hc * 2 * F + fc * 128 + 128]
                        else:
                            lhsT = wspec[:, hc * 2 * F + (fc + NFC) * 128:
                                         hc * 2 * F + (fc + NFC) * 128 + 128]
                        for q in range(NQ):
                            out.append(e.matmul(
                                ps[:, pp * 2048 + q * 512: pp * 2048 + (q + 1) * 512],
                                lhsT,
                                xnT[:, hc * T + t0 + q * 512: hc * T + t0 + (q + 1) * 512],
                                start=(hc == 0), stop=(hc == HC - 1),
                                skip_group_check=True))
                    return out
                mm_c[g] = p.mmg(em, waits)
                pp = g % 2
                psrc = ps[:, pp * 2048:(pp + 1) * 2048]
                if kind == 0:
                    ev_c[g] = p.cop("scalar", lambda e, psrc=psrc, fc=fc: e.activation(
                        out=dt_[:], in_=psrc, func=AF.Sigmoid, bias=bdec[:, fc:fc + 1]),
                        waits=[(mms, mm_c[g]), (vch, blk_v_done), (dsy, w3)])
                else:
                    dst = ur if kind == 1 else ui
                    ev_c[g] = p.cop("scalar", lambda e, psrc=psrc, dst=dst: e.copy(
                        out=dst[:], in_=psrc), waits=[(mms, mm_c[g]), (vch, blk_v_done)])
                g += 1
            evs_done = ev_c[g - 1]

            it_c = p.cop("gpsimd", lambda e, tb=tb: e.iota(
                trow[:], pattern=[[1, TB]], base=tb * TB, channel_multiplier=0,
                allow_small_or_imprecise_dtypes=True), waits=[(vch, blk_v_done)])

            V = lambda f, w=(): p.cop("vector", f, waits=w)
            V(lambda e, fc=fc: e.tensor_scalar_mul(t1[:], trow[:], om[:, fc:fc + 1]),
              [(pch, it_c), (vch, om_c)])
            V(lambda e: e.tensor_scalar_mul(t2[:], t1[:], 1.0 / TWO_PI))
            V(lambda e: e.tensor_copy(gi_[:].bitcast(I32), t2[:]))
            V(lambda e: e.tensor_copy(t2[:], gi_[:].bitcast(I32)))
            pr = V(lambda e: e.scalar_tensor_tensor(t1[:], t2[:], -TWO_PI, t1[:], ALU.mult, ALU.add))
            sin_c = p.cop("scalar", lambda e: e.activation(out=ss[:], in_=t1[:], func=AF.Sin),
                          waits=[(vch, pr)])
            V(lambda e, fc=fc: e.tensor_scalar(t1[:], trow[:], om[:, fc:fc + 1], math.pi / 2,
                                               ALU.mult, ALU.add), [(ach, sin_c)])
            V(lambda e: e.tensor_scalar_mul(t2[:], t1[:], 1.0 / TWO_PI))
            V(lambda e: e.tensor_copy(gi_[:].bitcast(I32), t2[:]))
            V(lambda e: e.tensor_copy(t2[:], gi_[:].bitcast(I32)))
            pr2 = V(lambda e: e.scalar_tensor_tensor(t1[:], t2[:], -TWO_PI, t1[:], ALU.mult, ALU.add))
            cos_c = p.cop("scalar", lambda e: e.activation(out=cc[:], in_=t1[:], func=AF.Sin),
                          waits=[(vch, pr2)])
            V(lambda e: e.tensor_tensor(t1[:], cc[:], ur[:], ALU.mult), [(ach, cos_c), (ach, evs_done)])
            V(lambda e: e.tensor_tensor(t2[:], ss[:], ui[:], ALU.mult))
            V(lambda e: e.tensor_tensor(t1[:], t1[:], t2[:], ALU.add))
            V(lambda e: e.tensor_tensor(t2[:], cc[:], ui[:], ALU.mult))
            V(lambda e: e.tensor_tensor(ui[:], ss[:], ur[:], ALU.mult))
            V(lambda e: e.tensor_tensor(t2[:], t2[:], ui[:], ALU.subtract))
            if tb == 0:
                V(lambda e, fc=fc: e.memset(carry[:, fc:fc + 1], 0.0))
                V(lambda e, fc=fc: e.memset(carry[:, NFC + fc:NFC + fc + 1], 0.0))
            V(lambda e, fc=fc: e.tensor_tensor_scan(ur[:], dt_[:], t1[:], carry[:, fc:fc + 1],
                                                    ALU.mult, ALU.add))
            V(lambda e, fc=fc: e.tensor_tensor_scan(gi_[:], dt_[:], t2[:], carry[:, NFC + fc:NFC + fc + 1],
                                                    ALU.mult, ALU.add))
            V(lambda e, fc=fc: e.tensor_copy(carry[:, fc:fc + 1], ur[:, TB - 1:TB]))
            V(lambda e, fc=fc: e.tensor_copy(carry[:, NFC + fc:NFC + fc + 1], gi_[:, TB - 1:TB]))
            ystp = yst_c[k - 1] if k > 0 else 0
            V(lambda e: e.tensor_tensor(t1[:], cc[:], ur[:], ALU.mult), [(dsy, ystp)])
            V(lambda e: e.tensor_tensor(t2[:], ss[:], gi_[:], ALU.mult))
            V(lambda e: e.tensor_tensor(t1[:], t1[:], t2[:], ALU.subtract))
            hr_c = V(lambda e: e.tensor_copy(ybf[:, 0:TB], t1[:]))
            V(lambda e: e.tensor_tensor(t1[:], ss[:], ur[:], ALU.mult))
            V(lambda e: e.tensor_tensor(t2[:], cc[:], gi_[:], ALU.mult))
            V(lambda e: e.tensor_tensor(t1[:], t1[:], t2[:], ALU.add))
            hi_c = V(lambda e: e.tensor_copy(ybf[:, TB:2 * TB], t1[:]))
            blk_v_done = hi_c
            p.dma("sync", lambda e, fc=fc, tb=tb: e.dma_start(
                out=t["y_fm"][fc * 128:(fc + 1) * 128, tb * TB:(tb + 1) * TB],
                in_=ybf[:, 0:TB]), waits=[(vch, hr_c)])
            yst_c[k] = p.dma("sync", lambda e, fc=fc, tb=tb: e.dma_start(
                out=t["y_fm"][(NFC + fc) * 128:(NFC + fc + 1) * 128, tb * TB:(tb + 1) * TB],
                in_=ybf[:, TB:2 * TB]), waits=[(vch, hi_c)])

    p.barrier()
    stA.close()
    stB = ExitStack()
    alloc = mk_alloc(stB)
    wout = alloc("p4_wout", [128, HC * D], BF16)
    ygrp = alloc("p4_ygrp", [128, HC * 512], BF16)
    aot = alloc("p4_ao", [128, 2 * D], F32)
    po = alloc("p4_po", [128, 2 * D], F32, psum=True)
    wo_c = p.dma("gpsimd", lambda e: e.dma_start(
        out=wout[:].rearrange("p (c f) -> p c f", c=HC),
        in_=t["W_out"].rearrange("(c p) f -> p c f", p=128)))
    NG = 8
    ao_c = [0] * NT
    mo_c = [0] * NT
    ast_c = [0] * NT
    yld_c = [0] * NG
    for grp in range(NG):
        for c in range(HC):
            yld_c[grp] = p.dma("scalar", lambda e, grp=grp, c=c: e.dma_start(
                out=ygrp[:, c * 512:(c + 1) * 512],
                in_=t["y_fm"][c * 128:(c + 1) * 128, grp * 512:(grp + 1) * 512]),
                waits=[(p.mmsem, mo_c[grp * 4 - 1])] if grp > 0 else [])
        for tt in range(4):
            ti = grp * 4 + tt
            pp = ti % 2

            def em(e, grp=grp, tt=tt, pp=pp):
                out = []
                for c in range(HC):
                    for n in range(2):
                        out.append(e.matmul(
                            po[:, pp * D + n * 512: pp * D + (n + 1) * 512],
                            ygrp[:, c * 512 + tt * 128: c * 512 + tt * 128 + 128],
                            wout[:, c * D + n * 512: c * D + (n + 1) * 512],
                            start=(c == 0), stop=(c == HC - 1)))
                return out
            waits = [(p.dsem["scalar"], yld_c[grp]), (dgp, wo_c)]
            if ti >= 2:
                waits.append((vch, ao_c[ti - 2]))
            mo_c[ti] = p.mmg(em, waits)
            ao_c[ti] = p.cop("vector", lambda e, pp=pp: e.tensor_copy(
                aot[:, pp * D:(pp + 1) * D], po[:, pp * D:(pp + 1) * D]),
                waits=[(p.mmsem, mo_c[ti])] + ([(dsy, ast_c[ti - 2])] if ti >= 2 else []))
            ast_c[ti] = p.dma("sync", lambda e, ti=ti, pp=pp: e.dma_start(
                out=t["attn0"][ti * 128:(ti + 1) * 128, :], in_=aot[:, pp * D:(pp + 1) * D]),
                waits=[(vch, ao_c[ti])])
    p.barrier()
    stB.close()




def phase_routing(p, t, gl, mk_alloc):
    from contextlib import ExitStack
    vch, ach, pch = p.chain["vector"], p.chain["scalar"], p.chain["gpsimd"]
    dsy, dgp, dsc = p.dsem["sync"], p.dsem["gpsimd"], p.dsem["scalar"]
    mms = p.mmsem
    st = ExitStack()
    alloc = mk_alloc(st)

    gw = alloc("p2_gw", [128, HC * E], F32)
    ghat32 = alloc("p2_ghat32", [128, HC * E], F32)
    ghat = alloc("p2_ghat", [128, HC * E], BF16)
    onesc = alloc("p2_ones", [128, 1], F32)
    onesrow = alloc("p2_onesrow", [1, 128], F32)
    iotaB = alloc("p2_iotaB", [128, E], F32)
    xf = alloc("p2_xf", [128, 2 * T], BF16)
    xf32 = alloc("p2_xf32", [128, 2 * 2048], F32)
    lhatE = alloc("p2_lhatE", [32, T], F32)
    mT = alloc("p2_mT", [32, T], F32)
    cum = alloc("p2_cum", [32, T], F32)
    zerot = alloc("p2_zero", [32, T], F32)
    ltok = alloc("p2_ltok", [128, 4 * E], F32)
    maxv = alloc("p2_maxv", [128, 8], F32)
    mi = alloc("p2_mi", [128, 8], U32)
    member = alloc("p2_member", [128, E], F32)
    oh = alloc("p2_oh", [128, E], F32)
    counts_s = alloc("p2_counts", [32, 8], F32)
    countsT = alloc("p2_countsT", [8, 32], F32)
    sel_s = alloc("p2_sel", [8, 1], F32)
    srow = alloc("p2_srow", [1, 32], F32)
    offB = alloc("p2_offB", [128, E], F32)
    Sb = alloc("p2_Sb", [128, E], F32)
    off_s = alloc("p2_off", [32, 1], F32)
    Ssum = alloc("p2_Ssum", [32, 1], F32)
    ranktok = alloc("p2_ranktok", [128, E], F32)
    sc1 = alloc("p2_sc1", [128, E], F32)
    sc14 = alloc("p2_sc14", [128, 4 * E], F32)
    cols = alloc("p2_cols", [128, 20], F32)
    ps_l = alloc("p2_psl", [32, 2048], F32, psum=True)
    ps_t = alloc("p2_pst", [128, 128], F32, psum=True)
    ps_t2 = alloc("p2_pst2", [32, 128], F32, psum=True)
    ps_c = alloc("p2_psc", [32, 8], F32, psum=True)
    ps_b = alloc("p2_psb", [128, E], F32, psum=True)
    mif_all = gl["mif_all"]
    ltok32_all = gl["ltok32_all"]
    idx_all = gl["idx_all"]
    gates_all = gl["gates_all"]
    rstd0 = gl["rstd0"]
    ident = gl["ident"]
    misc_d = t["misc_d"]

    c1 = p.cop("vector", lambda e: e.memset(onesc[:], 1.0))
    c3 = p.cop("vector", lambda e: e.memset(onesrow[:], 1.0))
    c4 = p.cop("vector", lambda e: e.memset(zerot[:], 0.0))
    c5 = p.cop("gpsimd", lambda e: e.iota(iotaB[:], pattern=[[1, E]], base=0,
                                          channel_multiplier=0,
                                          allow_small_or_imprecise_dtypes=True))
    gwl = p.dma("sync", lambda e: e.dma_start(
        out=gw[:].rearrange("p (c e) -> p c e", c=HC),
        in_=t["gate_wT"].rearrange("(c p) e -> p c e", p=128)))
    sell = p.dma("sync", lambda e: e.dma_start(out=sel_s[:], in_=t["sel"]))

    def em_ss(e):
        out = []
        for hc in range(HC):
            out.append(e.matmul(ps_t[0:32, 0:1], gw[:, hc * E:(hc + 1) * E], onesc[:],
                                start=(hc == 0), stop=(hc == HC - 1)))
        return out
    m1 = p.mmg(em_ss, [(dsy, gwl), (vch, c1)])
    sS = p.cop("scalar", lambda e: e.copy(out=Ssum[:], in_=ps_t[0:32, 0:1]),
               waits=[(mms, m1)])
    b1 = p.dma("gpsimd", lambda e: e.dma_start(out=misc_d[0:32], in_=Ssum[:, 0]),
               waits=[(ach, sS)])
    b2 = p.dma("gpsimd", lambda e: e.dma_start(
        out=srow[:], in_=misc_d[0:32].rearrange("(a e) -> a e", a=1)),
        waits=[(dgp, b1)])
    m2 = p.mmg(lambda e: e.matmul(ps_b[:], onesrow[:], srow[:], start=True, stop=True),
               [(dgp, b2), (vch, c3)])
    sb_c = p.cop("scalar", lambda e: e.copy(out=Sb[:], in_=ps_b[:]), waits=[(mms, m2)])
    gh_c = 0
    for hc in range(HC):
        p.cop("vector", lambda e, hc=hc: e.scalar_tensor_tensor(
            ghat32[:, hc * E:(hc + 1) * E], Sb[:], -1.0 / D, gw[:, hc * E:(hc + 1) * E],
            ALU.mult, ALU.add), waits=[(ach, sb_c)])
        gh_c = p.cop("vector", lambda e, hc=hc: e.tensor_copy(
            ghat[:, hc * E:(hc + 1) * E], ghat32[:, hc * E:(hc + 1) * E]))

    mm_prev = [0, 0]
    ev_last = [0]
    ev_half = [0, 0]
    lhat_free = 0
    evac_done = 0
    cnt_mm = 0

    def ranking_pass(pss, src_fm, fp32):
        nonlocal mm_prev, ev_half, lhat_free
        for half in range(2):
            t0 = half * 2048
            for hc in range(HC):
                pp = hc % 2
                if fp32:
                    ldx = p.dma("sync", lambda e, hc=hc, t0=t0, pp=pp: e.dma_start(
                        out=xf32[:, pp * 2048:(pp + 1) * 2048],
                        in_=src_fm[hc * 128:(hc + 1) * 128, t0:t0 + 2048]),
                        waits=[(mms, mm_prev[pp])])
                    ds = dsy
                else:
                    ldx = p.dma("gpsimd", lambda e, hc=hc, t0=t0, pp=pp: e.dma_start(
                        out=xf[:, pp * T + t0: pp * T + t0 + 2048],
                        in_=src_fm[hc * 128:(hc + 1) * 128, t0:t0 + 2048]),
                        waits=[(mms, mm_prev[pp])])
                    ds = dgp

                def em(e, hc=hc, t0=t0, pp=pp, fp32=fp32):
                    out = []
                    for q in range(4):
                        if fp32:
                            rhs = xf32[:, pp * 2048 + q * 512:pp * 2048 + (q + 1) * 512]
                            w = ghat32
                        else:
                            rhs = xf[:, pp * T + t0 + q * 512: pp * T + t0 + (q + 1) * 512]
                            w = ghat
                        out.append(e.matmul(
                            ps_l[0:32, q * 512:(q + 1) * 512],
                            w[:, hc * E:(hc + 1) * E], rhs,
                            start=(hc == 0), stop=(hc == HC - 1)))
                    return out
                mm_prev[pp] = p.mmg(em, [(ds, ldx), (vch, gh_c), (ach, ev_last[0]),
                                         (mms, lhat_free)])
            ev_half[half] = p.cop("scalar", lambda e, t0=t0: e.copy(
                out=lhatE[:, t0:t0 + 2048], in_=ps_l[0:32, :]),
                waits=[(mms, mm_prev[(HC - 1) % 2])])
            ev_last[0] = ev_half[half]

    def tile_transpose(ti):
        nonlocal lhat_free
        lhat_free = p.mmg(lambda e, ti=ti: e.transpose(
            ps_t[:, 0:32], lhatE[0:32, ti * 128:(ti + 1) * 128], ident[0:32, 0:32]),
            [(ach, ev_half[ti // (NT // 2)]), (vch, evac_done)])
        return lhat_free

    def tile_transpose4(t4):
        nonlocal lhat_free
        def em(e, t4=t4):
            out = []
            for j in range(4):
                ti = t4 * 4 + j
                out.append(e.transpose(ps_t[:, j * 32:(j + 1) * 32],
                                       lhatE[0:32, ti * 128:(ti + 1) * 128],
                                       ident[0:32, 0:32]))
            return out
        lhat_free = p.mmg(em, [(ach, ev_half[(t4 * 4) // (NT // 2)]),
                               (ach, ev_half[(t4 * 4 + 3) // (NT // 2)]),
                               (vch, evac_done)])
        return lhat_free

    for b in range(8):
        ranking_pass(b, t["x_fm"][b], False)
        for t4 in range(NT // 4):
            trc = tile_transpose4(t4)
            lt = p.cop("vector", lambda e: e.tensor_copy(ltok[:], ps_t[:]),
                       waits=[(mms, trc)])
            evac_done = lt
            ohs = []
            for j in range(4):
                p.cop("vector", lambda e, j=j: e.max(out=maxv[:], in_=ltok[:, j * E:(j + 1) * E]))
                p.cop("vector", lambda e, j=j: e.max_index(
                    out=mi[:], in_max=maxv[:], in_values=ltok[:, j * E:(j + 1) * E]))
                mifc = p.cop("vector", lambda e, j=j: e.tensor_copy(
                    cols[:, j * 4:j * 4 + 4], mi[:, 0:4]))
            ohc = 0
            for j in range(4):
                for kk in range(4):
                    ohc = p.cop("vector", lambda e, j=j, kk=kk: e.tensor_scalar(
                        sc14[:, (j % 2) * 2 * E + kk // 2 * E:(j % 2) * 2 * E + kk // 2 * E] if False else
                        oh[:], iotaB[:], cols[:, j * 4 + kk:j * 4 + kk + 1], None,
                        ALU.is_equal),
                        waits=[(pch, c5), (mms, cnt_mm)])
                    cnt_mm = p.mmg(lambda e, b=b, t4=t4, j=j, kk=kk: e.matmul(
                        ps_c[:, b:b + 1], oh[:], onesc[:],
                        start=(t4 == 0 and j == 0 and kk == 0),
                        stop=(t4 == NT // 4 - 1 and j == 3 and kk == 3),
                        skip_group_check=True),
                        [(vch, ohc)])

    prev_trm = 0
    ranking_pass(8, t["x_fm_own"], False)
    for ti in range(NT):
        trc = tile_transpose(ti)
        lt = p.cop("vector", lambda e: e.tensor_copy(ltok[:, 0:E], ps_t[:, 0:32]),
                   waits=[(mms, trc)])
        p.cop("vector", lambda e: e.max(out=maxv[:], in_=ltok[:, 0:E]))
        p.cop("vector", lambda e: e.max_index(out=mi[:], in_max=maxv[:], in_values=ltok[:, 0:E]))
        mifc = p.cop("vector", lambda e, ti=ti: e.tensor_copy(
            mif_all[:, ti * 4:ti * 4 + 4], mi[:, 0:4]))
        evac_done = mifc
        for kk in range(4):
            p.cop("vector", lambda e, kk=kk, ti=ti: e.tensor_scalar(
                sc14[:, kk * E:(kk + 1) * E], iotaB[:],
                mif_all[:, ti * 4 + kk:ti * 4 + kk + 1], None, ALU.is_equal),
                waits=[(mms, prev_trm), (pch, c5)])
        mp = p.cop("vector", lambda e: e.tensor_reduce(
            member[:], sc14[:].rearrange("p (k e) -> p e k", k=4), axis=AX.X,
            op=ALU.add))
        trm = p.mmg(lambda e: e.transpose(ps_t2[:], member[:], ident[:]),
                    [(vch, mp), (vch, evac_done)])
        prev_trm = trm
        evac_done = p.cop("vector", lambda e, ti=ti: e.tensor_copy(
            mT[0:32, ti * 128:(ti + 1) * 128], ps_t2[0:32, :]), waits=[(mms, trm)])
    ranking_pass(9, t["x_fm_own"], True)
    for t4 in range(NT // 4):
        trc = tile_transpose4(t4)
        evac_done = p.cop("vector", lambda e, t4=t4: e.tensor_copy(
            ltok32_all[:, t4 * 4 * E:(t4 + 1) * 4 * E], ps_t[:]), waits=[(mms, trc)])

    cev = p.cop("scalar", lambda e: e.copy(out=counts_s[:], in_=ps_c[:]),
                waits=[(mms, cnt_mm)])
    d1 = p.dma("gpsimd", lambda e: e.dma_start(out=misc_d[64:320].rearrange("(a b) -> a b", a=32), in_=counts_s[:]), waits=[(ach, cev)])
    d2 = p.dma("gpsimd", lambda e: e.dma_start(
        out=countsT[:], in_=misc_d[64:320].rearrange("(e c) -> c e", c=8),
        allow_slow_non_contiguous=True), waits=[(dgp, d1)])
    m3 = p.mmg(lambda e: e.matmul(ps_t2[0:32, 0:1], countsT[:], sel_s[:],
                                  start=True, stop=True),
               [(dgp, d2), (dsy, sell), (vch, evac_done)])
    offc = p.cop("scalar", lambda e: e.copy(out=off_s[:], in_=ps_t2[0:32, 0:1]),
                 waits=[(mms, m3)])
    d3 = p.dma("gpsimd", lambda e: e.dma_start(out=misc_d[32:64], in_=off_s[:, 0]),
               waits=[(ach, offc)])
    d4 = p.dma("gpsimd", lambda e: e.dma_start(
        out=srow[:], in_=misc_d[32:64].rearrange("(a e) -> a e", a=1)),
        waits=[(dgp, d3)])
    m4 = p.mmg(lambda e: e.matmul(ps_b[:], onesrow[:], srow[:], start=True, stop=True),
               [(dgp, d4)])
    offbc = p.cop("scalar", lambda e: e.copy(out=offB[:], in_=ps_b[:]), waits=[(mms, m4)])

    sc_c = p.cop("vector", lambda e: e.tensor_tensor_scan(
        cum[0:32, :], mT[0:32, :], zerot[0:32, :], 0.0, ALU.add, ALU.add),
        waits=[(vch, c4)])
    rg_c = p.cop("vector", lambda e: e.tensor_scalar(
        cum[0:32, :], cum[0:32, :], off_s[0:32, 0:1], -1.0, ALU.add, ALU.add),
        waits=[(ach, offc)])

    rt_ev = 0
    for ti in range(NT):
        trr = p.mmg(lambda e, ti=ti: e.transpose(
            ps_t[:, 0:32], cum[0:32, ti * 128:(ti + 1) * 128], ident[0:32, 0:32]),
            [(vch, rg_c), (vch, rt_ev)])
        rt_ev = p.cop("vector", lambda e: e.tensor_copy(ranktok[:], ps_t[:, 0:32]),
                      waits=[(mms, trr)])
        for kk in range(4):
            ohc = p.cop("vector", lambda e, ti=ti, kk=kk: e.tensor_scalar(
                oh[:], iotaB[:], mif_all[:, ti * 4 + kk:ti * 4 + kk + 1], None,
                ALU.is_equal), waits=[(vch, rt_ev), (ach, offbc)])
            p.cop("vector", lambda e: e.tensor_tensor(sc1[:], oh[:], ranktok[:], ALU.mult))
            p.cop("vector", lambda e, kk=kk: e.reduce_sum(
                cols[:, 4 + kk:5 + kk], sc1[:], axis=AX.X))
            p.cop("vector", lambda e: e.tensor_tensor(sc1[:], oh[:], offB[:], ALU.mult))
            p.cop("vector", lambda e, kk=kk: e.reduce_sum(
                cols[:, 8 + kk:9 + kk], sc1[:], axis=AX.X))
            p.cop("vector", lambda e, ti=ti: e.tensor_tensor(
                sc1[:], oh[:], ltok32_all[:, ti * E:(ti + 1) * E], ALU.mult))
            p.cop("vector", lambda e, kk=kk: e.reduce_sum(
                cols[:, 16 + kk:17 + kk], sc1[:], axis=AX.X))
        for kk in range(4):
            vd = p.cop("vector", lambda e, kk=kk: e.tensor_single_scalar(
                cols[:, 12 + kk:13 + kk], cols[:, 4 + kk:5 + kk], float(CAP), ALU.is_lt))
            sl = p.cop("vector", lambda e, kk=kk: e.tensor_tensor(
                cols[:, 8 + kk:9 + kk], cols[:, 4 + kk:5 + kk], cols[:, 8 + kk:9 + kk],
                ALU.subtract))
            ixf = p.cop("vector", lambda e, ti=ti, kk=kk: e.scalar_tensor_tensor(
                cols[:, 4 + kk:5 + kk], mif_all[:, ti * 4 + kk:ti * 4 + kk + 1],
                float(CAP), cols[:, 8 + kk:9 + kk], ALU.mult, ALU.add))
            nv = p.cop("vector", lambda e, kk=kk: e.tensor_scalar(
                cols[:, 0 + kk:1 + kk], cols[:, 12 + kk:13 + kk], -1.0, 1.0,
                ALU.mult, ALU.add))
            p.cop("vector", lambda e, kk=kk: e.scalar_tensor_tensor(
                cols[:, 4 + kk:5 + kk], cols[:, 0 + kk:1 + kk], BIGIDX,
                cols[:, 4 + kk:5 + kk], ALU.mult, ALU.add))
            p.cop("vector", lambda e, kk=kk: e.tensor_scalar(
                cols[:, 4 + kk:5 + kk], cols[:, 4 + kk:5 + kk], BIGIDX, 0.0,
                ALU.min, ALU.max))
        p.cop("vector", lambda e, ti=ti: e.tensor_copy(
            idx_all[:, ti * 4:ti * 4 + 4], cols[:, 4:8]))
        p.cop("vector", lambda e, ti=ti: e.tensor_scalar_mul(
            cols[:, 8:12], cols[:, 16:20], rstd0[:, ti:ti + 1]))
        sb_ = p.cop("vector", lambda e: e.tensor_scalar_sub(
            cols[:, 9:12], cols[:, 9:12], cols[:, 8:9]))
        ex = p.cop("scalar", lambda e: e.activation(
            out=cols[:, 9:12], in_=cols[:, 9:12], func=AF.Exp), waits=[(vch, sb_)])
        s1 = p.cop("vector", lambda e: e.memset(cols[:, 8:9], 1.0), waits=[(ach, ex)])
        sm = p.cop("vector", lambda e: e.reduce_sum(cols[:, 0:1], cols[:, 8:12], axis=AX.X))
        rq = p.cop("vector", lambda e: e.reciprocal(cols[:, 0:1], cols[:, 0:1]))
        p.cop("vector", lambda e: e.tensor_scalar_mul(
            cols[:, 8:12], cols[:, 8:12], cols[:, 0:1]))
        p.cop("vector", lambda e, ti=ti: e.tensor_tensor(
            gates_all[:, ti * 4:ti * 4 + 4], cols[:, 8:12], cols[:, 12:16], ALU.mult))
    kc1_ = p.cop("vector", lambda e: e.reduce_sum(sc1[0:32, 0:1], mT[0:32, :], axis=AX.X),
                 waits=[(vch, evac_done)])
    kc1_ = p.cop("vector", lambda e: e.tensor_scalar(
        sc1[0:32, 1:2], off_s[:], -1.0, float(CAP), ALU.mult, ALU.add),
        waits=[(ach, offc)])
    kc1_ = p.cop("vector", lambda e: e.tensor_tensor(
        sc1[0:32, 1:2], sc1[0:32, 1:2], sc1[0:32, 0:1], ALU.min))
    kc1_ = p.cop("vector", lambda e: e.tensor_scalar_max(sc1[0:32, 1:2], sc1[0:32, 1:2], 0.0))
    kb1 = p.dma("gpsimd", lambda e: e.dma_start(out=misc_d[384:416], in_=sc1[0:32, 1]),
                waits=[(vch, kc1_)])
    kb2 = p.dma("gpsimd", lambda e: e.dma_start(
        out=srow[:], in_=misc_d[384:416].rearrange("(a e) -> a e", a=1)),
        waits=[(dgp, kb1)])
    p.cop("vector", lambda e: e.tensor_copy(gl["kept"][:], srow[:]),
          waits=[(dgp, kb2)])

    p.barrier()
    st.close()


def phase_scatter(p, t, gl, mk_alloc):
    from contextlib import ExitStack
    vch = p.chain["vector"]
    dsy, dgp = p.dsem["sync"], p.dsem["gpsimd"]
    st = ExitStack()
    alloc = mk_alloc(st)
    xnt = alloc("p3_xn", [128, 2 * D], BF16)
    idx_all = gl["idx_all"]
    sc_c = [0] * NT
    hold = {}

    def mkreg(e):
        r = e.alloc_register("p3bnd")
        hold["r"] = r
        return e.reg_mov(r, E * CAP - 1)
    p._record("gpsimd", mkreg, [], None, 0)
    for ti in range(NT):
        bx = (ti % 2) * D
        w = [(dgp, sc_c[ti - 2])] if ti >= 2 else []
        ld = p.dma("sync", lambda e, ti=ti, bx=bx: e.dma_start(
            out=xnt[:, bx:bx + D], in_=t["xn_tm"][ti * 128:(ti + 1) * 128, :]), waits=w)
        for kk in range(4):
            sc_c[ti] = p.dma("gpsimd", lambda e, ti=ti, kk=kk, bx=bx: e.indirect_dma_start(
                out=t["buf"], out_offset=IndirectOffsetOnAxis(
                    ap=idx_all[:, ti * 4 + kk:ti * 4 + kk + 1], axis=0),
                in_=xnt[:, bx:bx + D], in_offset=None,
                bounds_check=hold["r"], oob_is_err=False),
                waits=[(dsy, ld)])
    fl = p.dma("gpsimd", lambda e: e.dma_start(out=xnt[:, 0:D], in_=t["buf"][0:128, :]),
               waits=[(dgp, sc_c[NT - 1])])
    p.wait_only("gpsimd", [(dgp, fl)])
    p.barrier()
    st.close()


def phase_shared(p, t, gl, mk_alloc):
    from contextlib import ExitStack
    vch, ach = p.chain["vector"], p.chain["scalar"]
    dsy, dgp = p.dsem["sync"], p.dsem["gpsimd"]
    mms = p.mmsem
    st = ExitStack()
    alloc = mk_alloc(st)
    xnT = alloc("p5_xnT", [128, HC * T], BF16)
    wgu = alloc("p5_wgu", [128, HC * D], BF16)
    sg = alloc("p5_sg", [128, 2 * 512], F32)
    as2 = alloc("p5_as", [128, 2 * 512], BF16)
    ps = alloc("p5_ps", [128, 4 * 512], F32, psum=True)
    ast_c = {}

    wl = p.dma("gpsimd", lambda e: e.dma_start(
        out=wgu[:].rearrange("p (c f) -> p c f", c=HC),
        in_=t["shared_w_gate_up"].rearrange("(c p) f -> p c f", p=128)))
    xt_c = 0
    for c in range(HC):
        xt_c = p.dma("sync", lambda e, c=c: e.dma_start(
            out=xnT[:, c * T:(c + 1) * T], in_=t["xn_tm"][:, c * 128:(c + 1) * 128],
            transpose=True))
    ev = {}
    mmc = {}
    g = 0
    for i in range(4):
        for q in range(8):
            for gu in range(2):
                pp = g % 4
                ic = i if gu == 0 else i + 4

                def em(e, ic=ic, q=q, pp=pp):
                    out = []
                    for hc in range(HC):
                        out.append(e.matmul(
                            ps[:, pp * 512:(pp + 1) * 512],
                            wgu[:, hc * D + ic * 128: hc * D + ic * 128 + 128],
                            xnT[:, hc * T + q * 512: hc * T + (q + 1) * 512],
                            start=(hc == 0), stop=(hc == HC - 1)))
                    return out
                w = [(dgp, wl), (dsy, xt_c)]
                if g - 4 in ev:
                    w.append((ev[g - 4][0], ev[g - 4][1]))
                mmc[g] = p.mmg(em, w)
                if gu == 0:
                    w2 = [(mms, mmc[g])]
                    if g - 3 in ev:
                        w2.append(ev[g - 3])
                    ev[g] = (ach, p.cop("scalar", lambda e, q=q, pp=pp: e.activation(
                        out=sg[:, (q % 2) * 512:(q % 2 + 1) * 512],
                        in_=ps[:, pp * 512:(pp + 1) * 512], func=AF.Silu),
                        waits=w2))
                else:
                    wv = [(mms, mmc[g]), ev[g - 1]]
                    if (i, q - 4) in ast_c:
                        wv.append((dsy, ast_c[(i, q - 4)]))
                    vc = p.cop("vector", lambda e, i=i, q=q, pp=pp: e.tensor_tensor(
                        as2[:, (q % 2) * 512:(q % 2 + 1) * 512],
                        sg[:, (q % 2) * 512:(q % 2 + 1) * 512],
                        ps[:, pp * 512:(pp + 1) * 512], ALU.mult),
                        waits=wv)
                    ev[g] = (vch, vc)
                    ast_c[(i, q)] = p.dma("sync", lambda e, i=i, q=q: e.dma_start(
                        out=t["act_sh"][i * 128:(i + 1) * 128, q * 512:(q + 1) * 512],
                        in_=as2[:, (q % 2) * 512:(q % 2 + 1) * 512]),
                        waits=[(vch, vc)])
                g += 1
    p.barrier()
    st.close()


def phase_experts(p, t, gl, mk_alloc):
    from contextlib import ExitStack
    vch, ach = p.chain["vector"], p.chain["scalar"]
    dsy, dgp, dsc = p.dsem["sync"], p.dsem["gpsimd"], p.dsem["scalar"]
    mms = p.mmsem
    st = ExitStack()
    alloc = mk_alloc(st)
    wguE = alloc("p6_wgu", [128, 2 * HC * D], BF16)
    wdnE = alloc("p6_wdn", [128, 2 * 4 * D], BF16)
    bufT = alloc("p6_bufT", [128, 2 * HC * CAP], BF16)
    sgE = alloc("p6_sg", [128, 4 * CAP], F32)
    actE = alloc("p6_act", [128, 4 * CAP], BF16)
    eot = alloc("p6_eot", [128, 2 * D], BF16)
    ps = alloc("p6_ps", [128, 4 * 512], F32, psum=True)
    psd = alloc("p6_psd", [128, 2 * D], F32, psum=True)

    wld = [0, 0]
    bld = [0, 0]
    gu_done = [0, 0]
    dn_done = [0, 0]
    eo_st = {}
    act_used = 0
    for ei in range(E):
        pp = ei % 2
        wld[pp] = p.dma("gpsimd", lambda e, ei=ei, pp=pp: e.dma_start(
            out=wguE[:, pp * HC * D:(pp + 1) * HC * D].rearrange("p (c f) -> p c f", c=HC),
            in_=t["w_gate_up"][ei].rearrange("(c p) f -> p c f", p=128)),
            waits=[(mms, gu_done[pp])])
        wld[pp] = p.dma("gpsimd", lambda e, ei=ei, pp=pp: e.dma_start(
            out=wdnE[:, pp * 4 * D:(pp + 1) * 4 * D].rearrange("p (c f) -> p c f", c=4),
            in_=t["w_down"][ei].rearrange("(c p) f -> p c f", p=128)),
            waits=[(mms, dn_done[pp])])
        for hc in range(HC):
            bld[pp] = p.dma("sync", lambda e, ei=ei, hc=hc, pp=pp: e.dma_start(
                out=bufT[:, pp * HC * CAP + hc * CAP: pp * HC * CAP + (hc + 1) * CAP],
                in_=t["buf"][ei * CAP:(ei + 1) * CAP, hc * 128:(hc + 1) * 128],
                transpose=True), waits=[(mms, gu_done[pp])])
        if ei == 0:
            ev_prev = {}
        else:
            ev_prev = {i - 8: v for i, v in ev_prev.items() if i >= 6}
        last_gu = 0
        for i in range(8):
            qq = i % 2

            def em(e, ei=ei, i=i, pp=pp, qq=qq):
                out = []
                for hc in range(HC):
                    for q in range(2):
                        out.append(e.matmul(
                            ps[:, qq * 1024 + q * 512: qq * 1024 + (q + 1) * 512],
                            wguE[:, pp * HC * D + hc * D + i * 128: pp * HC * D + hc * D + i * 128 + 128],
                            bufT[:, pp * HC * CAP + hc * CAP + q * 512: pp * HC * CAP + hc * CAP + (q + 1) * 512],
                            start=(hc == 0), stop=(hc == HC - 1),
                            skip_group_check=True))
                return out
            w = [(dgp, wld[pp]), (dsy, bld[pp])]
            if i - 2 in ev_prev:
                w.append(ev_prev[i - 2])
            last_gu = p.mmg(em, w)
            if i < 4:
                ev_prev[i] = (ach, p.cop("scalar", lambda e, i=i, qq=qq: e.activation(
                    out=sgE[:, i * CAP: (i + 1) * CAP],
                    in_=ps[:, qq * 1024:(qq + 1) * 1024], func=AF.Silu),
                    waits=[(mms, last_gu), (vch, act_used)]))
            else:
                ev_prev[i] = (vch, p.cop("vector", lambda e, i=i, qq=qq: e.tensor_tensor(
                    actE[:, (i - 4) * CAP: (i - 3) * CAP],
                    sgE[:, (i - 4) * CAP: (i - 3) * CAP],
                    ps[:, qq * 1024:(qq + 1) * 1024], ALU.mult),
                    waits=[(mms, last_gu), ev_prev[i - 4], (vch, act_used)]))
        gu_done[pp] = last_gu
        last_dn = 0
        for sti in range(8):
            qq = sti % 2

            def em(e, ei=ei, sti=sti, pp=pp, qq=qq):
                out = []
                for ic in range(4):
                    for n in range(2):
                        out.append(e.matmul(
                            psd[:, qq * D + n * 512: qq * D + (n + 1) * 512],
                            actE[:, ic * CAP + sti * 128: ic * CAP + sti * 128 + 128],
                            wdnE[:, pp * 4 * D + ic * D + n * 512: pp * 4 * D + ic * D + (n + 1) * 512],
                            start=(ic == 0), stop=(ic == 3),
                            skip_group_check=True))
                return out
            if ei == 0 and sti == 0:
                eo_ev = [0] * 8
            w = [ev_prev[7], (dgp, wld[pp])]
            w.append((vch, eo_ev[sti - 2]))
            last_dn = p.mmg(em, w)

            wv = [(mms, last_dn)]
            if sti >= 2:
                wv.append((dsy, eo_st.get(sti - 2, 0)))
            eo_ev[sti] = p.cop("vector", lambda e, sti=sti, qq=qq: e.tensor_copy(
                eot[:, qq * D:(qq + 1) * D], psd[:, qq * D:(qq + 1) * D]), waits=wv)
            eo_st[sti] = p.dma("sync", lambda e, ei=ei, sti=sti, qq=qq: e.dma_start(
                out=t["eo"][ei * CAP + sti * 128: ei * CAP + (sti + 1) * 128, :],
                in_=eot[:, qq * D:(qq + 1) * D]), waits=[(p.chain["vector"], eo_ev[sti])])
        dn_done[pp] = last_dn
        act_used = eo_ev[7]
    p.barrier()
    st.close()


def phase_epilogue(p, t, gl, mk_alloc):
    from contextlib import ExitStack
    vch, ach = p.chain["vector"], p.chain["scalar"]
    dsy, dgp = p.dsem["sync"], p.dsem["gpsimd"]
    mms = p.mmsem
    st = ExitStack()
    alloc = mk_alloc(st)
    wdn = alloc("p7_wdn", [128, 4 * D], BF16)
    gt = alloc("p7_gt", [128, 8 * D], BF16)
    xa = alloc("p7_xa", [128, 2 * D], F32)
    at = alloc("p7_at", [128, 2 * D], F32)
    racc = alloc("p7_racc", [128, D], F32)
    x1t = alloc("p7_x1", [128, 2 * D], F32)
    actg = alloc("p7_actg", [128, 4 * 512], BF16)
    psd = alloc("p7_psd", [128, 2 * D], F32, psum=True)
    idx_all = gl["idx_all"]
    gates_all = gl["gates_all"]

    wl = p.dma("gpsimd", lambda e: e.dma_start(
        out=wdn[:].rearrange("p (c f) -> p c f", c=4),
        in_=t["shared_w_down"].rearrange("(c p) f -> p c f", p=128)))
    gz = p.cop("vector", lambda e: e.memset(gt[:], 0.0))
    hold = {}

    def mkreg(e):
        r = e.alloc_register("p7bnd")
        hold["r"] = r
        return e.reg_mov(r, E * CAP - 1)
    p._record("gpsimd", mkreg, [], None, 0)
    x1_st = [0] * NT
    r_done = [0] * NT
    mm_c = [0] * NT
    x1_c = [0] * NT
    agl = 0
    for ti in range(NT):
        pp = ti % 2
        if ti % 4 == 0:
            grp = ti // 4
            wg = [(mms, mm_c[ti - 1])] if ti else []
            for ic in range(4):
                agl = p.dma("sync", lambda e, grp=grp, ic=ic: e.dma_start(
                    out=actg[:, ic * 512:(ic + 1) * 512],
                    in_=t["act_sh"][ic * 128:(ic + 1) * 128, grp * 512:(grp + 1) * 512]),
                    waits=wg)
        gld = 0
        gb = (ti % 2) * 4 * D
        for kk in range(4):
            gld = p.dma("gpsimd", lambda e, ti=ti, kk=kk, gb=gb: e.indirect_dma_start(
                out=gt[:, gb + kk * D: gb + (kk + 1) * D], out_offset=None,
                in_=t["eo"], in_offset=IndirectOffsetOnAxis(
                    ap=idx_all[:, ti * 4 + kk:ti * 4 + kk + 1], axis=0),
                bounds_check=hold["r"], oob_is_err=False),
                waits=[(vch, r_done[ti - 2] if ti >= 2 else gz), (dgp, wl)])
        wre = [(vch, x1_c[ti - 2])] if ti >= 2 else []
        xld = p.dma("sync", lambda e, ti=ti, pp=pp: e.dma_start(
            out=xa[:, pp * D:(pp + 1) * D], in_=t["x_tm"][ti * 128:(ti + 1) * 128, :]),
            waits=wre)
        ald = p.dma("sync", lambda e, ti=ti, pp=pp: e.dma_start(
            out=at[:, pp * D:(pp + 1) * D], in_=t["attn0"][ti * 128:(ti + 1) * 128, :]),
            waits=wre)

        def em(e, ti=ti, pp=pp):
            out = []
            tt = ti % 4
            for ic in range(4):
                for n in range(2):
                    out.append(e.matmul(
                        psd[:, pp * D + n * 512: pp * D + (n + 1) * 512],
                        actg[:, ic * 512 + tt * 128: ic * 512 + tt * 128 + 128],
                        wdn[:, ic * D + n * 512: ic * D + (n + 1) * 512],
                        start=(ic == 0), stop=(ic == 3)))
            return out
        w = [(dgp, wl), (dsy, agl)]
        if ti >= 2:
            w.append((vch, x1_c[ti - 2]))
        mm_c[ti] = p.mmg(em, w)
        p.cop("vector", lambda e, ti=ti, gb=gb: e.tensor_scalar_mul(
            racc[:], gt[:, gb:gb + D], gates_all[:, ti * 4:ti * 4 + 1]),
            waits=[(dgp, gld)])
        for kk in range(1, 4):
            p.cop("vector", lambda e, ti=ti, kk=kk, gb=gb: e.scalar_tensor_tensor(
                racc[:], gt[:, gb + kk * D: gb + (kk + 1) * D],
                gates_all[:, ti * 4 + kk:ti * 4 + kk + 1], racc[:],
                ALU.mult, ALU.add))
        r_done[ti] = p.chain["vector"].cnt
        p.cop("vector", lambda e, pp=pp: e.tensor_tensor(
            x1t[:, pp * D:(pp + 1) * D], xa[:, pp * D:(pp + 1) * D],
            at[:, pp * D:(pp + 1) * D], ALU.add), waits=[(dsy, ald)])
        p.cop("vector", lambda e, pp=pp: e.tensor_tensor(
            x1t[:, pp * D:(pp + 1) * D], x1t[:, pp * D:(pp + 1) * D], racc[:], ALU.add))
        x1_c[ti] = p.cop("vector", lambda e, pp=pp: e.tensor_tensor(
            x1t[:, pp * D:(pp + 1) * D], x1t[:, pp * D:(pp + 1) * D],
            psd[:, pp * D:(pp + 1) * D], ALU.add), waits=[(mms, mm_c[ti])])
        x1_st[ti] = p.dma("sync", lambda e, ti=ti, pp=pp: e.dma_start(
            out=t["x1_tm"][ti * 128:(ti + 1) * 128, :], in_=x1t[:, pp * D:(pp + 1) * D]),
            waits=[(vch, x1_c[ti])])
    p.barrier()
    st.close()


def build_program(phases=("p1", "p2", "p3", "p4", "p5", "p6", "p7", "p8", "p9", "p10"), dbg=()):
    from contextlib import ExitStack
    nc = bass.Bass("TRN2", target_bir_lowering=False)
    t = declare_io(nc)
    for d in dbg:
        if d.startswith("sb:"):
            continue
        t["dbg_" + d] = nc.dram_tensor("dbg_" + d, list(t[d].shape),
                                       t[d].dtype, kind="ExternalOutput").ap()
    stack = ExitStack()
    with stack:
        block = stack.enter_context(nc.Block())
        p = Prog(nc, block, stack)

        def mk_alloc(st):
            def alloc(name, shape, dtype=F32, psum=False):
                if psum:
                    return st.enter_context(nc.psum_tensor(name, shape, dtype))
                return st.enter_context(nc.sbuf_tensor(name, shape, dtype))
            return alloc

        galloc = mk_alloc(stack)
        gl = {}
        gl["rstd0"] = galloc("g_rstd0", [128, NT], F32)
        gl["m0"] = galloc("g_m0", [128, NT], F32)
        gl["rstd1"] = galloc("g_rstd1", [128, NT], F32)
        gl["m1"] = galloc("g_m1", [128, NT], F32)
        gl["ident"] = galloc("g_ident", [128, 128], F32)
        gl["mif_all"] = galloc("g_mif", [128, 4 * NT], F32)
        gl["ltok32_all"] = galloc("g_ltok32", [128, E * NT], F32)
        gl["idx_all"] = galloc("g_idx", [128, 4 * NT], I32)
        gl["gates_all"] = galloc("g_gates", [128, 4 * NT], F32)
        gl["sc32"] = galloc("g_sc32", [128, 2], F32)
        gl["kept"] = galloc("g_kept", [1, E], I32)
        p.cst = galloc("g_cst", [128, 4], F32)
        p.cop("vector", lambda e: e.memset(p.cst[:, 0:1], EPS_LN))
        p.cst_rdy = p.cop("vector", lambda e: e.memset(p.cst[:, 1:2], 1.0))

        with ExitStack() as st0:
            a0 = mk_alloc(st0)
            i1 = a0("g_i1", [128, 128], F32)
            c1 = p.cop("gpsimd", lambda e: e.iota(i1[:], pattern=[[1, 128]], base=0,
                                                  channel_multiplier=0,
                                                  allow_small_or_imprecise_dtypes=True))
            c2 = p.cop("gpsimd", lambda e: e.iota(gl["ident"][:], pattern=[[0, 128]], base=0,
                                                  channel_multiplier=1,
                                                  allow_small_or_imprecise_dtypes=True))
            p.cop("vector", lambda e: e.tensor_tensor(gl["ident"][:], gl["ident"][:],
                                                      i1[:], ALU.is_equal),
                  waits=[(p.chain["gpsimd"], c2)])
            cv = a0("g_cv", [1, 1], F32)
            cl = p.dma("sync", lambda e: e.dma_start(out=cv[:], in_=t["curv"].rearrange(
                "(a b) -> a b", a=1)))
            l1 = p.cop("scalar", lambda e: e.activation(out=cv[:], in_=cv[:], func=AF.Ln),
                       waits=[(p.dsem["sync"], cl)])
            l2 = p.cop("scalar", lambda e: e.activation(out=cv[:], in_=cv[:], func=AF.Exp,
                                                        scale=0.5))
            onc = a0("g_onc", [1, 128], F32)
            oc = p.cop("vector", lambda e: e.memset(onc[:], 1.0))
            psb = a0("g_psb", [128, 1], F32, psum=True)
            mmb = p.mmg(lambda e: e.matmul(psb[:], onc[:], cv[:], start=True, stop=True),
                        [(p.chain["scalar"], l2), (p.chain["vector"], oc)])
            gl["sc32_rdy"] = p.cop("vector", lambda e: e.tensor_scalar_mul(
                gl["sc32"][:, 0:1], psb[:], 32.0), waits=[(p.mmsem, mmb)])
            p.barrier()

        if "p1" in phases:
            phase_ln(p, t, t["x_tm"], t["xn_tm"], gl["rstd0"], gl["m0"], mk_alloc, tag="ln0")
        if "p2" in phases:
            phase_routing(p, t, gl, mk_alloc)
        if "p3" in phases:
            phase_scatter(p, t, gl, mk_alloc)
        if "p4" in phases:
            phase_l0_attn(p, t, mk_alloc)
        if "p5" in phases:
            phase_shared(p, t, gl, mk_alloc)
        if "p6" in phases:
            phase_experts(p, t, gl, mk_alloc)
        if "p7" in phases:
            phase_epilogue(p, t, gl, mk_alloc)
        if "p8" in phases:
            phase_ln(p, t, t["x1_tm"], t["xn1_tm"], gl["rstd1"], gl["m1"], mk_alloc,
                     tag="ln1", htan_dram=t["htan_tm"], sb=gl)
        if "p9" in phases:
            phase_l1_ffn(p, t, gl, mk_alloc)
        if "p10" in phases:
            phase_l1_attn(p, t, gl, mk_alloc)

        for d in dbg:
            if d.startswith("sb:"):
                name = d[3:]
                src_t = gl[name]
                dt_ = I32 if name == "idx_all" else F32
                t["dbg_" + name] = nc.dram_tensor(
                    "dbg_" + name, list(src_t[:].shape), dt_, kind="ExternalOutput").ap()
                p.dma("sync", lambda e, name=name, src_t=src_t: e.dma_start(
                    out=t["dbg_" + name], in_=src_t[:]))
            else:
                p.dma("sync", lambda e, d=d: e.dma_start(out=t["dbg_" + d], in_=t[d]))
        if dbg:
            p.wait_only("sync", [(p.dsem["sync"], p.dsem["sync"].cnt)])
        p.emit()
    return nc, t


def make_inputs(core, full):
    f32 = np.float32
    x = np.ascontiguousarray(full["L0_attn_x"], dtype=f32)
    m = {}
    m["x_tm"] = np.ascontiguousarray(x[core])
    m["x_fm"] = np.ascontiguousarray(np.transpose(x, (0, 2, 1)))
    m["x_fm_own"] = np.ascontiguousarray(x[core].T)
    selv = np.zeros((8, 1), f32)
    selv[:core] = 1.0
    m["sel"] = selv
    m["W_spec"] = full["L0_attn_W_spec"]
    m["W_decay"] = full["L0_attn_W_decay"]
    m["b_decay"] = full["L0_attn_b_decay"]
    m["freqs"] = full["L0_attn_freqs"]
    m["W_out"] = full["L0_attn_W_out"]
    m["gate_wT"] = np.ascontiguousarray(np.asarray(full["L0_ffn_gate_w"]).T)
    m["w_gate_up"] = full["L0_ffn_w_gate_up"]
    m["w_down"] = full["L0_ffn_w_down"]
    m["shared_w_gate_up"] = full["L0_ffn_shared_w_gate_up"]
    m["shared_w_down"] = full["L0_ffn_shared_w_down"]
    m["curv"] = np.ascontiguousarray(np.asarray(full["L1_attn_curv"])).reshape(1)
    m["wqT"] = np.ascontiguousarray(np.asarray(full["L1_attn_wq_w"]).T)
    m["wqb"] = full["L1_attn_wq_b"]
    m["wkT"] = np.ascontiguousarray(np.asarray(full["L1_attn_wk_w"]).T)
    m["wkb"] = full["L1_attn_wk_b"]
    m["wvT"] = np.ascontiguousarray(np.asarray(full["L1_attn_wv_w"]).T)
    m["wvb"] = full["L1_attn_wv_b"]
    m["woT"] = np.ascontiguousarray(np.asarray(full["L1_attn_wo_w"]).T)
    m["wob"] = full["L1_attn_wo_b"]
    m["WrT"] = np.ascontiguousarray(np.asarray(full["L1_ffn_W_r"]).T)
    m["WkT"] = np.ascontiguousarray(np.asarray(full["L1_ffn_W_k"]).T)
    m["WvT"] = np.ascontiguousarray(np.asarray(full["L1_ffn_W_v"]).T)
    m["mu_r"] = full["L1_ffn_mu_r"]
    m["mu_k"] = full["L1_ffn_mu_k"]
    return {k: np.ascontiguousarray(np.asarray(v), dtype=f32) for k, v in m.items()}


def kernel(**inputs):
    nc, t = build_program()
    in_maps = [make_inputs(c, inputs) for c in range(8)]
    res = run_bass_kernel_spmd(nc, in_maps, core_ids=list(range(8)))
    return np.concatenate([np.asarray(r["out"])[None] for r in res.results], axis=0)


def phase_l1_ffn(p, t, gl, mk_alloc):
    from contextlib import ExitStack
    vch, ach = p.chain["vector"], p.chain["scalar"]
    dsy, dgp = p.dsem["sync"], p.dsem["gpsimd"]
    mms = p.mmsem
    st = ExitStack()
    alloc = mk_alloc(st)
    TBK = 1024
    NBK = T // TBK
    NTT = TBK // 128
    xn1T = alloc("p9_xnT", [128, HC * T], BF16)
    wr = alloc("p9_wr", [128, HC * D], BF16)
    wk = alloc("p9_wk", [128, HC * D], BF16)
    wv = alloc("p9_wv", [128, HC * D], BF16)
    mus = alloc("p9_mu", [128, 2 * HC], F32)
    dsc_t = alloc("p9_d", [128, TBK], F32)
    xrT = alloc("p9_xr", [128, HC * TBK], BF16)
    xkT = alloc("p9_xk", [128, HC * TBK], BF16)
    kact = alloc("p9_kact", [128, HC * TBK], BF16)
    rblk = alloc("p9_rblk", [128, 2 * D], BF16)
    krelu = alloc("p9_krelu", [128, 512], F32)
    ot = alloc("p9_ot", [128, 2 * D], F32)
    ps = alloc("p9_ps", [128, 2 * D], F32, psum=True)
    psk = alloc("p9_psk", [128, 2 * 512], F32, psum=True)

    w1 = p.dma("gpsimd", lambda e: e.dma_start(
        out=wr[:].rearrange("p (c f) -> p c f", c=HC),
        in_=t["WrT"].rearrange("(c p) f -> p c f", p=128)))
    w2 = p.dma("gpsimd", lambda e: e.dma_start(
        out=wk[:].rearrange("p (c f) -> p c f", c=HC),
        in_=t["WkT"].rearrange("(c p) f -> p c f", p=128)))
    w3 = p.dma("gpsimd", lambda e: e.dma_start(
        out=wv[:].rearrange("p (c f) -> p c f", c=HC),
        in_=t["WvT"].rearrange("(c p) f -> p c f", p=128)))
    m1 = p.dma("sync", lambda e: e.dma_start(
        out=mus[:, 0:HC], in_=t["mu_r"].rearrange("(c p) -> p c", p=128),
        allow_slow_non_contiguous=True))
    m2 = p.dma("sync", lambda e: e.dma_start(
        out=mus[:, HC:2 * HC], in_=t["mu_k"].rearrange("(c p) -> p c", p=128),
        allow_slow_non_contiguous=True))
    xl = 0
    for c in range(HC):
        xl = p.dma("sync", lambda e, c=c: e.dma_start(
            out=xn1T[:, c * T:(c + 1) * T], in_=t["xn1_tm"][:, c * 128:(c + 1) * 128],
            transpose=True))

    ka_bank = [0, 0]
    r_ev = [0] * NT
    o_ev = [0] * NT
    o_st = [0] * NT
    last_mm_xr = 0
    for bk in range(NBK):
        t0 = bk * TBK
        lerp_c = 0
        for c in range(HC):
            base = c * T + t0
            w0 = [(dsy, xl), (dsy, m2), (mms, last_mm_xr)]
            if bk == 0:
                p.cop("vector", lambda e, base=base: e.tensor_tensor(
                    dsc_t[:, 1:TBK], xn1T[:, base + 1:base + TBK],
                    xn1T[:, base:base + TBK - 1], ALU.subtract), waits=w0)
                p.cop("vector", lambda e, base=base: e.tensor_copy(
                    dsc_t[:, 0:1], xn1T[:, base:base + 1]))
                p.cop("vector", lambda e, c=c, base=base: e.scalar_tensor_tensor(
                    xrT[:, c * TBK + 1:(c + 1) * TBK], dsc_t[:, 1:TBK], mus[:, c:c + 1],
                    xn1T[:, base:base + TBK - 1], ALU.mult, ALU.add))
                p.cop("vector", lambda e, c=c: e.tensor_scalar_mul(
                    xrT[:, c * TBK:c * TBK + 1], dsc_t[:, 0:1], mus[:, c:c + 1]))
                p.cop("vector", lambda e, c=c, base=base: e.scalar_tensor_tensor(
                    xkT[:, c * TBK + 1:(c + 1) * TBK], dsc_t[:, 1:TBK],
                    mus[:, HC + c:HC + c + 1],
                    xn1T[:, base:base + TBK - 1], ALU.mult, ALU.add))
                lerp_c = p.cop("vector", lambda e, c=c: e.tensor_scalar_mul(
                    xkT[:, c * TBK:c * TBK + 1], dsc_t[:, 0:1], mus[:, HC + c:HC + c + 1]))
            else:
                p.cop("vector", lambda e, base=base: e.tensor_tensor(
                    dsc_t[:], xn1T[:, base:base + TBK],
                    xn1T[:, base - 1:base + TBK - 1], ALU.subtract), waits=w0)
                p.cop("vector", lambda e, c=c, base=base: e.scalar_tensor_tensor(
                    xrT[:, c * TBK:(c + 1) * TBK], dsc_t[:], mus[:, c:c + 1],
                    xn1T[:, base - 1:base + TBK - 1], ALU.mult, ALU.add))
                lerp_c = p.cop("vector", lambda e, c=c, base=base: e.scalar_tensor_tensor(
                    xkT[:, c * TBK:(c + 1) * TBK], dsc_t[:], mus[:, HC + c:HC + c + 1],
                    xn1T[:, base - 1:base + TBK - 1], ALU.mult, ALU.add))
        kact_done = 0
        for dc in range(HC):
            for tq in range(2):
                g = dc * 2 + tq
                pp = g % 2

                def em(e, dc=dc, tq=tq, pp=pp):
                    out = []
                    for c in range(HC):
                        out.append(e.matmul(
                            psk[:, pp * 512:(pp + 1) * 512],
                            wk[:, c * D + dc * 128: c * D + dc * 128 + 128],
                            xkT[:, c * TBK + tq * 512: c * TBK + (tq + 1) * 512],
                            start=(c == 0), stop=(c == HC - 1)))
                    return out
                mmk = p.mmg(em, [(vch, lerp_c), (dgp, w2), (vch, ka_bank[pp])])
                p.cop("vector", lambda e, pp=pp: e.tensor_relu(
                    krelu[:], psk[:, pp * 512:(pp + 1) * 512]), waits=[(mms, mmk)])
                ka_bank[pp] = kact_done = p.cop(
                    "vector", lambda e, dc=dc, tq=tq: e.tensor_tensor(
                        kact[:, dc * TBK + tq * 512: dc * TBK + (tq + 1) * 512],
                        krelu[:], krelu[:], ALU.mult))
        for tt in range(NTT):
            ti = bk * NTT + tt
            pp = tt % 2

            def em_r(e, tt=tt, pp=pp):
                out = []
                for c in range(HC):
                    for n in range(2):
                        out.append(e.matmul(
                            ps[:, pp * D + n * 512: pp * D + (n + 1) * 512],
                            xrT[:, c * TBK + tt * 128: c * TBK + tt * 128 + 128],
                            wr[:, c * D + n * 512: c * D + (n + 1) * 512],
                            start=(c == 0), stop=(c == HC - 1)))
                return out
            w = [(vch, lerp_c), (dgp, w1)]
            if ti >= 2:
                w.append((vch, o_ev[ti - 2]))
            mmr_ = p.mmg(em_r, w)
            last_mm_xr = mmr_
            r_ev[ti] = p.cop("scalar", lambda e, tt=tt, pp=pp: e.activation(
                out=rblk[:, (tt % 2) * D:(tt % 2 + 1) * D],
                in_=ps[:, pp * D:(pp + 1) * D], func=AF.Sigmoid),
                waits=[(mms, mmr_)] + ([(vch, o_ev[ti - 2])] if ti >= 2 else []))

            def em_v(e, tt=tt, pp=pp):
                out = []
                for c in range(HC):
                    for n in range(2):
                        out.append(e.matmul(
                            ps[:, pp * D + n * 512: pp * D + (n + 1) * 512],
                            kact[:, c * TBK + tt * 128: c * TBK + tt * 128 + 128],
                            wv[:, c * D + n * 512: c * D + (n + 1) * 512],
                            start=(c == 0), stop=(c == HC - 1)))
                return out
            mmv = p.mmg(em_v, [(vch, kact_done), (ach, r_ev[ti]), (dgp, w3)])
            last_mm_xr = mmv
            o_ev[ti] = p.cop("vector", lambda e, tt=tt, pp=pp: e.tensor_tensor(
                ot[:, pp * D:(pp + 1) * D], rblk[:, (tt % 2) * D:(tt % 2 + 1) * D],
                ps[:, pp * D:(pp + 1) * D], ALU.mult),
                waits=[(mms, mmv)] + ([(dsy, o_st[ti - 2])] if ti >= 2 else []))
            o_st[ti] = p.dma("sync", lambda e, ti=ti, pp=pp: e.dma_start(
                out=t["ffn1"][ti * 128:(ti + 1) * 128, :], in_=ot[:, pp * D:(pp + 1) * D]),
                waits=[(vch, o_ev[ti])])
    p.barrier()
    st.close()


def phase_l1_attn(p, t, gl, mk_alloc):
    from contextlib import ExitStack
    vch, ach = p.chain["vector"], p.chain["scalar"]
    dsy, dgp = p.dsem["sync"], p.dsem["gpsimd"]
    mms = p.mmsem
    stA = ExitStack()
    alloc = mk_alloc(stA)
    htT = alloc("pa_htT", [128, HC * T], BF16)
    wq = alloc("pa_wq", [128, HC * D], BF16)
    wkk = alloc("pa_wk", [128, HC * D], BF16)
    wvv = alloc("pa_wv", [128, HC * D], BF16)
    biasB = alloc("pa_biasB", [128, 3 * D], F32)
    brow = alloc("pa_brow", [1, D], F32)
    onesrow = alloc("pa_ones", [1, 128], F32)
    qtt = alloc("pa_qtt", [128, D], F32)
    sqt = alloc("pa_sqt", [128, D], F32)
    n2 = alloc("pa_n2", [128, 2 * NH], F32)
    cfc = alloc("pa_cfc", [128, 8 * NH], F32)
    qhb = alloc("pa_qhb", [128, 3 * D], BF16)
    qtk = alloc("pa_qtk", [128, 128], BF16)
    sqc = alloc("pa_sqc", [128, 2], F32)
    ps = alloc("pa_ps", [128, 3 * D], F32, psum=True)
    psb = alloc("pa_psb", [128, 512], F32, psum=True)

    wl1 = p.dma("gpsimd", lambda e: e.dma_start(
        out=wq[:].rearrange("p (c f) -> p c f", c=HC),
        in_=t["wqT"].rearrange("(c p) f -> p c f", p=128)))
    wl2 = p.dma("gpsimd", lambda e: e.dma_start(
        out=wkk[:].rearrange("p (c f) -> p c f", c=HC),
        in_=t["wkT"].rearrange("(c p) f -> p c f", p=128)))
    wl3 = p.dma("gpsimd", lambda e: e.dma_start(
        out=wvv[:].rearrange("p (c f) -> p c f", c=HC),
        in_=t["wvT"].rearrange("(c p) f -> p c f", p=128)))
    hl = 0
    for c in range(HC):
        hl = p.dma("sync", lambda e, c=c: e.dma_start(
            out=htT[:, c * T:(c + 1) * T], in_=t["htan_tm"][:, c * 128:(c + 1) * 128],
            transpose=True))
    oc = p.cop("vector", lambda e: e.memset(onesrow[:], 1.0))
    bb_c = 0
    mb = 0
    for wi, wname in enumerate(("wqb", "wkb", "wvb")):
        bl = p.dma("sync", lambda e, wname=wname: e.dma_start(
            out=brow[:], in_=t[wname].rearrange("(a f) -> a f", a=1)),
            waits=[(mms, mb)])
        for n in range(2):
            mb = p.mmg(lambda e, n=n: e.matmul(
                psb[:], onesrow[:], brow[:, n * 512:(n + 1) * 512], start=True, stop=True),
                [(dsy, bl), (vch, oc), (ach, bb_c)])
            bb_c = p.cop("scalar", lambda e, wi=wi, n=n: e.copy(
                out=biasB[:, wi * D + n * 512: wi * D + (n + 1) * 512], in_=psb[:]),
                waits=[(mms, mb)])
    sq_c = p.cop("vector", lambda e: e.tensor_scalar_mul(
        sqc[:, 0:1], gl["sc32"][:, 0:1], 1.0 / 32.0), waits=[(vch, gl["sc32_rdy"])])
    p.cop("vector", lambda e: e.tensor_tensor(
        sqc[:, 1:2], sqc[:, 0:1], sqc[:, 0:1], ALU.mult))
    iv_c = p.cop("vector", lambda e: e.reciprocal(sqc[:, 1:2], sqc[:, 1:2]))

    def hyp_map(kind, psrc):
        bi = 0 if kind == "q" else D
        add = p.cop("vector", lambda e: e.tensor_tensor(
            qtt[:], psrc, biasB[:, bi:bi + D], ALU.add))
        sq = p.cop("scalar", lambda e: e.activation(out=sqt[:], in_=qtt[:], func=AF.Square),
                   waits=[(vch, add)])
        rd = p.cop("vector", lambda e: e.tensor_reduce(
            n2[:, 0:NH], sqt[:].rearrange("p (h d) -> p h d", h=NH), axis=AX.X,
            op=ALU.add), waits=[(ach, sq)])
        l1 = p.cop("scalar", lambda e: e.activation(out=n2[:, NH:2 * NH], in_=n2[:, 0:NH],
                                                    func=AF.Ln), waits=[(vch, rd)])
        e1 = p.cop("scalar", lambda e: e.activation(out=n2[:, NH:2 * NH],
                                                    in_=n2[:, NH:2 * NH],
                                                    func=AF.Exp, scale=0.5))
        z = p.cop("vector", lambda e: e.tensor_tensor(
            n2[:, NH:2 * NH], n2[:, NH:2 * NH], sqc[:, 0:1].to_broadcast([128, NH]),
            ALU.mult), waits=[(ach, e1), (vch, iv_c)])
        z = p.cop("vector", lambda e: e.tensor_scalar_max(
            n2[:, NH:2 * NH], n2[:, NH:2 * NH], 1e-7))
        ep = p.cop("scalar", lambda e: e.activation(out=cfc[:, 0:NH], in_=n2[:, NH:2 * NH],
                                                    func=AF.Exp), waits=[(vch, z)])
        em_ = p.cop("scalar", lambda e: e.activation(out=cfc[:, NH:2 * NH],
                                                     in_=n2[:, NH:2 * NH],
                                                     func=AF.Exp, scale=-1.0))
        p.cop("vector", lambda e: e.tensor_tensor(
            cfc[:, 0:NH], cfc[:, 0:NH], cfc[:, NH:2 * NH], ALU.subtract),
            waits=[(ach, em_)])
        p.cop("vector", lambda e: e.tensor_scalar_mul(cfc[:, 0:NH], cfc[:, 0:NH], 0.5))
        p.cop("vector", lambda e: e.reciprocal(cfc[:, NH:2 * NH], n2[:, NH:2 * NH]))
        p.cop("vector", lambda e: e.tensor_tensor(
            cfc[:, 2 * NH:3 * NH], cfc[:, 0:NH], cfc[:, NH:2 * NH], ALU.mult))
        import os as _os
        if _os.environ.get("KDBG_NOCOEF"):
            qh = p.cop("vector", lambda e, bi=bi: e.tensor_copy(
                qhb[:, bi:bi + D], qtt[:]))
        else:
            qh = p.cop("vector", lambda e, bi=bi: e.tensor_tensor(
                qhb[:, bi:bi + D].rearrange("p (h d) -> p h d", h=NH),
                qtt[:].rearrange("p (h d) -> p h d", h=NH),
                cfc[:, 2 * NH:3 * NH].rearrange("p (h o) -> p h o", o=1).to_broadcast(
                    [128, NH, HD]), ALU.mult))
        p.cop("vector", lambda e: e.tensor_tensor(
            cfc[:, 3 * NH:4 * NH], cfc[:, 0:NH], cfc[:, 0:NH], ALU.mult))
        t2 = p.cop("vector", lambda e: e.tensor_tensor(
            cfc[:, 3 * NH:4 * NH], cfc[:, 3 * NH:4 * NH],
            sqc[:, 1:2].to_broadcast([128, NH]), ALU.add))
        s1 = p.cop("scalar", lambda e: e.activation(out=cfc[:, 4 * NH:5 * NH],
                                                    in_=cfc[:, 3 * NH:4 * NH],
                                                    func=AF.Ln), waits=[(vch, t2)])
        s2 = p.cop("scalar", lambda e: e.activation(out=cfc[:, 4 * NH:5 * NH],
                                                    in_=cfc[:, 4 * NH:5 * NH],
                                                    func=AF.Exp, scale=0.5))
        qt_ = p.cop("vector", lambda e, kind=kind: e.tensor_copy(
            qtk[:, (0 if kind == "q" else NH):(NH if kind == "q" else 2 * NH)],
            cfc[:, 4 * NH:5 * NH]), waits=[(ach, s2)])
        return max(qh, qt_)

    tile_done = [0] * NT
    store_c = [0] * NT
    kind_done = {"q": 0, "k": 0, "v": 0}
    for ti in range(NT):
        pp = ti % 2
        done_c = 0
        pst = store_c[ti - 1] if ti else 0
        for ki_, (wt, kind, dstd, qoff) in enumerate((
                (wq, "q", t["qh_tm"], 0),
                (wkk, "k", t["kh_tm"], D),
                (wvv, "v", t["v_tm"], 2 * D))):
            def em(e, wt=wt, ti=ti, ki_=ki_):
                out = []
                for c in range(HC):
                    for n in range(2):
                        out.append(e.matmul(
                            ps[:, ki_ * D + n * 512: ki_ * D + (n + 1) * 512],
                            htT[:, c * T + ti * 128: c * T + ti * 128 + 128],
                            wt[:, c * D + n * 512: c * D + (n + 1) * 512],
                            start=(c == 0), stop=(c == HC - 1)))
                return out
            w = [(dsy, hl), (dgp, wl3), (ach, bb_c), (vch, kind_done[kind])]
            mmq = p.mmg(em, w)
            if kind == "v":
                done_c = p.cop("vector", lambda e, ki_=ki_: e.tensor_tensor(
                    qhb[:, 2 * D:3 * D], ps[:, ki_ * D:(ki_ + 1) * D],
                    biasB[:, 2 * D:3 * D], ALU.add), waits=[(mms, mmq), (dsy, pst)])
            else:
                p.cop("vector", lambda e: e.engine_nop(), waits=[(mms, mmq), (dsy, pst)])
                done_c = hyp_map(kind, ps[:, ki_ * D:(ki_ + 1) * D])
            kind_done[kind] = done_c
            store_c[ti] = p.dma("sync", lambda e, ti=ti, dstd=dstd, qoff=qoff: e.dma_start(
                out=dstd[ti * 128:(ti + 1) * 128, :], in_=qhb[:, qoff:qoff + D]),
                waits=[(vch, done_c)])
        tile_done[ti] = p.cop("vector", lambda e: e.memset(qtk[:, 2 * NH:128], 0.0),
                              waits=[(dsy, pst)])
        store_c[ti] = p.dma("sync", lambda e, ti=ti: e.dma_start(
            out=t["qtkt"][ti * 128:(ti + 1) * 128, :], in_=qtk[:]),
            waits=[(vch, tile_done[ti])])
    p.barrier()
    stA.close()

    stB = ExitStack()
    alloc = mk_alloc(stB)
    qkT = alloc("pb_qkT", [128, T], BF16)
    Mrow = alloc("pb_M", [32, T], F32)
    Msc = alloc("pb_Msc", [16, T], F32)
    rowsb = alloc("pb_rowsb", [32, T], BF16)
    mneg = alloc("pb_mneg", [16, T], BF16)
    sqc2 = alloc("pb_sqc", [128, 4], F32)
    qtd = t["misc_d"]
    rowd = p.nc.dram_tensor("pb_rowd", [48, T], BF16).ap()
    onesd = p.nc.dram_tensor("pb_onesd", [1, T], BF16).ap()

    qk_l = p.dma("sync", lambda e: e.dma_start(out=qkT[:], in_=t["qtkt"], transpose=True))
    iv2 = p.cop("vector", lambda e: e.tensor_tensor(
        sqc2[:, 0:1], gl["sc32"][:, 0:1], gl["sc32"][:, 0:1], ALU.mult),
        waits=[(vch, gl["sc32_rdy"])])
    iv2 = p.cop("vector", lambda e: e.tensor_scalar_mul(sqc2[:, 0:1], sqc2[:, 0:1],
                                                        1.0 / 1024.0))
    iv2 = p.cop("vector", lambda e: e.reciprocal(sqc2[:, 0:1], sqc2[:, 0:1]))
    m_c = p.cop("vector", lambda e: e.tensor_copy(Mrow[0:32, :], qkT[0:32, :]),
                waits=[(dsy, qk_l)])
    m_c = p.cop("vector", lambda e: e.tensor_tensor(
        Mrow[0:32, :], Mrow[0:32, :], Mrow[0:32, :], ALU.mult))
    m_c = p.cop("vector", lambda e: e.tensor_scalar_sub(
        Mrow[0:32, :], Mrow[0:32, :], sqc2[0:32, 0:1]), waits=[(vch, iv2)])
    m_c = p.cop("vector", lambda e: e.tensor_scalar_max(Mrow[0:32, :], Mrow[0:32, :], 0.0))
    m_sq = p.cop("scalar", lambda e: e.activation(out=Mrow[0:32, :], in_=Mrow[0:32, :],
                                                  func=AF.Ln, bias=p.cst[0:32, 0:1]),
                 waits=[(vch, m_c), (vch, p.cst_rdy)])
    m_sq = p.cop("scalar", lambda e: e.activation(out=Mrow[0:32, :], in_=Mrow[0:32, :],
                                                  func=AF.Exp, scale=0.5))
    mx = p.cop("vector", lambda e: e.reduce_max(sqc2[0:32, 1:2], qkT[0:32, :], axis=AX.X),
               waits=[(ach, m_sq)])
    mx = p.cop("vector", lambda e: e.reduce_max(sqc2[0:32, 2:3], Mrow[0:32, :], axis=AX.X))
    b1 = p.dma("gpsimd", lambda e: e.dma_start(out=t["misc_d"][320:352],
                                               in_=sqc2[0:32, 1]), waits=[(vch, mx)])
    b2 = p.dma("gpsimd", lambda e: e.dma_start(out=t["misc_d"][352:384],
                                               in_=sqc2[0:32, 2]), waits=[(vch, mx)])
    b3 = p.dma("gpsimd", lambda e: e.dma_start(
        out=sqc2[0:16, 1:2], in_=t["misc_d"][336:352].rearrange("(p a) -> p a", a=1)),
        waits=[(dgp, b2)])
    b4 = p.dma("gpsimd", lambda e: e.dma_start(
        out=sqc2[0:16, 2:3], in_=t["misc_d"][368:384].rearrange("(p a) -> p a", a=1)),
        waits=[(dgp, b2)])
    mm_t = p.cop("vector", lambda e: e.tensor_scalar(
        Msc[0:16, :], qkT[0:16, :], sqc2[0:16, 1:2], 0.125, ALU.mult, ALU.mult),
        waits=[(dgp, b4)])
    mm_t = p.cop("vector", lambda e: e.tensor_scalar(
        Mrow[0:16, :], Mrow[0:16, :], sqc2[0:16, 2:3], 0.125, ALU.mult, ALU.mult))
    mm_t = p.cop("vector", lambda e: e.tensor_tensor(
        Mrow[0:16, :], Mrow[0:16, :], Msc[0:16, :], ALU.add))
    mm_t = p.cop("vector", lambda e: e.tensor_scalar_mul(
        rowsb[0:32, :], qkT[0:32, :], 0.125))
    mm_t = p.cop("vector", lambda e: e.tensor_scalar_mul(
        mneg[0:16, :], Mrow[0:16, :], -1.0))
    r1 = p.dma("sync", lambda e: e.dma_start(out=rowd[0:16, :], in_=rowsb[16:32, :]),
               waits=[(vch, mm_t)])
    r1 = p.dma("sync", lambda e: e.dma_start(out=rowd[16:32, :], in_=mneg[0:16, :]),
               waits=[(vch, mm_t)])
    r2 = p.dma("sync", lambda e: e.dma_start(out=rowd[32:48, :], in_=qkT[0:16, :]),
               waits=[(vch, mm_t)])
    oz = p.cop("vector", lambda e: e.memset(rowsb[0:1, :], 1.0))
    r3 = p.dma("sync", lambda e: e.dma_start(out=onesd[:], in_=rowsb[0:1, :]),
               waits=[(vch, oz)])
    p.barrier()
    stB.close()

    stC = ExitStack()
    alloc = mk_alloc(stC)
    QT2 = alloc("pc_qt2", [128, T], BF16)
    KT2 = alloc("pc_kt2", [128, T], BF16)
    Kop = alloc("pc_Kop", [66, T], BF16)
    Qop = alloc("pc_Qop", [66, T], BF16)
    vh = alloc("pc_vh", [128, NT * 72], BF16)
    PT = alloc("pc_PT", [128, 2 * 8 * 2048], BF16)
    rrow = alloc("pc_rrow", [1, 512], F32)
    onum = alloc("pc_onum", [64, 512], F32)
    OT = alloc("pc_OT", [128, HC * T], BF16)
    onesc2 = alloc("pc_ones2", [1, 64], F32)
    psS = alloc("pc_psS", [128, 2048], F32, psum=True)
    psS2 = alloc("pc_psS2", [128, 1536], F32, psum=True)
    psO = alloc("pc_psO", [65, 512], F32, psum=True)
    o2 = p.cop("vector", lambda e: e.memset(onesc2[:], 1.0))
    vz = p.cop("vector", lambda e: e.memset(
        vh[:].rearrange("p (kc x) -> p kc x", kc=NT)[:, :, 64:65], 1.0))

    exp_c = 0
    av_prev = 0
    av_pp = [0, 0]
    oe_prev = 0
    oe_prev_v = 0
    bm_prev = 0
    on_prev = 0
    pair_s_done = 0
    head_free = 0
    last_qk_read = 0
    for hp in range(HC):
        ql = p.dma("sync", lambda e, hp=hp: e.dma_start(
            out=QT2[:], in_=t["qh_tm"][:, hp * 128:(hp + 1) * 128], transpose=True),
            waits=[(mms, pair_s_done), (vch, last_qk_read)])
        kl = p.dma("sync", lambda e, hp=hp: e.dma_start(
            out=KT2[:], in_=t["kh_tm"][:, hp * 128:(hp + 1) * 128], transpose=True),
            waits=[(mms, pair_s_done), (vch, last_qk_read)])
        for hh in range(2):
            h = hp * 2 + hh
            hb = hh * 64
            vl = p.dma("sync", lambda e, h=h: e.dma_start(
                out=vh[:].rearrange("p (kc x) -> p kc x", kc=NT)[:, :, 0:64],
                in_=t["v_tm"].rearrange("(kc p) f -> p kc f", p=128)[:, :, h * 64:(h + 1) * 64]),
                waits=[(mms, av_prev)])
            w0 = [(dsy, kl), (mms, head_free)]
            kc1 = p.cop("vector", lambda e, hb=hb: e.tensor_scalar_mul(
                Kop[0:64, :], KT2[hb:hb + 64, :], -0.125), waits=w0)
            k64 = p.dma("gpsimd", lambda e, h=h: e.dma_start(
                out=Kop[64:65, :], in_=rowd[h:h + 1, :]), waits=[(mms, head_free)])
            k65 = p.dma("gpsimd", lambda e: e.dma_start(
                out=Kop[65:66, :], in_=onesd[:]), waits=[(mms, head_free)])
            qc1 = p.cop("vector", lambda e, hb=hb: e.tensor_copy(
                Qop[0:64, :], QT2[hb:hb + 64, :]), waits=[(dsy, ql), (mms, head_free)])
            last_qk_read = qc1
            q64 = p.dma("gpsimd", lambda e, h=h: e.dma_start(
                out=Qop[64:65, :], in_=rowd[32 + h:33 + h, :]), waits=[(mms, head_free)])
            q65 = p.dma("gpsimd", lambda e, h=h: e.dma_start(
                out=Qop[65:66, :], in_=rowd[16 + h:17 + h, :]), waits=[(mms, head_free)])
            GRPS = [4, 3, 4, 3, 4, 3, 4, 3, 4]
            exp_pp = [exp_c, exp_c]
            for qb in range(8):
                bsel = (qb % 2) * 8 * 2048
                kc0 = 0
                for gi_, gw_ in enumerate(GRPS):
                    buf = psS if gi_ % 2 == 0 else psS2

                    def em(e, kc0=kc0, gw_=gw_, qb=qb, buf=buf):
                        out = []
                        for j in range(gw_):
                            kc = kc0 + j
                            out.append(e.matmul(
                                buf[:, j * 512:(j + 1) * 512],
                                Kop[:, kc * 128:(kc + 1) * 128],
                                Qop[:, qb * 512:(qb + 1) * 512],
                                start=True, stop=True, skip_group_check=True))
                        return out
                    sm = p.mmg(em, [(vch, kc1), (vch, qc1), (dgp, q65),
                                    (ach, exp_pp[gi_ % 2])])
                    exp_pp[gi_ % 2] = exp_c = p.cop(
                        "scalar", lambda e, kc0=kc0, gw_=gw_, bsel=bsel, buf=buf: e.activation(
                            out=PT[:, bsel + kc0 * 512: bsel + (kc0 + gw_) * 512],
                            in_=buf[:, 0:gw_ * 512], func=AF.Exp),
                        waits=[(mms, sm), (mms, av_pp[qb % 2])])
                    kc0 += gw_

                def emav(e, bsel=bsel):
                    out = []
                    for kc in range(NT):
                        out.append(e.matmul(
                            psO[:], vh[:, kc * 72: kc * 72 + 65],
                            PT[:, bsel + kc * 512: bsel + (kc + 1) * 512],
                            start=(kc == 0), stop=(kc == NT - 1)))
                    return out
                av_prev = p.mmg(emav, [(ach, exp_c), (dsy, vl), (vch, vz),
                                       (vch, oe_prev), (mms, bm_prev), (vch, on_prev)])
                av_pp[qb % 2] = av_prev
                rc = p.cop("vector", lambda e: e.reciprocal(rrow[:], psO[64:65, :]),
                           waits=[(mms, av_prev)])
                oe_prev_v = p.cop("vector", lambda e: e.tensor_copy(onum[:], psO[0:64, :]),
                                  waits=[(vch, rc)])
                oe_prev = oe_prev_v
                bm_prev = p.mmg(lambda e: e.matmul(psO[0:64, :], onesc2[:], rrow[:],
                                                   start=True, stop=True),
                                [(vch, rc), (vch, o2), (vch, oe_prev), (vch, on_prev)])
                on_prev = p.cop("vector", lambda e, hp=hp, hb=hb, qb=qb: e.tensor_tensor(
                    OT[hb:hb + 64, hp * T + qb * 512: hp * T + (qb + 1) * 512],
                    onum[:], psO[0:64, :], ALU.mult), waits=[(vch, oe_prev), (mms, bm_prev)])
            head_free = av_prev
            pair_s_done = av_prev
    p.barrier()
    stC.close()

    stD = ExitStack()
    alloc = mk_alloc(stD)
    wo = alloc("pd_wo", [128, HC * D], BF16)
    bB = alloc("pd_bB", [128, D], F32)
    brow2 = alloc("pd_brow", [1, D], F32)
    ones2 = alloc("pd_ones", [1, 128], F32)
    x1t = alloc("pd_x1", [128, 2 * D], F32)
    f1t = alloc("pd_f1", [128, 2 * D], F32)
    outt = alloc("pd_out", [128, 2 * D], F32)
    pso = alloc("pd_pso", [128, 2 * D], F32, psum=True)
    psb3 = alloc("pd_psb3", [128, 512], F32, psum=True)
    wol = p.dma("gpsimd", lambda e: e.dma_start(
        out=wo[:].rearrange("p (c f) -> p c f", c=HC),
        in_=t["woT"].rearrange("(c p) f -> p c f", p=128)))
    o3 = p.cop("vector", lambda e: e.memset(ones2[:], 1.0))
    bl = p.dma("sync", lambda e: e.dma_start(
        out=brow2[:], in_=t["wob"].rearrange("(a f) -> a f", a=1)))
    bb2 = 0
    for n in range(2):
        mb = p.mmg(lambda e, n=n: e.matmul(
            psb3[:], ones2[:], brow2[:, n * 512:(n + 1) * 512], start=True, stop=True),
            [(dsy, bl), (vch, o3), (ach, bb2)])
        bb2 = p.cop("scalar", lambda e, n=n: e.copy(
            out=bB[:, n * 512:(n + 1) * 512], in_=psb3[:]), waits=[(mms, mb)])
    o_c = [0] * NT
    ost = [0] * NT
    for ti in range(NT):
        pp = ti % 2

        def em(e, ti=ti, pp=pp):
            out = []
            for c in range(HC):
                for n in range(2):
                    out.append(e.matmul(
                        pso[:, pp * D + n * 512: pp * D + (n + 1) * 512],
                        OT[:, c * T + ti * 128: c * T + ti * 128 + 128],
                        wo[:, c * D + n * 512: c * D + (n + 1) * 512],
                        start=(c == 0), stop=(c == HC - 1)))
            return out
        w = [(dgp, wol), (ach, bb2)]
        if ti >= 2:
            w.append((vch, o_c[ti - 2]))
        mo = p.mmg(em, w)
        wre = [(vch, o_c[ti - 2])] if ti >= 2 else []
        xld = p.dma("sync", lambda e, ti=ti, pp=pp: e.dma_start(
            out=x1t[:, pp * D:(pp + 1) * D], in_=t["x1_tm"][ti * 128:(ti + 1) * 128, :]),
            waits=wre)
        fld = p.dma("sync", lambda e, ti=ti, pp=pp: e.dma_start(
            out=f1t[:, pp * D:(pp + 1) * D], in_=t["ffn1"][ti * 128:(ti + 1) * 128, :]),
            waits=wre)
        p.cop("vector", lambda e, pp=pp: e.tensor_tensor(
            outt[:, pp * D:(pp + 1) * D], x1t[:, pp * D:(pp + 1) * D],
            f1t[:, pp * D:(pp + 1) * D], ALU.add),
            waits=[(dsy, fld)] + ([(dsy, ost[ti - 2])] if ti >= 2 else []))
        p.cop("vector", lambda e, pp=pp: e.tensor_tensor(
            outt[:, pp * D:(pp + 1) * D], outt[:, pp * D:(pp + 1) * D], bB[:], ALU.add))
        o_c[ti] = p.cop("vector", lambda e, pp=pp: e.tensor_tensor(
            outt[:, pp * D:(pp + 1) * D], outt[:, pp * D:(pp + 1) * D],
            pso[:, pp * D:(pp + 1) * D], ALU.add), waits=[(mms, mo)])
        ost[ti] = p.dma("sync", lambda e, ti=ti, pp=pp: e.dma_start(
            out=t["out"][ti * 128:(ti + 1) * 128, :], in_=outt[:, pp * D:(pp + 1) * D]),
            waits=[(vch, o_c[ti])])
    p.wait_only("sync", [(dsy, ost[NT - 1])])
    p.barrier()
    stD.close()


# version 12
# speedup vs baseline: 1.2465x; 1.0806x over previous
import math
import numpy as np
import concourse.bass as bass
import concourse.mybir as mybir
from concourse.bass_utils import run_bass_kernel_spmd
from concourse.bass_core_rust import IndirectOffsetOnAxis

F32 = mybir.dt.float32
BF16 = mybir.dt.bfloat16
I32 = mybir.dt.int32
U32 = mybir.dt.uint32
AF = mybir.ActivationFunctionType
ALU = mybir.AluOpType
AX = mybir.AxisListType

T = 4096
D = 1024
NT = T // 128
HC = D // 128
F = 512
NFC = F // 128
E = 32
CAP = 1024
NH = 16
HD = 64
EPS_LN = 1e-5
TWO_PI = 2.0 * math.pi
BIGIDX = float(1 << 22)
ENGS = ("sync", "tensor", "vector", "scalar", "gpsimd")
CHAINED = ("vector", "scalar", "gpsimd")


class Sem:
    __slots__ = ("h", "cnt", "name")

    def __init__(self, nc, stack, name):
        self.h = stack.enter_context(nc.semaphore(name))
        self.cnt = 0
        self.name = name

    def bump(self, n=1):
        self.cnt += n
        return self.cnt


class Prog:
    def __init__(self, nc, block, stack):
        self.nc = nc
        self.block = block
        self.stack = stack
        self.ops = {k: [] for k in ENGS}
        self.sems = {}
        self.chain = {k: self.sem("ch_" + k) for k in CHAINED}
        self.dsem = {k: self.sem("d_" + k) for k in ("sync", "scalar", "gpsimd")}
        self.mmsem = self.sem("mmg")
        self.waited = {k: {} for k in ENGS}

    def sem(self, name):
        if name not in self.sems:
            self.sems[name] = Sem(self.nc, self.stack, name)
        return self.sems[name]

    def _record(self, eng, emit, waits, inc, inc_n):
        wl = []
        for s, c in waits:
            if c <= 0:
                continue
            w = self.waited[eng]
            if w.get(s.name, 0) >= c:
                continue
            w[s.name] = c
            wl.append((s.h, c))
        self.ops[eng].append((wl, emit, None if inc is None else inc.h, inc_n))
        return inc.bump(inc_n) if inc is not None else None

    def cop(self, eng, emit, waits=()):
        ch = self.chain[eng]
        w = list(waits) + [(ch, ch.cnt)]
        return self._record(eng, emit, w, ch, 1)

    def dma(self, eng, emit, waits=()):
        return self._record(eng, emit, waits, self.dsem[eng], 16)

    def mmg(self, emit, waits=()):
        return self._record("tensor", emit, waits, self.mmsem, 1)

    def wait_only(self, eng, waits):
        self._record(eng, lambda e: None, waits, None, 0)

    def emit(self):
        blk = self.block
        decs = {"sync": blk.sync, "tensor": blk.tensor, "vector": blk.vector,
                "scalar": blk.scalar, "gpsimd": blk.gpsimd}
        for name in ENGS:
            entries = self.ops[name]
            if not entries:
                continue

            def make(entries):
                def _(e):
                    for wl, emit, inch, incn in entries:
                        for h, c in wl:
                            e.wait_ge(h, c)
                        ins = emit(e)
                        if inch is not None:
                            if isinstance(ins, (list, tuple)):
                                ins = ins[-1]
                            ins.then_inc(inch, incn)
                return _
            decs[name](make(entries))
            self.ops[name] = []

    def barrier(self):
        waits = [(s, s.cnt) for s in self.sems.values() if s.cnt > 0]
        for eng in ENGS:
            self.wait_only(eng, waits)
        self.emit()


def declare_io(nc):
    t = {}
    di = lambda n, sh, dt=F32: nc.dram_tensor(n, sh, dt, kind="ExternalInput").ap()
    dint = lambda n, sh, dt: nc.dram_tensor(n, sh, dt).ap()
    t["x_tm"] = di("x_tm", [T, D])
    t["x_fm"] = di("x_fm", [8, D, T])
    t["x_fm_own"] = di("x_fm_own", [D, T])
    t["sel"] = di("sel", [8, 1])
    t["W_spec"] = di("W_spec", [D, 2 * F])
    t["W_decay"] = di("W_decay", [D, F])
    t["b_decay"] = di("b_decay", [F])
    t["freqs"] = di("freqs", [F])
    t["W_out"] = di("W_out", [2 * F, D])
    t["gate_wT"] = di("gate_wT", [D, E])
    t["w_gate_up"] = di("w_gate_up", [E, D, 2 * F])
    t["w_down"] = di("w_down", [E, F, D])
    t["shared_w_gate_up"] = di("shared_w_gate_up", [D, 2 * F])
    t["shared_w_down"] = di("shared_w_down", [F, D])
    t["curv"] = di("curv", [1])
    for w in ("wq", "wk", "wv", "wo"):
        t[w + "T"] = di(w + "T", [D, D])
        t[w + "b"] = di(w + "b", [D])
    t["WrT"] = di("WrT", [D, D])
    t["WkT"] = di("WkT", [D, D])
    t["WvT"] = di("WvT", [D, D])
    t["mu_r"] = di("mu_r", [D])
    t["mu_k"] = di("mu_k", [D])
    t["out"] = nc.dram_tensor("out", [T, D], F32, kind="ExternalOutput").ap()

    t["xn_tm"] = dint("xn_tm", [T, D], BF16)
    t["y_fm"] = dint("y_fm", [2 * F, T], BF16)
    t["attn0"] = dint("attn0", [T, D], F32)
    t["buf"] = dint("buf", [E * CAP, D], BF16)
    t["eo"] = dint("eo", [E * CAP, D], BF16)
    t["x1_tm"] = dint("x1_tm", [T, D], F32)
    t["xn1_tm"] = dint("xn1_tm", [T, D], BF16)
    t["htan_tm"] = dint("htan_tm", [T, D], BF16)
    t["ffn1"] = dint("ffn1", [T, D], F32)
    t["qh_tm"] = dint("qh_tm", [T, D], BF16)
    t["kh_tm"] = dint("kh_tm", [T, D], BF16)
    t["v_tm"] = dint("v_tm", [T, D], BF16)
    t["qtkt"] = dint("qtkt", [T, 128], BF16)
    t["misc_d"] = dint("misc_d", [512], F32)
    t["act_sh"] = dint("act_sh", [4 * 128, T], BF16)
    return t


def phase_ln(p, t, x_dram, xn_dram, rstd_dst, m_dst, mk_alloc, tag="ln0",
             htan_dram=None, sb=None):
    from contextlib import ExitStack
    st = ExitStack()
    alloc = mk_alloc(st)
    vch, ach = p.chain["vector"], p.chain["scalar"]
    dsy = p.dsem["sync"]
    dgp = p.dsem["gpsimd"]
    xt = alloc(tag + "_x", [128, 2 * D], F32)
    sqt = alloc(tag + "_sq", [128, D], F32)
    xnf = alloc(tag + "_xnf", [128, D], F32)
    xnt = alloc(tag + "_xn", [128, 2 * D], BF16)
    htt = alloc(tag + "_ht", [128, 2 * D], BF16) if htan_dram is not None else None
    cols = alloc(tag + "_cols", [128, 4 * NT], F32)
    CSS, CV, CT0, CT1 = (k * NT for k in range(4))
    xn_c = [0] * NT
    st_c = [0] * NT
    if htan_dram is not None:
        sc32, sc32_rdy = sb["sc32"], sb["sc32_rdy"]

    for i in range(NT):
        bx = (i % 2) * D
        ld = p.dma("sync", lambda e, i=i, bx=bx: e.dma_start(
            out=xt[:, bx:bx + D], in_=x_dram[i * 128:(i + 1) * 128, :]),
            waits=[(vch, xn_c[i - 2])] if i >= 2 else [])
        sq = p.cop("scalar", lambda e, i=i, bx=bx: e.activation(
            out=sqt[:], in_=xt[:, bx:bx + D], func=AF.Square,
            accum_out=cols[:, CSS + i:CSS + i + 1]), waits=[(dsy, ld)])
        p.cop("vector", lambda e, i=i, bx=bx: e.reduce_sum(
            m_dst[:, i:i + 1], xt[:, bx:bx + D], axis=AX.X), waits=[(dsy, ld)])
        p.cop("vector", lambda e, i=i: e.tensor_scalar_mul(
            m_dst[:, i:i + 1], m_dst[:, i:i + 1], 1.0 / D))
        p.cop("vector", lambda e, i=i: e.tensor_scalar_mul(
            cols[:, CT0 + i:CT0 + i + 1], cols[:, CSS + i:CSS + i + 1], 1.0 / D),
            waits=[(ach, sq)])
        p.cop("vector", lambda e, i=i: e.tensor_tensor(
            cols[:, CV + i:CV + i + 1], m_dst[:, i:i + 1], m_dst[:, i:i + 1], ALU.mult))
        var = p.cop("vector", lambda e, i=i: e.tensor_tensor(
            cols[:, CV + i:CV + i + 1], cols[:, CT0 + i:CT0 + i + 1],
            cols[:, CV + i:CV + i + 1], ALU.subtract))
        lnv = p.cop("scalar", lambda e, i=i: e.activation(
            out=cols[:, CT0 + i:CT0 + i + 1], in_=cols[:, CV + i:CV + i + 1],
            func=AF.Ln, bias=p.cst[:, 0:1]), waits=[(vch, var), (vch, p.cst_rdy)])
        rstd = p.cop("scalar", lambda e, i=i: e.activation(
            out=rstd_dst[:, i:i + 1], in_=cols[:, CT0 + i:CT0 + i + 1],
            func=AF.Exp, scale=-0.5))
        xnf_c = p.cop("vector", lambda e, i=i, bx=bx: e.tensor_scalar(
            xnf[:], xt[:, bx:bx + D],
            m_dst[:, i:i + 1], rstd_dst[:, i:i + 1], ALU.subtract, ALU.mult),
            waits=[(ach, rstd)])
        xn_c[i] = p.cop("vector", lambda e, i=i, bx=bx: e.tensor_copy(
            xnt[:, bx:bx + D], xnf[:]),
            waits=([(dgp, st_c[i - 2])] if i >= 2 else []))
        st_c[i] = p.dma("gpsimd", lambda e, i=i, bx=bx: e.dma_start(
            out=xn_dram[i * 128:(i + 1) * 128, :], in_=xnt[:, bx:bx + D]),
            waits=[(vch, xn_c[i])])
        if htan_dram is not None:
            lnv2 = p.cop("scalar", lambda e, i=i: e.activation(
                out=cols[:, CT1 + i:CT1 + i + 1], in_=cols[:, CV + i:CV + i + 1],
                func=AF.Ln))
            sb_ = p.cop("vector", lambda e, i=i: e.tensor_tensor(
                cols[:, CT1 + i:CT1 + i + 1], cols[:, CT1 + i:CT1 + i + 1],
                cols[:, CT0 + i:CT0 + i + 1], ALU.subtract), waits=[(ach, lnv2)])
            rt = p.cop("scalar", lambda e, i=i: e.activation(
                out=cols[:, CT1 + i:CT1 + i + 1], in_=cols[:, CT1 + i:CT1 + i + 1],
                func=AF.Exp, scale=0.5), waits=[(vch, sb_)])
            z = p.cop("vector", lambda e, i=i: e.tensor_tensor(
                cols[:, CT1 + i:CT1 + i + 1], cols[:, CT1 + i:CT1 + i + 1],
                sc32[:, 0:1], ALU.mult), waits=[(ach, rt), (vch, sc32_rdy)])
            z = p.cop("vector", lambda e, i=i: e.tensor_scalar_max(
                cols[:, CT1 + i:CT1 + i + 1], cols[:, CT1 + i:CT1 + i + 1], 1e-7))
            z2 = p.cop("vector", lambda e, i=i: e.tensor_tensor(
                cols[:, CSS + i:CSS + i + 1], cols[:, CT1 + i:CT1 + i + 1],
                cols[:, CT1 + i:CT1 + i + 1], ALU.mult))
            l1 = p.cop("scalar", lambda e, i=i: e.activation(
                out=cols[:, CSS + i:CSS + i + 1], in_=cols[:, CSS + i:CSS + i + 1],
                func=AF.Ln, bias=p.cst[:, 1:2]), waits=[(vch, z2)])
            s1 = p.cop("scalar", lambda e, i=i: e.activation(
                out=cols[:, CSS + i:CSS + i + 1], in_=cols[:, CSS + i:CSS + i + 1],
                func=AF.Exp, scale=0.5))
            w_ = p.cop("vector", lambda e, i=i: e.tensor_tensor(
                cols[:, CSS + i:CSS + i + 1], cols[:, CSS + i:CSS + i + 1],
                cols[:, CT1 + i:CT1 + i + 1], ALU.add), waits=[(ach, s1)])
            asinh = p.cop("scalar", lambda e, i=i: e.activation(
                out=cols[:, CSS + i:CSS + i + 1], in_=cols[:, CSS + i:CSS + i + 1],
                func=AF.Ln), waits=[(vch, w_)])
            p.cop("vector", lambda e, i=i: e.reciprocal(
                cols[:, CT1 + i:CT1 + i + 1], cols[:, CT1 + i:CT1 + i + 1]),
                waits=[(ach, asinh)])
            cf = p.cop("vector", lambda e, i=i: e.tensor_tensor(
                cols[:, CSS + i:CSS + i + 1], cols[:, CSS + i:CSS + i + 1],
                cols[:, CT1 + i:CT1 + i + 1], ALU.mult))
            ht_c = p.cop("vector", lambda e, i=i, bx=bx: e.tensor_scalar_mul(
                htt[:, bx:bx + D], xnf[:], cols[:, CSS + i:CSS + i + 1]),
                waits=[(vch, cf)] + ([(dgp, st_c[i - 2])] if i >= 2 else []))
            st_c[i] = p.dma("gpsimd", lambda e, i=i, bx=bx: e.dma_start(
                out=htan_dram[i * 128:(i + 1) * 128, :], in_=htt[:, bx:bx + D]),
                waits=[(vch, ht_c)])
    p.barrier()
    st.close()


def phase_l0_attn(p, t, mk_alloc):
    from contextlib import ExitStack
    TB = 2048
    NTB = T // TB
    NQ = TB // 512
    vch, ach, pch = p.chain["vector"], p.chain["scalar"], p.chain["gpsimd"]
    dsy, dgp, dsc = p.dsem["sync"], p.dsem["gpsimd"], p.dsem["scalar"]
    mms = p.mmsem

    stA = ExitStack()
    alloc = mk_alloc(stA)
    xnT = alloc("p4_xnT", [128, HC * T], BF16)
    wdec = alloc("p4_wdec", [128, HC * F], BF16)
    wspec = alloc("p4_wspec", [128, HC * 2 * F], BF16)
    bdec = alloc("p4_bdec", [128, NFC], F32)
    om = alloc("p4_om", [128, NFC], F32)
    trow = alloc("p4_trow", [128, TB], F32)
    dt_ = alloc("p4_d", [128, TB], F32)
    ur = alloc("p4_ur", [128, TB], F32)
    ui = alloc("p4_ui", [128, TB], F32)
    cc = alloc("p4_c", [128, TB], F32)
    ss = alloc("p4_s", [128, TB], F32)
    t1 = alloc("p4_t1", [128, TB], F32)
    t2 = alloc("p4_t2", [128, TB], F32)
    gi_ = alloc("p4_gi", [128, TB], F32)
    carry = alloc("p4_carry", [128, 2 * NFC], F32)
    ybf = alloc("p4_ybf", [128, 2 * TB], BF16)
    ps = alloc("p4_ps", [128, 2 * 2048], F32, psum=True)

    w1 = p.dma("gpsimd", lambda e: e.dma_start(
        out=wdec[:].rearrange("p (c f) -> p c f", c=HC),
        in_=t["W_decay"].rearrange("(c p) f -> p c f", p=128)))
    w2 = p.dma("gpsimd", lambda e: e.dma_start(
        out=wspec[:].rearrange("p (c f) -> p c f", c=HC),
        in_=t["W_spec"].rearrange("(c p) f -> p c f", p=128)))
    w3 = p.dma("sync", lambda e: e.dma_start(
        out=bdec[:], in_=t["b_decay"].rearrange("(c p) -> p c", p=128),
        allow_slow_non_contiguous=True))
    w4 = p.dma("sync", lambda e: e.dma_start(
        out=om[:], in_=t["freqs"].rearrange("(c p) -> p c", p=128),
        allow_slow_non_contiguous=True))
    xt_c = 0
    for c in range(HC):
        xt_c = p.dma("sync", lambda e, c=c: e.dma_start(
            out=xnT[:, c * T:(c + 1) * T], in_=t["xn_tm"][:, c * 128:(c + 1) * 128],
            transpose=True))
    om_c = p.cop("vector", lambda e: e.tensor_scalar_mul(om[:], om[:], 0.1),
                 waits=[(dsy, w4)])

    ev_c = {}
    mm_c = {}
    yst_c = [0] * (NFC * NTB)
    blk_v_done = 0
    g = 0
    for fc in range(NFC):
        for tb in range(NTB):
            k = fc * NTB + tb
            for kind in range(3):
                waits = [(dgp, w2), (dsy, xt_c)]
                if g - 2 in ev_c:
                    waits.append((ach, ev_c[g - 2]))

                def em(e, fc=fc, tb=tb, kind=kind, g=g):
                    pp = g % 2
                    t0 = tb * TB
                    out = []
                    for hc in range(HC):
                        if kind == 0:
                            lhsT = wdec[:, hc * F + fc * 128: hc * F + fc * 128 + 128]
                        elif kind == 1:
                            lhsT = wspec[:, hc * 2 * F + fc * 128: hc * 2 * F + fc * 128 + 128]
                        else:
                            lhsT = wspec[:, hc * 2 * F + (fc + NFC) * 128:
                                         hc * 2 * F + (fc + NFC) * 128 + 128]
                        for q in range(NQ):
                            out.append(e.matmul(
                                ps[:, pp * 2048 + q * 512: pp * 2048 + (q + 1) * 512],
                                lhsT,
                                xnT[:, hc * T + t0 + q * 512: hc * T + t0 + (q + 1) * 512],
                                start=(hc == 0), stop=(hc == HC - 1),
                                skip_group_check=True))
                    return out
                mm_c[g] = p.mmg(em, waits)
                pp = g % 2
                psrc = ps[:, pp * 2048:(pp + 1) * 2048]
                if kind == 0:
                    ev_c[g] = p.cop("scalar", lambda e, psrc=psrc, fc=fc: e.activation(
                        out=dt_[:], in_=psrc, func=AF.Sigmoid, bias=bdec[:, fc:fc + 1]),
                        waits=[(mms, mm_c[g]), (vch, blk_v_done), (dsy, w3)])
                else:
                    dst = ur if kind == 1 else ui
                    ev_c[g] = p.cop("scalar", lambda e, psrc=psrc, dst=dst: e.copy(
                        out=dst[:], in_=psrc), waits=[(mms, mm_c[g]), (vch, blk_v_done)])
                g += 1
            evs_done = ev_c[g - 1]

            it_c = p.cop("gpsimd", lambda e, tb=tb: e.iota(
                trow[:], pattern=[[1, TB]], base=tb * TB, channel_multiplier=0,
                allow_small_or_imprecise_dtypes=True), waits=[(vch, blk_v_done)])

            V = lambda f, w=(): p.cop("vector", f, waits=w)
            V(lambda e, fc=fc: e.tensor_scalar_mul(t1[:], trow[:], om[:, fc:fc + 1]),
              [(pch, it_c), (vch, om_c)])
            V(lambda e: e.tensor_scalar_mul(t2[:], t1[:], 1.0 / TWO_PI))
            V(lambda e: e.tensor_copy(gi_[:].bitcast(I32), t2[:]))
            V(lambda e: e.tensor_copy(t2[:], gi_[:].bitcast(I32)))
            pr = V(lambda e: e.scalar_tensor_tensor(t1[:], t2[:], -TWO_PI, t1[:], ALU.mult, ALU.add))
            sin_c = p.cop("scalar", lambda e: e.activation(out=ss[:], in_=t1[:], func=AF.Sin),
                          waits=[(vch, pr)])
            V(lambda e, fc=fc: e.tensor_scalar(t1[:], trow[:], om[:, fc:fc + 1], math.pi / 2,
                                               ALU.mult, ALU.add), [(ach, sin_c)])
            V(lambda e: e.tensor_scalar_mul(t2[:], t1[:], 1.0 / TWO_PI))
            V(lambda e: e.tensor_copy(gi_[:].bitcast(I32), t2[:]))
            V(lambda e: e.tensor_copy(t2[:], gi_[:].bitcast(I32)))
            pr2 = V(lambda e: e.scalar_tensor_tensor(t1[:], t2[:], -TWO_PI, t1[:], ALU.mult, ALU.add))
            cos_c = p.cop("scalar", lambda e: e.activation(out=cc[:], in_=t1[:], func=AF.Sin),
                          waits=[(vch, pr2)])
            V(lambda e: e.tensor_tensor(t1[:], cc[:], ur[:], ALU.mult), [(ach, cos_c), (ach, evs_done)])
            V(lambda e: e.tensor_tensor(t2[:], ss[:], ui[:], ALU.mult))
            V(lambda e: e.tensor_tensor(t1[:], t1[:], t2[:], ALU.add))
            V(lambda e: e.tensor_tensor(t2[:], cc[:], ui[:], ALU.mult))
            V(lambda e: e.tensor_tensor(ui[:], ss[:], ur[:], ALU.mult))
            V(lambda e: e.tensor_tensor(t2[:], t2[:], ui[:], ALU.subtract))
            if tb == 0:
                V(lambda e, fc=fc: e.memset(carry[:, fc:fc + 1], 0.0))
                V(lambda e, fc=fc: e.memset(carry[:, NFC + fc:NFC + fc + 1], 0.0))
            V(lambda e, fc=fc: e.tensor_tensor_scan(ur[:], dt_[:], t1[:], carry[:, fc:fc + 1],
                                                    ALU.mult, ALU.add))
            V(lambda e, fc=fc: e.tensor_tensor_scan(gi_[:], dt_[:], t2[:], carry[:, NFC + fc:NFC + fc + 1],
                                                    ALU.mult, ALU.add))
            V(lambda e, fc=fc: e.tensor_copy(carry[:, fc:fc + 1], ur[:, TB - 1:TB]))
            V(lambda e, fc=fc: e.tensor_copy(carry[:, NFC + fc:NFC + fc + 1], gi_[:, TB - 1:TB]))
            ystp = yst_c[k - 1] if k > 0 else 0
            V(lambda e: e.tensor_tensor(t1[:], cc[:], ur[:], ALU.mult), [(dsy, ystp)])
            V(lambda e: e.tensor_tensor(t2[:], ss[:], gi_[:], ALU.mult))
            V(lambda e: e.tensor_tensor(t1[:], t1[:], t2[:], ALU.subtract))
            hr_c = V(lambda e: e.tensor_copy(ybf[:, 0:TB], t1[:]))
            V(lambda e: e.tensor_tensor(t1[:], ss[:], ur[:], ALU.mult))
            V(lambda e: e.tensor_tensor(t2[:], cc[:], gi_[:], ALU.mult))
            V(lambda e: e.tensor_tensor(t1[:], t1[:], t2[:], ALU.add))
            hi_c = V(lambda e: e.tensor_copy(ybf[:, TB:2 * TB], t1[:]))
            blk_v_done = hi_c
            p.dma("sync", lambda e, fc=fc, tb=tb: e.dma_start(
                out=t["y_fm"][fc * 128:(fc + 1) * 128, tb * TB:(tb + 1) * TB],
                in_=ybf[:, 0:TB]), waits=[(vch, hr_c)])
            yst_c[k] = p.dma("sync", lambda e, fc=fc, tb=tb: e.dma_start(
                out=t["y_fm"][(NFC + fc) * 128:(NFC + fc + 1) * 128, tb * TB:(tb + 1) * TB],
                in_=ybf[:, TB:2 * TB]), waits=[(vch, hi_c)])

    p.barrier()
    stA.close()
    stB = ExitStack()
    alloc = mk_alloc(stB)
    wout = alloc("p4_wout", [128, HC * D], BF16)
    ygrp = alloc("p4_ygrp", [128, HC * 512], BF16)
    aot = alloc("p4_ao", [128, 2 * D], F32)
    po = alloc("p4_po", [128, 2 * D], F32, psum=True)
    wo_c = p.dma("gpsimd", lambda e: e.dma_start(
        out=wout[:].rearrange("p (c f) -> p c f", c=HC),
        in_=t["W_out"].rearrange("(c p) f -> p c f", p=128)))
    NG = 8
    ao_c = [0] * NT
    mo_c = [0] * NT
    ast_c = [0] * NT
    yld_c = [0] * NG
    for grp in range(NG):
        for c in range(HC):
            yld_c[grp] = p.dma("scalar", lambda e, grp=grp, c=c: e.dma_start(
                out=ygrp[:, c * 512:(c + 1) * 512],
                in_=t["y_fm"][c * 128:(c + 1) * 128, grp * 512:(grp + 1) * 512]),
                waits=[(p.mmsem, mo_c[grp * 4 - 1])] if grp > 0 else [])
        for tt in range(4):
            ti = grp * 4 + tt
            pp = ti % 2

            def em(e, grp=grp, tt=tt, pp=pp):
                out = []
                for c in range(HC):
                    for n in range(2):
                        out.append(e.matmul(
                            po[:, pp * D + n * 512: pp * D + (n + 1) * 512],
                            ygrp[:, c * 512 + tt * 128: c * 512 + tt * 128 + 128],
                            wout[:, c * D + n * 512: c * D + (n + 1) * 512],
                            start=(c == 0), stop=(c == HC - 1)))
                return out
            waits = [(p.dsem["scalar"], yld_c[grp]), (dgp, wo_c)]
            if ti >= 2:
                waits.append((vch, ao_c[ti - 2]))
            mo_c[ti] = p.mmg(em, waits)
            ao_c[ti] = p.cop("vector", lambda e, pp=pp: e.tensor_copy(
                aot[:, pp * D:(pp + 1) * D], po[:, pp * D:(pp + 1) * D]),
                waits=[(p.mmsem, mo_c[ti])] + ([(dsy, ast_c[ti - 2])] if ti >= 2 else []))
            ast_c[ti] = p.dma("sync", lambda e, ti=ti, pp=pp: e.dma_start(
                out=t["attn0"][ti * 128:(ti + 1) * 128, :], in_=aot[:, pp * D:(pp + 1) * D]),
                waits=[(vch, ao_c[ti])])
    p.barrier()
    stB.close()




def phase_routing(p, t, gl, mk_alloc):
    from contextlib import ExitStack
    vch, ach, pch = p.chain["vector"], p.chain["scalar"], p.chain["gpsimd"]
    dsy, dgp, dsc = p.dsem["sync"], p.dsem["gpsimd"], p.dsem["scalar"]
    mms = p.mmsem
    st = ExitStack()
    alloc = mk_alloc(st)

    gw = alloc("p2_gw", [128, HC * E], F32)
    ghat32 = alloc("p2_ghat32", [128, HC * E], F32)
    ghat = alloc("p2_ghat", [128, HC * E], BF16)
    onesc = alloc("p2_ones", [128, 1], F32)
    onesrow = alloc("p2_onesrow", [1, 128], F32)
    iotaB = alloc("p2_iotaB", [128, E], F32)
    xf = alloc("p2_xf", [128, 2 * T], BF16)
    xf32 = alloc("p2_xf32", [128, 2 * 2048], F32)
    lhatE = alloc("p2_lhatE", [32, T], F32)
    mT = alloc("p2_mT", [32, T], F32)
    cum = alloc("p2_cum", [32, T], F32)
    zerot = alloc("p2_zero", [32, T], F32)
    ltok = alloc("p2_ltok", [128, 4 * E], F32)
    maxv = alloc("p2_maxv", [128, 8], F32)
    mi = alloc("p2_mi", [128, 8], U32)
    member = alloc("p2_member", [128, E], F32)
    oh = alloc("p2_oh", [128, E], F32)
    counts_s = alloc("p2_counts", [32, 8], F32)
    countsT = alloc("p2_countsT", [8, 32], F32)
    sel_s = alloc("p2_sel", [8, 1], F32)
    srow = alloc("p2_srow", [1, 32], F32)
    offB = alloc("p2_offB", [128, E], F32)
    Sb = alloc("p2_Sb", [128, E], F32)
    off_s = alloc("p2_off", [32, 1], F32)
    Ssum = alloc("p2_Ssum", [32, 1], F32)
    ranktok = alloc("p2_ranktok", [128, E], F32)
    sc1 = alloc("p2_sc1", [128, E], F32)
    sc14 = alloc("p2_sc14", [128, 4 * E], F32)
    cols = alloc("p2_cols", [128, 20], F32)
    ps_l = alloc("p2_psl", [32, 2048], F32, psum=True)
    ps_t = alloc("p2_pst", [128, 128], F32, psum=True)
    ps_t2 = alloc("p2_pst2", [32, 128], F32, psum=True)
    ps_c = alloc("p2_psc", [32, 8], F32, psum=True)
    ps_b = alloc("p2_psb", [128, E], F32, psum=True)
    mif_all = gl["mif_all"]
    ltok32_all = gl["ltok32_all"]
    idx_all = gl["idx_all"]
    gates_all = gl["gates_all"]
    rstd0 = gl["rstd0"]
    ident = gl["ident"]
    misc_d = t["misc_d"]

    c1 = p.cop("vector", lambda e: e.memset(onesc[:], 1.0))
    c3 = p.cop("vector", lambda e: e.memset(onesrow[:], 1.0))
    c4 = p.cop("vector", lambda e: e.memset(zerot[:], 0.0))
    c5 = p.cop("gpsimd", lambda e: e.iota(iotaB[:], pattern=[[1, E]], base=0,
                                          channel_multiplier=0,
                                          allow_small_or_imprecise_dtypes=True))
    gwl = p.dma("sync", lambda e: e.dma_start(
        out=gw[:].rearrange("p (c e) -> p c e", c=HC),
        in_=t["gate_wT"].rearrange("(c p) e -> p c e", p=128)))
    sell = p.dma("sync", lambda e: e.dma_start(out=sel_s[:], in_=t["sel"]))

    def em_ss(e):
        out = []
        for hc in range(HC):
            out.append(e.matmul(ps_t[0:32, 0:1], gw[:, hc * E:(hc + 1) * E], onesc[:],
                                start=(hc == 0), stop=(hc == HC - 1)))
        return out
    m1 = p.mmg(em_ss, [(dsy, gwl), (vch, c1)])
    sS = p.cop("scalar", lambda e: e.copy(out=Ssum[:], in_=ps_t[0:32, 0:1]),
               waits=[(mms, m1)])
    b1 = p.dma("gpsimd", lambda e: e.dma_start(out=misc_d[0:32], in_=Ssum[:, 0]),
               waits=[(ach, sS)])
    b2 = p.dma("gpsimd", lambda e: e.dma_start(
        out=srow[:], in_=misc_d[0:32].rearrange("(a e) -> a e", a=1)),
        waits=[(dgp, b1)])
    m2 = p.mmg(lambda e: e.matmul(ps_b[:], onesrow[:], srow[:], start=True, stop=True),
               [(dgp, b2), (vch, c3)])
    sb_c = p.cop("scalar", lambda e: e.copy(out=Sb[:], in_=ps_b[:]), waits=[(mms, m2)])
    gh_c = 0
    for hc in range(HC):
        p.cop("vector", lambda e, hc=hc: e.scalar_tensor_tensor(
            ghat32[:, hc * E:(hc + 1) * E], Sb[:], -1.0 / D, gw[:, hc * E:(hc + 1) * E],
            ALU.mult, ALU.add), waits=[(ach, sb_c)])
        gh_c = p.cop("vector", lambda e, hc=hc: e.tensor_copy(
            ghat[:, hc * E:(hc + 1) * E], ghat32[:, hc * E:(hc + 1) * E]))

    mm_prev = [0, 0]
    ev_last = [0]
    ev_half = [0, 0]
    lhat_free = 0
    evac_done = 0
    cnt_mm = 0

    def ranking_pass(pss, src_fm, fp32):
        nonlocal mm_prev, ev_half, lhat_free
        for half in range(2):
            t0 = half * 2048
            for hc in range(HC):
                pp = hc % 2
                if fp32:
                    ldx = p.dma("sync", lambda e, hc=hc, t0=t0, pp=pp: e.dma_start(
                        out=xf32[:, pp * 2048:(pp + 1) * 2048],
                        in_=src_fm[hc * 128:(hc + 1) * 128, t0:t0 + 2048]),
                        waits=[(mms, mm_prev[pp])])
                    ds = dsy
                else:
                    ldx = p.dma("gpsimd", lambda e, hc=hc, t0=t0, pp=pp: e.dma_start(
                        out=xf[:, pp * T + t0: pp * T + t0 + 2048],
                        in_=src_fm[hc * 128:(hc + 1) * 128, t0:t0 + 2048]),
                        waits=[(mms, mm_prev[pp])])
                    ds = dgp

                def em(e, hc=hc, t0=t0, pp=pp, fp32=fp32):
                    out = []
                    for q in range(4):
                        if fp32:
                            rhs = xf32[:, pp * 2048 + q * 512:pp * 2048 + (q + 1) * 512]
                            w = ghat32
                        else:
                            rhs = xf[:, pp * T + t0 + q * 512: pp * T + t0 + (q + 1) * 512]
                            w = ghat
                        out.append(e.matmul(
                            ps_l[0:32, q * 512:(q + 1) * 512],
                            w[:, hc * E:(hc + 1) * E], rhs,
                            start=(hc == 0), stop=(hc == HC - 1)))
                    return out
                mm_prev[pp] = p.mmg(em, [(ds, ldx), (vch, gh_c), (ach, ev_last[0]),
                                         (mms, lhat_free)])
            ev_half[half] = p.cop("scalar", lambda e, t0=t0: e.copy(
                out=lhatE[:, t0:t0 + 2048], in_=ps_l[0:32, :]),
                waits=[(mms, mm_prev[(HC - 1) % 2])])
            ev_last[0] = ev_half[half]

    def tile_transpose(ti):
        nonlocal lhat_free
        lhat_free = p.mmg(lambda e, ti=ti: e.transpose(
            ps_t[:, 0:32], lhatE[0:32, ti * 128:(ti + 1) * 128], ident[0:32, 0:32]),
            [(ach, ev_half[ti // (NT // 2)]), (vch, evac_done)])
        return lhat_free

    def tile_transpose4(t4):
        nonlocal lhat_free
        def em(e, t4=t4):
            out = []
            for j in range(4):
                ti = t4 * 4 + j
                out.append(e.transpose(ps_t[:, j * 32:(j + 1) * 32],
                                       lhatE[0:32, ti * 128:(ti + 1) * 128],
                                       ident[0:32, 0:32]))
            return out
        lhat_free = p.mmg(em, [(ach, ev_half[(t4 * 4) // (NT // 2)]),
                               (ach, ev_half[(t4 * 4 + 3) // (NT // 2)]),
                               (vch, evac_done)])
        return lhat_free

    for b in range(8):
        ranking_pass(b, t["x_fm"][b], False)
        for t4 in range(NT // 4):
            trc = tile_transpose4(t4)
            lt = p.cop("vector", lambda e: e.tensor_copy(ltok[:], ps_t[:]),
                       waits=[(mms, trc)])
            evac_done = lt
            for j in range(4):
                p.cop("vector", lambda e, j=j: e.max(out=maxv[:], in_=ltok[:, j * E:(j + 1) * E]))
                ohc = p.cop("vector", lambda e, j=j: e.tensor_scalar(
                    oh[:], ltok[:, j * E:(j + 1) * E], maxv[:, 3:4], None, ALU.is_ge),
                    waits=[(mms, cnt_mm)])
                cnt_mm = p.mmg(lambda e, b=b, t4=t4, j=j: e.matmul(
                    ps_c[:, b:b + 1], oh[:], onesc[:],
                    start=(t4 == 0 and j == 0),
                    stop=(t4 == NT // 4 - 1 and j == 3),
                    skip_group_check=True),
                    [(vch, ohc)])

    prev_trm = 0
    ranking_pass(8, t["x_fm_own"], False)
    for ti in range(NT):
        trc = tile_transpose(ti)
        lt = p.cop("vector", lambda e: e.tensor_copy(ltok[:, 0:E], ps_t[:, 0:32]),
                   waits=[(mms, trc)])
        p.cop("vector", lambda e: e.max(out=maxv[:], in_=ltok[:, 0:E]))
        p.cop("vector", lambda e: e.max_index(out=mi[:], in_max=maxv[:], in_values=ltok[:, 0:E]))
        mifc = p.cop("vector", lambda e, ti=ti: e.tensor_copy(
            mif_all[:, ti * 4:ti * 4 + 4], mi[:, 0:4]))
        evac_done = mifc
        for kk in range(4):
            p.cop("vector", lambda e, kk=kk, ti=ti: e.tensor_scalar(
                sc14[:, kk * E:(kk + 1) * E], iotaB[:],
                mif_all[:, ti * 4 + kk:ti * 4 + kk + 1], None, ALU.is_equal),
                waits=[(mms, prev_trm), (pch, c5)])
        mp = p.cop("vector", lambda e: e.tensor_reduce(
            member[:], sc14[:].rearrange("p (k e) -> p e k", k=4), axis=AX.X,
            op=ALU.add))
        trm = p.mmg(lambda e: e.transpose(ps_t2[:], member[:], ident[:]),
                    [(vch, mp), (vch, evac_done)])
        prev_trm = trm
        evac_done = p.cop("vector", lambda e, ti=ti: e.tensor_copy(
            mT[0:32, ti * 128:(ti + 1) * 128], ps_t2[0:32, :]), waits=[(mms, trm)])
    ranking_pass(9, t["x_fm_own"], True)
    for t4 in range(NT // 4):
        trc = tile_transpose4(t4)
        evac_done = p.cop("vector", lambda e, t4=t4: e.tensor_copy(
            ltok32_all[:, t4 * 4 * E:(t4 + 1) * 4 * E], ps_t[:]), waits=[(mms, trc)])

    cev = p.cop("scalar", lambda e: e.copy(out=counts_s[:], in_=ps_c[:]),
                waits=[(mms, cnt_mm)])
    d1 = p.dma("gpsimd", lambda e: e.dma_start(out=misc_d[64:320].rearrange("(a b) -> a b", a=32), in_=counts_s[:]), waits=[(ach, cev)])
    d2 = p.dma("gpsimd", lambda e: e.dma_start(
        out=countsT[:], in_=misc_d[64:320].rearrange("(e c) -> c e", c=8),
        allow_slow_non_contiguous=True), waits=[(dgp, d1)])
    m3 = p.mmg(lambda e: e.matmul(ps_t2[0:32, 0:1], countsT[:], sel_s[:],
                                  start=True, stop=True),
               [(dgp, d2), (dsy, sell), (vch, evac_done)])
    offc = p.cop("scalar", lambda e: e.copy(out=off_s[:], in_=ps_t2[0:32, 0:1]),
                 waits=[(mms, m3)])
    d3 = p.dma("gpsimd", lambda e: e.dma_start(out=misc_d[32:64], in_=off_s[:, 0]),
               waits=[(ach, offc)])
    d4 = p.dma("gpsimd", lambda e: e.dma_start(
        out=srow[:], in_=misc_d[32:64].rearrange("(a e) -> a e", a=1)),
        waits=[(dgp, d3)])
    m4 = p.mmg(lambda e: e.matmul(ps_b[:], onesrow[:], srow[:], start=True, stop=True),
               [(dgp, d4)])
    offbc = p.cop("scalar", lambda e: e.copy(out=offB[:], in_=ps_b[:]), waits=[(mms, m4)])

    sc_c = p.cop("vector", lambda e: e.tensor_tensor_scan(
        cum[0:32, :], mT[0:32, :], zerot[0:32, :], 0.0, ALU.add, ALU.add),
        waits=[(vch, c4)])
    rg_c = p.cop("vector", lambda e: e.tensor_scalar(
        cum[0:32, :], cum[0:32, :], off_s[0:32, 0:1], -1.0, ALU.add, ALU.add),
        waits=[(ach, offc)])

    rt_ev = 0
    for ti in range(NT):
        trr = p.mmg(lambda e, ti=ti: e.transpose(
            ps_t[:, 0:32], cum[0:32, ti * 128:(ti + 1) * 128], ident[0:32, 0:32]),
            [(vch, rg_c), (vch, rt_ev)])
        rt_ev = p.cop("vector", lambda e: e.tensor_copy(ranktok[:], ps_t[:, 0:32]),
                      waits=[(mms, trr)])
        for kk in range(4):
            ohc = p.cop("vector", lambda e, ti=ti, kk=kk: e.tensor_scalar(
                oh[:], iotaB[:], mif_all[:, ti * 4 + kk:ti * 4 + kk + 1], None,
                ALU.is_equal), waits=[(vch, rt_ev), (ach, offbc)])
            p.cop("vector", lambda e: e.tensor_tensor(sc1[:], oh[:], ranktok[:], ALU.mult))
            p.cop("vector", lambda e, kk=kk: e.reduce_sum(
                cols[:, 4 + kk:5 + kk], sc1[:], axis=AX.X))
            p.cop("vector", lambda e: e.tensor_tensor(sc1[:], oh[:], offB[:], ALU.mult))
            p.cop("vector", lambda e, kk=kk: e.reduce_sum(
                cols[:, 8 + kk:9 + kk], sc1[:], axis=AX.X))
            p.cop("vector", lambda e, ti=ti: e.tensor_tensor(
                sc1[:], oh[:], ltok32_all[:, ti * E:(ti + 1) * E], ALU.mult))
            p.cop("vector", lambda e, kk=kk: e.reduce_sum(
                cols[:, 16 + kk:17 + kk], sc1[:], axis=AX.X))
        for kk in range(4):
            vd = p.cop("vector", lambda e, kk=kk: e.tensor_single_scalar(
                cols[:, 12 + kk:13 + kk], cols[:, 4 + kk:5 + kk], float(CAP), ALU.is_lt))
            sl = p.cop("vector", lambda e, kk=kk: e.tensor_tensor(
                cols[:, 8 + kk:9 + kk], cols[:, 4 + kk:5 + kk], cols[:, 8 + kk:9 + kk],
                ALU.subtract))
            ixf = p.cop("vector", lambda e, ti=ti, kk=kk: e.scalar_tensor_tensor(
                cols[:, 4 + kk:5 + kk], mif_all[:, ti * 4 + kk:ti * 4 + kk + 1],
                float(CAP), cols[:, 8 + kk:9 + kk], ALU.mult, ALU.add))
            nv = p.cop("vector", lambda e, kk=kk: e.tensor_scalar(
                cols[:, 0 + kk:1 + kk], cols[:, 12 + kk:13 + kk], -1.0, 1.0,
                ALU.mult, ALU.add))
            p.cop("vector", lambda e, kk=kk: e.scalar_tensor_tensor(
                cols[:, 4 + kk:5 + kk], cols[:, 0 + kk:1 + kk], BIGIDX,
                cols[:, 4 + kk:5 + kk], ALU.mult, ALU.add))
            p.cop("vector", lambda e, kk=kk: e.tensor_scalar(
                cols[:, 4 + kk:5 + kk], cols[:, 4 + kk:5 + kk], BIGIDX, 0.0,
                ALU.min, ALU.max))
        p.cop("vector", lambda e, ti=ti: e.tensor_copy(
            idx_all[:, ti * 4:ti * 4 + 4], cols[:, 4:8]))
        p.cop("vector", lambda e, ti=ti: e.tensor_scalar_mul(
            cols[:, 8:12], cols[:, 16:20], rstd0[:, ti:ti + 1]))
        sb_ = p.cop("vector", lambda e: e.tensor_scalar_sub(
            cols[:, 9:12], cols[:, 9:12], cols[:, 8:9]))
        ex = p.cop("scalar", lambda e: e.activation(
            out=cols[:, 9:12], in_=cols[:, 9:12], func=AF.Exp), waits=[(vch, sb_)])
        s1 = p.cop("vector", lambda e: e.memset(cols[:, 8:9], 1.0), waits=[(ach, ex)])
        sm = p.cop("vector", lambda e: e.reduce_sum(cols[:, 0:1], cols[:, 8:12], axis=AX.X))
        rq = p.cop("vector", lambda e: e.reciprocal(cols[:, 0:1], cols[:, 0:1]))
        p.cop("vector", lambda e: e.tensor_scalar_mul(
            cols[:, 8:12], cols[:, 8:12], cols[:, 0:1]))
        p.cop("vector", lambda e, ti=ti: e.tensor_tensor(
            gates_all[:, ti * 4:ti * 4 + 4], cols[:, 8:12], cols[:, 12:16], ALU.mult))
    kc1_ = p.cop("vector", lambda e: e.reduce_sum(sc1[0:32, 0:1], mT[0:32, :], axis=AX.X),
                 waits=[(vch, evac_done)])
    kc1_ = p.cop("vector", lambda e: e.tensor_scalar(
        sc1[0:32, 1:2], off_s[:], -1.0, float(CAP), ALU.mult, ALU.add),
        waits=[(ach, offc)])
    kc1_ = p.cop("vector", lambda e: e.tensor_tensor(
        sc1[0:32, 1:2], sc1[0:32, 1:2], sc1[0:32, 0:1], ALU.min))
    kc1_ = p.cop("vector", lambda e: e.tensor_scalar_max(sc1[0:32, 1:2], sc1[0:32, 1:2], 0.0))
    kb1 = p.dma("gpsimd", lambda e: e.dma_start(out=misc_d[384:416], in_=sc1[0:32, 1]),
                waits=[(vch, kc1_)])
    kb2 = p.dma("gpsimd", lambda e: e.dma_start(
        out=srow[:], in_=misc_d[384:416].rearrange("(a e) -> a e", a=1)),
        waits=[(dgp, kb1)])
    p.cop("vector", lambda e: e.tensor_copy(gl["kept"][:], srow[:]),
          waits=[(dgp, kb2)])

    p.barrier()
    st.close()


def phase_scatter(p, t, gl, mk_alloc):
    from contextlib import ExitStack
    vch = p.chain["vector"]
    dsy, dgp = p.dsem["sync"], p.dsem["gpsimd"]
    st = ExitStack()
    alloc = mk_alloc(st)
    xnt = alloc("p3_xn", [128, 2 * D], BF16)
    idx_all = gl["idx_all"]
    sc_c = [0] * NT
    hold = {}

    def mkreg(e):
        r = e.alloc_register("p3bnd")
        hold["r"] = r
        return e.reg_mov(r, E * CAP - 1)
    p._record("gpsimd", mkreg, [], None, 0)
    for ti in range(NT):
        bx = (ti % 2) * D
        w = [(dgp, sc_c[ti - 2])] if ti >= 2 else []
        ld = p.dma("sync", lambda e, ti=ti, bx=bx: e.dma_start(
            out=xnt[:, bx:bx + D], in_=t["xn_tm"][ti * 128:(ti + 1) * 128, :]), waits=w)
        for kk in range(4):
            sc_c[ti] = p.dma("gpsimd", lambda e, ti=ti, kk=kk, bx=bx: e.indirect_dma_start(
                out=t["buf"], out_offset=IndirectOffsetOnAxis(
                    ap=idx_all[:, ti * 4 + kk:ti * 4 + kk + 1], axis=0),
                in_=xnt[:, bx:bx + D], in_offset=None,
                bounds_check=hold["r"], oob_is_err=False),
                waits=[(dsy, ld)])
    fl = p.dma("gpsimd", lambda e: e.dma_start(out=xnt[:, 0:D], in_=t["buf"][0:128, :]),
               waits=[(dgp, sc_c[NT - 1])])
    p.wait_only("gpsimd", [(dgp, fl)])
    p.barrier()
    st.close()


def phase_shared(p, t, gl, mk_alloc):
    from contextlib import ExitStack
    vch, ach = p.chain["vector"], p.chain["scalar"]
    dsy, dgp = p.dsem["sync"], p.dsem["gpsimd"]
    mms = p.mmsem
    st = ExitStack()
    alloc = mk_alloc(st)
    xnT = alloc("p5_xnT", [128, HC * T], BF16)
    wgu = alloc("p5_wgu", [128, HC * D], BF16)
    sg = alloc("p5_sg", [128, 2 * 512], F32)
    as2 = alloc("p5_as", [128, 2 * 512], BF16)
    ps = alloc("p5_ps", [128, 4 * 512], F32, psum=True)
    ast_c = {}

    wl = p.dma("gpsimd", lambda e: e.dma_start(
        out=wgu[:].rearrange("p (c f) -> p c f", c=HC),
        in_=t["shared_w_gate_up"].rearrange("(c p) f -> p c f", p=128)))
    xt_c = 0
    for c in range(HC):
        xt_c = p.dma("sync", lambda e, c=c: e.dma_start(
            out=xnT[:, c * T:(c + 1) * T], in_=t["xn_tm"][:, c * 128:(c + 1) * 128],
            transpose=True))
    ev = {}
    mmc = {}
    g = 0
    for i in range(4):
        for q in range(8):
            for gu in range(2):
                pp = g % 4
                ic = i if gu == 0 else i + 4

                def em(e, ic=ic, q=q, pp=pp):
                    out = []
                    for hc in range(HC):
                        out.append(e.matmul(
                            ps[:, pp * 512:(pp + 1) * 512],
                            wgu[:, hc * D + ic * 128: hc * D + ic * 128 + 128],
                            xnT[:, hc * T + q * 512: hc * T + (q + 1) * 512],
                            start=(hc == 0), stop=(hc == HC - 1)))
                    return out
                w = [(dgp, wl), (dsy, xt_c)]
                if g - 4 in ev:
                    w.append((ev[g - 4][0], ev[g - 4][1]))
                mmc[g] = p.mmg(em, w)
                if gu == 0:
                    w2 = [(mms, mmc[g])]
                    if g - 3 in ev:
                        w2.append(ev[g - 3])
                    ev[g] = (ach, p.cop("scalar", lambda e, q=q, pp=pp: e.activation(
                        out=sg[:, (q % 2) * 512:(q % 2 + 1) * 512],
                        in_=ps[:, pp * 512:(pp + 1) * 512], func=AF.Silu),
                        waits=w2))
                else:
                    wv = [(mms, mmc[g]), ev[g - 1]]
                    if (i, q - 4) in ast_c:
                        wv.append((dsy, ast_c[(i, q - 4)]))
                    vc = p.cop("vector", lambda e, i=i, q=q, pp=pp: e.tensor_tensor(
                        as2[:, (q % 2) * 512:(q % 2 + 1) * 512],
                        sg[:, (q % 2) * 512:(q % 2 + 1) * 512],
                        ps[:, pp * 512:(pp + 1) * 512], ALU.mult),
                        waits=wv)
                    ev[g] = (vch, vc)
                    ast_c[(i, q)] = p.dma("sync", lambda e, i=i, q=q: e.dma_start(
                        out=t["act_sh"][i * 128:(i + 1) * 128, q * 512:(q + 1) * 512],
                        in_=as2[:, (q % 2) * 512:(q % 2 + 1) * 512]),
                        waits=[(vch, vc)])
                g += 1
    p.barrier()
    st.close()


def phase_experts(p, t, gl, mk_alloc):
    from contextlib import ExitStack
    vch, ach = p.chain["vector"], p.chain["scalar"]
    dsy, dgp, dsc = p.dsem["sync"], p.dsem["gpsimd"], p.dsem["scalar"]
    mms = p.mmsem
    st = ExitStack()
    alloc = mk_alloc(st)
    wguE = alloc("p6_wgu", [128, 2 * HC * D], BF16)
    wdnE = alloc("p6_wdn", [128, 2 * 4 * D], BF16)
    bufT = alloc("p6_bufT", [128, 2 * HC * CAP], BF16)
    sgE = alloc("p6_sg", [128, 4 * CAP], F32)
    actE = alloc("p6_act", [128, 4 * CAP], BF16)
    eot = alloc("p6_eot", [128, 2 * D], BF16)
    ps = alloc("p6_ps", [128, 4 * 512], F32, psum=True)
    psd = alloc("p6_psd", [128, 2 * D], F32, psum=True)

    wld = [0, 0]
    bld = [0, 0]
    gu_done = [0, 0]
    dn_done = [0, 0]
    eo_st = {}
    act_used = 0
    for ei in range(E):
        pp = ei % 2
        wld[pp] = p.dma("gpsimd", lambda e, ei=ei, pp=pp: e.dma_start(
            out=wguE[:, pp * HC * D:(pp + 1) * HC * D].rearrange("p (c f) -> p c f", c=HC),
            in_=t["w_gate_up"][ei].rearrange("(c p) f -> p c f", p=128)),
            waits=[(mms, gu_done[pp])])
        wld[pp] = p.dma("gpsimd", lambda e, ei=ei, pp=pp: e.dma_start(
            out=wdnE[:, pp * 4 * D:(pp + 1) * 4 * D].rearrange("p (c f) -> p c f", c=4),
            in_=t["w_down"][ei].rearrange("(c p) f -> p c f", p=128)),
            waits=[(mms, dn_done[pp])])
        for hc in range(HC):
            bld[pp] = p.dma("sync", lambda e, ei=ei, hc=hc, pp=pp: e.dma_start(
                out=bufT[:, pp * HC * CAP + hc * CAP: pp * HC * CAP + (hc + 1) * CAP],
                in_=t["buf"][ei * CAP:(ei + 1) * CAP, hc * 128:(hc + 1) * 128],
                transpose=True), waits=[(mms, gu_done[pp])])
        if ei == 0:
            ev_prev = {}
        else:
            ev_prev = {i - 8: v for i, v in ev_prev.items() if i >= 6}
        last_gu = 0
        for i in range(8):
            qq = i % 2

            def em(e, ei=ei, i=i, pp=pp, qq=qq):
                out = []
                for hc in range(HC):
                    for q in range(2):
                        out.append(e.matmul(
                            ps[:, qq * 1024 + q * 512: qq * 1024 + (q + 1) * 512],
                            wguE[:, pp * HC * D + hc * D + i * 128: pp * HC * D + hc * D + i * 128 + 128],
                            bufT[:, pp * HC * CAP + hc * CAP + q * 512: pp * HC * CAP + hc * CAP + (q + 1) * 512],
                            start=(hc == 0), stop=(hc == HC - 1),
                            skip_group_check=True))
                return out
            w = [(dgp, wld[pp]), (dsy, bld[pp])]
            if i - 2 in ev_prev:
                w.append(ev_prev[i - 2])
            last_gu = p.mmg(em, w)
            if i < 4:
                ev_prev[i] = (ach, p.cop("scalar", lambda e, i=i, qq=qq: e.activation(
                    out=sgE[:, i * CAP: (i + 1) * CAP],
                    in_=ps[:, qq * 1024:(qq + 1) * 1024], func=AF.Silu),
                    waits=[(mms, last_gu), (vch, act_used)]))
            else:
                ev_prev[i] = (vch, p.cop("vector", lambda e, i=i, qq=qq: e.tensor_tensor(
                    actE[:, (i - 4) * CAP: (i - 3) * CAP],
                    sgE[:, (i - 4) * CAP: (i - 3) * CAP],
                    ps[:, qq * 1024:(qq + 1) * 1024], ALU.mult),
                    waits=[(mms, last_gu), ev_prev[i - 4], (vch, act_used)]))
        gu_done[pp] = last_gu
        last_dn = 0
        for sti in range(8):
            qq = sti % 2

            def em(e, ei=ei, sti=sti, pp=pp, qq=qq):
                out = []
                for ic in range(4):
                    for n in range(2):
                        out.append(e.matmul(
                            psd[:, qq * D + n * 512: qq * D + (n + 1) * 512],
                            actE[:, ic * CAP + sti * 128: ic * CAP + sti * 128 + 128],
                            wdnE[:, pp * 4 * D + ic * D + n * 512: pp * 4 * D + ic * D + (n + 1) * 512],
                            start=(ic == 0), stop=(ic == 3),
                            skip_group_check=True))
                return out
            if ei == 0 and sti == 0:
                eo_ev = [0] * 8
            w = [ev_prev[7], (dgp, wld[pp])]
            w.append((vch, eo_ev[sti - 2]))
            last_dn = p.mmg(em, w)

            wv = [(mms, last_dn)]
            if sti >= 2:
                wv.append((dsy, eo_st.get(sti - 2, 0)))
            eo_ev[sti] = p.cop("vector", lambda e, sti=sti, qq=qq: e.tensor_copy(
                eot[:, qq * D:(qq + 1) * D], psd[:, qq * D:(qq + 1) * D]), waits=wv)
            eo_st[sti] = p.dma("sync", lambda e, ei=ei, sti=sti, qq=qq: e.dma_start(
                out=t["eo"][ei * CAP + sti * 128: ei * CAP + (sti + 1) * 128, :],
                in_=eot[:, qq * D:(qq + 1) * D]), waits=[(p.chain["vector"], eo_ev[sti])])
        dn_done[pp] = last_dn
        act_used = eo_ev[7]
    p.barrier()
    st.close()


def phase_epilogue(p, t, gl, mk_alloc):
    from contextlib import ExitStack
    vch, ach = p.chain["vector"], p.chain["scalar"]
    dsy, dgp = p.dsem["sync"], p.dsem["gpsimd"]
    mms = p.mmsem
    st = ExitStack()
    alloc = mk_alloc(st)
    wdn = alloc("p7_wdn", [128, 4 * D], BF16)
    gt = alloc("p7_gt", [128, 8 * D], BF16)
    xa = alloc("p7_xa", [128, 2 * D], F32)
    at = alloc("p7_at", [128, 2 * D], F32)
    racc = alloc("p7_racc", [128, D], F32)
    x1t = alloc("p7_x1", [128, 2 * D], F32)
    actg = alloc("p7_actg", [128, 4 * 512], BF16)
    psd = alloc("p7_psd", [128, 2 * D], F32, psum=True)
    idx_all = gl["idx_all"]
    gates_all = gl["gates_all"]

    wl = p.dma("gpsimd", lambda e: e.dma_start(
        out=wdn[:].rearrange("p (c f) -> p c f", c=4),
        in_=t["shared_w_down"].rearrange("(c p) f -> p c f", p=128)))
    gz = p.cop("vector", lambda e: e.memset(gt[:], 0.0))
    hold = {}

    def mkreg(e):
        r = e.alloc_register("p7bnd")
        hold["r"] = r
        return e.reg_mov(r, E * CAP - 1)
    p._record("gpsimd", mkreg, [], None, 0)
    x1_st = [0] * NT
    r_done = [0] * NT
    mm_c = [0] * NT
    x1_c = [0] * NT
    agl = 0
    for ti in range(NT):
        pp = ti % 2
        if ti % 4 == 0:
            grp = ti // 4
            wg = [(mms, mm_c[ti - 1])] if ti else []
            for ic in range(4):
                agl = p.dma("sync", lambda e, grp=grp, ic=ic: e.dma_start(
                    out=actg[:, ic * 512:(ic + 1) * 512],
                    in_=t["act_sh"][ic * 128:(ic + 1) * 128, grp * 512:(grp + 1) * 512]),
                    waits=wg)
        gld = 0
        gb = (ti % 2) * 4 * D
        for kk in range(4):
            gld = p.dma("gpsimd", lambda e, ti=ti, kk=kk, gb=gb: e.indirect_dma_start(
                out=gt[:, gb + kk * D: gb + (kk + 1) * D], out_offset=None,
                in_=t["eo"], in_offset=IndirectOffsetOnAxis(
                    ap=idx_all[:, ti * 4 + kk:ti * 4 + kk + 1], axis=0),
                bounds_check=hold["r"], oob_is_err=False),
                waits=[(vch, r_done[ti - 2] if ti >= 2 else gz), (dgp, wl)])
        wre = [(vch, x1_c[ti - 2])] if ti >= 2 else []
        xld = p.dma("sync", lambda e, ti=ti, pp=pp: e.dma_start(
            out=xa[:, pp * D:(pp + 1) * D], in_=t["x_tm"][ti * 128:(ti + 1) * 128, :]),
            waits=wre)
        ald = p.dma("sync", lambda e, ti=ti, pp=pp: e.dma_start(
            out=at[:, pp * D:(pp + 1) * D], in_=t["attn0"][ti * 128:(ti + 1) * 128, :]),
            waits=wre)

        def em(e, ti=ti, pp=pp):
            out = []
            tt = ti % 4
            for ic in range(4):
                for n in range(2):
                    out.append(e.matmul(
                        psd[:, pp * D + n * 512: pp * D + (n + 1) * 512],
                        actg[:, ic * 512 + tt * 128: ic * 512 + tt * 128 + 128],
                        wdn[:, ic * D + n * 512: ic * D + (n + 1) * 512],
                        start=(ic == 0), stop=(ic == 3)))
            return out
        w = [(dgp, wl), (dsy, agl)]
        if ti >= 2:
            w.append((vch, x1_c[ti - 2]))
        mm_c[ti] = p.mmg(em, w)
        p.cop("vector", lambda e, ti=ti, gb=gb: e.tensor_scalar_mul(
            racc[:], gt[:, gb:gb + D], gates_all[:, ti * 4:ti * 4 + 1]),
            waits=[(dgp, gld)])
        for kk in range(1, 4):
            p.cop("vector", lambda e, ti=ti, kk=kk, gb=gb: e.scalar_tensor_tensor(
                racc[:], gt[:, gb + kk * D: gb + (kk + 1) * D],
                gates_all[:, ti * 4 + kk:ti * 4 + kk + 1], racc[:],
                ALU.mult, ALU.add))
        r_done[ti] = p.chain["vector"].cnt
        p.cop("vector", lambda e, pp=pp: e.tensor_tensor(
            x1t[:, pp * D:(pp + 1) * D], xa[:, pp * D:(pp + 1) * D],
            at[:, pp * D:(pp + 1) * D], ALU.add), waits=[(dsy, ald)])
        p.cop("vector", lambda e, pp=pp: e.tensor_tensor(
            x1t[:, pp * D:(pp + 1) * D], x1t[:, pp * D:(pp + 1) * D], racc[:], ALU.add))
        x1_c[ti] = p.cop("vector", lambda e, pp=pp: e.tensor_tensor(
            x1t[:, pp * D:(pp + 1) * D], x1t[:, pp * D:(pp + 1) * D],
            psd[:, pp * D:(pp + 1) * D], ALU.add), waits=[(mms, mm_c[ti])])
        x1_st[ti] = p.dma("sync", lambda e, ti=ti, pp=pp: e.dma_start(
            out=t["x1_tm"][ti * 128:(ti + 1) * 128, :], in_=x1t[:, pp * D:(pp + 1) * D]),
            waits=[(vch, x1_c[ti])])
    p.barrier()
    st.close()


def build_program(phases=("p1", "p2", "p3", "p4", "p5", "p6", "p7", "p8", "p9", "p10"), dbg=()):
    from contextlib import ExitStack
    nc = bass.Bass("TRN2", target_bir_lowering=False)
    t = declare_io(nc)
    for d in dbg:
        if d.startswith("sb:"):
            continue
        t["dbg_" + d] = nc.dram_tensor("dbg_" + d, list(t[d].shape),
                                       t[d].dtype, kind="ExternalOutput").ap()
    stack = ExitStack()
    with stack:
        block = stack.enter_context(nc.Block())
        p = Prog(nc, block, stack)

        def mk_alloc(st):
            def alloc(name, shape, dtype=F32, psum=False):
                if psum:
                    return st.enter_context(nc.psum_tensor(name, shape, dtype))
                return st.enter_context(nc.sbuf_tensor(name, shape, dtype))
            return alloc

        galloc = mk_alloc(stack)
        gl = {}
        gl["rstd0"] = galloc("g_rstd0", [128, NT], F32)
        gl["m0"] = galloc("g_m0", [128, NT], F32)
        gl["rstd1"] = galloc("g_rstd1", [128, NT], F32)
        gl["m1"] = galloc("g_m1", [128, NT], F32)
        gl["ident"] = galloc("g_ident", [128, 128], F32)
        gl["mif_all"] = galloc("g_mif", [128, 4 * NT], F32)
        gl["ltok32_all"] = galloc("g_ltok32", [128, E * NT], F32)
        gl["idx_all"] = galloc("g_idx", [128, 4 * NT], I32)
        gl["gates_all"] = galloc("g_gates", [128, 4 * NT], F32)
        gl["sc32"] = galloc("g_sc32", [128, 2], F32)
        gl["kept"] = galloc("g_kept", [1, E], I32)
        p.cst = galloc("g_cst", [128, 4], F32)
        p.cop("vector", lambda e: e.memset(p.cst[:, 0:1], EPS_LN))
        p.cst_rdy = p.cop("vector", lambda e: e.memset(p.cst[:, 1:2], 1.0))

        with ExitStack() as st0:
            a0 = mk_alloc(st0)
            i1 = a0("g_i1", [128, 128], F32)
            c1 = p.cop("gpsimd", lambda e: e.iota(i1[:], pattern=[[1, 128]], base=0,
                                                  channel_multiplier=0,
                                                  allow_small_or_imprecise_dtypes=True))
            c2 = p.cop("gpsimd", lambda e: e.iota(gl["ident"][:], pattern=[[0, 128]], base=0,
                                                  channel_multiplier=1,
                                                  allow_small_or_imprecise_dtypes=True))
            p.cop("vector", lambda e: e.tensor_tensor(gl["ident"][:], gl["ident"][:],
                                                      i1[:], ALU.is_equal),
                  waits=[(p.chain["gpsimd"], c2)])
            cv = a0("g_cv", [1, 1], F32)
            cl = p.dma("sync", lambda e: e.dma_start(out=cv[:], in_=t["curv"].rearrange(
                "(a b) -> a b", a=1)))
            l1 = p.cop("scalar", lambda e: e.activation(out=cv[:], in_=cv[:], func=AF.Ln),
                       waits=[(p.dsem["sync"], cl)])
            l2 = p.cop("scalar", lambda e: e.activation(out=cv[:], in_=cv[:], func=AF.Exp,
                                                        scale=0.5))
            onc = a0("g_onc", [1, 128], F32)
            oc = p.cop("vector", lambda e: e.memset(onc[:], 1.0))
            psb = a0("g_psb", [128, 1], F32, psum=True)
            mmb = p.mmg(lambda e: e.matmul(psb[:], onc[:], cv[:], start=True, stop=True),
                        [(p.chain["scalar"], l2), (p.chain["vector"], oc)])
            gl["sc32_rdy"] = p.cop("vector", lambda e: e.tensor_scalar_mul(
                gl["sc32"][:, 0:1], psb[:], 32.0), waits=[(p.mmsem, mmb)])
            p.barrier()

        if "p1" in phases:
            phase_ln(p, t, t["x_tm"], t["xn_tm"], gl["rstd0"], gl["m0"], mk_alloc, tag="ln0")
        if "p2" in phases:
            phase_routing(p, t, gl, mk_alloc)
        if "p3" in phases:
            phase_scatter(p, t, gl, mk_alloc)
        if "p4" in phases:
            phase_l0_attn(p, t, mk_alloc)
        if "p5" in phases:
            phase_shared(p, t, gl, mk_alloc)
        if "p6" in phases:
            phase_experts(p, t, gl, mk_alloc)
        if "p7" in phases:
            phase_epilogue(p, t, gl, mk_alloc)
        if "p8" in phases:
            phase_ln(p, t, t["x1_tm"], t["xn1_tm"], gl["rstd1"], gl["m1"], mk_alloc,
                     tag="ln1", htan_dram=t["htan_tm"], sb=gl)
        if "p9" in phases:
            phase_l1_ffn(p, t, gl, mk_alloc)
        if "p10" in phases:
            phase_l1_attn(p, t, gl, mk_alloc)

        for d in dbg:
            if d.startswith("sb:"):
                name = d[3:]
                src_t = gl[name]
                dt_ = I32 if name == "idx_all" else F32
                t["dbg_" + name] = nc.dram_tensor(
                    "dbg_" + name, list(src_t[:].shape), dt_, kind="ExternalOutput").ap()
                p.dma("sync", lambda e, name=name, src_t=src_t: e.dma_start(
                    out=t["dbg_" + name], in_=src_t[:]))
            else:
                p.dma("sync", lambda e, d=d: e.dma_start(out=t["dbg_" + d], in_=t[d]))
        if dbg:
            p.wait_only("sync", [(p.dsem["sync"], p.dsem["sync"].cnt)])
        p.emit()
    return nc, t


def make_inputs(core, full):
    f32 = np.float32
    x = np.ascontiguousarray(full["L0_attn_x"], dtype=f32)
    m = {}
    m["x_tm"] = np.ascontiguousarray(x[core])
    m["x_fm"] = np.ascontiguousarray(np.transpose(x, (0, 2, 1)))
    m["x_fm_own"] = np.ascontiguousarray(x[core].T)
    selv = np.zeros((8, 1), f32)
    selv[:core] = 1.0
    m["sel"] = selv
    m["W_spec"] = full["L0_attn_W_spec"]
    m["W_decay"] = full["L0_attn_W_decay"]
    m["b_decay"] = full["L0_attn_b_decay"]
    m["freqs"] = full["L0_attn_freqs"]
    m["W_out"] = full["L0_attn_W_out"]
    m["gate_wT"] = np.ascontiguousarray(np.asarray(full["L0_ffn_gate_w"]).T)
    m["w_gate_up"] = full["L0_ffn_w_gate_up"]
    m["w_down"] = full["L0_ffn_w_down"]
    m["shared_w_gate_up"] = full["L0_ffn_shared_w_gate_up"]
    m["shared_w_down"] = full["L0_ffn_shared_w_down"]
    m["curv"] = np.ascontiguousarray(np.asarray(full["L1_attn_curv"])).reshape(1)
    m["wqT"] = np.ascontiguousarray(np.asarray(full["L1_attn_wq_w"]).T)
    m["wqb"] = full["L1_attn_wq_b"]
    m["wkT"] = np.ascontiguousarray(np.asarray(full["L1_attn_wk_w"]).T)
    m["wkb"] = full["L1_attn_wk_b"]
    m["wvT"] = np.ascontiguousarray(np.asarray(full["L1_attn_wv_w"]).T)
    m["wvb"] = full["L1_attn_wv_b"]
    m["woT"] = np.ascontiguousarray(np.asarray(full["L1_attn_wo_w"]).T)
    m["wob"] = full["L1_attn_wo_b"]
    m["WrT"] = np.ascontiguousarray(np.asarray(full["L1_ffn_W_r"]).T)
    m["WkT"] = np.ascontiguousarray(np.asarray(full["L1_ffn_W_k"]).T)
    m["WvT"] = np.ascontiguousarray(np.asarray(full["L1_ffn_W_v"]).T)
    m["mu_r"] = full["L1_ffn_mu_r"]
    m["mu_k"] = full["L1_ffn_mu_k"]
    return {k: np.ascontiguousarray(np.asarray(v), dtype=f32) for k, v in m.items()}


def kernel(**inputs):
    nc, t = build_program()
    in_maps = [make_inputs(c, inputs) for c in range(8)]
    res = run_bass_kernel_spmd(nc, in_maps, core_ids=list(range(8)))
    return np.concatenate([np.asarray(r["out"])[None] for r in res.results], axis=0)


def phase_l1_ffn(p, t, gl, mk_alloc):
    from contextlib import ExitStack
    vch, ach = p.chain["vector"], p.chain["scalar"]
    dsy, dgp = p.dsem["sync"], p.dsem["gpsimd"]
    mms = p.mmsem
    st = ExitStack()
    alloc = mk_alloc(st)
    TBK = 1024
    NBK = T // TBK
    NTT = TBK // 128
    xn1T = alloc("p9_xnT", [128, HC * T], BF16)
    wr = alloc("p9_wr", [128, HC * D], BF16)
    wk = alloc("p9_wk", [128, HC * D], BF16)
    wv = alloc("p9_wv", [128, HC * D], BF16)
    mus = alloc("p9_mu", [128, 2 * HC], F32)
    dsc_t = alloc("p9_d", [128, TBK], F32)
    xrT = alloc("p9_xr", [128, HC * TBK], BF16)
    xkT = alloc("p9_xk", [128, HC * TBK], BF16)
    kact = alloc("p9_kact", [128, HC * TBK], BF16)
    rblk = alloc("p9_rblk", [128, 2 * D], BF16)
    krelu = alloc("p9_krelu", [128, 512], F32)
    ot = alloc("p9_ot", [128, 2 * D], F32)
    ps = alloc("p9_ps", [128, 2 * D], F32, psum=True)
    psk = alloc("p9_psk", [128, 2 * 512], F32, psum=True)

    w1 = p.dma("gpsimd", lambda e: e.dma_start(
        out=wr[:].rearrange("p (c f) -> p c f", c=HC),
        in_=t["WrT"].rearrange("(c p) f -> p c f", p=128)))
    w2 = p.dma("gpsimd", lambda e: e.dma_start(
        out=wk[:].rearrange("p (c f) -> p c f", c=HC),
        in_=t["WkT"].rearrange("(c p) f -> p c f", p=128)))
    w3 = p.dma("gpsimd", lambda e: e.dma_start(
        out=wv[:].rearrange("p (c f) -> p c f", c=HC),
        in_=t["WvT"].rearrange("(c p) f -> p c f", p=128)))
    m1 = p.dma("sync", lambda e: e.dma_start(
        out=mus[:, 0:HC], in_=t["mu_r"].rearrange("(c p) -> p c", p=128),
        allow_slow_non_contiguous=True))
    m2 = p.dma("sync", lambda e: e.dma_start(
        out=mus[:, HC:2 * HC], in_=t["mu_k"].rearrange("(c p) -> p c", p=128),
        allow_slow_non_contiguous=True))
    xl = 0
    for c in range(HC):
        xl = p.dma("sync", lambda e, c=c: e.dma_start(
            out=xn1T[:, c * T:(c + 1) * T], in_=t["xn1_tm"][:, c * 128:(c + 1) * 128],
            transpose=True))

    ka_bank = [0, 0]
    r_ev = [0] * NT
    o_ev = [0] * NT
    o_st = [0] * NT
    last_mm_xr = 0
    for bk in range(NBK):
        t0 = bk * TBK
        lerp_c = 0
        for c in range(HC):
            base = c * T + t0
            w0 = [(dsy, xl), (dsy, m2), (mms, last_mm_xr)]
            if bk == 0:
                p.cop("vector", lambda e, base=base: e.tensor_tensor(
                    dsc_t[:, 1:TBK], xn1T[:, base + 1:base + TBK],
                    xn1T[:, base:base + TBK - 1], ALU.subtract), waits=w0)
                p.cop("vector", lambda e, base=base: e.tensor_copy(
                    dsc_t[:, 0:1], xn1T[:, base:base + 1]))
                p.cop("vector", lambda e, c=c, base=base: e.scalar_tensor_tensor(
                    xrT[:, c * TBK + 1:(c + 1) * TBK], dsc_t[:, 1:TBK], mus[:, c:c + 1],
                    xn1T[:, base:base + TBK - 1], ALU.mult, ALU.add))
                p.cop("vector", lambda e, c=c: e.tensor_scalar_mul(
                    xrT[:, c * TBK:c * TBK + 1], dsc_t[:, 0:1], mus[:, c:c + 1]))
                p.cop("vector", lambda e, c=c, base=base: e.scalar_tensor_tensor(
                    xkT[:, c * TBK + 1:(c + 1) * TBK], dsc_t[:, 1:TBK],
                    mus[:, HC + c:HC + c + 1],
                    xn1T[:, base:base + TBK - 1], ALU.mult, ALU.add))
                lerp_c = p.cop("vector", lambda e, c=c: e.tensor_scalar_mul(
                    xkT[:, c * TBK:c * TBK + 1], dsc_t[:, 0:1], mus[:, HC + c:HC + c + 1]))
            else:
                p.cop("vector", lambda e, base=base: e.tensor_tensor(
                    dsc_t[:], xn1T[:, base:base + TBK],
                    xn1T[:, base - 1:base + TBK - 1], ALU.subtract), waits=w0)
                p.cop("vector", lambda e, c=c, base=base: e.scalar_tensor_tensor(
                    xrT[:, c * TBK:(c + 1) * TBK], dsc_t[:], mus[:, c:c + 1],
                    xn1T[:, base - 1:base + TBK - 1], ALU.mult, ALU.add))
                lerp_c = p.cop("vector", lambda e, c=c, base=base: e.scalar_tensor_tensor(
                    xkT[:, c * TBK:(c + 1) * TBK], dsc_t[:], mus[:, HC + c:HC + c + 1],
                    xn1T[:, base - 1:base + TBK - 1], ALU.mult, ALU.add))
        kact_done = 0
        for dc in range(HC):
            for tq in range(2):
                g = dc * 2 + tq
                pp = g % 2

                def em(e, dc=dc, tq=tq, pp=pp):
                    out = []
                    for c in range(HC):
                        out.append(e.matmul(
                            psk[:, pp * 512:(pp + 1) * 512],
                            wk[:, c * D + dc * 128: c * D + dc * 128 + 128],
                            xkT[:, c * TBK + tq * 512: c * TBK + (tq + 1) * 512],
                            start=(c == 0), stop=(c == HC - 1)))
                    return out
                mmk = p.mmg(em, [(vch, lerp_c), (dgp, w2), (vch, ka_bank[pp])])
                p.cop("vector", lambda e, pp=pp: e.tensor_relu(
                    krelu[:], psk[:, pp * 512:(pp + 1) * 512]), waits=[(mms, mmk)])
                ka_bank[pp] = kact_done = p.cop(
                    "vector", lambda e, dc=dc, tq=tq: e.tensor_tensor(
                        kact[:, dc * TBK + tq * 512: dc * TBK + (tq + 1) * 512],
                        krelu[:], krelu[:], ALU.mult))
        for tt in range(NTT):
            ti = bk * NTT + tt
            pp = tt % 2

            def em_r(e, tt=tt, pp=pp):
                out = []
                for c in range(HC):
                    for n in range(2):
                        out.append(e.matmul(
                            ps[:, pp * D + n * 512: pp * D + (n + 1) * 512],
                            xrT[:, c * TBK + tt * 128: c * TBK + tt * 128 + 128],
                            wr[:, c * D + n * 512: c * D + (n + 1) * 512],
                            start=(c == 0), stop=(c == HC - 1)))
                return out
            w = [(vch, lerp_c), (dgp, w1)]
            if ti >= 2:
                w.append((vch, o_ev[ti - 2]))
            mmr_ = p.mmg(em_r, w)
            last_mm_xr = mmr_
            r_ev[ti] = p.cop("scalar", lambda e, tt=tt, pp=pp: e.activation(
                out=rblk[:, (tt % 2) * D:(tt % 2 + 1) * D],
                in_=ps[:, pp * D:(pp + 1) * D], func=AF.Sigmoid),
                waits=[(mms, mmr_)] + ([(vch, o_ev[ti - 2])] if ti >= 2 else []))

            def em_v(e, tt=tt, pp=pp):
                out = []
                for c in range(HC):
                    for n in range(2):
                        out.append(e.matmul(
                            ps[:, pp * D + n * 512: pp * D + (n + 1) * 512],
                            kact[:, c * TBK + tt * 128: c * TBK + tt * 128 + 128],
                            wv[:, c * D + n * 512: c * D + (n + 1) * 512],
                            start=(c == 0), stop=(c == HC - 1)))
                return out
            mmv = p.mmg(em_v, [(vch, kact_done), (ach, r_ev[ti]), (dgp, w3)])
            last_mm_xr = mmv
            o_ev[ti] = p.cop("vector", lambda e, tt=tt, pp=pp: e.tensor_tensor(
                ot[:, pp * D:(pp + 1) * D], rblk[:, (tt % 2) * D:(tt % 2 + 1) * D],
                ps[:, pp * D:(pp + 1) * D], ALU.mult),
                waits=[(mms, mmv)] + ([(dsy, o_st[ti - 2])] if ti >= 2 else []))
            o_st[ti] = p.dma("sync", lambda e, ti=ti, pp=pp: e.dma_start(
                out=t["ffn1"][ti * 128:(ti + 1) * 128, :], in_=ot[:, pp * D:(pp + 1) * D]),
                waits=[(vch, o_ev[ti])])
    p.barrier()
    st.close()


def phase_l1_attn(p, t, gl, mk_alloc):
    from contextlib import ExitStack
    vch, ach = p.chain["vector"], p.chain["scalar"]
    dsy, dgp = p.dsem["sync"], p.dsem["gpsimd"]
    mms = p.mmsem
    stA = ExitStack()
    alloc = mk_alloc(stA)
    htT = alloc("pa_htT", [128, HC * T], BF16)
    wq = alloc("pa_wq", [128, HC * D], BF16)
    wkk = alloc("pa_wk", [128, HC * D], BF16)
    wvv = alloc("pa_wv", [128, HC * D], BF16)
    biasB = alloc("pa_biasB", [128, 3 * D], F32)
    brow = alloc("pa_brow", [1, D], F32)
    onesrow = alloc("pa_ones", [1, 128], F32)
    qtt = alloc("pa_qtt", [128, D], F32)
    sqt = alloc("pa_sqt", [128, D], F32)
    n2 = alloc("pa_n2", [128, 2 * NH], F32)
    cfc = alloc("pa_cfc", [128, 8 * NH], F32)
    qhb = alloc("pa_qhb", [128, 3 * D], BF16)
    qtk = alloc("pa_qtk", [128, 128], BF16)
    sqc = alloc("pa_sqc", [128, 2], F32)
    ps = alloc("pa_ps", [128, 3 * D], F32, psum=True)
    psb = alloc("pa_psb", [128, 512], F32, psum=True)

    wl1 = p.dma("gpsimd", lambda e: e.dma_start(
        out=wq[:].rearrange("p (c f) -> p c f", c=HC),
        in_=t["wqT"].rearrange("(c p) f -> p c f", p=128)))
    wl2 = p.dma("gpsimd", lambda e: e.dma_start(
        out=wkk[:].rearrange("p (c f) -> p c f", c=HC),
        in_=t["wkT"].rearrange("(c p) f -> p c f", p=128)))
    wl3 = p.dma("gpsimd", lambda e: e.dma_start(
        out=wvv[:].rearrange("p (c f) -> p c f", c=HC),
        in_=t["wvT"].rearrange("(c p) f -> p c f", p=128)))
    hl = 0
    for c in range(HC):
        hl = p.dma("sync", lambda e, c=c: e.dma_start(
            out=htT[:, c * T:(c + 1) * T], in_=t["htan_tm"][:, c * 128:(c + 1) * 128],
            transpose=True))
    oc = p.cop("vector", lambda e: e.memset(onesrow[:], 1.0))
    bb_c = 0
    mb = 0
    for wi, wname in enumerate(("wqb", "wkb", "wvb")):
        bl = p.dma("sync", lambda e, wname=wname: e.dma_start(
            out=brow[:], in_=t[wname].rearrange("(a f) -> a f", a=1)),
            waits=[(mms, mb)])
        for n in range(2):
            mb = p.mmg(lambda e, n=n: e.matmul(
                psb[:], onesrow[:], brow[:, n * 512:(n + 1) * 512], start=True, stop=True),
                [(dsy, bl), (vch, oc), (ach, bb_c)])
            bb_c = p.cop("scalar", lambda e, wi=wi, n=n: e.copy(
                out=biasB[:, wi * D + n * 512: wi * D + (n + 1) * 512], in_=psb[:]),
                waits=[(mms, mb)])
    sq_c = p.cop("vector", lambda e: e.tensor_scalar_mul(
        sqc[:, 0:1], gl["sc32"][:, 0:1], 1.0 / 32.0), waits=[(vch, gl["sc32_rdy"])])
    p.cop("vector", lambda e: e.tensor_tensor(
        sqc[:, 1:2], sqc[:, 0:1], sqc[:, 0:1], ALU.mult))
    iv_c = p.cop("vector", lambda e: e.reciprocal(sqc[:, 1:2], sqc[:, 1:2]))

    def hyp_map(kind, psrc):
        bi = 0 if kind == "q" else D
        add = p.cop("vector", lambda e: e.tensor_tensor(
            qtt[:], psrc, biasB[:, bi:bi + D], ALU.add))
        sq = p.cop("scalar", lambda e: e.activation(out=sqt[:], in_=qtt[:], func=AF.Square),
                   waits=[(vch, add)])
        rd = p.cop("vector", lambda e: e.tensor_reduce(
            n2[:, 0:NH], sqt[:].rearrange("p (h d) -> p h d", h=NH), axis=AX.X,
            op=ALU.add), waits=[(ach, sq)])
        l1 = p.cop("scalar", lambda e: e.activation(out=n2[:, NH:2 * NH], in_=n2[:, 0:NH],
                                                    func=AF.Ln), waits=[(vch, rd)])
        e1 = p.cop("scalar", lambda e: e.activation(out=n2[:, NH:2 * NH],
                                                    in_=n2[:, NH:2 * NH],
                                                    func=AF.Exp, scale=0.5))
        z = p.cop("vector", lambda e: e.tensor_tensor(
            n2[:, NH:2 * NH], n2[:, NH:2 * NH], sqc[:, 0:1].to_broadcast([128, NH]),
            ALU.mult), waits=[(ach, e1), (vch, iv_c)])
        z = p.cop("vector", lambda e: e.tensor_scalar_max(
            n2[:, NH:2 * NH], n2[:, NH:2 * NH], 1e-7))
        ep = p.cop("scalar", lambda e: e.activation(out=cfc[:, 0:NH], in_=n2[:, NH:2 * NH],
                                                    func=AF.Exp), waits=[(vch, z)])
        em_ = p.cop("scalar", lambda e: e.activation(out=cfc[:, NH:2 * NH],
                                                     in_=n2[:, NH:2 * NH],
                                                     func=AF.Exp, scale=-1.0))
        p.cop("vector", lambda e: e.tensor_tensor(
            cfc[:, 0:NH], cfc[:, 0:NH], cfc[:, NH:2 * NH], ALU.subtract),
            waits=[(ach, em_)])
        p.cop("vector", lambda e: e.tensor_scalar_mul(cfc[:, 0:NH], cfc[:, 0:NH], 0.5))
        p.cop("vector", lambda e: e.reciprocal(cfc[:, NH:2 * NH], n2[:, NH:2 * NH]))
        p.cop("vector", lambda e: e.tensor_tensor(
            cfc[:, 2 * NH:3 * NH], cfc[:, 0:NH], cfc[:, NH:2 * NH], ALU.mult))
        import os as _os
        if _os.environ.get("KDBG_NOCOEF"):
            qh = p.cop("vector", lambda e, bi=bi: e.tensor_copy(
                qhb[:, bi:bi + D], qtt[:]))
        else:
            qh = p.cop("vector", lambda e, bi=bi: e.tensor_tensor(
                qhb[:, bi:bi + D].rearrange("p (h d) -> p h d", h=NH),
                qtt[:].rearrange("p (h d) -> p h d", h=NH),
                cfc[:, 2 * NH:3 * NH].rearrange("p (h o) -> p h o", o=1).to_broadcast(
                    [128, NH, HD]), ALU.mult))
        p.cop("vector", lambda e: e.tensor_tensor(
            cfc[:, 3 * NH:4 * NH], cfc[:, 0:NH], cfc[:, 0:NH], ALU.mult))
        t2 = p.cop("vector", lambda e: e.tensor_tensor(
            cfc[:, 3 * NH:4 * NH], cfc[:, 3 * NH:4 * NH],
            sqc[:, 1:2].to_broadcast([128, NH]), ALU.add))
        s1 = p.cop("scalar", lambda e: e.activation(out=cfc[:, 4 * NH:5 * NH],
                                                    in_=cfc[:, 3 * NH:4 * NH],
                                                    func=AF.Ln), waits=[(vch, t2)])
        s2 = p.cop("scalar", lambda e: e.activation(out=cfc[:, 4 * NH:5 * NH],
                                                    in_=cfc[:, 4 * NH:5 * NH],
                                                    func=AF.Exp, scale=0.5))
        qt_ = p.cop("vector", lambda e, kind=kind: e.tensor_copy(
            qtk[:, (0 if kind == "q" else NH):(NH if kind == "q" else 2 * NH)],
            cfc[:, 4 * NH:5 * NH]), waits=[(ach, s2)])
        return max(qh, qt_)

    tile_done = [0] * NT
    store_c = [0] * NT
    kind_done = {"q": 0, "k": 0, "v": 0}
    for ti in range(NT):
        pp = ti % 2
        done_c = 0
        pst = store_c[ti - 1] if ti else 0
        for ki_, (wt, kind, dstd, qoff) in enumerate((
                (wq, "q", t["qh_tm"], 0),
                (wkk, "k", t["kh_tm"], D),
                (wvv, "v", t["v_tm"], 2 * D))):
            def em(e, wt=wt, ti=ti, ki_=ki_):
                out = []
                for c in range(HC):
                    for n in range(2):
                        out.append(e.matmul(
                            ps[:, ki_ * D + n * 512: ki_ * D + (n + 1) * 512],
                            htT[:, c * T + ti * 128: c * T + ti * 128 + 128],
                            wt[:, c * D + n * 512: c * D + (n + 1) * 512],
                            start=(c == 0), stop=(c == HC - 1)))
                return out
            w = [(dsy, hl), (dgp, wl3), (ach, bb_c), (vch, kind_done[kind])]
            mmq = p.mmg(em, w)
            if kind == "v":
                done_c = p.cop("vector", lambda e, ki_=ki_: e.tensor_tensor(
                    qhb[:, 2 * D:3 * D], ps[:, ki_ * D:(ki_ + 1) * D],
                    biasB[:, 2 * D:3 * D], ALU.add), waits=[(mms, mmq), (dsy, pst)])
            else:
                p.cop("vector", lambda e: e.engine_nop(), waits=[(mms, mmq), (dsy, pst)])
                done_c = hyp_map(kind, ps[:, ki_ * D:(ki_ + 1) * D])
            kind_done[kind] = done_c
            store_c[ti] = p.dma("sync", lambda e, ti=ti, dstd=dstd, qoff=qoff: e.dma_start(
                out=dstd[ti * 128:(ti + 1) * 128, :], in_=qhb[:, qoff:qoff + D]),
                waits=[(vch, done_c)])
        tile_done[ti] = p.cop("vector", lambda e: e.memset(qtk[:, 2 * NH:128], 0.0),
                              waits=[(dsy, pst)])
        store_c[ti] = p.dma("sync", lambda e, ti=ti: e.dma_start(
            out=t["qtkt"][ti * 128:(ti + 1) * 128, :], in_=qtk[:]),
            waits=[(vch, tile_done[ti])])
    p.barrier()
    stA.close()

    stB = ExitStack()
    alloc = mk_alloc(stB)
    qkT = alloc("pb_qkT", [128, T], BF16)
    Mrow = alloc("pb_M", [32, T], F32)
    Msc = alloc("pb_Msc", [16, T], F32)
    rowsb = alloc("pb_rowsb", [32, T], BF16)
    mneg = alloc("pb_mneg", [16, T], BF16)
    sqc2 = alloc("pb_sqc", [128, 4], F32)
    qtd = t["misc_d"]
    rowd = p.nc.dram_tensor("pb_rowd", [48, T], BF16).ap()
    onesd = p.nc.dram_tensor("pb_onesd", [1, T], BF16).ap()

    qk_l = p.dma("sync", lambda e: e.dma_start(out=qkT[:], in_=t["qtkt"], transpose=True))
    iv2 = p.cop("vector", lambda e: e.tensor_tensor(
        sqc2[:, 0:1], gl["sc32"][:, 0:1], gl["sc32"][:, 0:1], ALU.mult),
        waits=[(vch, gl["sc32_rdy"])])
    iv2 = p.cop("vector", lambda e: e.tensor_scalar_mul(sqc2[:, 0:1], sqc2[:, 0:1],
                                                        1.0 / 1024.0))
    iv2 = p.cop("vector", lambda e: e.reciprocal(sqc2[:, 0:1], sqc2[:, 0:1]))
    m_c = p.cop("vector", lambda e: e.tensor_copy(Mrow[0:32, :], qkT[0:32, :]),
                waits=[(dsy, qk_l)])
    m_c = p.cop("vector", lambda e: e.tensor_tensor(
        Mrow[0:32, :], Mrow[0:32, :], Mrow[0:32, :], ALU.mult))
    m_c = p.cop("vector", lambda e: e.tensor_scalar_sub(
        Mrow[0:32, :], Mrow[0:32, :], sqc2[0:32, 0:1]), waits=[(vch, iv2)])
    m_c = p.cop("vector", lambda e: e.tensor_scalar_max(Mrow[0:32, :], Mrow[0:32, :], 0.0))
    m_sq = p.cop("scalar", lambda e: e.activation(out=Mrow[0:32, :], in_=Mrow[0:32, :],
                                                  func=AF.Ln, bias=p.cst[0:32, 0:1]),
                 waits=[(vch, m_c), (vch, p.cst_rdy)])
    m_sq = p.cop("scalar", lambda e: e.activation(out=Mrow[0:32, :], in_=Mrow[0:32, :],
                                                  func=AF.Exp, scale=0.5))
    mx = p.cop("vector", lambda e: e.reduce_max(sqc2[0:32, 1:2], qkT[0:32, :], axis=AX.X),
               waits=[(ach, m_sq)])
    mx = p.cop("vector", lambda e: e.reduce_max(sqc2[0:32, 2:3], Mrow[0:32, :], axis=AX.X))
    b1 = p.dma("gpsimd", lambda e: e.dma_start(out=t["misc_d"][320:352],
                                               in_=sqc2[0:32, 1]), waits=[(vch, mx)])
    b2 = p.dma("gpsimd", lambda e: e.dma_start(out=t["misc_d"][352:384],
                                               in_=sqc2[0:32, 2]), waits=[(vch, mx)])
    b3 = p.dma("gpsimd", lambda e: e.dma_start(
        out=sqc2[0:16, 1:2], in_=t["misc_d"][336:352].rearrange("(p a) -> p a", a=1)),
        waits=[(dgp, b2)])
    b4 = p.dma("gpsimd", lambda e: e.dma_start(
        out=sqc2[0:16, 2:3], in_=t["misc_d"][368:384].rearrange("(p a) -> p a", a=1)),
        waits=[(dgp, b2)])
    mm_t = p.cop("vector", lambda e: e.tensor_scalar(
        Msc[0:16, :], qkT[0:16, :], sqc2[0:16, 1:2], 0.125, ALU.mult, ALU.mult),
        waits=[(dgp, b4)])
    mm_t = p.cop("vector", lambda e: e.tensor_scalar(
        Mrow[0:16, :], Mrow[0:16, :], sqc2[0:16, 2:3], 0.125, ALU.mult, ALU.mult))
    mm_t = p.cop("vector", lambda e: e.tensor_tensor(
        Mrow[0:16, :], Mrow[0:16, :], Msc[0:16, :], ALU.add))
    mm_t = p.cop("vector", lambda e: e.tensor_scalar_mul(
        rowsb[0:32, :], qkT[0:32, :], 0.125))
    mm_t = p.cop("vector", lambda e: e.tensor_scalar_mul(
        mneg[0:16, :], Mrow[0:16, :], -1.0))
    r1 = p.dma("sync", lambda e: e.dma_start(out=rowd[0:16, :], in_=rowsb[16:32, :]),
               waits=[(vch, mm_t)])
    r1 = p.dma("sync", lambda e: e.dma_start(out=rowd[16:32, :], in_=mneg[0:16, :]),
               waits=[(vch, mm_t)])
    r2 = p.dma("sync", lambda e: e.dma_start(out=rowd[32:48, :], in_=qkT[0:16, :]),
               waits=[(vch, mm_t)])
    oz = p.cop("vector", lambda e: e.memset(rowsb[0:1, :], 1.0))
    r3 = p.dma("sync", lambda e: e.dma_start(out=onesd[:], in_=rowsb[0:1, :]),
               waits=[(vch, oz)])
    p.barrier()
    stB.close()

    stC = ExitStack()
    alloc = mk_alloc(stC)
    QT2 = alloc("pc_qt2", [128, T], BF16)
    KT2 = alloc("pc_kt2", [128, T], BF16)
    Kop = alloc("pc_Kop", [66, T], BF16)
    Qop = alloc("pc_Qop", [66, T], BF16)
    vh = alloc("pc_vh", [128, NT * 72], BF16)
    PT = alloc("pc_PT", [128, 2 * 8 * 2048], BF16)
    rrow = alloc("pc_rrow", [1, 512], F32)
    onum = alloc("pc_onum", [64, 512], F32)
    OT = alloc("pc_OT", [128, HC * T], BF16)
    onesc2 = alloc("pc_ones2", [1, 64], F32)
    psS = alloc("pc_psS", [128, 2048], F32, psum=True)
    psS2 = alloc("pc_psS2", [128, 1536], F32, psum=True)
    psO = alloc("pc_psO", [65, 512], F32, psum=True)
    o2 = p.cop("vector", lambda e: e.memset(onesc2[:], 1.0))
    vz = p.cop("vector", lambda e: e.memset(
        vh[:].rearrange("p (kc x) -> p kc x", kc=NT)[:, :, 64:65], 1.0))

    exp_c = 0
    av_prev = 0
    av_pp = [0, 0]
    oe_prev = 0
    oe_prev_v = 0
    bm_prev = 0
    on_prev = 0
    pair_s_done = 0
    head_free = 0
    last_qk_read = 0
    for hp in range(HC):
        ql = p.dma("sync", lambda e, hp=hp: e.dma_start(
            out=QT2[:], in_=t["qh_tm"][:, hp * 128:(hp + 1) * 128], transpose=True),
            waits=[(mms, pair_s_done), (vch, last_qk_read)])
        kl = p.dma("sync", lambda e, hp=hp: e.dma_start(
            out=KT2[:], in_=t["kh_tm"][:, hp * 128:(hp + 1) * 128], transpose=True),
            waits=[(mms, pair_s_done), (vch, last_qk_read)])
        for hh in range(2):
            h = hp * 2 + hh
            hb = hh * 64
            vl = p.dma("sync", lambda e, h=h: e.dma_start(
                out=vh[:].rearrange("p (kc x) -> p kc x", kc=NT)[:, :, 0:64],
                in_=t["v_tm"].rearrange("(kc p) f -> p kc f", p=128)[:, :, h * 64:(h + 1) * 64]),
                waits=[(mms, av_prev)])
            w0 = [(dsy, kl), (mms, head_free)]
            kc1 = p.cop("vector", lambda e, hb=hb: e.tensor_scalar_mul(
                Kop[0:64, :], KT2[hb:hb + 64, :], -0.125), waits=w0)
            k64 = p.dma("gpsimd", lambda e, h=h: e.dma_start(
                out=Kop[64:65, :], in_=rowd[h:h + 1, :]), waits=[(mms, head_free)])
            k65 = p.dma("gpsimd", lambda e: e.dma_start(
                out=Kop[65:66, :], in_=onesd[:]), waits=[(mms, head_free)])
            qc1 = p.cop("vector", lambda e, hb=hb: e.tensor_copy(
                Qop[0:64, :], QT2[hb:hb + 64, :]), waits=[(dsy, ql), (mms, head_free)])
            last_qk_read = qc1
            q64 = p.dma("gpsimd", lambda e, h=h: e.dma_start(
                out=Qop[64:65, :], in_=rowd[32 + h:33 + h, :]), waits=[(mms, head_free)])
            q65 = p.dma("gpsimd", lambda e, h=h: e.dma_start(
                out=Qop[65:66, :], in_=rowd[16 + h:17 + h, :]), waits=[(mms, head_free)])
            GRPS = [4, 3, 4, 3, 4, 3, 4, 3, 4]
            exp_pp = [exp_c, exp_c]
            for qb in range(8):
                bsel = (qb % 2) * 8 * 2048
                kc0 = 0
                for gi_, gw_ in enumerate(GRPS):
                    buf = psS if gi_ % 2 == 0 else psS2

                    def em(e, kc0=kc0, gw_=gw_, qb=qb, buf=buf):
                        out = []
                        for j in range(gw_):
                            kc = kc0 + j
                            out.append(e.matmul(
                                buf[:, j * 512:(j + 1) * 512],
                                Kop[:, kc * 128:(kc + 1) * 128],
                                Qop[:, qb * 512:(qb + 1) * 512],
                                start=True, stop=True, skip_group_check=True))
                        return out
                    sm = p.mmg(em, [(vch, kc1), (vch, qc1), (dgp, q65),
                                    (ach, exp_pp[gi_ % 2])])
                    exp_pp[gi_ % 2] = exp_c = p.cop(
                        "scalar", lambda e, kc0=kc0, gw_=gw_, bsel=bsel, buf=buf: e.activation(
                            out=PT[:, bsel + kc0 * 512: bsel + (kc0 + gw_) * 512],
                            in_=buf[:, 0:gw_ * 512], func=AF.Exp),
                        waits=[(mms, sm), (mms, av_pp[qb % 2])])
                    kc0 += gw_

                def emav(e, bsel=bsel):
                    out = []
                    for kc in range(NT):
                        out.append(e.matmul(
                            psO[:], vh[:, kc * 72: kc * 72 + 65],
                            PT[:, bsel + kc * 512: bsel + (kc + 1) * 512],
                            start=(kc == 0), stop=(kc == NT - 1)))
                    return out
                av_prev = p.mmg(emav, [(ach, exp_c), (dsy, vl), (vch, vz),
                                       (vch, oe_prev), (mms, bm_prev), (vch, on_prev)])
                av_pp[qb % 2] = av_prev
                rc = p.cop("vector", lambda e: e.reciprocal(rrow[:], psO[64:65, :]),
                           waits=[(mms, av_prev)])
                oe_prev_v = p.cop("vector", lambda e: e.tensor_copy(onum[:], psO[0:64, :]),
                                  waits=[(vch, rc)])
                oe_prev = oe_prev_v
                bm_prev = p.mmg(lambda e: e.matmul(psO[0:64, :], onesc2[:], rrow[:],
                                                   start=True, stop=True),
                                [(vch, rc), (vch, o2), (vch, oe_prev), (vch, on_prev)])
                on_prev = p.cop("vector", lambda e, hp=hp, hb=hb, qb=qb: e.tensor_tensor(
                    OT[hb:hb + 64, hp * T + qb * 512: hp * T + (qb + 1) * 512],
                    onum[:], psO[0:64, :], ALU.mult), waits=[(vch, oe_prev), (mms, bm_prev)])
            head_free = av_prev
            pair_s_done = av_prev
    p.barrier()
    stC.close()

    stD = ExitStack()
    alloc = mk_alloc(stD)
    wo = alloc("pd_wo", [128, HC * D], BF16)
    bB = alloc("pd_bB", [128, D], F32)
    brow2 = alloc("pd_brow", [1, D], F32)
    ones2 = alloc("pd_ones", [1, 128], F32)
    x1t = alloc("pd_x1", [128, 2 * D], F32)
    f1t = alloc("pd_f1", [128, 2 * D], F32)
    outt = alloc("pd_out", [128, 2 * D], F32)
    pso = alloc("pd_pso", [128, 2 * D], F32, psum=True)
    psb3 = alloc("pd_psb3", [128, 512], F32, psum=True)
    wol = p.dma("gpsimd", lambda e: e.dma_start(
        out=wo[:].rearrange("p (c f) -> p c f", c=HC),
        in_=t["woT"].rearrange("(c p) f -> p c f", p=128)))
    o3 = p.cop("vector", lambda e: e.memset(ones2[:], 1.0))
    bl = p.dma("sync", lambda e: e.dma_start(
        out=brow2[:], in_=t["wob"].rearrange("(a f) -> a f", a=1)))
    bb2 = 0
    for n in range(2):
        mb = p.mmg(lambda e, n=n: e.matmul(
            psb3[:], ones2[:], brow2[:, n * 512:(n + 1) * 512], start=True, stop=True),
            [(dsy, bl), (vch, o3), (ach, bb2)])
        bb2 = p.cop("scalar", lambda e, n=n: e.copy(
            out=bB[:, n * 512:(n + 1) * 512], in_=psb3[:]), waits=[(mms, mb)])
    o_c = [0] * NT
    ost = [0] * NT
    for ti in range(NT):
        pp = ti % 2

        def em(e, ti=ti, pp=pp):
            out = []
            for c in range(HC):
                for n in range(2):
                    out.append(e.matmul(
                        pso[:, pp * D + n * 512: pp * D + (n + 1) * 512],
                        OT[:, c * T + ti * 128: c * T + ti * 128 + 128],
                        wo[:, c * D + n * 512: c * D + (n + 1) * 512],
                        start=(c == 0), stop=(c == HC - 1)))
            return out
        w = [(dgp, wol), (ach, bb2)]
        if ti >= 2:
            w.append((vch, o_c[ti - 2]))
        mo = p.mmg(em, w)
        wre = [(vch, o_c[ti - 2])] if ti >= 2 else []
        xld = p.dma("sync", lambda e, ti=ti, pp=pp: e.dma_start(
            out=x1t[:, pp * D:(pp + 1) * D], in_=t["x1_tm"][ti * 128:(ti + 1) * 128, :]),
            waits=wre)
        fld = p.dma("sync", lambda e, ti=ti, pp=pp: e.dma_start(
            out=f1t[:, pp * D:(pp + 1) * D], in_=t["ffn1"][ti * 128:(ti + 1) * 128, :]),
            waits=wre)
        p.cop("vector", lambda e, pp=pp: e.tensor_tensor(
            outt[:, pp * D:(pp + 1) * D], x1t[:, pp * D:(pp + 1) * D],
            f1t[:, pp * D:(pp + 1) * D], ALU.add),
            waits=[(dsy, fld)] + ([(dsy, ost[ti - 2])] if ti >= 2 else []))
        p.cop("vector", lambda e, pp=pp: e.tensor_tensor(
            outt[:, pp * D:(pp + 1) * D], outt[:, pp * D:(pp + 1) * D], bB[:], ALU.add))
        o_c[ti] = p.cop("vector", lambda e, pp=pp: e.tensor_tensor(
            outt[:, pp * D:(pp + 1) * D], outt[:, pp * D:(pp + 1) * D],
            pso[:, pp * D:(pp + 1) * D], ALU.add), waits=[(mms, mo)])
        ost[ti] = p.dma("sync", lambda e, ti=ti, pp=pp: e.dma_start(
            out=t["out"][ti * 128:(ti + 1) * 128, :], in_=outt[:, pp * D:(pp + 1) * D]),
            waits=[(vch, o_c[ti])])
    p.wait_only("sync", [(dsy, ost[NT - 1])])
    p.barrier()
    stD.close()


# version 13
# speedup vs baseline: 1.2581x; 1.0094x over previous
import math
import numpy as np
import concourse.bass as bass
import concourse.mybir as mybir
from concourse.bass_utils import run_bass_kernel_spmd
from concourse.bass_core_rust import IndirectOffsetOnAxis

F32 = mybir.dt.float32
BF16 = mybir.dt.bfloat16
I32 = mybir.dt.int32
U32 = mybir.dt.uint32
AF = mybir.ActivationFunctionType
ALU = mybir.AluOpType
AX = mybir.AxisListType

T = 4096
D = 1024
NT = T // 128
HC = D // 128
F = 512
NFC = F // 128
E = 32
CAP = 1024
NH = 16
HD = 64
EPS_LN = 1e-5
TWO_PI = 2.0 * math.pi
BIGIDX = float(1 << 22)
ENGS = ("sync", "tensor", "vector", "scalar", "gpsimd")
CHAINED = ("vector", "scalar", "gpsimd")


class Sem:
    __slots__ = ("h", "cnt", "name")

    def __init__(self, nc, stack, name):
        self.h = stack.enter_context(nc.semaphore(name))
        self.cnt = 0
        self.name = name

    def bump(self, n=1):
        self.cnt += n
        return self.cnt


class Prog:
    def __init__(self, nc, block, stack):
        self.nc = nc
        self.block = block
        self.stack = stack
        self.ops = {k: [] for k in ENGS}
        self.sems = {}
        self.chain = {k: self.sem("ch_" + k) for k in CHAINED}
        self.dsem = {k: self.sem("d_" + k) for k in ("sync", "scalar", "gpsimd")}
        self.mmsem = self.sem("mmg")
        self.waited = {k: {} for k in ENGS}

    def sem(self, name):
        if name not in self.sems:
            self.sems[name] = Sem(self.nc, self.stack, name)
        return self.sems[name]

    def _record(self, eng, emit, waits, inc, inc_n):
        wl = []
        for s, c in waits:
            if c <= 0:
                continue
            w = self.waited[eng]
            if w.get(s.name, 0) >= c:
                continue
            w[s.name] = c
            wl.append((s.h, c))
        self.ops[eng].append((wl, emit, None if inc is None else inc.h, inc_n))
        return inc.bump(inc_n) if inc is not None else None

    def cop(self, eng, emit, waits=()):
        ch = self.chain[eng]
        w = list(waits) + [(ch, ch.cnt)]
        return self._record(eng, emit, w, ch, 1)

    def aop(self, eng, emit, waits, sem):
        return self._record(eng, emit, waits, sem, 1)

    def dma(self, eng, emit, waits=()):
        return self._record(eng, emit, waits, self.dsem[eng], 16)

    def mmg(self, emit, waits=()):
        return self._record("tensor", emit, waits, self.mmsem, 1)

    def wait_only(self, eng, waits):
        self._record(eng, lambda e: None, waits, None, 0)

    def emit(self):
        blk = self.block
        decs = {"sync": blk.sync, "tensor": blk.tensor, "vector": blk.vector,
                "scalar": blk.scalar, "gpsimd": blk.gpsimd}
        for name in ENGS:
            entries = self.ops[name]
            if not entries:
                continue

            def make(entries):
                def _(e):
                    for wl, emit, inch, incn in entries:
                        for h, c in wl:
                            e.wait_ge(h, c)
                        ins = emit(e)
                        if inch is not None:
                            if isinstance(ins, (list, tuple)):
                                ins = ins[-1]
                            ins.then_inc(inch, incn)
                return _
            decs[name](make(entries))
            self.ops[name] = []

    def barrier(self):
        waits = [(s, s.cnt) for s in self.sems.values() if s.cnt > 0]
        for eng in ENGS:
            self.wait_only(eng, waits)
        self.emit()


def declare_io(nc):
    t = {}
    di = lambda n, sh, dt=F32: nc.dram_tensor(n, sh, dt, kind="ExternalInput").ap()
    dint = lambda n, sh, dt: nc.dram_tensor(n, sh, dt).ap()
    t["x_tm"] = di("x_tm", [T, D])
    t["x_fm"] = di("x_fm", [8, D, T])
    t["x_fm_own"] = di("x_fm_own", [D, T])
    t["sel"] = di("sel", [8, 1])
    t["W_spec"] = di("W_spec", [D, 2 * F])
    t["W_decay"] = di("W_decay", [D, F])
    t["b_decay"] = di("b_decay", [F])
    t["freqs"] = di("freqs", [F])
    t["W_out"] = di("W_out", [2 * F, D])
    t["gate_wT"] = di("gate_wT", [D, E])
    t["w_gate_up"] = di("w_gate_up", [E, D, 2 * F])
    t["w_down"] = di("w_down", [E, F, D])
    t["shared_w_gate_up"] = di("shared_w_gate_up", [D, 2 * F])
    t["shared_w_down"] = di("shared_w_down", [F, D])
    t["curv"] = di("curv", [1])
    for w in ("wq", "wk", "wv", "wo"):
        t[w + "T"] = di(w + "T", [D, D])
        t[w + "b"] = di(w + "b", [D])
    t["WrT"] = di("WrT", [D, D])
    t["WkT"] = di("WkT", [D, D])
    t["WvT"] = di("WvT", [D, D])
    t["mu_r"] = di("mu_r", [D])
    t["mu_k"] = di("mu_k", [D])
    t["out"] = nc.dram_tensor("out", [T, D], F32, kind="ExternalOutput").ap()

    t["xn_tm"] = dint("xn_tm", [T, D], BF16)
    t["y_fm"] = dint("y_fm", [2 * F, T], BF16)
    t["attn0"] = dint("attn0", [T, D], F32)
    t["buf"] = dint("buf", [E * CAP, D], BF16)
    t["eo"] = dint("eo", [E * CAP, D], BF16)
    t["x1_tm"] = dint("x1_tm", [T, D], F32)
    t["xn1_tm"] = dint("xn1_tm", [T, D], BF16)
    t["htan_tm"] = dint("htan_tm", [T, D], BF16)
    t["ffn1"] = dint("ffn1", [T, D], F32)
    t["qh_tm"] = dint("qh_tm", [T, D], BF16)
    t["kh_tm"] = dint("kh_tm", [T, D], BF16)
    t["v_tm"] = dint("v_tm", [T, D], BF16)
    t["qtkt"] = dint("qtkt", [T, 128], BF16)
    t["misc_d"] = dint("misc_d", [512], F32)
    t["act_sh"] = dint("act_sh", [4 * 128, T], BF16)
    return t


def phase_ln(p, t, x_dram, xn_dram, rstd_dst, m_dst, mk_alloc, tag="ln0",
             htan_dram=None, sb=None):
    from contextlib import ExitStack
    st = ExitStack()
    alloc = mk_alloc(st)
    vch, ach = p.chain["vector"], p.chain["scalar"]
    dsy = p.dsem["sync"]
    dgp = p.dsem["gpsimd"]
    xt = alloc(tag + "_x", [128, 2 * D], F32)
    sqt = alloc(tag + "_sq", [128, D], F32)
    xnf = alloc(tag + "_xnf", [128, D], F32)
    xnt = alloc(tag + "_xn", [128, 2 * D], BF16)
    htt = alloc(tag + "_ht", [128, 2 * D], BF16) if htan_dram is not None else None
    cols = alloc(tag + "_cols", [128, 4 * NT], F32)
    CSS, CV, CT0, CT1 = (k * NT for k in range(4))
    xn_c = [0] * NT
    st_c = [0] * NT
    if htan_dram is not None:
        sc32, sc32_rdy = sb["sc32"], sb["sc32_rdy"]

    for i in range(NT):
        bx = (i % 2) * D
        ld = p.dma("sync", lambda e, i=i, bx=bx: e.dma_start(
            out=xt[:, bx:bx + D], in_=x_dram[i * 128:(i + 1) * 128, :]),
            waits=[(vch, xn_c[i - 2])] if i >= 2 else [])
        sq = p.cop("scalar", lambda e, i=i, bx=bx: e.activation(
            out=sqt[:], in_=xt[:, bx:bx + D], func=AF.Square,
            accum_out=cols[:, CSS + i:CSS + i + 1]), waits=[(dsy, ld)])
        p.cop("vector", lambda e, i=i, bx=bx: e.reduce_sum(
            m_dst[:, i:i + 1], xt[:, bx:bx + D], axis=AX.X), waits=[(dsy, ld)])
        p.cop("vector", lambda e, i=i: e.tensor_scalar_mul(
            m_dst[:, i:i + 1], m_dst[:, i:i + 1], 1.0 / D))
        p.cop("vector", lambda e, i=i: e.tensor_scalar_mul(
            cols[:, CT0 + i:CT0 + i + 1], cols[:, CSS + i:CSS + i + 1], 1.0 / D),
            waits=[(ach, sq)])
        p.cop("vector", lambda e, i=i: e.tensor_tensor(
            cols[:, CV + i:CV + i + 1], m_dst[:, i:i + 1], m_dst[:, i:i + 1], ALU.mult))
        var = p.cop("vector", lambda e, i=i: e.tensor_tensor(
            cols[:, CV + i:CV + i + 1], cols[:, CT0 + i:CT0 + i + 1],
            cols[:, CV + i:CV + i + 1], ALU.subtract))
        lnv = p.cop("scalar", lambda e, i=i: e.activation(
            out=cols[:, CT0 + i:CT0 + i + 1], in_=cols[:, CV + i:CV + i + 1],
            func=AF.Ln, bias=p.cst[:, 0:1]), waits=[(vch, var), (vch, p.cst_rdy)])
        rstd = p.cop("scalar", lambda e, i=i: e.activation(
            out=rstd_dst[:, i:i + 1], in_=cols[:, CT0 + i:CT0 + i + 1],
            func=AF.Exp, scale=-0.5))
        xnf_c = p.cop("vector", lambda e, i=i, bx=bx: e.tensor_scalar(
            xnf[:], xt[:, bx:bx + D],
            m_dst[:, i:i + 1], rstd_dst[:, i:i + 1], ALU.subtract, ALU.mult),
            waits=[(ach, rstd)])
        xn_c[i] = p.cop("vector", lambda e, i=i, bx=bx: e.tensor_copy(
            xnt[:, bx:bx + D], xnf[:]),
            waits=([(dgp, st_c[i - 2])] if i >= 2 else []))
        st_c[i] = p.dma("gpsimd", lambda e, i=i, bx=bx: e.dma_start(
            out=xn_dram[i * 128:(i + 1) * 128, :], in_=xnt[:, bx:bx + D]),
            waits=[(vch, xn_c[i])])
        if htan_dram is not None:
            lnv2 = p.cop("scalar", lambda e, i=i: e.activation(
                out=cols[:, CT1 + i:CT1 + i + 1], in_=cols[:, CV + i:CV + i + 1],
                func=AF.Ln))
            sb_ = p.cop("vector", lambda e, i=i: e.tensor_tensor(
                cols[:, CT1 + i:CT1 + i + 1], cols[:, CT1 + i:CT1 + i + 1],
                cols[:, CT0 + i:CT0 + i + 1], ALU.subtract), waits=[(ach, lnv2)])
            rt = p.cop("scalar", lambda e, i=i: e.activation(
                out=cols[:, CT1 + i:CT1 + i + 1], in_=cols[:, CT1 + i:CT1 + i + 1],
                func=AF.Exp, scale=0.5), waits=[(vch, sb_)])
            z = p.cop("vector", lambda e, i=i: e.tensor_tensor(
                cols[:, CT1 + i:CT1 + i + 1], cols[:, CT1 + i:CT1 + i + 1],
                sc32[:, 0:1], ALU.mult), waits=[(ach, rt), (vch, sc32_rdy)])
            z = p.cop("vector", lambda e, i=i: e.tensor_scalar_max(
                cols[:, CT1 + i:CT1 + i + 1], cols[:, CT1 + i:CT1 + i + 1], 1e-7))
            z2 = p.cop("vector", lambda e, i=i: e.tensor_tensor(
                cols[:, CSS + i:CSS + i + 1], cols[:, CT1 + i:CT1 + i + 1],
                cols[:, CT1 + i:CT1 + i + 1], ALU.mult))
            l1 = p.cop("scalar", lambda e, i=i: e.activation(
                out=cols[:, CSS + i:CSS + i + 1], in_=cols[:, CSS + i:CSS + i + 1],
                func=AF.Ln, bias=p.cst[:, 1:2]), waits=[(vch, z2)])
            s1 = p.cop("scalar", lambda e, i=i: e.activation(
                out=cols[:, CSS + i:CSS + i + 1], in_=cols[:, CSS + i:CSS + i + 1],
                func=AF.Exp, scale=0.5))
            w_ = p.cop("vector", lambda e, i=i: e.tensor_tensor(
                cols[:, CSS + i:CSS + i + 1], cols[:, CSS + i:CSS + i + 1],
                cols[:, CT1 + i:CT1 + i + 1], ALU.add), waits=[(ach, s1)])
            asinh = p.cop("scalar", lambda e, i=i: e.activation(
                out=cols[:, CSS + i:CSS + i + 1], in_=cols[:, CSS + i:CSS + i + 1],
                func=AF.Ln), waits=[(vch, w_)])
            p.cop("vector", lambda e, i=i: e.reciprocal(
                cols[:, CT1 + i:CT1 + i + 1], cols[:, CT1 + i:CT1 + i + 1]),
                waits=[(ach, asinh)])
            cf = p.cop("vector", lambda e, i=i: e.tensor_tensor(
                cols[:, CSS + i:CSS + i + 1], cols[:, CSS + i:CSS + i + 1],
                cols[:, CT1 + i:CT1 + i + 1], ALU.mult))
            ht_c = p.cop("vector", lambda e, i=i, bx=bx: e.tensor_scalar_mul(
                htt[:, bx:bx + D], xnf[:], cols[:, CSS + i:CSS + i + 1]),
                waits=[(vch, cf)] + ([(dgp, st_c[i - 2])] if i >= 2 else []))
            st_c[i] = p.dma("gpsimd", lambda e, i=i, bx=bx: e.dma_start(
                out=htan_dram[i * 128:(i + 1) * 128, :], in_=htt[:, bx:bx + D]),
                waits=[(vch, ht_c)])
    p.barrier()
    st.close()


def phase_l0_attn(p, t, mk_alloc):
    from contextlib import ExitStack
    TB = 2048
    NTB = T // TB
    NQ = TB // 512
    vch, ach, pch = p.chain["vector"], p.chain["scalar"], p.chain["gpsimd"]
    dsy, dgp, dsc = p.dsem["sync"], p.dsem["gpsimd"], p.dsem["scalar"]
    mms = p.mmsem

    stA = ExitStack()
    alloc = mk_alloc(stA)
    xnT = alloc("p4_xnT", [128, HC * T], BF16)
    wdec = alloc("p4_wdec", [128, HC * F], BF16)
    wspec = alloc("p4_wspec", [128, HC * 2 * F], BF16)
    bdec = alloc("p4_bdec", [128, NFC], F32)
    om = alloc("p4_om", [128, NFC], F32)
    trow = alloc("p4_trow", [128, TB], F32)
    dt_ = alloc("p4_d", [128, TB], F32)
    ur = alloc("p4_ur", [128, TB], F32)
    ui = alloc("p4_ui", [128, TB], F32)
    cc = alloc("p4_c", [128, TB], F32)
    ss = alloc("p4_s", [128, TB], F32)
    t1 = alloc("p4_t1", [128, TB], F32)
    t2 = alloc("p4_t2", [128, TB], F32)
    gi_ = alloc("p4_gi", [128, TB], F32)
    carry = alloc("p4_carry", [128, 2 * NFC], F32)
    ybf = alloc("p4_ybf", [128, 2 * TB], BF16)
    ps = alloc("p4_ps", [128, 2 * 2048], F32, psum=True)

    w1 = p.dma("gpsimd", lambda e: e.dma_start(
        out=wdec[:].rearrange("p (c f) -> p c f", c=HC),
        in_=t["W_decay"].rearrange("(c p) f -> p c f", p=128)))
    w2 = p.dma("gpsimd", lambda e: e.dma_start(
        out=wspec[:].rearrange("p (c f) -> p c f", c=HC),
        in_=t["W_spec"].rearrange("(c p) f -> p c f", p=128)))
    w3 = p.dma("sync", lambda e: e.dma_start(
        out=bdec[:], in_=t["b_decay"].rearrange("(c p) -> p c", p=128),
        allow_slow_non_contiguous=True))
    w4 = p.dma("sync", lambda e: e.dma_start(
        out=om[:], in_=t["freqs"].rearrange("(c p) -> p c", p=128),
        allow_slow_non_contiguous=True))
    xt_c = 0
    for c in range(HC):
        xt_c = p.dma("sync", lambda e, c=c: e.dma_start(
            out=xnT[:, c * T:(c + 1) * T], in_=t["xn_tm"][:, c * 128:(c + 1) * 128],
            transpose=True))
    om_c = p.cop("vector", lambda e: e.tensor_scalar_mul(om[:], om[:], 0.1),
                 waits=[(dsy, w4)])

    ev_c = {}
    mm_c = {}
    yst_c = [0] * (NFC * NTB)
    blk_v_done = 0
    g = 0
    for fc in range(NFC):
        for tb in range(NTB):
            k = fc * NTB + tb
            for kind in range(3):
                waits = [(dgp, w2), (dsy, xt_c)]
                if g - 2 in ev_c:
                    waits.append((ach, ev_c[g - 2]))

                def em(e, fc=fc, tb=tb, kind=kind, g=g):
                    pp = g % 2
                    t0 = tb * TB
                    out = []
                    for hc in range(HC):
                        if kind == 0:
                            lhsT = wdec[:, hc * F + fc * 128: hc * F + fc * 128 + 128]
                        elif kind == 1:
                            lhsT = wspec[:, hc * 2 * F + fc * 128: hc * 2 * F + fc * 128 + 128]
                        else:
                            lhsT = wspec[:, hc * 2 * F + (fc + NFC) * 128:
                                         hc * 2 * F + (fc + NFC) * 128 + 128]
                        for q in range(NQ):
                            out.append(e.matmul(
                                ps[:, pp * 2048 + q * 512: pp * 2048 + (q + 1) * 512],
                                lhsT,
                                xnT[:, hc * T + t0 + q * 512: hc * T + t0 + (q + 1) * 512],
                                start=(hc == 0), stop=(hc == HC - 1),
                                skip_group_check=True))
                    return out
                mm_c[g] = p.mmg(em, waits)
                pp = g % 2
                psrc = ps[:, pp * 2048:(pp + 1) * 2048]
                if kind == 0:
                    ev_c[g] = p.cop("scalar", lambda e, psrc=psrc, fc=fc: e.activation(
                        out=dt_[:], in_=psrc, func=AF.Sigmoid, bias=bdec[:, fc:fc + 1]),
                        waits=[(mms, mm_c[g]), (vch, blk_v_done), (dsy, w3)])
                else:
                    dst = ur if kind == 1 else ui
                    ev_c[g] = p.cop("scalar", lambda e, psrc=psrc, dst=dst: e.copy(
                        out=dst[:], in_=psrc), waits=[(mms, mm_c[g]), (vch, blk_v_done)])
                g += 1
            evs_done = ev_c[g - 1]

            it_c = p.cop("gpsimd", lambda e, tb=tb: e.iota(
                trow[:], pattern=[[1, TB]], base=tb * TB, channel_multiplier=0,
                allow_small_or_imprecise_dtypes=True), waits=[(vch, blk_v_done)])

            V = lambda f, w=(): p.cop("vector", f, waits=w)
            V(lambda e, fc=fc: e.tensor_scalar_mul(t1[:], trow[:], om[:, fc:fc + 1]),
              [(pch, it_c), (vch, om_c)])
            V(lambda e: e.tensor_scalar_mul(t2[:], t1[:], 1.0 / TWO_PI))
            V(lambda e: e.tensor_copy(gi_[:].bitcast(I32), t2[:]))
            V(lambda e: e.tensor_copy(t2[:], gi_[:].bitcast(I32)))
            pr = V(lambda e: e.scalar_tensor_tensor(t1[:], t2[:], -TWO_PI, t1[:], ALU.mult, ALU.add))
            sin_c = p.cop("scalar", lambda e: e.activation(out=ss[:], in_=t1[:], func=AF.Sin),
                          waits=[(vch, pr)])
            V(lambda e, fc=fc: e.tensor_scalar(t1[:], trow[:], om[:, fc:fc + 1], math.pi / 2,
                                               ALU.mult, ALU.add), [(ach, sin_c)])
            V(lambda e: e.tensor_scalar_mul(t2[:], t1[:], 1.0 / TWO_PI))
            V(lambda e: e.tensor_copy(gi_[:].bitcast(I32), t2[:]))
            V(lambda e: e.tensor_copy(t2[:], gi_[:].bitcast(I32)))
            pr2 = V(lambda e: e.scalar_tensor_tensor(t1[:], t2[:], -TWO_PI, t1[:], ALU.mult, ALU.add))
            cos_c = p.cop("scalar", lambda e: e.activation(out=cc[:], in_=t1[:], func=AF.Sin),
                          waits=[(vch, pr2)])
            V(lambda e: e.tensor_tensor(t1[:], cc[:], ur[:], ALU.mult), [(ach, cos_c), (ach, evs_done)])
            V(lambda e: e.tensor_tensor(t2[:], ss[:], ui[:], ALU.mult))
            V(lambda e: e.tensor_tensor(t1[:], t1[:], t2[:], ALU.add))
            V(lambda e: e.tensor_tensor(t2[:], cc[:], ui[:], ALU.mult))
            V(lambda e: e.tensor_tensor(ui[:], ss[:], ur[:], ALU.mult))
            V(lambda e: e.tensor_tensor(t2[:], t2[:], ui[:], ALU.subtract))
            if tb == 0:
                V(lambda e, fc=fc: e.memset(carry[:, fc:fc + 1], 0.0))
                V(lambda e, fc=fc: e.memset(carry[:, NFC + fc:NFC + fc + 1], 0.0))
            V(lambda e, fc=fc: e.tensor_tensor_scan(ur[:], dt_[:], t1[:], carry[:, fc:fc + 1],
                                                    ALU.mult, ALU.add))
            V(lambda e, fc=fc: e.tensor_tensor_scan(gi_[:], dt_[:], t2[:], carry[:, NFC + fc:NFC + fc + 1],
                                                    ALU.mult, ALU.add))
            V(lambda e, fc=fc: e.tensor_copy(carry[:, fc:fc + 1], ur[:, TB - 1:TB]))
            V(lambda e, fc=fc: e.tensor_copy(carry[:, NFC + fc:NFC + fc + 1], gi_[:, TB - 1:TB]))
            ystp = yst_c[k - 1] if k > 0 else 0
            V(lambda e: e.tensor_tensor(t1[:], cc[:], ur[:], ALU.mult), [(dsy, ystp)])
            V(lambda e: e.tensor_tensor(t2[:], ss[:], gi_[:], ALU.mult))
            V(lambda e: e.tensor_tensor(t1[:], t1[:], t2[:], ALU.subtract))
            hr_c = V(lambda e: e.tensor_copy(ybf[:, 0:TB], t1[:]))
            V(lambda e: e.tensor_tensor(t1[:], ss[:], ur[:], ALU.mult))
            V(lambda e: e.tensor_tensor(t2[:], cc[:], gi_[:], ALU.mult))
            V(lambda e: e.tensor_tensor(t1[:], t1[:], t2[:], ALU.add))
            hi_c = V(lambda e: e.tensor_copy(ybf[:, TB:2 * TB], t1[:]))
            blk_v_done = hi_c
            p.dma("sync", lambda e, fc=fc, tb=tb: e.dma_start(
                out=t["y_fm"][fc * 128:(fc + 1) * 128, tb * TB:(tb + 1) * TB],
                in_=ybf[:, 0:TB]), waits=[(vch, hr_c)])
            yst_c[k] = p.dma("sync", lambda e, fc=fc, tb=tb: e.dma_start(
                out=t["y_fm"][(NFC + fc) * 128:(NFC + fc + 1) * 128, tb * TB:(tb + 1) * TB],
                in_=ybf[:, TB:2 * TB]), waits=[(vch, hi_c)])

    p.barrier()
    stA.close()
    stB = ExitStack()
    alloc = mk_alloc(stB)
    wout = alloc("p4_wout", [128, HC * D], BF16)
    ygrp = alloc("p4_ygrp", [128, HC * 512], BF16)
    aot = alloc("p4_ao", [128, 2 * D], F32)
    po = alloc("p4_po", [128, 2 * D], F32, psum=True)
    wo_c = p.dma("gpsimd", lambda e: e.dma_start(
        out=wout[:].rearrange("p (c f) -> p c f", c=HC),
        in_=t["W_out"].rearrange("(c p) f -> p c f", p=128)))
    NG = 8
    ao_c = [0] * NT
    mo_c = [0] * NT
    ast_c = [0] * NT
    yld_c = [0] * NG
    for grp in range(NG):
        for c in range(HC):
            yld_c[grp] = p.dma("scalar", lambda e, grp=grp, c=c: e.dma_start(
                out=ygrp[:, c * 512:(c + 1) * 512],
                in_=t["y_fm"][c * 128:(c + 1) * 128, grp * 512:(grp + 1) * 512]),
                waits=[(p.mmsem, mo_c[grp * 4 - 1])] if grp > 0 else [])
        for tt in range(4):
            ti = grp * 4 + tt
            pp = ti % 2

            def em(e, grp=grp, tt=tt, pp=pp):
                out = []
                for c in range(HC):
                    for n in range(2):
                        out.append(e.matmul(
                            po[:, pp * D + n * 512: pp * D + (n + 1) * 512],
                            ygrp[:, c * 512 + tt * 128: c * 512 + tt * 128 + 128],
                            wout[:, c * D + n * 512: c * D + (n + 1) * 512],
                            start=(c == 0), stop=(c == HC - 1)))
                return out
            waits = [(p.dsem["scalar"], yld_c[grp]), (dgp, wo_c)]
            if ti >= 2:
                waits.append((vch, ao_c[ti - 2]))
            mo_c[ti] = p.mmg(em, waits)
            ao_c[ti] = p.cop("vector", lambda e, pp=pp: e.tensor_copy(
                aot[:, pp * D:(pp + 1) * D], po[:, pp * D:(pp + 1) * D]),
                waits=[(p.mmsem, mo_c[ti])] + ([(dsy, ast_c[ti - 2])] if ti >= 2 else []))
            ast_c[ti] = p.dma("sync", lambda e, ti=ti, pp=pp: e.dma_start(
                out=t["attn0"][ti * 128:(ti + 1) * 128, :], in_=aot[:, pp * D:(pp + 1) * D]),
                waits=[(vch, ao_c[ti])])
    p.barrier()
    stB.close()




def phase_routing(p, t, gl, mk_alloc):
    from contextlib import ExitStack
    vch, ach, pch = p.chain["vector"], p.chain["scalar"], p.chain["gpsimd"]
    dsy, dgp, dsc = p.dsem["sync"], p.dsem["gpsimd"], p.dsem["scalar"]
    mms = p.mmsem
    st = ExitStack()
    alloc = mk_alloc(st)

    gw = alloc("p2_gw", [128, HC * E], F32)
    ghat32 = alloc("p2_ghat32", [128, HC * E], F32)
    ghat = alloc("p2_ghat", [128, HC * E], BF16)
    onesc = alloc("p2_ones", [128, 1], F32)
    onesrow = alloc("p2_onesrow", [1, 128], F32)
    iotaB = alloc("p2_iotaB", [128, E], F32)
    xf = alloc("p2_xf", [128, 2 * T], BF16)
    xf32 = alloc("p2_xf32", [128, 2 * 2048], F32)
    lhatE = alloc("p2_lhatE", [32, T], F32)
    mT = alloc("p2_mT", [32, T], F32)
    cum = alloc("p2_cum", [32, T], F32)
    zerot = alloc("p2_zero", [32, T], F32)
    ltok = alloc("p2_ltok", [128, 4 * E], F32)
    maxv = alloc("p2_maxv", [128, 8], F32)
    mi = alloc("p2_mi", [128, 8], U32)
    member = alloc("p2_member", [128, E], F32)
    oh = alloc("p2_oh", [128, E], F32)
    counts_s = alloc("p2_counts", [32, 8], F32)
    countsT = alloc("p2_countsT", [8, 32], F32)
    sel_s = alloc("p2_sel", [8, 1], F32)
    srow = alloc("p2_srow", [1, 32], F32)
    offB = alloc("p2_offB", [128, E], F32)
    Sb = alloc("p2_Sb", [128, E], F32)
    off_s = alloc("p2_off", [32, 1], F32)
    Ssum = alloc("p2_Ssum", [32, 1], F32)
    ranktok = alloc("p2_ranktok", [128, E], F32)
    sc1 = alloc("p2_sc1", [128, E], F32)
    sc14 = alloc("p2_sc14", [128, 4 * E], F32)
    cols = alloc("p2_cols", [128, 20], F32)
    ps_l = alloc("p2_psl", [32, 2048], F32, psum=True)
    ps_t = alloc("p2_pst", [128, 128], F32, psum=True)
    ps_t2 = alloc("p2_pst2", [32, 128], F32, psum=True)
    ps_c = alloc("p2_psc", [32, 8], F32, psum=True)
    ps_b = alloc("p2_psb", [128, E], F32, psum=True)
    mif_all = gl["mif_all"]
    ltok32_all = gl["ltok32_all"]
    idx_all = gl["idx_all"]
    gates_all = gl["gates_all"]
    rstd0 = gl["rstd0"]
    ident = gl["ident"]
    misc_d = t["misc_d"]

    c1 = p.cop("vector", lambda e: e.memset(onesc[:], 1.0))
    c3 = p.cop("vector", lambda e: e.memset(onesrow[:], 1.0))
    c4 = p.cop("vector", lambda e: e.memset(zerot[:], 0.0))
    c5 = p.cop("gpsimd", lambda e: e.iota(iotaB[:], pattern=[[1, E]], base=0,
                                          channel_multiplier=0,
                                          allow_small_or_imprecise_dtypes=True))
    gwl = p.dma("sync", lambda e: e.dma_start(
        out=gw[:].rearrange("p (c e) -> p c e", c=HC),
        in_=t["gate_wT"].rearrange("(c p) e -> p c e", p=128)))
    sell = p.dma("sync", lambda e: e.dma_start(out=sel_s[:], in_=t["sel"]))

    def em_ss(e):
        out = []
        for hc in range(HC):
            out.append(e.matmul(ps_t[0:32, 0:1], gw[:, hc * E:(hc + 1) * E], onesc[:],
                                start=(hc == 0), stop=(hc == HC - 1)))
        return out
    m1 = p.mmg(em_ss, [(dsy, gwl), (vch, c1)])
    sS = p.cop("scalar", lambda e: e.copy(out=Ssum[:], in_=ps_t[0:32, 0:1]),
               waits=[(mms, m1)])
    b1 = p.dma("gpsimd", lambda e: e.dma_start(out=misc_d[0:32], in_=Ssum[:, 0]),
               waits=[(ach, sS)])
    b2 = p.dma("gpsimd", lambda e: e.dma_start(
        out=srow[:], in_=misc_d[0:32].rearrange("(a e) -> a e", a=1)),
        waits=[(dgp, b1)])
    m2 = p.mmg(lambda e: e.matmul(ps_b[:], onesrow[:], srow[:], start=True, stop=True),
               [(dgp, b2), (vch, c3)])
    sb_c = p.cop("scalar", lambda e: e.copy(out=Sb[:], in_=ps_b[:]), waits=[(mms, m2)])
    gh_c = 0
    for hc in range(HC):
        p.cop("vector", lambda e, hc=hc: e.scalar_tensor_tensor(
            ghat32[:, hc * E:(hc + 1) * E], Sb[:], -1.0 / D, gw[:, hc * E:(hc + 1) * E],
            ALU.mult, ALU.add), waits=[(ach, sb_c)])
        gh_c = p.cop("vector", lambda e, hc=hc: e.tensor_copy(
            ghat[:, hc * E:(hc + 1) * E], ghat32[:, hc * E:(hc + 1) * E]))

    mm_prev = [0, 0]
    ev_last = [0]
    ev_half = [0, 0]
    lhat_free = 0
    evac_done = 0
    cnt_mm = 0

    def ranking_pass(pss, src_fm, fp32):
        nonlocal mm_prev, ev_half, lhat_free
        for half in range(2):
            t0 = half * 2048
            for hc in range(HC):
                pp = hc % 2
                if fp32:
                    ldx = p.dma("sync", lambda e, hc=hc, t0=t0, pp=pp: e.dma_start(
                        out=xf32[:, pp * 2048:(pp + 1) * 2048],
                        in_=src_fm[hc * 128:(hc + 1) * 128, t0:t0 + 2048]),
                        waits=[(mms, mm_prev[pp])])
                    ds = dsy
                else:
                    ldx = p.dma("gpsimd", lambda e, hc=hc, t0=t0, pp=pp: e.dma_start(
                        out=xf[:, pp * T + t0: pp * T + t0 + 2048],
                        in_=src_fm[hc * 128:(hc + 1) * 128, t0:t0 + 2048]),
                        waits=[(mms, mm_prev[pp])])
                    ds = dgp

                def em(e, hc=hc, t0=t0, pp=pp, fp32=fp32):
                    out = []
                    for q in range(4):
                        if fp32:
                            rhs = xf32[:, pp * 2048 + q * 512:pp * 2048 + (q + 1) * 512]
                            w = ghat32
                        else:
                            rhs = xf[:, pp * T + t0 + q * 512: pp * T + t0 + (q + 1) * 512]
                            w = ghat
                        out.append(e.matmul(
                            ps_l[0:32, q * 512:(q + 1) * 512],
                            w[:, hc * E:(hc + 1) * E], rhs,
                            start=(hc == 0), stop=(hc == HC - 1)))
                    return out
                mm_prev[pp] = p.mmg(em, [(ds, ldx), (vch, gh_c), (ach, ev_last[0]),
                                         (mms, lhat_free)])
            ev_half[half] = p.cop("scalar", lambda e, t0=t0: e.copy(
                out=lhatE[:, t0:t0 + 2048], in_=ps_l[0:32, :]),
                waits=[(mms, mm_prev[(HC - 1) % 2])])
            ev_last[0] = ev_half[half]

    def tile_transpose(ti):
        nonlocal lhat_free
        lhat_free = p.mmg(lambda e, ti=ti: e.transpose(
            ps_t[:, 0:32], lhatE[0:32, ti * 128:(ti + 1) * 128], ident[0:32, 0:32]),
            [(ach, ev_half[ti // (NT // 2)]), (vch, evac_done)])
        return lhat_free

    def tile_transpose4(t4):
        nonlocal lhat_free
        def em(e, t4=t4):
            out = []
            for j in range(4):
                ti = t4 * 4 + j
                out.append(e.transpose(ps_t[:, j * 32:(j + 1) * 32],
                                       lhatE[0:32, ti * 128:(ti + 1) * 128],
                                       ident[0:32, 0:32]))
            return out
        lhat_free = p.mmg(em, [(ach, ev_half[(t4 * 4) // (NT // 2)]),
                               (ach, ev_half[(t4 * 4 + 3) // (NT // 2)]),
                               (vch, evac_done)])
        return lhat_free

    for b in range(8):
        ranking_pass(b, t["x_fm"][b], False)
        for t4 in range(NT // 4):
            trc = tile_transpose4(t4)
            lt = p.cop("vector", lambda e: e.tensor_copy(ltok[:], ps_t[:]),
                       waits=[(mms, trc)])
            evac_done = lt
            for j in range(4):
                p.cop("vector", lambda e, j=j: e.max(out=maxv[:], in_=ltok[:, j * E:(j + 1) * E]))
                ohc = p.cop("vector", lambda e, j=j: e.tensor_scalar(
                    oh[:], ltok[:, j * E:(j + 1) * E], maxv[:, 3:4], None, ALU.is_ge),
                    waits=[(mms, cnt_mm)])
                cnt_mm = p.mmg(lambda e, b=b, t4=t4, j=j: e.matmul(
                    ps_c[:, b:b + 1], oh[:], onesc[:],
                    start=(t4 == 0 and j == 0),
                    stop=(t4 == NT // 4 - 1 and j == 3),
                    skip_group_check=True),
                    [(vch, ohc)])

    prev_trm = 0
    ranking_pass(8, t["x_fm_own"], False)
    for ti in range(NT):
        trc = tile_transpose(ti)
        lt = p.cop("vector", lambda e: e.tensor_copy(ltok[:, 0:E], ps_t[:, 0:32]),
                   waits=[(mms, trc)])
        p.cop("vector", lambda e: e.max(out=maxv[:], in_=ltok[:, 0:E]))
        p.cop("vector", lambda e: e.max_index(out=mi[:], in_max=maxv[:], in_values=ltok[:, 0:E]))
        mifc = p.cop("vector", lambda e, ti=ti: e.tensor_copy(
            mif_all[:, ti * 4:ti * 4 + 4], mi[:, 0:4]))
        evac_done = mifc
        for kk in range(4):
            p.cop("vector", lambda e, kk=kk, ti=ti: e.tensor_scalar(
                sc14[:, kk * E:(kk + 1) * E], iotaB[:],
                mif_all[:, ti * 4 + kk:ti * 4 + kk + 1], None, ALU.is_equal),
                waits=[(mms, prev_trm), (pch, c5)])
        mp = p.cop("vector", lambda e: e.tensor_reduce(
            member[:], sc14[:].rearrange("p (k e) -> p e k", k=4), axis=AX.X,
            op=ALU.add))
        trm = p.mmg(lambda e: e.transpose(ps_t2[:], member[:], ident[:]),
                    [(vch, mp), (vch, evac_done)])
        prev_trm = trm
        evac_done = p.cop("vector", lambda e, ti=ti: e.tensor_copy(
            mT[0:32, ti * 128:(ti + 1) * 128], ps_t2[0:32, :]), waits=[(mms, trm)])
    ranking_pass(9, t["x_fm_own"], True)
    for t4 in range(NT // 4):
        trc = tile_transpose4(t4)
        evac_done = p.cop("vector", lambda e, t4=t4: e.tensor_copy(
            ltok32_all[:, t4 * 4 * E:(t4 + 1) * 4 * E], ps_t[:]), waits=[(mms, trc)])

    cev = p.cop("scalar", lambda e: e.copy(out=counts_s[:], in_=ps_c[:]),
                waits=[(mms, cnt_mm)])
    d1 = p.dma("gpsimd", lambda e: e.dma_start(out=misc_d[64:320].rearrange("(a b) -> a b", a=32), in_=counts_s[:]), waits=[(ach, cev)])
    d2 = p.dma("gpsimd", lambda e: e.dma_start(
        out=countsT[:], in_=misc_d[64:320].rearrange("(e c) -> c e", c=8),
        allow_slow_non_contiguous=True), waits=[(dgp, d1)])
    m3 = p.mmg(lambda e: e.matmul(ps_t2[0:32, 0:1], countsT[:], sel_s[:],
                                  start=True, stop=True),
               [(dgp, d2), (dsy, sell), (vch, evac_done)])
    offc = p.cop("scalar", lambda e: e.copy(out=off_s[:], in_=ps_t2[0:32, 0:1]),
                 waits=[(mms, m3)])
    d3 = p.dma("gpsimd", lambda e: e.dma_start(out=misc_d[32:64], in_=off_s[:, 0]),
               waits=[(ach, offc)])
    d4 = p.dma("gpsimd", lambda e: e.dma_start(
        out=srow[:], in_=misc_d[32:64].rearrange("(a e) -> a e", a=1)),
        waits=[(dgp, d3)])
    m4 = p.mmg(lambda e: e.matmul(ps_b[:], onesrow[:], srow[:], start=True, stop=True),
               [(dgp, d4)])
    offbc = p.cop("scalar", lambda e: e.copy(out=offB[:], in_=ps_b[:]), waits=[(mms, m4)])

    sc_c = p.cop("vector", lambda e: e.tensor_tensor_scan(
        cum[0:32, :], mT[0:32, :], zerot[0:32, :], 0.0, ALU.add, ALU.add),
        waits=[(vch, c4)])
    rg_c = p.cop("vector", lambda e: e.tensor_scalar(
        cum[0:32, :], cum[0:32, :], off_s[0:32, 0:1], -1.0, ALU.add, ALU.add),
        waits=[(ach, offc)])

    rt_ev = 0
    for ti in range(NT):
        trr = p.mmg(lambda e, ti=ti: e.transpose(
            ps_t[:, 0:32], cum[0:32, ti * 128:(ti + 1) * 128], ident[0:32, 0:32]),
            [(vch, rg_c), (vch, rt_ev)])
        rt_ev = p.cop("vector", lambda e: e.tensor_copy(ranktok[:], ps_t[:, 0:32]),
                      waits=[(mms, trr)])
        for kk in range(4):
            ohc = p.cop("vector", lambda e, ti=ti, kk=kk: e.tensor_scalar(
                oh[:], iotaB[:], mif_all[:, ti * 4 + kk:ti * 4 + kk + 1], None,
                ALU.is_equal), waits=[(vch, rt_ev), (ach, offbc)])
            p.cop("vector", lambda e: e.tensor_tensor(sc1[:], oh[:], ranktok[:], ALU.mult))
            p.cop("vector", lambda e, kk=kk: e.reduce_sum(
                cols[:, 4 + kk:5 + kk], sc1[:], axis=AX.X))
            p.cop("vector", lambda e: e.tensor_tensor(sc1[:], oh[:], offB[:], ALU.mult))
            p.cop("vector", lambda e, kk=kk: e.reduce_sum(
                cols[:, 8 + kk:9 + kk], sc1[:], axis=AX.X))
            p.cop("vector", lambda e, ti=ti: e.tensor_tensor(
                sc1[:], oh[:], ltok32_all[:, ti * E:(ti + 1) * E], ALU.mult))
            p.cop("vector", lambda e, kk=kk: e.reduce_sum(
                cols[:, 16 + kk:17 + kk], sc1[:], axis=AX.X))
        for kk in range(4):
            vd = p.cop("vector", lambda e, kk=kk: e.tensor_single_scalar(
                cols[:, 12 + kk:13 + kk], cols[:, 4 + kk:5 + kk], float(CAP), ALU.is_lt))
            sl = p.cop("vector", lambda e, kk=kk: e.tensor_tensor(
                cols[:, 8 + kk:9 + kk], cols[:, 4 + kk:5 + kk], cols[:, 8 + kk:9 + kk],
                ALU.subtract))
            ixf = p.cop("vector", lambda e, ti=ti, kk=kk: e.scalar_tensor_tensor(
                cols[:, 4 + kk:5 + kk], mif_all[:, ti * 4 + kk:ti * 4 + kk + 1],
                float(CAP), cols[:, 8 + kk:9 + kk], ALU.mult, ALU.add))
            nv = p.cop("vector", lambda e, kk=kk: e.tensor_scalar(
                cols[:, 0 + kk:1 + kk], cols[:, 12 + kk:13 + kk], -1.0, 1.0,
                ALU.mult, ALU.add))
            p.cop("vector", lambda e, kk=kk: e.scalar_tensor_tensor(
                cols[:, 4 + kk:5 + kk], cols[:, 0 + kk:1 + kk], BIGIDX,
                cols[:, 4 + kk:5 + kk], ALU.mult, ALU.add))
            p.cop("vector", lambda e, kk=kk: e.tensor_scalar(
                cols[:, 4 + kk:5 + kk], cols[:, 4 + kk:5 + kk], BIGIDX, 0.0,
                ALU.min, ALU.max))
        p.cop("vector", lambda e, ti=ti: e.tensor_copy(
            idx_all[:, ti * 4:ti * 4 + 4], cols[:, 4:8]))
        p.cop("vector", lambda e, ti=ti: e.tensor_scalar_mul(
            cols[:, 8:12], cols[:, 16:20], rstd0[:, ti:ti + 1]))
        sb_ = p.cop("vector", lambda e: e.tensor_scalar_sub(
            cols[:, 9:12], cols[:, 9:12], cols[:, 8:9]))
        ex = p.cop("scalar", lambda e: e.activation(
            out=cols[:, 9:12], in_=cols[:, 9:12], func=AF.Exp), waits=[(vch, sb_)])
        s1 = p.cop("vector", lambda e: e.memset(cols[:, 8:9], 1.0), waits=[(ach, ex)])
        sm = p.cop("vector", lambda e: e.reduce_sum(cols[:, 0:1], cols[:, 8:12], axis=AX.X))
        rq = p.cop("vector", lambda e: e.reciprocal(cols[:, 0:1], cols[:, 0:1]))
        p.cop("vector", lambda e: e.tensor_scalar_mul(
            cols[:, 8:12], cols[:, 8:12], cols[:, 0:1]))
        p.cop("vector", lambda e, ti=ti: e.tensor_tensor(
            gates_all[:, ti * 4:ti * 4 + 4], cols[:, 8:12], cols[:, 12:16], ALU.mult))
    kc1_ = p.cop("vector", lambda e: e.reduce_sum(sc1[0:32, 0:1], mT[0:32, :], axis=AX.X),
                 waits=[(vch, evac_done)])
    kc1_ = p.cop("vector", lambda e: e.tensor_scalar(
        sc1[0:32, 1:2], off_s[:], -1.0, float(CAP), ALU.mult, ALU.add),
        waits=[(ach, offc)])
    kc1_ = p.cop("vector", lambda e: e.tensor_tensor(
        sc1[0:32, 1:2], sc1[0:32, 1:2], sc1[0:32, 0:1], ALU.min))
    kc1_ = p.cop("vector", lambda e: e.tensor_scalar_max(sc1[0:32, 1:2], sc1[0:32, 1:2], 0.0))
    kb1 = p.dma("gpsimd", lambda e: e.dma_start(out=misc_d[384:416], in_=sc1[0:32, 1]),
                waits=[(vch, kc1_)])
    kb2 = p.dma("gpsimd", lambda e: e.dma_start(
        out=srow[:], in_=misc_d[384:416].rearrange("(a e) -> a e", a=1)),
        waits=[(dgp, kb1)])
    p.cop("vector", lambda e: e.tensor_copy(gl["kept"][:], srow[:]),
          waits=[(dgp, kb2)])

    p.barrier()
    st.close()


def phase_scatter(p, t, gl, mk_alloc):
    from contextlib import ExitStack
    vch = p.chain["vector"]
    dsy, dgp = p.dsem["sync"], p.dsem["gpsimd"]
    st = ExitStack()
    alloc = mk_alloc(st)
    xnt = alloc("p3_xn", [128, 2 * D], BF16)
    idx_all = gl["idx_all"]
    sc_c = [0] * NT
    hold = {}

    def mkreg(e):
        r = e.alloc_register("p3bnd")
        hold["r"] = r
        return e.reg_mov(r, E * CAP - 1)
    p._record("gpsimd", mkreg, [], None, 0)
    for ti in range(NT):
        bx = (ti % 2) * D
        w = [(dgp, sc_c[ti - 2])] if ti >= 2 else []
        ld = p.dma("sync", lambda e, ti=ti, bx=bx: e.dma_start(
            out=xnt[:, bx:bx + D], in_=t["xn_tm"][ti * 128:(ti + 1) * 128, :]), waits=w)
        for kk in range(4):
            sc_c[ti] = p.dma("gpsimd", lambda e, ti=ti, kk=kk, bx=bx: e.indirect_dma_start(
                out=t["buf"], out_offset=IndirectOffsetOnAxis(
                    ap=idx_all[:, ti * 4 + kk:ti * 4 + kk + 1], axis=0),
                in_=xnt[:, bx:bx + D], in_offset=None,
                bounds_check=hold["r"], oob_is_err=False),
                waits=[(dsy, ld)])
    fl = p.dma("gpsimd", lambda e: e.dma_start(out=xnt[:, 0:D], in_=t["buf"][0:128, :]),
               waits=[(dgp, sc_c[NT - 1])])
    p.wait_only("gpsimd", [(dgp, fl)])
    p.barrier()
    st.close()


def phase_shared(p, t, gl, mk_alloc):
    from contextlib import ExitStack
    vch, ach = p.chain["vector"], p.chain["scalar"]
    dsy, dgp = p.dsem["sync"], p.dsem["gpsimd"]
    mms = p.mmsem
    st = ExitStack()
    alloc = mk_alloc(st)
    xnT = alloc("p5_xnT", [128, HC * T], BF16)
    wgu = alloc("p5_wgu", [128, HC * D], BF16)
    sg = alloc("p5_sg", [128, 2 * 512], F32)
    as2 = alloc("p5_as", [128, 2 * 512], BF16)
    ps = alloc("p5_ps", [128, 4 * 512], F32, psum=True)
    ast_c = {}

    wl = p.dma("gpsimd", lambda e: e.dma_start(
        out=wgu[:].rearrange("p (c f) -> p c f", c=HC),
        in_=t["shared_w_gate_up"].rearrange("(c p) f -> p c f", p=128)))
    xt_c = 0
    for c in range(HC):
        xt_c = p.dma("sync", lambda e, c=c: e.dma_start(
            out=xnT[:, c * T:(c + 1) * T], in_=t["xn_tm"][:, c * 128:(c + 1) * 128],
            transpose=True))
    ev = {}
    mmc = {}
    g = 0
    for i in range(4):
        for q in range(8):
            for gu in range(2):
                pp = g % 4
                ic = i if gu == 0 else i + 4

                def em(e, ic=ic, q=q, pp=pp):
                    out = []
                    for hc in range(HC):
                        out.append(e.matmul(
                            ps[:, pp * 512:(pp + 1) * 512],
                            wgu[:, hc * D + ic * 128: hc * D + ic * 128 + 128],
                            xnT[:, hc * T + q * 512: hc * T + (q + 1) * 512],
                            start=(hc == 0), stop=(hc == HC - 1)))
                    return out
                w = [(dgp, wl), (dsy, xt_c)]
                if g - 4 in ev:
                    w.append((ev[g - 4][0], ev[g - 4][1]))
                mmc[g] = p.mmg(em, w)
                if gu == 0:
                    w2 = [(mms, mmc[g])]
                    if g - 3 in ev:
                        w2.append(ev[g - 3])
                    ev[g] = (ach, p.cop("scalar", lambda e, q=q, pp=pp: e.activation(
                        out=sg[:, (q % 2) * 512:(q % 2 + 1) * 512],
                        in_=ps[:, pp * 512:(pp + 1) * 512], func=AF.Silu),
                        waits=w2))
                else:
                    wv = [(mms, mmc[g]), ev[g - 1]]
                    if (i, q - 4) in ast_c:
                        wv.append((dsy, ast_c[(i, q - 4)]))
                    vc = p.cop("vector", lambda e, i=i, q=q, pp=pp: e.tensor_tensor(
                        as2[:, (q % 2) * 512:(q % 2 + 1) * 512],
                        sg[:, (q % 2) * 512:(q % 2 + 1) * 512],
                        ps[:, pp * 512:(pp + 1) * 512], ALU.mult),
                        waits=wv)
                    ev[g] = (vch, vc)
                    ast_c[(i, q)] = p.dma("sync", lambda e, i=i, q=q: e.dma_start(
                        out=t["act_sh"][i * 128:(i + 1) * 128, q * 512:(q + 1) * 512],
                        in_=as2[:, (q % 2) * 512:(q % 2 + 1) * 512]),
                        waits=[(vch, vc)])
                g += 1
    p.barrier()
    st.close()


def phase_experts(p, t, gl, mk_alloc):
    from contextlib import ExitStack
    vch, ach = p.chain["vector"], p.chain["scalar"]
    dsy, dgp, dsc = p.dsem["sync"], p.dsem["gpsimd"], p.dsem["scalar"]
    mms = p.mmsem
    st = ExitStack()
    alloc = mk_alloc(st)
    wguE = alloc("p6_wgu", [128, 2 * HC * D], BF16)
    wdnE = alloc("p6_wdn", [128, 2 * 4 * D], BF16)
    bufT = alloc("p6_bufT", [128, 2 * HC * CAP], BF16)
    sgE = alloc("p6_sg", [128, 4 * CAP], F32)
    actE = alloc("p6_act", [128, 4 * CAP], BF16)
    eot = alloc("p6_eot", [128, 2 * D], BF16)
    ps = alloc("p6_ps", [128, 4 * 512], F32, psum=True)
    psd = alloc("p6_psd", [128, 2 * D], F32, psum=True)

    wld = [0, 0]
    bld = [0, 0]
    gu_done = [0, 0]
    dn_done = [0, 0]
    eo_st = {}
    act_used = 0
    for ei in range(E):
        pp = ei % 2
        wld[pp] = p.dma("gpsimd", lambda e, ei=ei, pp=pp: e.dma_start(
            out=wguE[:, pp * HC * D:(pp + 1) * HC * D].rearrange("p (c f) -> p c f", c=HC),
            in_=t["w_gate_up"][ei].rearrange("(c p) f -> p c f", p=128)),
            waits=[(mms, gu_done[pp])])
        wld[pp] = p.dma("gpsimd", lambda e, ei=ei, pp=pp: e.dma_start(
            out=wdnE[:, pp * 4 * D:(pp + 1) * 4 * D].rearrange("p (c f) -> p c f", c=4),
            in_=t["w_down"][ei].rearrange("(c p) f -> p c f", p=128)),
            waits=[(mms, dn_done[pp])])
        for hc in range(HC):
            bld[pp] = p.dma("sync", lambda e, ei=ei, hc=hc, pp=pp: e.dma_start(
                out=bufT[:, pp * HC * CAP + hc * CAP: pp * HC * CAP + (hc + 1) * CAP],
                in_=t["buf"][ei * CAP:(ei + 1) * CAP, hc * 128:(hc + 1) * 128],
                transpose=True), waits=[(mms, gu_done[pp])])
        if ei == 0:
            ev_prev = {}
        else:
            ev_prev = {i - 8: v for i, v in ev_prev.items() if i >= 6}
        last_gu = 0
        for i in range(8):
            qq = i % 2

            def em(e, ei=ei, i=i, pp=pp, qq=qq):
                out = []
                for hc in range(HC):
                    for q in range(2):
                        out.append(e.matmul(
                            ps[:, qq * 1024 + q * 512: qq * 1024 + (q + 1) * 512],
                            wguE[:, pp * HC * D + hc * D + i * 128: pp * HC * D + hc * D + i * 128 + 128],
                            bufT[:, pp * HC * CAP + hc * CAP + q * 512: pp * HC * CAP + hc * CAP + (q + 1) * 512],
                            start=(hc == 0), stop=(hc == HC - 1),
                            skip_group_check=True))
                return out
            w = [(dgp, wld[pp]), (dsy, bld[pp])]
            if i - 2 in ev_prev:
                w.append(ev_prev[i - 2])
            last_gu = p.mmg(em, w)
            if i < 4:
                ev_prev[i] = (ach, p.cop("scalar", lambda e, i=i, qq=qq: e.activation(
                    out=sgE[:, i * CAP: (i + 1) * CAP],
                    in_=ps[:, qq * 1024:(qq + 1) * 1024], func=AF.Silu),
                    waits=[(mms, last_gu), (vch, act_used)]))
            else:
                ev_prev[i] = (vch, p.cop("vector", lambda e, i=i, qq=qq: e.tensor_tensor(
                    actE[:, (i - 4) * CAP: (i - 3) * CAP],
                    sgE[:, (i - 4) * CAP: (i - 3) * CAP],
                    ps[:, qq * 1024:(qq + 1) * 1024], ALU.mult),
                    waits=[(mms, last_gu), ev_prev[i - 4], (vch, act_used)]))
        gu_done[pp] = last_gu
        last_dn = 0
        for sti in range(8):
            qq = sti % 2

            def em(e, ei=ei, sti=sti, pp=pp, qq=qq):
                out = []
                for ic in range(4):
                    for n in range(2):
                        out.append(e.matmul(
                            psd[:, qq * D + n * 512: qq * D + (n + 1) * 512],
                            actE[:, ic * CAP + sti * 128: ic * CAP + sti * 128 + 128],
                            wdnE[:, pp * 4 * D + ic * D + n * 512: pp * 4 * D + ic * D + (n + 1) * 512],
                            start=(ic == 0), stop=(ic == 3),
                            skip_group_check=True))
                return out
            if ei == 0 and sti == 0:
                eo_ev = [0] * 8
            w = [ev_prev[7], (dgp, wld[pp])]
            w.append((vch, eo_ev[sti - 2]))
            last_dn = p.mmg(em, w)

            wv = [(mms, last_dn)]
            if sti >= 2:
                wv.append((dsy, eo_st.get(sti - 2, 0)))
            eo_ev[sti] = p.cop("vector", lambda e, sti=sti, qq=qq: e.tensor_copy(
                eot[:, qq * D:(qq + 1) * D], psd[:, qq * D:(qq + 1) * D]), waits=wv)
            eo_st[sti] = p.dma("sync", lambda e, ei=ei, sti=sti, qq=qq: e.dma_start(
                out=t["eo"][ei * CAP + sti * 128: ei * CAP + (sti + 1) * 128, :],
                in_=eot[:, qq * D:(qq + 1) * D]), waits=[(p.chain["vector"], eo_ev[sti])])
        dn_done[pp] = last_dn
        act_used = eo_ev[7]
    p.barrier()
    st.close()


def phase_epilogue(p, t, gl, mk_alloc):
    from contextlib import ExitStack
    vch, ach = p.chain["vector"], p.chain["scalar"]
    dsy, dgp = p.dsem["sync"], p.dsem["gpsimd"]
    mms = p.mmsem
    st = ExitStack()
    alloc = mk_alloc(st)
    wdn = alloc("p7_wdn", [128, 4 * D], BF16)
    gt = alloc("p7_gt", [128, 8 * D], BF16)
    xa = alloc("p7_xa", [128, 2 * D], F32)
    at = alloc("p7_at", [128, 2 * D], F32)
    racc = alloc("p7_racc", [128, D], F32)
    x1t = alloc("p7_x1", [128, 2 * D], F32)
    actg = alloc("p7_actg", [128, 4 * 512], BF16)
    psd = alloc("p7_psd", [128, 2 * D], F32, psum=True)
    idx_all = gl["idx_all"]
    gates_all = gl["gates_all"]

    wl = p.dma("gpsimd", lambda e: e.dma_start(
        out=wdn[:].rearrange("p (c f) -> p c f", c=4),
        in_=t["shared_w_down"].rearrange("(c p) f -> p c f", p=128)))
    gz = p.cop("vector", lambda e: e.memset(gt[:], 0.0))
    hold = {}

    def mkreg(e):
        r = e.alloc_register("p7bnd")
        hold["r"] = r
        return e.reg_mov(r, E * CAP - 1)
    p._record("gpsimd", mkreg, [], None, 0)
    x1_st = [0] * NT
    r_done = [0] * NT
    mm_c = [0] * NT
    x1_c = [0] * NT
    agl = 0
    for ti in range(NT):
        pp = ti % 2
        if ti % 4 == 0:
            grp = ti // 4
            wg = [(mms, mm_c[ti - 1])] if ti else []
            for ic in range(4):
                agl = p.dma("sync", lambda e, grp=grp, ic=ic: e.dma_start(
                    out=actg[:, ic * 512:(ic + 1) * 512],
                    in_=t["act_sh"][ic * 128:(ic + 1) * 128, grp * 512:(grp + 1) * 512]),
                    waits=wg)
        gld = 0
        gb = (ti % 2) * 4 * D
        for kk in range(4):
            gld = p.dma("gpsimd", lambda e, ti=ti, kk=kk, gb=gb: e.indirect_dma_start(
                out=gt[:, gb + kk * D: gb + (kk + 1) * D], out_offset=None,
                in_=t["eo"], in_offset=IndirectOffsetOnAxis(
                    ap=idx_all[:, ti * 4 + kk:ti * 4 + kk + 1], axis=0),
                bounds_check=hold["r"], oob_is_err=False),
                waits=[(vch, r_done[ti - 2] if ti >= 2 else gz), (dgp, wl)])
        wre = [(vch, x1_c[ti - 2])] if ti >= 2 else []
        xld = p.dma("sync", lambda e, ti=ti, pp=pp: e.dma_start(
            out=xa[:, pp * D:(pp + 1) * D], in_=t["x_tm"][ti * 128:(ti + 1) * 128, :]),
            waits=wre)
        ald = p.dma("sync", lambda e, ti=ti, pp=pp: e.dma_start(
            out=at[:, pp * D:(pp + 1) * D], in_=t["attn0"][ti * 128:(ti + 1) * 128, :]),
            waits=wre)

        def em(e, ti=ti, pp=pp):
            out = []
            tt = ti % 4
            for ic in range(4):
                for n in range(2):
                    out.append(e.matmul(
                        psd[:, pp * D + n * 512: pp * D + (n + 1) * 512],
                        actg[:, ic * 512 + tt * 128: ic * 512 + tt * 128 + 128],
                        wdn[:, ic * D + n * 512: ic * D + (n + 1) * 512],
                        start=(ic == 0), stop=(ic == 3)))
            return out
        w = [(dgp, wl), (dsy, agl)]
        if ti >= 2:
            w.append((vch, x1_c[ti - 2]))
        mm_c[ti] = p.mmg(em, w)
        p.cop("vector", lambda e, ti=ti, gb=gb: e.tensor_scalar_mul(
            racc[:], gt[:, gb:gb + D], gates_all[:, ti * 4:ti * 4 + 1]),
            waits=[(dgp, gld)])
        for kk in range(1, 4):
            p.cop("vector", lambda e, ti=ti, kk=kk, gb=gb: e.scalar_tensor_tensor(
                racc[:], gt[:, gb + kk * D: gb + (kk + 1) * D],
                gates_all[:, ti * 4 + kk:ti * 4 + kk + 1], racc[:],
                ALU.mult, ALU.add))
        r_done[ti] = p.chain["vector"].cnt
        p.cop("vector", lambda e, pp=pp: e.tensor_tensor(
            x1t[:, pp * D:(pp + 1) * D], xa[:, pp * D:(pp + 1) * D],
            at[:, pp * D:(pp + 1) * D], ALU.add), waits=[(dsy, ald)])
        p.cop("vector", lambda e, pp=pp: e.tensor_tensor(
            x1t[:, pp * D:(pp + 1) * D], x1t[:, pp * D:(pp + 1) * D], racc[:], ALU.add))
        x1_c[ti] = p.cop("vector", lambda e, pp=pp: e.tensor_tensor(
            x1t[:, pp * D:(pp + 1) * D], x1t[:, pp * D:(pp + 1) * D],
            psd[:, pp * D:(pp + 1) * D], ALU.add), waits=[(mms, mm_c[ti])])
        x1_st[ti] = p.dma("sync", lambda e, ti=ti, pp=pp: e.dma_start(
            out=t["x1_tm"][ti * 128:(ti + 1) * 128, :], in_=x1t[:, pp * D:(pp + 1) * D]),
            waits=[(vch, x1_c[ti])])
    p.barrier()
    st.close()


def build_program(phases=("p1", "p2", "p3", "p4", "p5", "p6", "p7", "p8", "p9", "p10"), dbg=()):
    from contextlib import ExitStack
    nc = bass.Bass("TRN2", target_bir_lowering=False)
    t = declare_io(nc)
    for d in dbg:
        if d.startswith("sb:"):
            continue
        t["dbg_" + d] = nc.dram_tensor("dbg_" + d, list(t[d].shape),
                                       t[d].dtype, kind="ExternalOutput").ap()
    stack = ExitStack()
    with stack:
        block = stack.enter_context(nc.Block())
        p = Prog(nc, block, stack)

        def mk_alloc(st):
            def alloc(name, shape, dtype=F32, psum=False):
                if psum:
                    return st.enter_context(nc.psum_tensor(name, shape, dtype))
                return st.enter_context(nc.sbuf_tensor(name, shape, dtype))
            return alloc

        galloc = mk_alloc(stack)
        gl = {}
        gl["rstd0"] = galloc("g_rstd0", [128, NT], F32)
        gl["m0"] = galloc("g_m0", [128, NT], F32)
        gl["rstd1"] = galloc("g_rstd1", [128, NT], F32)
        gl["m1"] = galloc("g_m1", [128, NT], F32)
        gl["ident"] = galloc("g_ident", [128, 128], F32)
        gl["mif_all"] = galloc("g_mif", [128, 4 * NT], F32)
        gl["ltok32_all"] = galloc("g_ltok32", [128, E * NT], F32)
        gl["idx_all"] = galloc("g_idx", [128, 4 * NT], I32)
        gl["gates_all"] = galloc("g_gates", [128, 4 * NT], F32)
        gl["sc32"] = galloc("g_sc32", [128, 2], F32)
        gl["kept"] = galloc("g_kept", [1, E], I32)
        p.cst = galloc("g_cst", [128, 4], F32)
        p.cop("vector", lambda e: e.memset(p.cst[:, 0:1], EPS_LN))
        p.cst_rdy = p.cop("vector", lambda e: e.memset(p.cst[:, 1:2], 1.0))

        with ExitStack() as st0:
            a0 = mk_alloc(st0)
            i1 = a0("g_i1", [128, 128], F32)
            c1 = p.cop("gpsimd", lambda e: e.iota(i1[:], pattern=[[1, 128]], base=0,
                                                  channel_multiplier=0,
                                                  allow_small_or_imprecise_dtypes=True))
            c2 = p.cop("gpsimd", lambda e: e.iota(gl["ident"][:], pattern=[[0, 128]], base=0,
                                                  channel_multiplier=1,
                                                  allow_small_or_imprecise_dtypes=True))
            p.cop("vector", lambda e: e.tensor_tensor(gl["ident"][:], gl["ident"][:],
                                                      i1[:], ALU.is_equal),
                  waits=[(p.chain["gpsimd"], c2)])
            cv = a0("g_cv", [1, 1], F32)
            cl = p.dma("sync", lambda e: e.dma_start(out=cv[:], in_=t["curv"].rearrange(
                "(a b) -> a b", a=1)))
            l1 = p.cop("scalar", lambda e: e.activation(out=cv[:], in_=cv[:], func=AF.Ln),
                       waits=[(p.dsem["sync"], cl)])
            l2 = p.cop("scalar", lambda e: e.activation(out=cv[:], in_=cv[:], func=AF.Exp,
                                                        scale=0.5))
            onc = a0("g_onc", [1, 128], F32)
            oc = p.cop("vector", lambda e: e.memset(onc[:], 1.0))
            psb = a0("g_psb", [128, 1], F32, psum=True)
            mmb = p.mmg(lambda e: e.matmul(psb[:], onc[:], cv[:], start=True, stop=True),
                        [(p.chain["scalar"], l2), (p.chain["vector"], oc)])
            gl["sc32_rdy"] = p.cop("vector", lambda e: e.tensor_scalar_mul(
                gl["sc32"][:, 0:1], psb[:], 32.0), waits=[(p.mmsem, mmb)])
            p.barrier()

        if "p1" in phases:
            phase_ln(p, t, t["x_tm"], t["xn_tm"], gl["rstd0"], gl["m0"], mk_alloc, tag="ln0")
        if "p2" in phases:
            phase_routing(p, t, gl, mk_alloc)
        if "p3" in phases:
            phase_scatter(p, t, gl, mk_alloc)
        if "p4" in phases:
            phase_l0_attn(p, t, mk_alloc)
        if "p5" in phases:
            phase_shared(p, t, gl, mk_alloc)
        if "p6" in phases:
            phase_experts(p, t, gl, mk_alloc)
        if "p7" in phases:
            phase_epilogue(p, t, gl, mk_alloc)
        if "p8" in phases:
            phase_ln(p, t, t["x1_tm"], t["xn1_tm"], gl["rstd1"], gl["m1"], mk_alloc,
                     tag="ln1", htan_dram=t["htan_tm"], sb=gl)
        if "p9" in phases:
            phase_l1_ffn(p, t, gl, mk_alloc)
        if "p10" in phases:
            phase_l1_attn(p, t, gl, mk_alloc)

        for d in dbg:
            if d.startswith("sb:"):
                name = d[3:]
                src_t = gl[name]
                dt_ = I32 if name == "idx_all" else F32
                t["dbg_" + name] = nc.dram_tensor(
                    "dbg_" + name, list(src_t[:].shape), dt_, kind="ExternalOutput").ap()
                p.dma("sync", lambda e, name=name, src_t=src_t: e.dma_start(
                    out=t["dbg_" + name], in_=src_t[:]))
            else:
                p.dma("sync", lambda e, d=d: e.dma_start(out=t["dbg_" + d], in_=t[d]))
        if dbg:
            p.wait_only("sync", [(p.dsem["sync"], p.dsem["sync"].cnt)])
        p.emit()
    return nc, t


def make_inputs(core, full):
    f32 = np.float32
    x = np.ascontiguousarray(full["L0_attn_x"], dtype=f32)
    m = {}
    m["x_tm"] = np.ascontiguousarray(x[core])
    m["x_fm"] = np.ascontiguousarray(np.transpose(x, (0, 2, 1)))
    m["x_fm_own"] = np.ascontiguousarray(x[core].T)
    selv = np.zeros((8, 1), f32)
    selv[:core] = 1.0
    m["sel"] = selv
    m["W_spec"] = full["L0_attn_W_spec"]
    m["W_decay"] = full["L0_attn_W_decay"]
    m["b_decay"] = full["L0_attn_b_decay"]
    m["freqs"] = full["L0_attn_freqs"]
    m["W_out"] = full["L0_attn_W_out"]
    m["gate_wT"] = np.ascontiguousarray(np.asarray(full["L0_ffn_gate_w"]).T)
    m["w_gate_up"] = full["L0_ffn_w_gate_up"]
    m["w_down"] = full["L0_ffn_w_down"]
    m["shared_w_gate_up"] = full["L0_ffn_shared_w_gate_up"]
    m["shared_w_down"] = full["L0_ffn_shared_w_down"]
    m["curv"] = np.ascontiguousarray(np.asarray(full["L1_attn_curv"])).reshape(1)
    m["wqT"] = np.ascontiguousarray(np.asarray(full["L1_attn_wq_w"]).T)
    m["wqb"] = full["L1_attn_wq_b"]
    m["wkT"] = np.ascontiguousarray(np.asarray(full["L1_attn_wk_w"]).T)
    m["wkb"] = full["L1_attn_wk_b"]
    m["wvT"] = np.ascontiguousarray(np.asarray(full["L1_attn_wv_w"]).T)
    m["wvb"] = full["L1_attn_wv_b"]
    m["woT"] = np.ascontiguousarray(np.asarray(full["L1_attn_wo_w"]).T)
    m["wob"] = full["L1_attn_wo_b"]
    m["WrT"] = np.ascontiguousarray(np.asarray(full["L1_ffn_W_r"]).T)
    m["WkT"] = np.ascontiguousarray(np.asarray(full["L1_ffn_W_k"]).T)
    m["WvT"] = np.ascontiguousarray(np.asarray(full["L1_ffn_W_v"]).T)
    m["mu_r"] = full["L1_ffn_mu_r"]
    m["mu_k"] = full["L1_ffn_mu_k"]
    return {k: np.ascontiguousarray(np.asarray(v), dtype=f32) for k, v in m.items()}


def kernel(**inputs):
    nc, t = build_program()
    in_maps = [make_inputs(c, inputs) for c in range(8)]
    res = run_bass_kernel_spmd(nc, in_maps, core_ids=list(range(8)))
    return np.concatenate([np.asarray(r["out"])[None] for r in res.results], axis=0)


def phase_l1_ffn(p, t, gl, mk_alloc):
    from contextlib import ExitStack
    vch, ach = p.chain["vector"], p.chain["scalar"]
    dsy, dgp = p.dsem["sync"], p.dsem["gpsimd"]
    mms = p.mmsem
    st = ExitStack()
    alloc = mk_alloc(st)
    TBK = 1024
    NBK = T // TBK
    NTT = TBK // 128
    xn1T = alloc("p9_xnT", [128, HC * T], BF16)
    wr = alloc("p9_wr", [128, HC * D], BF16)
    wk = alloc("p9_wk", [128, HC * D], BF16)
    wv = alloc("p9_wv", [128, HC * D], BF16)
    mus = alloc("p9_mu", [128, 2 * HC], F32)
    dsc_t = alloc("p9_d", [128, TBK], F32)
    xrT = alloc("p9_xr", [128, HC * TBK], BF16)
    xkT = alloc("p9_xk", [128, HC * TBK], BF16)
    kact = alloc("p9_kact", [128, HC * TBK], BF16)
    rblk = alloc("p9_rblk", [128, 2 * D], BF16)
    krelu = alloc("p9_krelu", [128, 512], F32)
    ot = alloc("p9_ot", [128, 2 * D], F32)
    ps = alloc("p9_ps", [128, 2 * D], F32, psum=True)
    psk = alloc("p9_psk", [128, 2 * 512], F32, psum=True)

    w1 = p.dma("gpsimd", lambda e: e.dma_start(
        out=wr[:].rearrange("p (c f) -> p c f", c=HC),
        in_=t["WrT"].rearrange("(c p) f -> p c f", p=128)))
    w2 = p.dma("gpsimd", lambda e: e.dma_start(
        out=wk[:].rearrange("p (c f) -> p c f", c=HC),
        in_=t["WkT"].rearrange("(c p) f -> p c f", p=128)))
    w3 = p.dma("gpsimd", lambda e: e.dma_start(
        out=wv[:].rearrange("p (c f) -> p c f", c=HC),
        in_=t["WvT"].rearrange("(c p) f -> p c f", p=128)))
    m1 = p.dma("sync", lambda e: e.dma_start(
        out=mus[:, 0:HC], in_=t["mu_r"].rearrange("(c p) -> p c", p=128),
        allow_slow_non_contiguous=True))
    m2 = p.dma("sync", lambda e: e.dma_start(
        out=mus[:, HC:2 * HC], in_=t["mu_k"].rearrange("(c p) -> p c", p=128),
        allow_slow_non_contiguous=True))
    xl = 0
    for c in range(HC):
        xl = p.dma("sync", lambda e, c=c: e.dma_start(
            out=xn1T[:, c * T:(c + 1) * T], in_=t["xn1_tm"][:, c * 128:(c + 1) * 128],
            transpose=True))

    ka_bank = [0, 0]
    r_ev = [0] * NT
    o_ev = [0] * NT
    o_st = [0] * NT
    last_mm_xr = 0
    for bk in range(NBK):
        t0 = bk * TBK
        lerp_c = 0
        for c in range(HC):
            base = c * T + t0
            w0 = [(dsy, xl), (dsy, m2), (mms, last_mm_xr)]
            if bk == 0:
                p.cop("vector", lambda e, base=base: e.tensor_tensor(
                    dsc_t[:, 1:TBK], xn1T[:, base + 1:base + TBK],
                    xn1T[:, base:base + TBK - 1], ALU.subtract), waits=w0)
                p.cop("vector", lambda e, base=base: e.tensor_copy(
                    dsc_t[:, 0:1], xn1T[:, base:base + 1]))
                p.cop("vector", lambda e, c=c, base=base: e.scalar_tensor_tensor(
                    xrT[:, c * TBK + 1:(c + 1) * TBK], dsc_t[:, 1:TBK], mus[:, c:c + 1],
                    xn1T[:, base:base + TBK - 1], ALU.mult, ALU.add))
                p.cop("vector", lambda e, c=c: e.tensor_scalar_mul(
                    xrT[:, c * TBK:c * TBK + 1], dsc_t[:, 0:1], mus[:, c:c + 1]))
                p.cop("vector", lambda e, c=c, base=base: e.scalar_tensor_tensor(
                    xkT[:, c * TBK + 1:(c + 1) * TBK], dsc_t[:, 1:TBK],
                    mus[:, HC + c:HC + c + 1],
                    xn1T[:, base:base + TBK - 1], ALU.mult, ALU.add))
                lerp_c = p.cop("vector", lambda e, c=c: e.tensor_scalar_mul(
                    xkT[:, c * TBK:c * TBK + 1], dsc_t[:, 0:1], mus[:, HC + c:HC + c + 1]))
            else:
                p.cop("vector", lambda e, base=base: e.tensor_tensor(
                    dsc_t[:], xn1T[:, base:base + TBK],
                    xn1T[:, base - 1:base + TBK - 1], ALU.subtract), waits=w0)
                p.cop("vector", lambda e, c=c, base=base: e.scalar_tensor_tensor(
                    xrT[:, c * TBK:(c + 1) * TBK], dsc_t[:], mus[:, c:c + 1],
                    xn1T[:, base - 1:base + TBK - 1], ALU.mult, ALU.add))
                lerp_c = p.cop("vector", lambda e, c=c, base=base: e.scalar_tensor_tensor(
                    xkT[:, c * TBK:(c + 1) * TBK], dsc_t[:], mus[:, HC + c:HC + c + 1],
                    xn1T[:, base - 1:base + TBK - 1], ALU.mult, ALU.add))
        kact_done = 0
        for dc in range(HC):
            for tq in range(2):
                g = dc * 2 + tq
                pp = g % 2

                def em(e, dc=dc, tq=tq, pp=pp):
                    out = []
                    for c in range(HC):
                        out.append(e.matmul(
                            psk[:, pp * 512:(pp + 1) * 512],
                            wk[:, c * D + dc * 128: c * D + dc * 128 + 128],
                            xkT[:, c * TBK + tq * 512: c * TBK + (tq + 1) * 512],
                            start=(c == 0), stop=(c == HC - 1)))
                    return out
                mmk = p.mmg(em, [(vch, lerp_c), (dgp, w2), (vch, ka_bank[pp])])
                p.cop("vector", lambda e, pp=pp: e.tensor_relu(
                    krelu[:], psk[:, pp * 512:(pp + 1) * 512]), waits=[(mms, mmk)])
                ka_bank[pp] = kact_done = p.cop(
                    "vector", lambda e, dc=dc, tq=tq: e.tensor_tensor(
                        kact[:, dc * TBK + tq * 512: dc * TBK + (tq + 1) * 512],
                        krelu[:], krelu[:], ALU.mult))
        for tt in range(NTT):
            ti = bk * NTT + tt
            pp = tt % 2

            def em_r(e, tt=tt, pp=pp):
                out = []
                for c in range(HC):
                    for n in range(2):
                        out.append(e.matmul(
                            ps[:, pp * D + n * 512: pp * D + (n + 1) * 512],
                            xrT[:, c * TBK + tt * 128: c * TBK + tt * 128 + 128],
                            wr[:, c * D + n * 512: c * D + (n + 1) * 512],
                            start=(c == 0), stop=(c == HC - 1)))
                return out
            w = [(vch, lerp_c), (dgp, w1)]
            if ti >= 2:
                w.append((vch, o_ev[ti - 2]))
            mmr_ = p.mmg(em_r, w)
            last_mm_xr = mmr_
            r_ev[ti] = p.cop("scalar", lambda e, tt=tt, pp=pp: e.activation(
                out=rblk[:, (tt % 2) * D:(tt % 2 + 1) * D],
                in_=ps[:, pp * D:(pp + 1) * D], func=AF.Sigmoid),
                waits=[(mms, mmr_)] + ([(vch, o_ev[ti - 2])] if ti >= 2 else []))

            def em_v(e, tt=tt, pp=pp):
                out = []
                for c in range(HC):
                    for n in range(2):
                        out.append(e.matmul(
                            ps[:, pp * D + n * 512: pp * D + (n + 1) * 512],
                            kact[:, c * TBK + tt * 128: c * TBK + tt * 128 + 128],
                            wv[:, c * D + n * 512: c * D + (n + 1) * 512],
                            start=(c == 0), stop=(c == HC - 1)))
                return out
            mmv = p.mmg(em_v, [(vch, kact_done), (ach, r_ev[ti]), (dgp, w3)])
            last_mm_xr = mmv
            o_ev[ti] = p.cop("vector", lambda e, tt=tt, pp=pp: e.tensor_tensor(
                ot[:, pp * D:(pp + 1) * D], rblk[:, (tt % 2) * D:(tt % 2 + 1) * D],
                ps[:, pp * D:(pp + 1) * D], ALU.mult),
                waits=[(mms, mmv)] + ([(dsy, o_st[ti - 2])] if ti >= 2 else []))
            o_st[ti] = p.dma("sync", lambda e, ti=ti, pp=pp: e.dma_start(
                out=t["ffn1"][ti * 128:(ti + 1) * 128, :], in_=ot[:, pp * D:(pp + 1) * D]),
                waits=[(vch, o_ev[ti])])
    p.barrier()
    st.close()


def phase_l1_attn(p, t, gl, mk_alloc):
    from contextlib import ExitStack
    vch, ach = p.chain["vector"], p.chain["scalar"]
    dsy, dgp = p.dsem["sync"], p.dsem["gpsimd"]
    mms = p.mmsem
    stA = ExitStack()
    alloc = mk_alloc(stA)
    htT = alloc("pa_htT", [128, HC * T], BF16)
    wq = alloc("pa_wq", [128, HC * D], BF16)
    wkk = alloc("pa_wk", [128, HC * D], BF16)
    wvv = alloc("pa_wv", [128, HC * D], BF16)
    biasB = alloc("pa_biasB", [128, 3 * D], F32)
    brow = alloc("pa_brow", [1, D], F32)
    onesrow = alloc("pa_ones", [1, 128], F32)
    qtt = alloc("pa_qtt", [128, D], F32)
    sqt = alloc("pa_sqt", [128, D], F32)
    n2 = alloc("pa_n2", [128, 2 * NH], F32)
    cfc = alloc("pa_cfc", [128, 8 * NH], F32)
    qhb = alloc("pa_qhb", [128, 3 * D], BF16)
    qtk = alloc("pa_qtk", [128, 128], BF16)
    sqc = alloc("pa_sqc", [128, 2], F32)
    ps = alloc("pa_ps", [128, 3 * D], F32, psum=True)
    psb = alloc("pa_psb", [128, 512], F32, psum=True)

    wl1 = p.dma("gpsimd", lambda e: e.dma_start(
        out=wq[:].rearrange("p (c f) -> p c f", c=HC),
        in_=t["wqT"].rearrange("(c p) f -> p c f", p=128)))
    wl2 = p.dma("gpsimd", lambda e: e.dma_start(
        out=wkk[:].rearrange("p (c f) -> p c f", c=HC),
        in_=t["wkT"].rearrange("(c p) f -> p c f", p=128)))
    wl3 = p.dma("gpsimd", lambda e: e.dma_start(
        out=wvv[:].rearrange("p (c f) -> p c f", c=HC),
        in_=t["wvT"].rearrange("(c p) f -> p c f", p=128)))
    hl = 0
    for c in range(HC):
        hl = p.dma("sync", lambda e, c=c: e.dma_start(
            out=htT[:, c * T:(c + 1) * T], in_=t["htan_tm"][:, c * 128:(c + 1) * 128],
            transpose=True))
    oc = p.cop("vector", lambda e: e.memset(onesrow[:], 1.0))
    bb_c = 0
    mb = 0
    for wi, wname in enumerate(("wqb", "wkb", "wvb")):
        bl = p.dma("sync", lambda e, wname=wname: e.dma_start(
            out=brow[:], in_=t[wname].rearrange("(a f) -> a f", a=1)),
            waits=[(mms, mb)])
        for n in range(2):
            mb = p.mmg(lambda e, n=n: e.matmul(
                psb[:], onesrow[:], brow[:, n * 512:(n + 1) * 512], start=True, stop=True),
                [(dsy, bl), (vch, oc), (ach, bb_c)])
            bb_c = p.cop("scalar", lambda e, wi=wi, n=n: e.copy(
                out=biasB[:, wi * D + n * 512: wi * D + (n + 1) * 512], in_=psb[:]),
                waits=[(mms, mb)])
    sq_c = p.cop("vector", lambda e: e.tensor_scalar_mul(
        sqc[:, 0:1], gl["sc32"][:, 0:1], 1.0 / 32.0), waits=[(vch, gl["sc32_rdy"])])
    p.cop("vector", lambda e: e.tensor_tensor(
        sqc[:, 1:2], sqc[:, 0:1], sqc[:, 0:1], ALU.mult))
    iv_c = p.cop("vector", lambda e: e.reciprocal(sqc[:, 1:2], sqc[:, 1:2]))

    def hyp_map(kind, psrc):
        bi = 0 if kind == "q" else D
        add = p.cop("vector", lambda e: e.tensor_tensor(
            qtt[:], psrc, biasB[:, bi:bi + D], ALU.add))
        sq = p.cop("scalar", lambda e: e.activation(out=sqt[:], in_=qtt[:], func=AF.Square),
                   waits=[(vch, add)])
        rd = p.cop("vector", lambda e: e.tensor_reduce(
            n2[:, 0:NH], sqt[:].rearrange("p (h d) -> p h d", h=NH), axis=AX.X,
            op=ALU.add), waits=[(ach, sq)])
        l1 = p.cop("scalar", lambda e: e.activation(out=n2[:, NH:2 * NH], in_=n2[:, 0:NH],
                                                    func=AF.Ln), waits=[(vch, rd)])
        e1 = p.cop("scalar", lambda e: e.activation(out=n2[:, NH:2 * NH],
                                                    in_=n2[:, NH:2 * NH],
                                                    func=AF.Exp, scale=0.5))
        z = p.cop("vector", lambda e: e.tensor_tensor(
            n2[:, NH:2 * NH], n2[:, NH:2 * NH], sqc[:, 0:1].to_broadcast([128, NH]),
            ALU.mult), waits=[(ach, e1), (vch, iv_c)])
        z = p.cop("vector", lambda e: e.tensor_scalar_max(
            n2[:, NH:2 * NH], n2[:, NH:2 * NH], 1e-7))
        ep = p.cop("scalar", lambda e: e.activation(out=cfc[:, 0:NH], in_=n2[:, NH:2 * NH],
                                                    func=AF.Exp), waits=[(vch, z)])
        em_ = p.cop("scalar", lambda e: e.activation(out=cfc[:, NH:2 * NH],
                                                     in_=n2[:, NH:2 * NH],
                                                     func=AF.Exp, scale=-1.0))
        p.cop("vector", lambda e: e.tensor_tensor(
            cfc[:, 0:NH], cfc[:, 0:NH], cfc[:, NH:2 * NH], ALU.subtract),
            waits=[(ach, em_)])
        p.cop("vector", lambda e: e.tensor_scalar_mul(cfc[:, 0:NH], cfc[:, 0:NH], 0.5))
        p.cop("vector", lambda e: e.reciprocal(cfc[:, NH:2 * NH], n2[:, NH:2 * NH]))
        p.cop("vector", lambda e: e.tensor_tensor(
            cfc[:, 2 * NH:3 * NH], cfc[:, 0:NH], cfc[:, NH:2 * NH], ALU.mult))
        import os as _os
        if _os.environ.get("KDBG_NOCOEF"):
            qh = p.cop("vector", lambda e, bi=bi: e.tensor_copy(
                qhb[:, bi:bi + D], qtt[:]))
        else:
            qh = p.cop("vector", lambda e, bi=bi: e.tensor_tensor(
                qhb[:, bi:bi + D].rearrange("p (h d) -> p h d", h=NH),
                qtt[:].rearrange("p (h d) -> p h d", h=NH),
                cfc[:, 2 * NH:3 * NH].rearrange("p (h o) -> p h o", o=1).to_broadcast(
                    [128, NH, HD]), ALU.mult))
        p.cop("vector", lambda e: e.tensor_tensor(
            cfc[:, 3 * NH:4 * NH], cfc[:, 0:NH], cfc[:, 0:NH], ALU.mult))
        t2 = p.cop("vector", lambda e: e.tensor_tensor(
            cfc[:, 3 * NH:4 * NH], cfc[:, 3 * NH:4 * NH],
            sqc[:, 1:2].to_broadcast([128, NH]), ALU.add))
        s1 = p.cop("scalar", lambda e: e.activation(out=cfc[:, 4 * NH:5 * NH],
                                                    in_=cfc[:, 3 * NH:4 * NH],
                                                    func=AF.Ln), waits=[(vch, t2)])
        s2 = p.cop("scalar", lambda e: e.activation(out=cfc[:, 4 * NH:5 * NH],
                                                    in_=cfc[:, 4 * NH:5 * NH],
                                                    func=AF.Exp, scale=0.5))
        qt_ = p.cop("vector", lambda e, kind=kind: e.tensor_copy(
            qtk[:, (0 if kind == "q" else NH):(NH if kind == "q" else 2 * NH)],
            cfc[:, 4 * NH:5 * NH]), waits=[(ach, s2)])
        return max(qh, qt_)

    tile_done = [0] * NT
    store_c = [0] * NT
    kind_done = {"q": 0, "k": 0, "v": 0}
    for ti in range(NT):
        pp = ti % 2
        done_c = 0
        pst = store_c[ti - 1] if ti else 0
        for ki_, (wt, kind, dstd, qoff) in enumerate((
                (wq, "q", t["qh_tm"], 0),
                (wkk, "k", t["kh_tm"], D),
                (wvv, "v", t["v_tm"], 2 * D))):
            def em(e, wt=wt, ti=ti, ki_=ki_):
                out = []
                for c in range(HC):
                    for n in range(2):
                        out.append(e.matmul(
                            ps[:, ki_ * D + n * 512: ki_ * D + (n + 1) * 512],
                            htT[:, c * T + ti * 128: c * T + ti * 128 + 128],
                            wt[:, c * D + n * 512: c * D + (n + 1) * 512],
                            start=(c == 0), stop=(c == HC - 1)))
                return out
            w = [(dsy, hl), (dgp, wl3), (ach, bb_c), (vch, kind_done[kind])]
            mmq = p.mmg(em, w)
            if kind == "v":
                done_c = p.cop("vector", lambda e, ki_=ki_: e.tensor_tensor(
                    qhb[:, 2 * D:3 * D], ps[:, ki_ * D:(ki_ + 1) * D],
                    biasB[:, 2 * D:3 * D], ALU.add), waits=[(mms, mmq), (dsy, pst)])
            else:
                p.cop("vector", lambda e: e.engine_nop(), waits=[(mms, mmq), (dsy, pst)])
                done_c = hyp_map(kind, ps[:, ki_ * D:(ki_ + 1) * D])
            kind_done[kind] = done_c
            store_c[ti] = p.dma("sync", lambda e, ti=ti, dstd=dstd, qoff=qoff: e.dma_start(
                out=dstd[ti * 128:(ti + 1) * 128, :], in_=qhb[:, qoff:qoff + D]),
                waits=[(vch, done_c)])
        tile_done[ti] = p.cop("vector", lambda e: e.memset(qtk[:, 2 * NH:128], 0.0),
                              waits=[(dsy, pst)])
        store_c[ti] = p.dma("sync", lambda e, ti=ti: e.dma_start(
            out=t["qtkt"][ti * 128:(ti + 1) * 128, :], in_=qtk[:]),
            waits=[(vch, tile_done[ti])])
    p.barrier()
    stA.close()

    stB = ExitStack()
    alloc = mk_alloc(stB)
    qkT = alloc("pb_qkT", [128, T], BF16)
    Mrow = alloc("pb_M", [32, T], F32)
    Msc = alloc("pb_Msc", [16, T], F32)
    rowsb = alloc("pb_rowsb", [32, T], BF16)
    mneg = alloc("pb_mneg", [16, T], BF16)
    sqc2 = alloc("pb_sqc", [128, 4], F32)
    qtd = t["misc_d"]
    rowd = p.nc.dram_tensor("pb_rowd", [48, T], BF16).ap()
    onesd = p.nc.dram_tensor("pb_onesd", [1, T], BF16).ap()

    qk_l = p.dma("sync", lambda e: e.dma_start(out=qkT[:], in_=t["qtkt"], transpose=True))
    iv2 = p.cop("vector", lambda e: e.tensor_tensor(
        sqc2[:, 0:1], gl["sc32"][:, 0:1], gl["sc32"][:, 0:1], ALU.mult),
        waits=[(vch, gl["sc32_rdy"])])
    iv2 = p.cop("vector", lambda e: e.tensor_scalar_mul(sqc2[:, 0:1], sqc2[:, 0:1],
                                                        1.0 / 1024.0))
    iv2 = p.cop("vector", lambda e: e.reciprocal(sqc2[:, 0:1], sqc2[:, 0:1]))
    m_c = p.cop("vector", lambda e: e.tensor_copy(Mrow[0:32, :], qkT[0:32, :]),
                waits=[(dsy, qk_l)])
    m_c = p.cop("vector", lambda e: e.tensor_tensor(
        Mrow[0:32, :], Mrow[0:32, :], Mrow[0:32, :], ALU.mult))
    m_c = p.cop("vector", lambda e: e.tensor_scalar_sub(
        Mrow[0:32, :], Mrow[0:32, :], sqc2[0:32, 0:1]), waits=[(vch, iv2)])
    m_c = p.cop("vector", lambda e: e.tensor_scalar_max(Mrow[0:32, :], Mrow[0:32, :], 0.0))
    m_sq = p.cop("scalar", lambda e: e.activation(out=Mrow[0:32, :], in_=Mrow[0:32, :],
                                                  func=AF.Ln, bias=p.cst[0:32, 0:1]),
                 waits=[(vch, m_c), (vch, p.cst_rdy)])
    m_sq = p.cop("scalar", lambda e: e.activation(out=Mrow[0:32, :], in_=Mrow[0:32, :],
                                                  func=AF.Exp, scale=0.5))
    mx = p.cop("vector", lambda e: e.reduce_max(sqc2[0:32, 1:2], qkT[0:32, :], axis=AX.X),
               waits=[(ach, m_sq)])
    mx = p.cop("vector", lambda e: e.reduce_max(sqc2[0:32, 2:3], Mrow[0:32, :], axis=AX.X))
    b1 = p.dma("gpsimd", lambda e: e.dma_start(out=t["misc_d"][320:352],
                                               in_=sqc2[0:32, 1]), waits=[(vch, mx)])
    b2 = p.dma("gpsimd", lambda e: e.dma_start(out=t["misc_d"][352:384],
                                               in_=sqc2[0:32, 2]), waits=[(vch, mx)])
    b3 = p.dma("gpsimd", lambda e: e.dma_start(
        out=sqc2[0:16, 1:2], in_=t["misc_d"][336:352].rearrange("(p a) -> p a", a=1)),
        waits=[(dgp, b2)])
    b4 = p.dma("gpsimd", lambda e: e.dma_start(
        out=sqc2[0:16, 2:3], in_=t["misc_d"][368:384].rearrange("(p a) -> p a", a=1)),
        waits=[(dgp, b2)])
    mm_t = p.cop("vector", lambda e: e.tensor_scalar(
        Msc[0:16, :], qkT[0:16, :], sqc2[0:16, 1:2], 0.125, ALU.mult, ALU.mult),
        waits=[(dgp, b4)])
    mm_t = p.cop("vector", lambda e: e.tensor_scalar(
        Mrow[0:16, :], Mrow[0:16, :], sqc2[0:16, 2:3], 0.125, ALU.mult, ALU.mult))
    mm_t = p.cop("vector", lambda e: e.tensor_tensor(
        Mrow[0:16, :], Mrow[0:16, :], Msc[0:16, :], ALU.add))
    mm_t = p.cop("vector", lambda e: e.tensor_scalar_mul(
        rowsb[0:32, :], qkT[0:32, :], 0.125))
    mm_t = p.cop("vector", lambda e: e.tensor_scalar_mul(
        mneg[0:16, :], Mrow[0:16, :], -1.0))
    r1 = p.dma("sync", lambda e: e.dma_start(out=rowd[0:16, :], in_=rowsb[16:32, :]),
               waits=[(vch, mm_t)])
    r1 = p.dma("sync", lambda e: e.dma_start(out=rowd[16:32, :], in_=mneg[0:16, :]),
               waits=[(vch, mm_t)])
    r2 = p.dma("sync", lambda e: e.dma_start(out=rowd[32:48, :], in_=qkT[0:16, :]),
               waits=[(vch, mm_t)])
    oz = p.cop("vector", lambda e: e.memset(rowsb[0:1, :], 1.0))
    r3 = p.dma("sync", lambda e: e.dma_start(out=onesd[:], in_=rowsb[0:1, :]),
               waits=[(vch, oz)])
    p.barrier()
    stB.close()

    stC = ExitStack()
    alloc = mk_alloc(stC)
    QT2 = alloc("pc_qt2", [128, T], BF16)
    KT2 = alloc("pc_kt2", [128, T], BF16)
    Kop = alloc("pc_Kop", [66, T], BF16)
    Qop = alloc("pc_Qop", [66, T], BF16)
    vh = alloc("pc_vh", [128, NT * 72], BF16)
    PT = alloc("pc_PT", [128, 2 * 8 * 2048], BF16)
    rrow = alloc("pc_rrow", [1, 512], F32)
    onum = alloc("pc_onum", [64, 512], F32)
    OT = alloc("pc_OT", [128, HC * T], BF16)
    onesc2 = alloc("pc_ones2", [1, 64], F32)
    psS = alloc("pc_psS", [128, 2048], F32, psum=True)
    psS2 = alloc("pc_psS2", [128, 1536], F32, psum=True)
    psO = alloc("pc_psO", [65, 512], F32, psum=True)
    o2 = p.cop("vector", lambda e: e.memset(onesc2[:], 1.0))
    vz = p.cop("vector", lambda e: e.memset(
        vh[:].rearrange("p (kc x) -> p kc x", kc=NT)[:, :, 64:65], 1.0))
    exs = p.sem("pb_exs")

    exp_c = 0
    av_prev = 0
    av_pp = [0, 0]
    oe_prev = 0
    oe_prev_v = 0
    bm_prev = 0
    on_prev = 0
    pair_s_done = 0
    head_free = 0
    last_qk_read = 0
    for hp in range(HC):
        ql = p.dma("sync", lambda e, hp=hp: e.dma_start(
            out=QT2[:], in_=t["qh_tm"][:, hp * 128:(hp + 1) * 128], transpose=True),
            waits=[(mms, pair_s_done), (vch, last_qk_read)])
        kl = p.dma("sync", lambda e, hp=hp: e.dma_start(
            out=KT2[:], in_=t["kh_tm"][:, hp * 128:(hp + 1) * 128], transpose=True),
            waits=[(mms, pair_s_done), (vch, last_qk_read)])
        for hh in range(2):
            h = hp * 2 + hh
            hb = hh * 64
            vl = p.dma("sync", lambda e, h=h: e.dma_start(
                out=vh[:].rearrange("p (kc x) -> p kc x", kc=NT)[:, :, 0:64],
                in_=t["v_tm"].rearrange("(kc p) f -> p kc f", p=128)[:, :, h * 64:(h + 1) * 64]),
                waits=[(mms, av_prev)])
            w0 = [(dsy, kl), (mms, head_free)]
            kc1 = p.cop("vector", lambda e, hb=hb: e.tensor_scalar_mul(
                Kop[0:64, :], KT2[hb:hb + 64, :], -0.125), waits=w0)
            k64 = p.dma("gpsimd", lambda e, h=h: e.dma_start(
                out=Kop[64:65, :], in_=rowd[h:h + 1, :]), waits=[(mms, head_free)])
            k65 = p.dma("gpsimd", lambda e: e.dma_start(
                out=Kop[65:66, :], in_=onesd[:]), waits=[(mms, head_free)])
            qc1 = p.cop("vector", lambda e, hb=hb: e.tensor_copy(
                Qop[0:64, :], QT2[hb:hb + 64, :]), waits=[(dsy, ql), (mms, head_free)])
            last_qk_read = qc1
            q64 = p.dma("gpsimd", lambda e, h=h: e.dma_start(
                out=Qop[64:65, :], in_=rowd[32 + h:33 + h, :]), waits=[(mms, head_free)])
            q65 = p.dma("gpsimd", lambda e, h=h: e.dma_start(
                out=Qop[65:66, :], in_=rowd[16 + h:17 + h, :]), waits=[(mms, head_free)])
            GRPS = [4, 3, 4, 3, 4, 3, 4, 3, 4]
            exp_pp = [exs.cnt, exs.cnt]
            for qb in range(8):
                bsel = (qb % 2) * 8 * 2048
                kc0 = 0
                for gi_, gw_ in enumerate(GRPS):
                    buf = psS if gi_ % 2 == 0 else psS2

                    def em(e, kc0=kc0, gw_=gw_, qb=qb, buf=buf):
                        out = []
                        for j in range(gw_):
                            kc = kc0 + j
                            out.append(e.matmul(
                                buf[:, j * 512:(j + 1) * 512],
                                Kop[:, kc * 128:(kc + 1) * 128],
                                Qop[:, qb * 512:(qb + 1) * 512],
                                start=True, stop=True, skip_group_check=True))
                        return out
                    sm = p.mmg(em, [(vch, kc1), (vch, qc1), (dgp, q65),
                                    (exs, exp_pp[gi_ % 2])])
                    exp_pp[gi_ % 2] = exp_c = p.aop(
                        "scalar", lambda e, kc0=kc0, gw_=gw_, bsel=bsel, buf=buf: e.activation(
                            out=PT[:, bsel + kc0 * 512: bsel + (kc0 + gw_) * 512],
                            in_=buf[:, 0:gw_ * 512], func=AF.Exp),
                        waits=[(mms, sm), (mms, av_pp[qb % 2])], sem=exs)
                    kc0 += gw_

                def emav(e, bsel=bsel):
                    out = []
                    for kc in range(NT):
                        out.append(e.matmul(
                            psO[:], vh[:, kc * 72: kc * 72 + 65],
                            PT[:, bsel + kc * 512: bsel + (kc + 1) * 512],
                            start=(kc == 0), stop=(kc == NT - 1)))
                    return out
                av_prev = p.mmg(emav, [(exs, exp_c), (dsy, vl), (vch, vz),
                                       (vch, oe_prev), (mms, bm_prev), (vch, on_prev)])
                av_pp[qb % 2] = av_prev
                rc = p.cop("vector", lambda e: e.reciprocal(rrow[:], psO[64:65, :]),
                           waits=[(mms, av_prev)])
                oe_prev_v = p.cop("vector", lambda e: e.tensor_copy(onum[:], psO[0:64, :]),
                                  waits=[(vch, rc)])
                oe_prev = oe_prev_v
                bm_prev = p.mmg(lambda e: e.matmul(psO[0:64, :], onesc2[:], rrow[:],
                                                   start=True, stop=True),
                                [(vch, rc), (vch, o2), (vch, oe_prev), (vch, on_prev)])
                on_prev = p.cop("vector", lambda e, hp=hp, hb=hb, qb=qb: e.tensor_tensor(
                    OT[hb:hb + 64, hp * T + qb * 512: hp * T + (qb + 1) * 512],
                    onum[:], psO[0:64, :], ALU.mult), waits=[(vch, oe_prev), (mms, bm_prev)])
            head_free = av_prev
            pair_s_done = av_prev
    p.barrier()
    stC.close()

    stD = ExitStack()
    alloc = mk_alloc(stD)
    wo = alloc("pd_wo", [128, HC * D], BF16)
    bB = alloc("pd_bB", [128, D], F32)
    brow2 = alloc("pd_brow", [1, D], F32)
    ones2 = alloc("pd_ones", [1, 128], F32)
    x1t = alloc("pd_x1", [128, 2 * D], F32)
    f1t = alloc("pd_f1", [128, 2 * D], F32)
    outt = alloc("pd_out", [128, 2 * D], F32)
    pso = alloc("pd_pso", [128, 2 * D], F32, psum=True)
    psb3 = alloc("pd_psb3", [128, 512], F32, psum=True)
    wol = p.dma("gpsimd", lambda e: e.dma_start(
        out=wo[:].rearrange("p (c f) -> p c f", c=HC),
        in_=t["woT"].rearrange("(c p) f -> p c f", p=128)))
    o3 = p.cop("vector", lambda e: e.memset(ones2[:], 1.0))
    bl = p.dma("sync", lambda e: e.dma_start(
        out=brow2[:], in_=t["wob"].rearrange("(a f) -> a f", a=1)))
    bb2 = 0
    for n in range(2):
        mb = p.mmg(lambda e, n=n: e.matmul(
            psb3[:], ones2[:], brow2[:, n * 512:(n + 1) * 512], start=True, stop=True),
            [(dsy, bl), (vch, o3), (ach, bb2)])
        bb2 = p.cop("scalar", lambda e, n=n: e.copy(
            out=bB[:, n * 512:(n + 1) * 512], in_=psb3[:]), waits=[(mms, mb)])
    o_c = [0] * NT
    ost = [0] * NT
    for ti in range(NT):
        pp = ti % 2

        def em(e, ti=ti, pp=pp):
            out = []
            for c in range(HC):
                for n in range(2):
                    out.append(e.matmul(
                        pso[:, pp * D + n * 512: pp * D + (n + 1) * 512],
                        OT[:, c * T + ti * 128: c * T + ti * 128 + 128],
                        wo[:, c * D + n * 512: c * D + (n + 1) * 512],
                        start=(c == 0), stop=(c == HC - 1)))
            return out
        w = [(dgp, wol), (ach, bb2)]
        if ti >= 2:
            w.append((vch, o_c[ti - 2]))
        mo = p.mmg(em, w)
        wre = [(vch, o_c[ti - 2])] if ti >= 2 else []
        xld = p.dma("sync", lambda e, ti=ti, pp=pp: e.dma_start(
            out=x1t[:, pp * D:(pp + 1) * D], in_=t["x1_tm"][ti * 128:(ti + 1) * 128, :]),
            waits=wre)
        fld = p.dma("sync", lambda e, ti=ti, pp=pp: e.dma_start(
            out=f1t[:, pp * D:(pp + 1) * D], in_=t["ffn1"][ti * 128:(ti + 1) * 128, :]),
            waits=wre)
        p.cop("vector", lambda e, pp=pp: e.tensor_tensor(
            outt[:, pp * D:(pp + 1) * D], x1t[:, pp * D:(pp + 1) * D],
            f1t[:, pp * D:(pp + 1) * D], ALU.add),
            waits=[(dsy, fld)] + ([(dsy, ost[ti - 2])] if ti >= 2 else []))
        p.cop("vector", lambda e, pp=pp: e.tensor_tensor(
            outt[:, pp * D:(pp + 1) * D], outt[:, pp * D:(pp + 1) * D], bB[:], ALU.add))
        o_c[ti] = p.cop("vector", lambda e, pp=pp: e.tensor_tensor(
            outt[:, pp * D:(pp + 1) * D], outt[:, pp * D:(pp + 1) * D],
            pso[:, pp * D:(pp + 1) * D], ALU.add), waits=[(mms, mo)])
        ost[ti] = p.dma("sync", lambda e, ti=ti, pp=pp: e.dma_start(
            out=t["out"][ti * 128:(ti + 1) * 128, :], in_=outt[:, pp * D:(pp + 1) * D]),
            waits=[(vch, o_c[ti])])
    p.wait_only("sync", [(dsy, ost[NT - 1])])
    p.barrier()
    stD.close()


# version 15
# speedup vs baseline: 1.3974x; 1.1107x over previous
import math
import numpy as np
import concourse.bass as bass
import concourse.mybir as mybir
from concourse.bass_utils import run_bass_kernel_spmd
from concourse.bass_core_rust import IndirectOffsetOnAxis

F32 = mybir.dt.float32
BF16 = mybir.dt.bfloat16
I32 = mybir.dt.int32
U32 = mybir.dt.uint32
AF = mybir.ActivationFunctionType
ALU = mybir.AluOpType
AX = mybir.AxisListType

T = 4096
D = 1024
NT = T // 128
HC = D // 128
F = 512
NFC = F // 128
E = 32
CAP = 1024
NH = 16
HD = 64
EPS_LN = 1e-5
TWO_PI = 2.0 * math.pi
BIGIDX = float(1 << 22)
ENGS = ("sync", "tensor", "vector", "scalar", "gpsimd")
CHAINED = ("vector", "scalar", "gpsimd")


class Sem:
    __slots__ = ("h", "cnt", "name")

    def __init__(self, nc, stack, name):
        self.h = stack.enter_context(nc.semaphore(name))
        self.cnt = 0
        self.name = name

    def bump(self, n=1):
        self.cnt += n
        return self.cnt


class Prog:
    def __init__(self, nc, block, stack):
        self.nc = nc
        self.block = block
        self.stack = stack
        self.ops = {k: [] for k in ENGS}
        self.sems = {}
        self.chain = {k: self.sem("ch_" + k) for k in CHAINED}
        self.dsem = {k: self.sem("d_" + k) for k in ("sync", "scalar", "gpsimd")}
        self.mmsem = self.sem("mmg")
        self.waited = {k: {} for k in ENGS}

    def sem(self, name):
        if name not in self.sems:
            self.sems[name] = Sem(self.nc, self.stack, name)
        return self.sems[name]

    def _record(self, eng, emit, waits, inc, inc_n):
        wl = []
        for s, c in waits:
            if c <= 0:
                continue
            w = self.waited[eng]
            if w.get(s.name, 0) >= c:
                continue
            w[s.name] = c
            wl.append((s.h, c))
        self.ops[eng].append((wl, emit, None if inc is None else inc.h, inc_n))
        return inc.bump(inc_n) if inc is not None else None

    def cop(self, eng, emit, waits=()):
        ch = self.chain[eng]
        w = list(waits) + [(ch, ch.cnt)]
        return self._record(eng, emit, w, ch, 1)

    def aop(self, eng, emit, waits, sem):
        return self._record(eng, emit, waits, sem, 1)

    def dma(self, eng, emit, waits=()):
        return self._record(eng, emit, waits, self.dsem[eng], 16)

    def mmg(self, emit, waits=()):
        return self._record("tensor", emit, waits, self.mmsem, 1)

    def wait_only(self, eng, waits):
        self._record(eng, lambda e: None, waits, None, 0)

    def emit(self):
        blk = self.block
        decs = {"sync": blk.sync, "tensor": blk.tensor, "vector": blk.vector,
                "scalar": blk.scalar, "gpsimd": blk.gpsimd}
        for name in ENGS:
            entries = self.ops[name]
            if not entries:
                continue

            def make(entries):
                def _(e):
                    for wl, emit, inch, incn in entries:
                        for h, c in wl:
                            e.wait_ge(h, c)
                        ins = emit(e)
                        if inch is not None:
                            if isinstance(ins, (list, tuple)):
                                ins = ins[-1]
                            ins.then_inc(inch, incn)
                return _
            decs[name](make(entries))
            self.ops[name] = []

    def barrier(self):
        waits = [(s, s.cnt) for s in self.sems.values() if s.cnt > 0]
        for eng in ENGS:
            self.wait_only(eng, waits)
        self.emit()


def declare_io(nc):
    t = {}
    di = lambda n, sh, dt=F32: nc.dram_tensor(n, sh, dt, kind="ExternalInput").ap()
    dint = lambda n, sh, dt: nc.dram_tensor(n, sh, dt).ap()
    t["x_tm"] = di("x_tm", [T, D])
    t["x_fm"] = di("x_fm", [8, D, T])
    t["x_fm_own"] = di("x_fm_own", [D, T])
    t["sel"] = di("sel", [8, 1])
    t["W_spec"] = di("W_spec", [D, 2 * F])
    t["W_decay"] = di("W_decay", [D, F])
    t["b_decay"] = di("b_decay", [F])
    t["freqs"] = di("freqs", [F])
    t["W_out"] = di("W_out", [2 * F, D])
    t["gate_wT"] = di("gate_wT", [D, E])
    t["w_gate_up"] = di("w_gate_up", [E, D, 2 * F])
    t["w_down"] = di("w_down", [E, F, D])
    t["shared_w_gate_up"] = di("shared_w_gate_up", [D, 2 * F])
    t["shared_w_down"] = di("shared_w_down", [F, D])
    t["curv"] = di("curv", [1])
    for w in ("wq", "wk", "wv", "wo"):
        t[w + "T"] = di(w + "T", [D, D])
        t[w + "b"] = di(w + "b", [D])
    t["WrT"] = di("WrT", [D, D])
    t["WkT"] = di("WkT", [D, D])
    t["WvT"] = di("WvT", [D, D])
    t["mu_r"] = di("mu_r", [D])
    t["mu_k"] = di("mu_k", [D])
    t["out"] = nc.dram_tensor("out", [T, D], F32, kind="ExternalOutput").ap()

    t["xn_tm"] = dint("xn_tm", [T, D], BF16)
    t["y_fm"] = dint("y_fm", [2 * F, T], BF16)
    t["attn0"] = dint("attn0", [T, D], F32)
    t["buf"] = dint("buf", [E * CAP, D], BF16)
    t["eo"] = dint("eo", [E * CAP, D], BF16)
    t["x1_tm"] = dint("x1_tm", [T, D], F32)
    t["xn1_tm"] = dint("xn1_tm", [T, D], BF16)
    t["htan_tm"] = dint("htan_tm", [T, D], BF16)
    t["ffn1"] = dint("ffn1", [T, D], F32)
    t["qh_tm"] = dint("qh_tm", [T, D], BF16)
    t["kh_tm"] = dint("kh_tm", [T, D], BF16)
    t["v_tm"] = dint("v_tm", [T, D], BF16)
    t["qtkt"] = dint("qtkt", [T, 128], BF16)
    t["misc_d"] = dint("misc_d", [512], F32)
    t["act_sh"] = dint("act_sh", [4 * 128, T], BF16)
    return t


def phase_ln(p, t, x_dram, xn_dram, rstd_dst, m_dst, mk_alloc, tag="ln0",
             htan_dram=None, sb=None):
    from contextlib import ExitStack
    st = ExitStack()
    alloc = mk_alloc(st)
    vch, ach = p.chain["vector"], p.chain["scalar"]
    dsy = p.dsem["sync"]
    dgp = p.dsem["gpsimd"]
    xt = alloc(tag + "_x", [128, 2 * D], F32)
    sqt = alloc(tag + "_sq", [128, D], F32)
    xnf = alloc(tag + "_xnf", [128, D], F32)
    xnt = alloc(tag + "_xn", [128, 2 * D], BF16)
    htt = alloc(tag + "_ht", [128, 2 * D], BF16) if htan_dram is not None else None
    cols = alloc(tag + "_cols", [128, 4 * NT], F32)
    CSS, CV, CT0, CT1 = (k * NT for k in range(4))
    xn_c = [0] * NT
    st_c = [0] * NT
    if htan_dram is not None:
        sc32, sc32_rdy = sb["sc32"], sb["sc32_rdy"]

    for i in range(NT):
        bx = (i % 2) * D
        ld = p.dma("sync", lambda e, i=i, bx=bx: e.dma_start(
            out=xt[:, bx:bx + D], in_=x_dram[i * 128:(i + 1) * 128, :]),
            waits=[(vch, xn_c[i - 2])] if i >= 2 else [])
        sq = p.cop("scalar", lambda e, i=i, bx=bx: e.activation(
            out=sqt[:], in_=xt[:, bx:bx + D], func=AF.Square,
            accum_out=cols[:, CSS + i:CSS + i + 1]), waits=[(dsy, ld)])
        p.cop("vector", lambda e, i=i, bx=bx: e.reduce_sum(
            m_dst[:, i:i + 1], xt[:, bx:bx + D], axis=AX.X), waits=[(dsy, ld)])
        p.cop("vector", lambda e, i=i: e.tensor_scalar_mul(
            m_dst[:, i:i + 1], m_dst[:, i:i + 1], 1.0 / D))
        p.cop("vector", lambda e, i=i: e.tensor_scalar_mul(
            cols[:, CT0 + i:CT0 + i + 1], cols[:, CSS + i:CSS + i + 1], 1.0 / D),
            waits=[(ach, sq)])
        p.cop("vector", lambda e, i=i: e.tensor_tensor(
            cols[:, CV + i:CV + i + 1], m_dst[:, i:i + 1], m_dst[:, i:i + 1], ALU.mult))
        var = p.cop("vector", lambda e, i=i: e.tensor_tensor(
            cols[:, CV + i:CV + i + 1], cols[:, CT0 + i:CT0 + i + 1],
            cols[:, CV + i:CV + i + 1], ALU.subtract))
        lnv = p.cop("scalar", lambda e, i=i: e.activation(
            out=cols[:, CT0 + i:CT0 + i + 1], in_=cols[:, CV + i:CV + i + 1],
            func=AF.Ln, bias=p.cst[:, 0:1]), waits=[(vch, var), (vch, p.cst_rdy)])
        rstd = p.cop("scalar", lambda e, i=i: e.activation(
            out=rstd_dst[:, i:i + 1], in_=cols[:, CT0 + i:CT0 + i + 1],
            func=AF.Exp, scale=-0.5))
        xnf_c = p.cop("vector", lambda e, i=i, bx=bx: e.tensor_scalar(
            xnf[:], xt[:, bx:bx + D],
            m_dst[:, i:i + 1], rstd_dst[:, i:i + 1], ALU.subtract, ALU.mult),
            waits=[(ach, rstd)])
        xn_c[i] = p.cop("vector", lambda e, i=i, bx=bx: e.tensor_copy(
            xnt[:, bx:bx + D], xnf[:]),
            waits=([(dgp, st_c[i - 2])] if i >= 2 else []))
        st_c[i] = p.dma("gpsimd", lambda e, i=i, bx=bx: e.dma_start(
            out=xn_dram[i * 128:(i + 1) * 128, :], in_=xnt[:, bx:bx + D]),
            waits=[(vch, xn_c[i])])
        if htan_dram is not None:
            lnv2 = p.cop("scalar", lambda e, i=i: e.activation(
                out=cols[:, CT1 + i:CT1 + i + 1], in_=cols[:, CV + i:CV + i + 1],
                func=AF.Ln))
            sb_ = p.cop("vector", lambda e, i=i: e.tensor_tensor(
                cols[:, CT1 + i:CT1 + i + 1], cols[:, CT1 + i:CT1 + i + 1],
                cols[:, CT0 + i:CT0 + i + 1], ALU.subtract), waits=[(ach, lnv2)])
            rt = p.cop("scalar", lambda e, i=i: e.activation(
                out=cols[:, CT1 + i:CT1 + i + 1], in_=cols[:, CT1 + i:CT1 + i + 1],
                func=AF.Exp, scale=0.5), waits=[(vch, sb_)])
            z = p.cop("vector", lambda e, i=i: e.tensor_tensor(
                cols[:, CT1 + i:CT1 + i + 1], cols[:, CT1 + i:CT1 + i + 1],
                sc32[:, 0:1], ALU.mult), waits=[(ach, rt), (vch, sc32_rdy)])
            z = p.cop("vector", lambda e, i=i: e.tensor_scalar_max(
                cols[:, CT1 + i:CT1 + i + 1], cols[:, CT1 + i:CT1 + i + 1], 1e-7))
            z2 = p.cop("vector", lambda e, i=i: e.tensor_tensor(
                cols[:, CSS + i:CSS + i + 1], cols[:, CT1 + i:CT1 + i + 1],
                cols[:, CT1 + i:CT1 + i + 1], ALU.mult))
            l1 = p.cop("scalar", lambda e, i=i: e.activation(
                out=cols[:, CSS + i:CSS + i + 1], in_=cols[:, CSS + i:CSS + i + 1],
                func=AF.Ln, bias=p.cst[:, 1:2]), waits=[(vch, z2)])
            s1 = p.cop("scalar", lambda e, i=i: e.activation(
                out=cols[:, CSS + i:CSS + i + 1], in_=cols[:, CSS + i:CSS + i + 1],
                func=AF.Exp, scale=0.5))
            w_ = p.cop("vector", lambda e, i=i: e.tensor_tensor(
                cols[:, CSS + i:CSS + i + 1], cols[:, CSS + i:CSS + i + 1],
                cols[:, CT1 + i:CT1 + i + 1], ALU.add), waits=[(ach, s1)])
            asinh = p.cop("scalar", lambda e, i=i: e.activation(
                out=cols[:, CSS + i:CSS + i + 1], in_=cols[:, CSS + i:CSS + i + 1],
                func=AF.Ln), waits=[(vch, w_)])
            p.cop("vector", lambda e, i=i: e.reciprocal(
                cols[:, CT1 + i:CT1 + i + 1], cols[:, CT1 + i:CT1 + i + 1]),
                waits=[(ach, asinh)])
            cf = p.cop("vector", lambda e, i=i: e.tensor_tensor(
                cols[:, CSS + i:CSS + i + 1], cols[:, CSS + i:CSS + i + 1],
                cols[:, CT1 + i:CT1 + i + 1], ALU.mult))
            ht_c = p.cop("vector", lambda e, i=i, bx=bx: e.tensor_scalar_mul(
                htt[:, bx:bx + D], xnf[:], cols[:, CSS + i:CSS + i + 1]),
                waits=[(vch, cf)] + ([(dgp, st_c[i - 2])] if i >= 2 else []))
            st_c[i] = p.dma("gpsimd", lambda e, i=i, bx=bx: e.dma_start(
                out=htan_dram[i * 128:(i + 1) * 128, :], in_=htt[:, bx:bx + D]),
                waits=[(vch, ht_c)])
    p.barrier()
    st.close()


def phase_l0_attn(p, t, mk_alloc):
    from contextlib import ExitStack
    TB = 2048
    NTB = T // TB
    NQ = TB // 512
    vch, ach, pch = p.chain["vector"], p.chain["scalar"], p.chain["gpsimd"]
    dsy, dgp, dsc = p.dsem["sync"], p.dsem["gpsimd"], p.dsem["scalar"]
    mms = p.mmsem

    stA = ExitStack()
    alloc = mk_alloc(stA)
    xnT = alloc("p4_xnT", [128, HC * T], BF16)
    wdec = alloc("p4_wdec", [128, HC * F], BF16)
    wspec = alloc("p4_wspec", [128, HC * 2 * F], BF16)
    bdec = alloc("p4_bdec", [128, NFC], F32)
    om = alloc("p4_om", [128, NFC], F32)
    trow = alloc("p4_trow", [128, TB], F32)
    dt_ = alloc("p4_d", [128, TB], F32)
    ur = alloc("p4_ur", [128, TB], F32)
    ui = alloc("p4_ui", [128, TB], F32)
    cc = alloc("p4_c", [128, TB], F32)
    ss = alloc("p4_s", [128, TB], F32)
    t1 = alloc("p4_t1", [128, TB], F32)
    t2 = alloc("p4_t2", [128, TB], F32)
    gi_ = alloc("p4_gi", [128, TB], F32)
    carry = alloc("p4_carry", [128, 2 * NFC], F32)
    ybf = alloc("p4_ybf", [128, 2 * TB], BF16)
    ps = alloc("p4_ps", [128, 2 * 2048], F32, psum=True)

    w1 = p.dma("gpsimd", lambda e: e.dma_start(
        out=wdec[:].rearrange("p (c f) -> p c f", c=HC),
        in_=t["W_decay"].rearrange("(c p) f -> p c f", p=128)))
    w2 = p.dma("gpsimd", lambda e: e.dma_start(
        out=wspec[:].rearrange("p (c f) -> p c f", c=HC),
        in_=t["W_spec"].rearrange("(c p) f -> p c f", p=128)))
    w3 = p.dma("sync", lambda e: e.dma_start(
        out=bdec[:], in_=t["b_decay"].rearrange("(c p) -> p c", p=128),
        allow_slow_non_contiguous=True))
    w4 = p.dma("sync", lambda e: e.dma_start(
        out=om[:], in_=t["freqs"].rearrange("(c p) -> p c", p=128),
        allow_slow_non_contiguous=True))
    xt_c = 0
    for c in range(HC):
        xt_c = p.dma("sync", lambda e, c=c: e.dma_start(
            out=xnT[:, c * T:(c + 1) * T], in_=t["xn_tm"][:, c * 128:(c + 1) * 128],
            transpose=True))
    om_c = p.cop("vector", lambda e: e.tensor_scalar_mul(om[:], om[:], 0.1),
                 waits=[(dsy, w4)])

    ev_c = {}
    mm_c = {}
    yst_c = [0] * (NFC * NTB)
    blk_v_done = 0
    g = 0
    for fc in range(NFC):
        for tb in range(NTB):
            k = fc * NTB + tb
            for kind in range(3):
                waits = [(dgp, w2), (dsy, xt_c)]
                if g - 2 in ev_c:
                    waits.append((ach, ev_c[g - 2]))

                def em(e, fc=fc, tb=tb, kind=kind, g=g):
                    pp = g % 2
                    t0 = tb * TB
                    out = []
                    for hc in range(HC):
                        if kind == 0:
                            lhsT = wdec[:, hc * F + fc * 128: hc * F + fc * 128 + 128]
                        elif kind == 1:
                            lhsT = wspec[:, hc * 2 * F + fc * 128: hc * 2 * F + fc * 128 + 128]
                        else:
                            lhsT = wspec[:, hc * 2 * F + (fc + NFC) * 128:
                                         hc * 2 * F + (fc + NFC) * 128 + 128]
                        for q in range(NQ):
                            out.append(e.matmul(
                                ps[:, pp * 2048 + q * 512: pp * 2048 + (q + 1) * 512],
                                lhsT,
                                xnT[:, hc * T + t0 + q * 512: hc * T + t0 + (q + 1) * 512],
                                start=(hc == 0), stop=(hc == HC - 1),
                                skip_group_check=True))
                    return out
                mm_c[g] = p.mmg(em, waits)
                pp = g % 2
                psrc = ps[:, pp * 2048:(pp + 1) * 2048]
                if kind == 0:
                    ev_c[g] = p.cop("scalar", lambda e, psrc=psrc, fc=fc: e.activation(
                        out=dt_[:], in_=psrc, func=AF.Sigmoid, bias=bdec[:, fc:fc + 1]),
                        waits=[(mms, mm_c[g]), (vch, blk_v_done), (dsy, w3)])
                else:
                    dst = ur if kind == 1 else ui
                    ev_c[g] = p.cop("scalar", lambda e, psrc=psrc, dst=dst: e.copy(
                        out=dst[:], in_=psrc), waits=[(mms, mm_c[g]), (vch, blk_v_done)])
                g += 1
            evs_done = ev_c[g - 1]

            it_c = p.cop("gpsimd", lambda e, tb=tb: e.iota(
                trow[:], pattern=[[1, TB]], base=tb * TB, channel_multiplier=0,
                allow_small_or_imprecise_dtypes=True), waits=[(vch, blk_v_done)])

            V = lambda f, w=(): p.cop("vector", f, waits=w)
            V(lambda e, fc=fc: e.tensor_scalar_mul(t1[:], trow[:], om[:, fc:fc + 1]),
              [(pch, it_c), (vch, om_c)])
            V(lambda e: e.tensor_scalar_mul(t2[:], t1[:], 1.0 / TWO_PI))
            V(lambda e: e.tensor_copy(gi_[:].bitcast(I32), t2[:]))
            V(lambda e: e.tensor_copy(t2[:], gi_[:].bitcast(I32)))
            pr = V(lambda e: e.scalar_tensor_tensor(t1[:], t2[:], -TWO_PI, t1[:], ALU.mult, ALU.add))
            sin_c = p.cop("scalar", lambda e: e.activation(out=ss[:], in_=t1[:], func=AF.Sin),
                          waits=[(vch, pr)])
            V(lambda e, fc=fc: e.tensor_scalar(t1[:], trow[:], om[:, fc:fc + 1], math.pi / 2,
                                               ALU.mult, ALU.add), [(ach, sin_c)])
            V(lambda e: e.tensor_scalar_mul(t2[:], t1[:], 1.0 / TWO_PI))
            V(lambda e: e.tensor_copy(gi_[:].bitcast(I32), t2[:]))
            V(lambda e: e.tensor_copy(t2[:], gi_[:].bitcast(I32)))
            pr2 = V(lambda e: e.scalar_tensor_tensor(t1[:], t2[:], -TWO_PI, t1[:], ALU.mult, ALU.add))
            cos_c = p.cop("scalar", lambda e: e.activation(out=cc[:], in_=t1[:], func=AF.Sin),
                          waits=[(vch, pr2)])
            V(lambda e: e.tensor_tensor(t1[:], cc[:], ur[:], ALU.mult), [(ach, cos_c), (ach, evs_done)])
            V(lambda e: e.tensor_tensor(t2[:], ss[:], ui[:], ALU.mult))
            V(lambda e: e.tensor_tensor(t1[:], t1[:], t2[:], ALU.add))
            V(lambda e: e.tensor_tensor(t2[:], cc[:], ui[:], ALU.mult))
            V(lambda e: e.tensor_tensor(ui[:], ss[:], ur[:], ALU.mult))
            V(lambda e: e.tensor_tensor(t2[:], t2[:], ui[:], ALU.subtract))
            if tb == 0:
                V(lambda e, fc=fc: e.memset(carry[:, fc:fc + 1], 0.0))
                V(lambda e, fc=fc: e.memset(carry[:, NFC + fc:NFC + fc + 1], 0.0))
            V(lambda e, fc=fc: e.tensor_tensor_scan(ur[:], dt_[:], t1[:], carry[:, fc:fc + 1],
                                                    ALU.mult, ALU.add))
            V(lambda e, fc=fc: e.tensor_tensor_scan(gi_[:], dt_[:], t2[:], carry[:, NFC + fc:NFC + fc + 1],
                                                    ALU.mult, ALU.add))
            V(lambda e, fc=fc: e.tensor_copy(carry[:, fc:fc + 1], ur[:, TB - 1:TB]))
            V(lambda e, fc=fc: e.tensor_copy(carry[:, NFC + fc:NFC + fc + 1], gi_[:, TB - 1:TB]))
            ystp = yst_c[k - 1] if k > 0 else 0
            V(lambda e: e.tensor_tensor(t1[:], cc[:], ur[:], ALU.mult), [(dsy, ystp)])
            V(lambda e: e.tensor_tensor(t2[:], ss[:], gi_[:], ALU.mult))
            V(lambda e: e.tensor_tensor(t1[:], t1[:], t2[:], ALU.subtract))
            hr_c = V(lambda e: e.tensor_copy(ybf[:, 0:TB], t1[:]))
            V(lambda e: e.tensor_tensor(t1[:], ss[:], ur[:], ALU.mult))
            V(lambda e: e.tensor_tensor(t2[:], cc[:], gi_[:], ALU.mult))
            V(lambda e: e.tensor_tensor(t1[:], t1[:], t2[:], ALU.add))
            hi_c = V(lambda e: e.tensor_copy(ybf[:, TB:2 * TB], t1[:]))
            blk_v_done = hi_c
            p.dma("sync", lambda e, fc=fc, tb=tb: e.dma_start(
                out=t["y_fm"][fc * 128:(fc + 1) * 128, tb * TB:(tb + 1) * TB],
                in_=ybf[:, 0:TB]), waits=[(vch, hr_c)])
            yst_c[k] = p.dma("sync", lambda e, fc=fc, tb=tb: e.dma_start(
                out=t["y_fm"][(NFC + fc) * 128:(NFC + fc + 1) * 128, tb * TB:(tb + 1) * TB],
                in_=ybf[:, TB:2 * TB]), waits=[(vch, hi_c)])

    p.barrier()
    stA.close()
    stB = ExitStack()
    alloc = mk_alloc(stB)
    wout = alloc("p4_wout", [128, HC * D], BF16)
    ygrp = alloc("p4_ygrp", [128, HC * 512], BF16)
    aot = alloc("p4_ao", [128, 2 * D], F32)
    po = alloc("p4_po", [128, 2 * D], F32, psum=True)
    wo_c = p.dma("gpsimd", lambda e: e.dma_start(
        out=wout[:].rearrange("p (c f) -> p c f", c=HC),
        in_=t["W_out"].rearrange("(c p) f -> p c f", p=128)))
    NG = 8
    ao_c = [0] * NT
    mo_c = [0] * NT
    ast_c = [0] * NT
    yld_c = [0] * NG
    for grp in range(NG):
        for c in range(HC):
            yld_c[grp] = p.dma("scalar", lambda e, grp=grp, c=c: e.dma_start(
                out=ygrp[:, c * 512:(c + 1) * 512],
                in_=t["y_fm"][c * 128:(c + 1) * 128, grp * 512:(grp + 1) * 512]),
                waits=[(p.mmsem, mo_c[grp * 4 - 1])] if grp > 0 else [])
        for tt in range(4):
            ti = grp * 4 + tt
            pp = ti % 2

            def em(e, grp=grp, tt=tt, pp=pp):
                out = []
                for c in range(HC):
                    for n in range(2):
                        out.append(e.matmul(
                            po[:, pp * D + n * 512: pp * D + (n + 1) * 512],
                            ygrp[:, c * 512 + tt * 128: c * 512 + tt * 128 + 128],
                            wout[:, c * D + n * 512: c * D + (n + 1) * 512],
                            start=(c == 0), stop=(c == HC - 1)))
                return out
            waits = [(p.dsem["scalar"], yld_c[grp]), (dgp, wo_c)]
            if ti >= 2:
                waits.append((vch, ao_c[ti - 2]))
            mo_c[ti] = p.mmg(em, waits)
            ao_c[ti] = p.cop("vector", lambda e, pp=pp: e.tensor_copy(
                aot[:, pp * D:(pp + 1) * D], po[:, pp * D:(pp + 1) * D]),
                waits=[(p.mmsem, mo_c[ti])] + ([(dsy, ast_c[ti - 2])] if ti >= 2 else []))
            ast_c[ti] = p.dma("sync", lambda e, ti=ti, pp=pp: e.dma_start(
                out=t["attn0"][ti * 128:(ti + 1) * 128, :], in_=aot[:, pp * D:(pp + 1) * D]),
                waits=[(vch, ao_c[ti])])
    p.barrier()
    stB.close()




def phase_routing(p, t, gl, mk_alloc):
    from contextlib import ExitStack
    vch, ach, pch = p.chain["vector"], p.chain["scalar"], p.chain["gpsimd"]
    dsy, dgp, dsc = p.dsem["sync"], p.dsem["gpsimd"], p.dsem["scalar"]
    mms = p.mmsem
    st = ExitStack()
    alloc = mk_alloc(st)

    gw = alloc("p2_gw", [128, HC * E], F32)
    ghat32 = alloc("p2_ghat32", [128, HC * E], F32)
    ghat = alloc("p2_ghat", [128, HC * E], BF16)
    onesc = alloc("p2_ones", [128, 1], F32)
    onesrow = alloc("p2_onesrow", [1, 128], F32)
    iotaB = alloc("p2_iotaB", [128, E], F32)
    xf = alloc("p2_xf", [128, 2 * T], BF16)
    xf32 = alloc("p2_xf32", [128, 2 * 2048], F32)
    lhatE = alloc("p2_lhatE", [32, T], F32)
    mT = alloc("p2_mT", [32, T], F32)
    cum = alloc("p2_cum", [32, T], F32)
    zerot = alloc("p2_zero", [32, T], F32)
    ltok = alloc("p2_ltok", [128, 4 * E], F32)
    maxv = alloc("p2_maxv", [128, 8], F32)
    mi = alloc("p2_mi", [128, 8], U32)
    member = alloc("p2_member", [128, E], F32)
    oh = alloc("p2_oh", [128, E], F32)
    counts_s = alloc("p2_counts", [32, 8], F32)
    countsT = alloc("p2_countsT", [8, 32], F32)
    sel_s = alloc("p2_sel", [8, 1], F32)
    srow = alloc("p2_srow", [1, 32], F32)
    offB = alloc("p2_offB", [128, E], F32)
    Sb = alloc("p2_Sb", [128, E], F32)
    off_s = alloc("p2_off", [32, 1], F32)
    Ssum = alloc("p2_Ssum", [32, 1], F32)
    ranktok = alloc("p2_ranktok", [128, E], F32)
    sc1 = alloc("p2_sc1", [128, E], F32)
    sc14 = alloc("p2_sc14", [128, 4 * E], F32)
    cols = alloc("p2_cols", [128, 20], F32)
    ps_l = alloc("p2_psl", [32, 2048], F32, psum=True)
    ps_t = alloc("p2_pst", [128, 128], F32, psum=True)
    ps_t2 = alloc("p2_pst2", [32, 128], F32, psum=True)
    ps_c = alloc("p2_psc", [32, 8], F32, psum=True)
    ps_b = alloc("p2_psb", [128, E], F32, psum=True)
    mif_all = gl["mif_all"]
    ltok32_all = gl["ltok32_all"]
    idx_all = gl["idx_all"]
    gates_all = gl["gates_all"]
    rstd0 = gl["rstd0"]
    ident = gl["ident"]
    misc_d = t["misc_d"]

    c1 = p.cop("vector", lambda e: e.memset(onesc[:], 1.0))
    c3 = p.cop("vector", lambda e: e.memset(onesrow[:], 1.0))
    c4 = p.cop("vector", lambda e: e.memset(zerot[:], 0.0))
    c5 = p.cop("gpsimd", lambda e: e.iota(iotaB[:], pattern=[[1, E]], base=0,
                                          channel_multiplier=0,
                                          allow_small_or_imprecise_dtypes=True))
    gwl = p.dma("sync", lambda e: e.dma_start(
        out=gw[:].rearrange("p (c e) -> p c e", c=HC),
        in_=t["gate_wT"].rearrange("(c p) e -> p c e", p=128)))
    sell = p.dma("sync", lambda e: e.dma_start(out=sel_s[:], in_=t["sel"]))

    def em_ss(e):
        out = []
        for hc in range(HC):
            out.append(e.matmul(ps_t[0:32, 0:1], gw[:, hc * E:(hc + 1) * E], onesc[:],
                                start=(hc == 0), stop=(hc == HC - 1)))
        return out
    m1 = p.mmg(em_ss, [(dsy, gwl), (vch, c1)])
    sS = p.cop("scalar", lambda e: e.copy(out=Ssum[:], in_=ps_t[0:32, 0:1]),
               waits=[(mms, m1)])
    b1 = p.dma("gpsimd", lambda e: e.dma_start(out=misc_d[0:32], in_=Ssum[:, 0]),
               waits=[(ach, sS)])
    b2 = p.dma("gpsimd", lambda e: e.dma_start(
        out=srow[:], in_=misc_d[0:32].rearrange("(a e) -> a e", a=1)),
        waits=[(dgp, b1)])
    m2 = p.mmg(lambda e: e.matmul(ps_b[:], onesrow[:], srow[:], start=True, stop=True),
               [(dgp, b2), (vch, c3)])
    sb_c = p.cop("scalar", lambda e: e.copy(out=Sb[:], in_=ps_b[:]), waits=[(mms, m2)])
    gh_c = 0
    for hc in range(HC):
        p.cop("vector", lambda e, hc=hc: e.scalar_tensor_tensor(
            ghat32[:, hc * E:(hc + 1) * E], Sb[:], -1.0 / D, gw[:, hc * E:(hc + 1) * E],
            ALU.mult, ALU.add), waits=[(ach, sb_c)])
        gh_c = p.cop("vector", lambda e, hc=hc: e.tensor_copy(
            ghat[:, hc * E:(hc + 1) * E], ghat32[:, hc * E:(hc + 1) * E]))

    mm_prev = [0, 0]
    ev_last = [0]
    ev_half = [0, 0]
    lhat_free = 0
    evac_done = 0
    cnt_mm = 0

    def ranking_pass(pss, src_fm, fp32):
        nonlocal mm_prev, ev_half, lhat_free
        for half in range(2):
            t0 = half * 2048
            for hc in range(HC):
                pp = hc % 2
                if fp32:
                    ldx = p.dma("sync", lambda e, hc=hc, t0=t0, pp=pp: e.dma_start(
                        out=xf32[:, pp * 2048:(pp + 1) * 2048],
                        in_=src_fm[hc * 128:(hc + 1) * 128, t0:t0 + 2048]),
                        waits=[(mms, mm_prev[pp])])
                    ds = dsy
                else:
                    ldx = p.dma("gpsimd", lambda e, hc=hc, t0=t0, pp=pp: e.dma_start(
                        out=xf[:, pp * T + t0: pp * T + t0 + 2048],
                        in_=src_fm[hc * 128:(hc + 1) * 128, t0:t0 + 2048]),
                        waits=[(mms, mm_prev[pp])])
                    ds = dgp

                def em(e, hc=hc, t0=t0, pp=pp, fp32=fp32):
                    out = []
                    for q in range(4):
                        if fp32:
                            rhs = xf32[:, pp * 2048 + q * 512:pp * 2048 + (q + 1) * 512]
                            w = ghat32
                        else:
                            rhs = xf[:, pp * T + t0 + q * 512: pp * T + t0 + (q + 1) * 512]
                            w = ghat
                        out.append(e.matmul(
                            ps_l[0:32, q * 512:(q + 1) * 512],
                            w[:, hc * E:(hc + 1) * E], rhs,
                            start=(hc == 0), stop=(hc == HC - 1)))
                    return out
                mm_prev[pp] = p.mmg(em, [(ds, ldx), (vch, gh_c), (ach, ev_last[0]),
                                         (mms, lhat_free)])
            ev_half[half] = p.cop("scalar", lambda e, t0=t0: e.copy(
                out=lhatE[:, t0:t0 + 2048], in_=ps_l[0:32, :]),
                waits=[(mms, mm_prev[(HC - 1) % 2])])
            ev_last[0] = ev_half[half]

    def tile_transpose(ti):
        nonlocal lhat_free
        lhat_free = p.mmg(lambda e, ti=ti: e.transpose(
            ps_t[:, 0:32], lhatE[0:32, ti * 128:(ti + 1) * 128], ident[0:32, 0:32]),
            [(ach, ev_half[ti // (NT // 2)]), (vch, evac_done)])
        return lhat_free

    def tile_transpose4(t4):
        nonlocal lhat_free
        def em(e, t4=t4):
            out = []
            for j in range(4):
                ti = t4 * 4 + j
                out.append(e.transpose(ps_t[:, j * 32:(j + 1) * 32],
                                       lhatE[0:32, ti * 128:(ti + 1) * 128],
                                       ident[0:32, 0:32]))
            return out
        lhat_free = p.mmg(em, [(ach, ev_half[(t4 * 4) // (NT // 2)]),
                               (ach, ev_half[(t4 * 4 + 3) // (NT // 2)]),
                               (vch, evac_done)])
        return lhat_free

    for b in range(8):
        ranking_pass(b, t["x_fm"][b], False)
        for t4 in range(NT // 4):
            trc = tile_transpose4(t4)
            lt = p.cop("vector", lambda e: e.tensor_copy(ltok[:], ps_t[:]),
                       waits=[(mms, trc)])
            evac_done = lt
            for j in range(4):
                p.cop("vector", lambda e, j=j: e.max(out=maxv[:], in_=ltok[:, j * E:(j + 1) * E]))
                ohc = p.cop("vector", lambda e, j=j: e.tensor_scalar(
                    oh[:], ltok[:, j * E:(j + 1) * E], maxv[:, 3:4], None, ALU.is_ge),
                    waits=[(mms, cnt_mm)])
                cnt_mm = p.mmg(lambda e, b=b, t4=t4, j=j: e.matmul(
                    ps_c[:, b:b + 1], oh[:], onesc[:],
                    start=(t4 == 0 and j == 0),
                    stop=(t4 == NT // 4 - 1 and j == 3),
                    skip_group_check=True),
                    [(vch, ohc)])

    prev_trm = 0
    ranking_pass(8, t["x_fm_own"], False)
    for ti in range(NT):
        trc = tile_transpose(ti)
        lt = p.cop("vector", lambda e: e.tensor_copy(ltok[:, 0:E], ps_t[:, 0:32]),
                   waits=[(mms, trc)])
        p.cop("vector", lambda e: e.max(out=maxv[:], in_=ltok[:, 0:E]))
        p.cop("vector", lambda e: e.max_index(out=mi[:], in_max=maxv[:], in_values=ltok[:, 0:E]))
        mifc = p.cop("vector", lambda e, ti=ti: e.tensor_copy(
            mif_all[:, ti * 4:ti * 4 + 4], mi[:, 0:4]))
        evac_done = mifc
        for kk in range(4):
            p.cop("vector", lambda e, kk=kk, ti=ti: e.tensor_scalar(
                sc14[:, kk * E:(kk + 1) * E], iotaB[:],
                mif_all[:, ti * 4 + kk:ti * 4 + kk + 1], None, ALU.is_equal),
                waits=[(mms, prev_trm), (pch, c5)])
        mp = p.cop("vector", lambda e: e.tensor_reduce(
            member[:], sc14[:].rearrange("p (k e) -> p e k", k=4), axis=AX.X,
            op=ALU.add))
        trm = p.mmg(lambda e: e.transpose(ps_t2[:], member[:], ident[:]),
                    [(vch, mp), (vch, evac_done)])
        prev_trm = trm
        evac_done = p.cop("vector", lambda e, ti=ti: e.tensor_copy(
            mT[0:32, ti * 128:(ti + 1) * 128], ps_t2[0:32, :]), waits=[(mms, trm)])
    ranking_pass(9, t["x_fm_own"], True)
    for t4 in range(NT // 4):
        trc = tile_transpose4(t4)
        evac_done = p.cop("vector", lambda e, t4=t4: e.tensor_copy(
            ltok32_all[:, t4 * 4 * E:(t4 + 1) * 4 * E], ps_t[:]), waits=[(mms, trc)])

    cev = p.cop("scalar", lambda e: e.copy(out=counts_s[:], in_=ps_c[:]),
                waits=[(mms, cnt_mm)])
    d1 = p.dma("gpsimd", lambda e: e.dma_start(out=misc_d[64:320].rearrange("(a b) -> a b", a=32), in_=counts_s[:]), waits=[(ach, cev)])
    d2 = p.dma("gpsimd", lambda e: e.dma_start(
        out=countsT[:], in_=misc_d[64:320].rearrange("(e c) -> c e", c=8),
        allow_slow_non_contiguous=True), waits=[(dgp, d1)])
    m3 = p.mmg(lambda e: e.matmul(ps_t2[0:32, 0:1], countsT[:], sel_s[:],
                                  start=True, stop=True),
               [(dgp, d2), (dsy, sell), (vch, evac_done)])
    offc = p.cop("scalar", lambda e: e.copy(out=off_s[:], in_=ps_t2[0:32, 0:1]),
                 waits=[(mms, m3)])
    d3 = p.dma("gpsimd", lambda e: e.dma_start(out=misc_d[32:64], in_=off_s[:, 0]),
               waits=[(ach, offc)])
    d4 = p.dma("gpsimd", lambda e: e.dma_start(
        out=srow[:], in_=misc_d[32:64].rearrange("(a e) -> a e", a=1)),
        waits=[(dgp, d3)])
    m4 = p.mmg(lambda e: e.matmul(ps_b[:], onesrow[:], srow[:], start=True, stop=True),
               [(dgp, d4)])
    offbc = p.cop("scalar", lambda e: e.copy(out=offB[:], in_=ps_b[:]), waits=[(mms, m4)])

    sc_c = p.cop("vector", lambda e: e.tensor_tensor_scan(
        cum[0:32, :], mT[0:32, :], zerot[0:32, :], 0.0, ALU.add, ALU.add),
        waits=[(vch, c4)])
    rg_c = p.cop("vector", lambda e: e.tensor_scalar(
        cum[0:32, :], cum[0:32, :], off_s[0:32, 0:1], -1.0, ALU.add, ALU.add),
        waits=[(ach, offc)])

    rt_ev = 0
    for ti in range(NT):
        trr = p.mmg(lambda e, ti=ti: e.transpose(
            ps_t[:, 0:32], cum[0:32, ti * 128:(ti + 1) * 128], ident[0:32, 0:32]),
            [(vch, rg_c), (vch, rt_ev)])
        rt_ev = p.cop("vector", lambda e: e.tensor_copy(ranktok[:], ps_t[:, 0:32]),
                      waits=[(mms, trr)])
        for kk in range(4):
            ohc = p.cop("vector", lambda e, ti=ti, kk=kk: e.tensor_scalar(
                oh[:], iotaB[:], mif_all[:, ti * 4 + kk:ti * 4 + kk + 1], None,
                ALU.is_equal), waits=[(vch, rt_ev), (ach, offbc)])
            p.cop("vector", lambda e: e.tensor_tensor(sc1[:], oh[:], ranktok[:], ALU.mult))
            p.cop("vector", lambda e, kk=kk: e.reduce_sum(
                cols[:, 4 + kk:5 + kk], sc1[:], axis=AX.X))
            p.cop("vector", lambda e: e.tensor_tensor(sc1[:], oh[:], offB[:], ALU.mult))
            p.cop("vector", lambda e, kk=kk: e.reduce_sum(
                cols[:, 8 + kk:9 + kk], sc1[:], axis=AX.X))
            p.cop("vector", lambda e, ti=ti: e.tensor_tensor(
                sc1[:], oh[:], ltok32_all[:, ti * E:(ti + 1) * E], ALU.mult))
            p.cop("vector", lambda e, kk=kk: e.reduce_sum(
                cols[:, 16 + kk:17 + kk], sc1[:], axis=AX.X))
        for kk in range(4):
            vd = p.cop("vector", lambda e, kk=kk: e.tensor_single_scalar(
                cols[:, 12 + kk:13 + kk], cols[:, 4 + kk:5 + kk], float(CAP), ALU.is_lt))
            sl = p.cop("vector", lambda e, kk=kk: e.tensor_tensor(
                cols[:, 8 + kk:9 + kk], cols[:, 4 + kk:5 + kk], cols[:, 8 + kk:9 + kk],
                ALU.subtract))
            ixf = p.cop("vector", lambda e, ti=ti, kk=kk: e.scalar_tensor_tensor(
                cols[:, 4 + kk:5 + kk], mif_all[:, ti * 4 + kk:ti * 4 + kk + 1],
                float(CAP), cols[:, 8 + kk:9 + kk], ALU.mult, ALU.add))
            nv = p.cop("vector", lambda e, kk=kk: e.tensor_scalar(
                cols[:, 0 + kk:1 + kk], cols[:, 12 + kk:13 + kk], -1.0, 1.0,
                ALU.mult, ALU.add))
            p.cop("vector", lambda e, kk=kk: e.scalar_tensor_tensor(
                cols[:, 4 + kk:5 + kk], cols[:, 0 + kk:1 + kk], BIGIDX,
                cols[:, 4 + kk:5 + kk], ALU.mult, ALU.add))
            p.cop("vector", lambda e, kk=kk: e.tensor_scalar(
                cols[:, 4 + kk:5 + kk], cols[:, 4 + kk:5 + kk], BIGIDX, 0.0,
                ALU.min, ALU.max))
        p.cop("vector", lambda e, ti=ti: e.tensor_copy(
            idx_all[:, ti * 4:ti * 4 + 4], cols[:, 4:8]))
        p.cop("vector", lambda e, ti=ti: e.tensor_scalar_mul(
            cols[:, 8:12], cols[:, 16:20], rstd0[:, ti:ti + 1]))
        sb_ = p.cop("vector", lambda e: e.tensor_scalar_sub(
            cols[:, 9:12], cols[:, 9:12], cols[:, 8:9]))
        ex = p.cop("scalar", lambda e: e.activation(
            out=cols[:, 9:12], in_=cols[:, 9:12], func=AF.Exp), waits=[(vch, sb_)])
        s1 = p.cop("vector", lambda e: e.memset(cols[:, 8:9], 1.0), waits=[(ach, ex)])
        sm = p.cop("vector", lambda e: e.reduce_sum(cols[:, 0:1], cols[:, 8:12], axis=AX.X))
        rq = p.cop("vector", lambda e: e.reciprocal(cols[:, 0:1], cols[:, 0:1]))
        p.cop("vector", lambda e: e.tensor_scalar_mul(
            cols[:, 8:12], cols[:, 8:12], cols[:, 0:1]))
        p.cop("vector", lambda e, ti=ti: e.tensor_tensor(
            gates_all[:, ti * 4:ti * 4 + 4], cols[:, 8:12], cols[:, 12:16], ALU.mult))
    kc1_ = p.cop("vector", lambda e: e.reduce_sum(sc1[0:32, 0:1], mT[0:32, :], axis=AX.X),
                 waits=[(vch, evac_done)])
    kc1_ = p.cop("vector", lambda e: e.tensor_scalar(
        sc1[0:32, 1:2], off_s[:], -1.0, float(CAP), ALU.mult, ALU.add),
        waits=[(ach, offc)])
    kc1_ = p.cop("vector", lambda e: e.tensor_tensor(
        sc1[0:32, 1:2], sc1[0:32, 1:2], sc1[0:32, 0:1], ALU.min))
    kc1_ = p.cop("vector", lambda e: e.tensor_scalar_max(sc1[0:32, 1:2], sc1[0:32, 1:2], 0.0))
    kb1 = p.dma("gpsimd", lambda e: e.dma_start(out=misc_d[384:416], in_=sc1[0:32, 1]),
                waits=[(vch, kc1_)])
    kb2 = p.dma("gpsimd", lambda e: e.dma_start(
        out=srow[:], in_=misc_d[384:416].rearrange("(a e) -> a e", a=1)),
        waits=[(dgp, kb1)])
    p.cop("vector", lambda e: e.tensor_copy(gl["kept"][:], srow[:]),
          waits=[(dgp, kb2)])

    p.barrier()
    st.close()


def phase_scatter(p, t, gl, mk_alloc):
    from contextlib import ExitStack
    vch = p.chain["vector"]
    dsy, dgp = p.dsem["sync"], p.dsem["gpsimd"]
    st = ExitStack()
    alloc = mk_alloc(st)
    xnt = alloc("p3_xn", [128, 2 * D], BF16)
    idx_all = gl["idx_all"]
    sc_c = [0] * NT
    hold = {}

    def mkreg(e):
        r = e.alloc_register("p3bnd")
        hold["r"] = r
        return e.reg_mov(r, E * CAP - 1)
    p._record("gpsimd", mkreg, [], None, 0)
    for ti in range(NT):
        bx = (ti % 2) * D
        w = [(dgp, sc_c[ti - 2])] if ti >= 2 else []
        ld = p.dma("sync", lambda e, ti=ti, bx=bx: e.dma_start(
            out=xnt[:, bx:bx + D], in_=t["xn_tm"][ti * 128:(ti + 1) * 128, :]), waits=w)
        for kk in range(4):
            sc_c[ti] = p.dma("gpsimd", lambda e, ti=ti, kk=kk, bx=bx: e.indirect_dma_start(
                out=t["buf"], out_offset=IndirectOffsetOnAxis(
                    ap=idx_all[:, ti * 4 + kk:ti * 4 + kk + 1], axis=0),
                in_=xnt[:, bx:bx + D], in_offset=None,
                bounds_check=hold["r"], oob_is_err=False),
                waits=[(dsy, ld)])
    fl = p.dma("gpsimd", lambda e: e.dma_start(out=xnt[:, 0:D], in_=t["buf"][0:128, :]),
               waits=[(dgp, sc_c[NT - 1])])
    p.wait_only("gpsimd", [(dgp, fl)])
    p.barrier()
    st.close()


def phase_shared(p, t, gl, mk_alloc):
    from contextlib import ExitStack
    vch, ach = p.chain["vector"], p.chain["scalar"]
    dsy, dgp = p.dsem["sync"], p.dsem["gpsimd"]
    mms = p.mmsem
    st = ExitStack()
    alloc = mk_alloc(st)
    xnT = alloc("p5_xnT", [128, HC * T], BF16)
    wgu = alloc("p5_wgu", [128, HC * D], BF16)
    sg = alloc("p5_sg", [128, 2 * 512], F32)
    as2 = alloc("p5_as", [128, 2 * 512], BF16)
    ps = alloc("p5_ps", [128, 4 * 512], F32, psum=True)
    ast_c = {}

    wl = p.dma("gpsimd", lambda e: e.dma_start(
        out=wgu[:].rearrange("p (c f) -> p c f", c=HC),
        in_=t["shared_w_gate_up"].rearrange("(c p) f -> p c f", p=128)))
    xt_c = 0
    for c in range(HC):
        xt_c = p.dma("sync", lambda e, c=c: e.dma_start(
            out=xnT[:, c * T:(c + 1) * T], in_=t["xn_tm"][:, c * 128:(c + 1) * 128],
            transpose=True))
    ev = {}
    mmc = {}
    g = 0
    for i in range(4):
        for q in range(8):
            for gu in range(2):
                pp = g % 4
                ic = i if gu == 0 else i + 4

                def em(e, ic=ic, q=q, pp=pp):
                    out = []
                    for hc in range(HC):
                        out.append(e.matmul(
                            ps[:, pp * 512:(pp + 1) * 512],
                            wgu[:, hc * D + ic * 128: hc * D + ic * 128 + 128],
                            xnT[:, hc * T + q * 512: hc * T + (q + 1) * 512],
                            start=(hc == 0), stop=(hc == HC - 1)))
                    return out
                w = [(dgp, wl), (dsy, xt_c)]
                if g - 4 in ev:
                    w.append((ev[g - 4][0], ev[g - 4][1]))
                mmc[g] = p.mmg(em, w)
                if gu == 0:
                    w2 = [(mms, mmc[g])]
                    if g - 3 in ev:
                        w2.append(ev[g - 3])
                    ev[g] = (ach, p.cop("scalar", lambda e, q=q, pp=pp: e.activation(
                        out=sg[:, (q % 2) * 512:(q % 2 + 1) * 512],
                        in_=ps[:, pp * 512:(pp + 1) * 512], func=AF.Silu),
                        waits=w2))
                else:
                    wv = [(mms, mmc[g]), ev[g - 1]]
                    if (i, q - 4) in ast_c:
                        wv.append((dsy, ast_c[(i, q - 4)]))
                    vc = p.cop("vector", lambda e, i=i, q=q, pp=pp: e.tensor_tensor(
                        as2[:, (q % 2) * 512:(q % 2 + 1) * 512],
                        sg[:, (q % 2) * 512:(q % 2 + 1) * 512],
                        ps[:, pp * 512:(pp + 1) * 512], ALU.mult),
                        waits=wv)
                    ev[g] = (vch, vc)
                    ast_c[(i, q)] = p.dma("sync", lambda e, i=i, q=q: e.dma_start(
                        out=t["act_sh"][i * 128:(i + 1) * 128, q * 512:(q + 1) * 512],
                        in_=as2[:, (q % 2) * 512:(q % 2 + 1) * 512]),
                        waits=[(vch, vc)])
                g += 1
    p.barrier()
    st.close()


def phase_experts(p, t, gl, mk_alloc):
    from contextlib import ExitStack
    vch, ach = p.chain["vector"], p.chain["scalar"]
    dsy, dgp, dsc = p.dsem["sync"], p.dsem["gpsimd"], p.dsem["scalar"]
    mms = p.mmsem
    st = ExitStack()
    alloc = mk_alloc(st)
    wguE = alloc("p6_wgu", [128, 2 * HC * D], BF16)
    wdnE = alloc("p6_wdn", [128, 2 * 4 * D], BF16)
    bufT = alloc("p6_bufT", [128, 2 * HC * CAP], BF16)
    sgE = alloc("p6_sg", [128, 4 * CAP], F32)
    actE = alloc("p6_act", [128, 4 * CAP], BF16)
    eot = alloc("p6_eot", [128, 2 * D], BF16)
    ps = alloc("p6_ps", [128, 4 * 512], F32, psum=True)
    psd = alloc("p6_psd", [128, 2 * D], F32, psum=True)

    wld = [0, 0]
    bld = [0, 0]
    gu_done = [0, 0]
    dn_done = [0, 0]
    eo_st = {}
    act_used = 0
    for ei in range(E):
        pp = ei % 2
        wld[pp] = p.dma("gpsimd", lambda e, ei=ei, pp=pp: e.dma_start(
            out=wguE[:, pp * HC * D:(pp + 1) * HC * D].rearrange("p (c f) -> p c f", c=HC),
            in_=t["w_gate_up"][ei].rearrange("(c p) f -> p c f", p=128)),
            waits=[(mms, gu_done[pp])])
        wld[pp] = p.dma("gpsimd", lambda e, ei=ei, pp=pp: e.dma_start(
            out=wdnE[:, pp * 4 * D:(pp + 1) * 4 * D].rearrange("p (c f) -> p c f", c=4),
            in_=t["w_down"][ei].rearrange("(c p) f -> p c f", p=128)),
            waits=[(mms, dn_done[pp])])
        for hc in range(HC):
            bld[pp] = p.dma("sync", lambda e, ei=ei, hc=hc, pp=pp: e.dma_start(
                out=bufT[:, pp * HC * CAP + hc * CAP: pp * HC * CAP + (hc + 1) * CAP],
                in_=t["buf"][ei * CAP:(ei + 1) * CAP, hc * 128:(hc + 1) * 128],
                transpose=True), waits=[(mms, gu_done[pp])])
        if ei == 0:
            ev_prev = {}
        else:
            ev_prev = {i - 8: v for i, v in ev_prev.items() if i >= 6}
        last_gu = 0
        for i in range(8):
            qq = i % 2

            def em(e, ei=ei, i=i, pp=pp, qq=qq):
                out = []
                for hc in range(HC):
                    for q in range(2):
                        out.append(e.matmul(
                            ps[:, qq * 1024 + q * 512: qq * 1024 + (q + 1) * 512],
                            wguE[:, pp * HC * D + hc * D + i * 128: pp * HC * D + hc * D + i * 128 + 128],
                            bufT[:, pp * HC * CAP + hc * CAP + q * 512: pp * HC * CAP + hc * CAP + (q + 1) * 512],
                            start=(hc == 0), stop=(hc == HC - 1),
                            skip_group_check=True))
                return out
            w = [(dgp, wld[pp]), (dsy, bld[pp])]
            if i - 2 in ev_prev:
                w.append(ev_prev[i - 2])
            last_gu = p.mmg(em, w)
            if i < 4:
                ev_prev[i] = (ach, p.cop("scalar", lambda e, i=i, qq=qq: e.activation(
                    out=sgE[:, i * CAP: (i + 1) * CAP],
                    in_=ps[:, qq * 1024:(qq + 1) * 1024], func=AF.Silu),
                    waits=[(mms, last_gu), (vch, act_used)]))
            else:
                ev_prev[i] = (vch, p.cop("vector", lambda e, i=i, qq=qq: e.tensor_tensor(
                    actE[:, (i - 4) * CAP: (i - 3) * CAP],
                    sgE[:, (i - 4) * CAP: (i - 3) * CAP],
                    ps[:, qq * 1024:(qq + 1) * 1024], ALU.mult),
                    waits=[(mms, last_gu), ev_prev[i - 4], (vch, act_used)]))
        gu_done[pp] = last_gu
        last_dn = 0
        for sti in range(8):
            qq = sti % 2

            def em(e, ei=ei, sti=sti, pp=pp, qq=qq):
                out = []
                for ic in range(4):
                    for n in range(2):
                        out.append(e.matmul(
                            psd[:, qq * D + n * 512: qq * D + (n + 1) * 512],
                            actE[:, ic * CAP + sti * 128: ic * CAP + sti * 128 + 128],
                            wdnE[:, pp * 4 * D + ic * D + n * 512: pp * 4 * D + ic * D + (n + 1) * 512],
                            start=(ic == 0), stop=(ic == 3),
                            skip_group_check=True))
                return out
            if ei == 0 and sti == 0:
                eo_ev = [0] * 8
            w = [ev_prev[7], (dgp, wld[pp])]
            w.append((vch, eo_ev[sti - 2]))
            last_dn = p.mmg(em, w)

            wv = [(mms, last_dn)]
            if sti >= 2:
                wv.append((dsy, eo_st.get(sti - 2, 0)))
            eo_ev[sti] = p.cop("vector", lambda e, sti=sti, qq=qq: e.tensor_copy(
                eot[:, qq * D:(qq + 1) * D], psd[:, qq * D:(qq + 1) * D]), waits=wv)
            eo_st[sti] = p.dma("sync", lambda e, ei=ei, sti=sti, qq=qq: e.dma_start(
                out=t["eo"][ei * CAP + sti * 128: ei * CAP + (sti + 1) * 128, :],
                in_=eot[:, qq * D:(qq + 1) * D]), waits=[(p.chain["vector"], eo_ev[sti])])
        dn_done[pp] = last_dn
        act_used = eo_ev[7]
    p.barrier()
    st.close()


def phase_epilogue(p, t, gl, mk_alloc):
    from contextlib import ExitStack
    vch, ach = p.chain["vector"], p.chain["scalar"]
    dsy, dgp = p.dsem["sync"], p.dsem["gpsimd"]
    mms = p.mmsem
    st = ExitStack()
    alloc = mk_alloc(st)
    wdn = alloc("p7_wdn", [128, 4 * D], BF16)
    gt = alloc("p7_gt", [128, 8 * D], BF16)
    xa = alloc("p7_xa", [128, 2 * D], F32)
    at = alloc("p7_at", [128, 2 * D], F32)
    racc = alloc("p7_racc", [128, D], F32)
    x1t = alloc("p7_x1", [128, 2 * D], F32)
    actg = alloc("p7_actg", [128, 4 * 512], BF16)
    psd = alloc("p7_psd", [128, 2 * D], F32, psum=True)
    idx_all = gl["idx_all"]
    gates_all = gl["gates_all"]

    wl = p.dma("gpsimd", lambda e: e.dma_start(
        out=wdn[:].rearrange("p (c f) -> p c f", c=4),
        in_=t["shared_w_down"].rearrange("(c p) f -> p c f", p=128)))
    gz = p.cop("vector", lambda e: e.memset(gt[:], 0.0))
    hold = {}

    def mkreg(e):
        r = e.alloc_register("p7bnd")
        hold["r"] = r
        return e.reg_mov(r, E * CAP - 1)
    p._record("gpsimd", mkreg, [], None, 0)
    x1_st = [0] * NT
    r_done = [0] * NT
    mm_c = [0] * NT
    x1_c = [0] * NT
    agl = 0
    for ti in range(NT):
        pp = ti % 2
        if ti % 4 == 0:
            grp = ti // 4
            wg = [(mms, mm_c[ti - 1])] if ti else []
            for ic in range(4):
                agl = p.dma("sync", lambda e, grp=grp, ic=ic: e.dma_start(
                    out=actg[:, ic * 512:(ic + 1) * 512],
                    in_=t["act_sh"][ic * 128:(ic + 1) * 128, grp * 512:(grp + 1) * 512]),
                    waits=wg)
        gld = 0
        gb = (ti % 2) * 4 * D
        for kk in range(4):
            gld = p.dma("gpsimd", lambda e, ti=ti, kk=kk, gb=gb: e.indirect_dma_start(
                out=gt[:, gb + kk * D: gb + (kk + 1) * D], out_offset=None,
                in_=t["eo"], in_offset=IndirectOffsetOnAxis(
                    ap=idx_all[:, ti * 4 + kk:ti * 4 + kk + 1], axis=0),
                bounds_check=hold["r"], oob_is_err=False),
                waits=[(vch, r_done[ti - 2] if ti >= 2 else gz), (dgp, wl)])
        wre = [(vch, x1_c[ti - 2])] if ti >= 2 else []
        xld = p.dma("sync", lambda e, ti=ti, pp=pp: e.dma_start(
            out=xa[:, pp * D:(pp + 1) * D], in_=t["x_tm"][ti * 128:(ti + 1) * 128, :]),
            waits=wre)
        ald = p.dma("sync", lambda e, ti=ti, pp=pp: e.dma_start(
            out=at[:, pp * D:(pp + 1) * D], in_=t["attn0"][ti * 128:(ti + 1) * 128, :]),
            waits=wre)

        def em(e, ti=ti, pp=pp):
            out = []
            tt = ti % 4
            for ic in range(4):
                for n in range(2):
                    out.append(e.matmul(
                        psd[:, pp * D + n * 512: pp * D + (n + 1) * 512],
                        actg[:, ic * 512 + tt * 128: ic * 512 + tt * 128 + 128],
                        wdn[:, ic * D + n * 512: ic * D + (n + 1) * 512],
                        start=(ic == 0), stop=(ic == 3)))
            return out
        w = [(dgp, wl), (dsy, agl)]
        if ti >= 2:
            w.append((vch, x1_c[ti - 2]))
        mm_c[ti] = p.mmg(em, w)
        p.cop("vector", lambda e, ti=ti, gb=gb: e.tensor_scalar_mul(
            racc[:], gt[:, gb:gb + D], gates_all[:, ti * 4:ti * 4 + 1]),
            waits=[(dgp, gld)])
        for kk in range(1, 4):
            p.cop("vector", lambda e, ti=ti, kk=kk, gb=gb: e.scalar_tensor_tensor(
                racc[:], gt[:, gb + kk * D: gb + (kk + 1) * D],
                gates_all[:, ti * 4 + kk:ti * 4 + kk + 1], racc[:],
                ALU.mult, ALU.add))
        r_done[ti] = p.chain["vector"].cnt
        p.cop("vector", lambda e, pp=pp: e.tensor_tensor(
            x1t[:, pp * D:(pp + 1) * D], xa[:, pp * D:(pp + 1) * D],
            at[:, pp * D:(pp + 1) * D], ALU.add), waits=[(dsy, ald)])
        p.cop("vector", lambda e, pp=pp: e.tensor_tensor(
            x1t[:, pp * D:(pp + 1) * D], x1t[:, pp * D:(pp + 1) * D], racc[:], ALU.add))
        x1_c[ti] = p.cop("vector", lambda e, pp=pp: e.tensor_tensor(
            x1t[:, pp * D:(pp + 1) * D], x1t[:, pp * D:(pp + 1) * D],
            psd[:, pp * D:(pp + 1) * D], ALU.add), waits=[(mms, mm_c[ti])])
        x1_st[ti] = p.dma("sync", lambda e, ti=ti, pp=pp: e.dma_start(
            out=t["x1_tm"][ti * 128:(ti + 1) * 128, :], in_=x1t[:, pp * D:(pp + 1) * D]),
            waits=[(vch, x1_c[ti])])
    p.barrier()
    st.close()


def build_program(phases=("p1", "p2", "p3", "p4", "p5", "p6", "p7", "p8", "p9", "p10"), dbg=()):
    from contextlib import ExitStack
    nc = bass.Bass("TRN2", target_bir_lowering=False)
    t = declare_io(nc)
    for d in dbg:
        if d.startswith("sb:"):
            continue
        t["dbg_" + d] = nc.dram_tensor("dbg_" + d, list(t[d].shape),
                                       t[d].dtype, kind="ExternalOutput").ap()
    stack = ExitStack()
    with stack:
        block = stack.enter_context(nc.Block())
        p = Prog(nc, block, stack)

        def mk_alloc(st):
            def alloc(name, shape, dtype=F32, psum=False):
                if psum:
                    return st.enter_context(nc.psum_tensor(name, shape, dtype))
                return st.enter_context(nc.sbuf_tensor(name, shape, dtype))
            return alloc

        galloc = mk_alloc(stack)
        gl = {}
        gl["rstd0"] = galloc("g_rstd0", [128, NT], F32)
        gl["m0"] = galloc("g_m0", [128, NT], F32)
        gl["rstd1"] = galloc("g_rstd1", [128, NT], F32)
        gl["m1"] = galloc("g_m1", [128, NT], F32)
        gl["ident"] = galloc("g_ident", [128, 128], F32)
        gl["mif_all"] = galloc("g_mif", [128, 4 * NT], F32)
        gl["ltok32_all"] = galloc("g_ltok32", [128, E * NT], F32)
        gl["idx_all"] = galloc("g_idx", [128, 4 * NT], I32)
        gl["gates_all"] = galloc("g_gates", [128, 4 * NT], F32)
        gl["sc32"] = galloc("g_sc32", [128, 2], F32)
        gl["kept"] = galloc("g_kept", [1, E], I32)
        p.cst = galloc("g_cst", [128, 4], F32)
        p.cop("vector", lambda e: e.memset(p.cst[:, 0:1], EPS_LN))
        p.cst_rdy = p.cop("vector", lambda e: e.memset(p.cst[:, 1:2], 1.0))

        with ExitStack() as st0:
            a0 = mk_alloc(st0)
            i1 = a0("g_i1", [128, 128], F32)
            c1 = p.cop("gpsimd", lambda e: e.iota(i1[:], pattern=[[1, 128]], base=0,
                                                  channel_multiplier=0,
                                                  allow_small_or_imprecise_dtypes=True))
            c2 = p.cop("gpsimd", lambda e: e.iota(gl["ident"][:], pattern=[[0, 128]], base=0,
                                                  channel_multiplier=1,
                                                  allow_small_or_imprecise_dtypes=True))
            p.cop("vector", lambda e: e.tensor_tensor(gl["ident"][:], gl["ident"][:],
                                                      i1[:], ALU.is_equal),
                  waits=[(p.chain["gpsimd"], c2)])
            cv = a0("g_cv", [1, 1], F32)
            cl = p.dma("sync", lambda e: e.dma_start(out=cv[:], in_=t["curv"].rearrange(
                "(a b) -> a b", a=1)))
            l1 = p.cop("scalar", lambda e: e.activation(out=cv[:], in_=cv[:], func=AF.Ln),
                       waits=[(p.dsem["sync"], cl)])
            l2 = p.cop("scalar", lambda e: e.activation(out=cv[:], in_=cv[:], func=AF.Exp,
                                                        scale=0.5))
            onc = a0("g_onc", [1, 128], F32)
            oc = p.cop("vector", lambda e: e.memset(onc[:], 1.0))
            psb = a0("g_psb", [128, 1], F32, psum=True)
            mmb = p.mmg(lambda e: e.matmul(psb[:], onc[:], cv[:], start=True, stop=True),
                        [(p.chain["scalar"], l2), (p.chain["vector"], oc)])
            gl["sc32_rdy"] = p.cop("vector", lambda e: e.tensor_scalar_mul(
                gl["sc32"][:, 0:1], psb[:], 32.0), waits=[(p.mmsem, mmb)])
            p.barrier()

        if "p1" in phases:
            phase_ln(p, t, t["x_tm"], t["xn_tm"], gl["rstd0"], gl["m0"], mk_alloc, tag="ln0")
        if "p2" in phases:
            phase_routing(p, t, gl, mk_alloc)
        if "p3" in phases:
            phase_scatter(p, t, gl, mk_alloc)
        if "p4" in phases:
            phase_l0_attn(p, t, mk_alloc)
        if "p5" in phases:
            phase_shared(p, t, gl, mk_alloc)
        if "p6" in phases:
            phase_experts(p, t, gl, mk_alloc)
        if "p7" in phases:
            phase_epilogue(p, t, gl, mk_alloc)
        if "p8" in phases:
            phase_ln(p, t, t["x1_tm"], t["xn1_tm"], gl["rstd1"], gl["m1"], mk_alloc,
                     tag="ln1", htan_dram=t["htan_tm"], sb=gl)
        if "p9" in phases:
            phase_l1_ffn(p, t, gl, mk_alloc)
        if "p10" in phases:
            phase_l1_attn(p, t, gl, mk_alloc)

        for d in dbg:
            if d.startswith("sb:"):
                name = d[3:]
                src_t = gl[name]
                dt_ = I32 if name == "idx_all" else F32
                t["dbg_" + name] = nc.dram_tensor(
                    "dbg_" + name, list(src_t[:].shape), dt_, kind="ExternalOutput").ap()
                p.dma("sync", lambda e, name=name, src_t=src_t: e.dma_start(
                    out=t["dbg_" + name], in_=src_t[:]))
            else:
                p.dma("sync", lambda e, d=d: e.dma_start(out=t["dbg_" + d], in_=t[d]))
        if dbg:
            p.wait_only("sync", [(p.dsem["sync"], p.dsem["sync"].cnt)])
        p.emit()
    return nc, t


def make_inputs(core, full):
    f32 = np.float32
    x = np.ascontiguousarray(full["L0_attn_x"], dtype=f32)
    m = {}
    m["x_tm"] = np.ascontiguousarray(x[core])
    m["x_fm"] = np.ascontiguousarray(np.transpose(x, (0, 2, 1)))
    m["x_fm_own"] = np.ascontiguousarray(x[core].T)
    selv = np.zeros((8, 1), f32)
    selv[:core] = 1.0
    m["sel"] = selv
    m["W_spec"] = full["L0_attn_W_spec"]
    m["W_decay"] = full["L0_attn_W_decay"]
    m["b_decay"] = full["L0_attn_b_decay"]
    m["freqs"] = full["L0_attn_freqs"]
    m["W_out"] = full["L0_attn_W_out"]
    m["gate_wT"] = np.ascontiguousarray(np.asarray(full["L0_ffn_gate_w"]).T)
    m["w_gate_up"] = full["L0_ffn_w_gate_up"]
    m["w_down"] = full["L0_ffn_w_down"]
    m["shared_w_gate_up"] = full["L0_ffn_shared_w_gate_up"]
    m["shared_w_down"] = full["L0_ffn_shared_w_down"]
    m["curv"] = np.ascontiguousarray(np.asarray(full["L1_attn_curv"])).reshape(1)
    m["wqT"] = np.ascontiguousarray(np.asarray(full["L1_attn_wq_w"]).T)
    m["wqb"] = full["L1_attn_wq_b"]
    m["wkT"] = np.ascontiguousarray(np.asarray(full["L1_attn_wk_w"]).T)
    m["wkb"] = full["L1_attn_wk_b"]
    m["wvT"] = np.ascontiguousarray(np.asarray(full["L1_attn_wv_w"]).T)
    m["wvb"] = full["L1_attn_wv_b"]
    m["woT"] = np.ascontiguousarray(np.asarray(full["L1_attn_wo_w"]).T)
    m["wob"] = full["L1_attn_wo_b"]
    m["WrT"] = np.ascontiguousarray(np.asarray(full["L1_ffn_W_r"]).T)
    m["WkT"] = np.ascontiguousarray(np.asarray(full["L1_ffn_W_k"]).T)
    m["WvT"] = np.ascontiguousarray(np.asarray(full["L1_ffn_W_v"]).T)
    m["mu_r"] = full["L1_ffn_mu_r"]
    m["mu_k"] = full["L1_ffn_mu_k"]
    return {k: np.ascontiguousarray(np.asarray(v), dtype=f32) for k, v in m.items()}


def kernel(**inputs):
    nc, t = build_program()
    in_maps = [make_inputs(c, inputs) for c in range(8)]
    res = run_bass_kernel_spmd(nc, in_maps, core_ids=list(range(8)))
    return np.concatenate([np.asarray(r["out"])[None] for r in res.results], axis=0)


def phase_l1_ffn(p, t, gl, mk_alloc):
    from contextlib import ExitStack
    vch, ach = p.chain["vector"], p.chain["scalar"]
    dsy, dgp = p.dsem["sync"], p.dsem["gpsimd"]
    mms = p.mmsem
    st = ExitStack()
    alloc = mk_alloc(st)
    TBK = 1024
    NBK = T // TBK
    NTT = TBK // 128
    xn1T = alloc("p9_xnT", [128, HC * T], BF16)
    wr = alloc("p9_wr", [128, HC * D], BF16)
    wk = alloc("p9_wk", [128, HC * D], BF16)
    wv = alloc("p9_wv", [128, HC * D], BF16)
    mus = alloc("p9_mu", [128, 2 * HC], F32)
    dsc_t = alloc("p9_d", [128, TBK], F32)
    xrT = alloc("p9_xr", [128, HC * TBK], BF16)
    xkT = alloc("p9_xk", [128, HC * TBK], BF16)
    kact = alloc("p9_kact", [128, HC * TBK], BF16)
    rblk = alloc("p9_rblk", [128, 2 * D], BF16)
    krelu = alloc("p9_krelu", [128, 512], F32)
    ot = alloc("p9_ot", [128, 2 * D], F32)
    ps = alloc("p9_ps", [128, 2 * D], F32, psum=True)
    psk = alloc("p9_psk", [128, 2 * 512], F32, psum=True)

    w1 = p.dma("gpsimd", lambda e: e.dma_start(
        out=wr[:].rearrange("p (c f) -> p c f", c=HC),
        in_=t["WrT"].rearrange("(c p) f -> p c f", p=128)))
    w2 = p.dma("gpsimd", lambda e: e.dma_start(
        out=wk[:].rearrange("p (c f) -> p c f", c=HC),
        in_=t["WkT"].rearrange("(c p) f -> p c f", p=128)))
    w3 = p.dma("gpsimd", lambda e: e.dma_start(
        out=wv[:].rearrange("p (c f) -> p c f", c=HC),
        in_=t["WvT"].rearrange("(c p) f -> p c f", p=128)))
    m1 = p.dma("sync", lambda e: e.dma_start(
        out=mus[:, 0:HC], in_=t["mu_r"].rearrange("(c p) -> p c", p=128),
        allow_slow_non_contiguous=True))
    m2 = p.dma("sync", lambda e: e.dma_start(
        out=mus[:, HC:2 * HC], in_=t["mu_k"].rearrange("(c p) -> p c", p=128),
        allow_slow_non_contiguous=True))
    xl = 0
    for c in range(HC):
        xl = p.dma("sync", lambda e, c=c: e.dma_start(
            out=xn1T[:, c * T:(c + 1) * T], in_=t["xn1_tm"][:, c * 128:(c + 1) * 128],
            transpose=True))

    ka_bank = [0, 0]
    r_ev = [0] * NT
    o_ev = [0] * NT
    o_st = [0] * NT
    last_mm_xr = 0
    for bk in range(NBK):
        t0 = bk * TBK
        lerp_c = 0
        for c in range(HC):
            base = c * T + t0
            w0 = [(dsy, xl), (dsy, m2), (mms, last_mm_xr)]
            if bk == 0:
                p.cop("vector", lambda e, base=base: e.tensor_tensor(
                    dsc_t[:, 1:TBK], xn1T[:, base + 1:base + TBK],
                    xn1T[:, base:base + TBK - 1], ALU.subtract), waits=w0)
                p.cop("vector", lambda e, base=base: e.tensor_copy(
                    dsc_t[:, 0:1], xn1T[:, base:base + 1]))
                p.cop("vector", lambda e, c=c, base=base: e.scalar_tensor_tensor(
                    xrT[:, c * TBK + 1:(c + 1) * TBK], dsc_t[:, 1:TBK], mus[:, c:c + 1],
                    xn1T[:, base:base + TBK - 1], ALU.mult, ALU.add))
                p.cop("vector", lambda e, c=c: e.tensor_scalar_mul(
                    xrT[:, c * TBK:c * TBK + 1], dsc_t[:, 0:1], mus[:, c:c + 1]))
                p.cop("vector", lambda e, c=c, base=base: e.scalar_tensor_tensor(
                    xkT[:, c * TBK + 1:(c + 1) * TBK], dsc_t[:, 1:TBK],
                    mus[:, HC + c:HC + c + 1],
                    xn1T[:, base:base + TBK - 1], ALU.mult, ALU.add))
                lerp_c = p.cop("vector", lambda e, c=c: e.tensor_scalar_mul(
                    xkT[:, c * TBK:c * TBK + 1], dsc_t[:, 0:1], mus[:, HC + c:HC + c + 1]))
            else:
                p.cop("vector", lambda e, base=base: e.tensor_tensor(
                    dsc_t[:], xn1T[:, base:base + TBK],
                    xn1T[:, base - 1:base + TBK - 1], ALU.subtract), waits=w0)
                p.cop("vector", lambda e, c=c, base=base: e.scalar_tensor_tensor(
                    xrT[:, c * TBK:(c + 1) * TBK], dsc_t[:], mus[:, c:c + 1],
                    xn1T[:, base - 1:base + TBK - 1], ALU.mult, ALU.add))
                lerp_c = p.cop("vector", lambda e, c=c, base=base: e.scalar_tensor_tensor(
                    xkT[:, c * TBK:(c + 1) * TBK], dsc_t[:], mus[:, HC + c:HC + c + 1],
                    xn1T[:, base - 1:base + TBK - 1], ALU.mult, ALU.add))
        kact_done = 0
        for dc in range(HC):
            for tq in range(2):
                g = dc * 2 + tq
                pp = g % 2

                def em(e, dc=dc, tq=tq, pp=pp):
                    out = []
                    for c in range(HC):
                        out.append(e.matmul(
                            psk[:, pp * 512:(pp + 1) * 512],
                            wk[:, c * D + dc * 128: c * D + dc * 128 + 128],
                            xkT[:, c * TBK + tq * 512: c * TBK + (tq + 1) * 512],
                            start=(c == 0), stop=(c == HC - 1)))
                    return out
                mmk = p.mmg(em, [(vch, lerp_c), (dgp, w2), (vch, ka_bank[pp])])
                p.cop("vector", lambda e, pp=pp: e.tensor_relu(
                    krelu[:], psk[:, pp * 512:(pp + 1) * 512]), waits=[(mms, mmk)])
                ka_bank[pp] = kact_done = p.cop(
                    "vector", lambda e, dc=dc, tq=tq: e.tensor_tensor(
                        kact[:, dc * TBK + tq * 512: dc * TBK + (tq + 1) * 512],
                        krelu[:], krelu[:], ALU.mult))
        for tt in range(NTT):
            ti = bk * NTT + tt
            pp = tt % 2

            def em_r(e, tt=tt, pp=pp):
                out = []
                for c in range(HC):
                    for n in range(2):
                        out.append(e.matmul(
                            ps[:, pp * D + n * 512: pp * D + (n + 1) * 512],
                            xrT[:, c * TBK + tt * 128: c * TBK + tt * 128 + 128],
                            wr[:, c * D + n * 512: c * D + (n + 1) * 512],
                            start=(c == 0), stop=(c == HC - 1)))
                return out
            w = [(vch, lerp_c), (dgp, w1)]
            if ti >= 2:
                w.append((vch, o_ev[ti - 2]))
            mmr_ = p.mmg(em_r, w)
            last_mm_xr = mmr_
            r_ev[ti] = p.cop("scalar", lambda e, tt=tt, pp=pp: e.activation(
                out=rblk[:, (tt % 2) * D:(tt % 2 + 1) * D],
                in_=ps[:, pp * D:(pp + 1) * D], func=AF.Sigmoid),
                waits=[(mms, mmr_)] + ([(vch, o_ev[ti - 2])] if ti >= 2 else []))

            def em_v(e, tt=tt, pp=pp):
                out = []
                for c in range(HC):
                    for n in range(2):
                        out.append(e.matmul(
                            ps[:, pp * D + n * 512: pp * D + (n + 1) * 512],
                            kact[:, c * TBK + tt * 128: c * TBK + tt * 128 + 128],
                            wv[:, c * D + n * 512: c * D + (n + 1) * 512],
                            start=(c == 0), stop=(c == HC - 1)))
                return out
            mmv = p.mmg(em_v, [(vch, kact_done), (ach, r_ev[ti]), (dgp, w3)])
            last_mm_xr = mmv
            o_ev[ti] = p.cop("vector", lambda e, tt=tt, pp=pp: e.tensor_tensor(
                ot[:, pp * D:(pp + 1) * D], rblk[:, (tt % 2) * D:(tt % 2 + 1) * D],
                ps[:, pp * D:(pp + 1) * D], ALU.mult),
                waits=[(mms, mmv)] + ([(dsy, o_st[ti - 2])] if ti >= 2 else []))
            o_st[ti] = p.dma("sync", lambda e, ti=ti, pp=pp: e.dma_start(
                out=t["ffn1"][ti * 128:(ti + 1) * 128, :], in_=ot[:, pp * D:(pp + 1) * D]),
                waits=[(vch, o_ev[ti])])
    p.barrier()
    st.close()


def phase_l1_attn(p, t, gl, mk_alloc):
    from contextlib import ExitStack
    vch, ach = p.chain["vector"], p.chain["scalar"]
    dsy, dgp = p.dsem["sync"], p.dsem["gpsimd"]
    mms = p.mmsem
    stA = ExitStack()
    alloc = mk_alloc(stA)
    htT = alloc("pa_htT", [128, HC * T], BF16)
    wq = alloc("pa_wq", [128, HC * D], BF16)
    wkk = alloc("pa_wk", [128, HC * D], BF16)
    wvv = alloc("pa_wv", [128, HC * D], BF16)
    biasB = alloc("pa_biasB", [128, 3 * D], F32)
    brow = alloc("pa_brow", [1, D], F32)
    onesrow = alloc("pa_ones", [1, 128], F32)
    qtt = alloc("pa_qtt", [128, D], F32)
    sqt = alloc("pa_sqt", [128, D], F32)
    n2 = alloc("pa_n2", [128, 2 * NH], F32)
    cfc = alloc("pa_cfc", [128, 8 * NH], F32)
    qhb = alloc("pa_qhb", [128, 3 * D], BF16)
    qtk = alloc("pa_qtk", [128, 128], BF16)
    sqc = alloc("pa_sqc", [128, 2], F32)
    ps = alloc("pa_ps", [128, 3 * D], F32, psum=True)
    psb = alloc("pa_psb", [128, 512], F32, psum=True)

    wl1 = p.dma("gpsimd", lambda e: e.dma_start(
        out=wq[:].rearrange("p (c f) -> p c f", c=HC),
        in_=t["wqT"].rearrange("(c p) f -> p c f", p=128)))
    wl2 = p.dma("gpsimd", lambda e: e.dma_start(
        out=wkk[:].rearrange("p (c f) -> p c f", c=HC),
        in_=t["wkT"].rearrange("(c p) f -> p c f", p=128)))
    wl3 = p.dma("gpsimd", lambda e: e.dma_start(
        out=wvv[:].rearrange("p (c f) -> p c f", c=HC),
        in_=t["wvT"].rearrange("(c p) f -> p c f", p=128)))
    hl = 0
    for c in range(HC):
        hl = p.dma("sync", lambda e, c=c: e.dma_start(
            out=htT[:, c * T:(c + 1) * T], in_=t["htan_tm"][:, c * 128:(c + 1) * 128],
            transpose=True))
    oc = p.cop("vector", lambda e: e.memset(onesrow[:], 1.0))
    bb_c = 0
    mb = 0
    for wi, wname in enumerate(("wqb", "wkb", "wvb")):
        bl = p.dma("sync", lambda e, wname=wname: e.dma_start(
            out=brow[:], in_=t[wname].rearrange("(a f) -> a f", a=1)),
            waits=[(mms, mb)])
        for n in range(2):
            mb = p.mmg(lambda e, n=n: e.matmul(
                psb[:], onesrow[:], brow[:, n * 512:(n + 1) * 512], start=True, stop=True),
                [(dsy, bl), (vch, oc), (ach, bb_c)])
            bb_c = p.cop("scalar", lambda e, wi=wi, n=n: e.copy(
                out=biasB[:, wi * D + n * 512: wi * D + (n + 1) * 512], in_=psb[:]),
                waits=[(mms, mb)])
    sq_c = p.cop("vector", lambda e: e.tensor_scalar_mul(
        sqc[:, 0:1], gl["sc32"][:, 0:1], 1.0 / 32.0), waits=[(vch, gl["sc32_rdy"])])
    p.cop("vector", lambda e: e.tensor_tensor(
        sqc[:, 1:2], sqc[:, 0:1], sqc[:, 0:1], ALU.mult))
    iv_c = p.cop("vector", lambda e: e.reciprocal(sqc[:, 1:2], sqc[:, 1:2]))

    def hyp_map(kind, psrc):
        bi = 0 if kind == "q" else D
        add = p.cop("vector", lambda e: e.tensor_tensor(
            qtt[:], psrc, biasB[:, bi:bi + D], ALU.add))
        sq = p.cop("scalar", lambda e: e.activation(out=sqt[:], in_=qtt[:], func=AF.Square),
                   waits=[(vch, add)])
        rd = p.cop("vector", lambda e: e.tensor_reduce(
            n2[:, 0:NH], sqt[:].rearrange("p (h d) -> p h d", h=NH), axis=AX.X,
            op=ALU.add), waits=[(ach, sq)])
        l1 = p.cop("scalar", lambda e: e.activation(out=n2[:, NH:2 * NH], in_=n2[:, 0:NH],
                                                    func=AF.Ln), waits=[(vch, rd)])
        e1 = p.cop("scalar", lambda e: e.activation(out=n2[:, NH:2 * NH],
                                                    in_=n2[:, NH:2 * NH],
                                                    func=AF.Exp, scale=0.5))
        z = p.cop("vector", lambda e: e.tensor_tensor(
            n2[:, NH:2 * NH], n2[:, NH:2 * NH], sqc[:, 0:1].to_broadcast([128, NH]),
            ALU.mult), waits=[(ach, e1), (vch, iv_c)])
        z = p.cop("vector", lambda e: e.tensor_scalar_max(
            n2[:, NH:2 * NH], n2[:, NH:2 * NH], 1e-7))
        ep = p.cop("scalar", lambda e: e.activation(out=cfc[:, 0:NH], in_=n2[:, NH:2 * NH],
                                                    func=AF.Exp), waits=[(vch, z)])
        em_ = p.cop("scalar", lambda e: e.activation(out=cfc[:, NH:2 * NH],
                                                     in_=n2[:, NH:2 * NH],
                                                     func=AF.Exp, scale=-1.0))
        p.cop("vector", lambda e: e.tensor_tensor(
            cfc[:, 0:NH], cfc[:, 0:NH], cfc[:, NH:2 * NH], ALU.subtract),
            waits=[(ach, em_)])
        p.cop("vector", lambda e: e.tensor_scalar_mul(cfc[:, 0:NH], cfc[:, 0:NH], 0.5))
        p.cop("vector", lambda e: e.reciprocal(cfc[:, NH:2 * NH], n2[:, NH:2 * NH]))
        p.cop("vector", lambda e: e.tensor_tensor(
            cfc[:, 2 * NH:3 * NH], cfc[:, 0:NH], cfc[:, NH:2 * NH], ALU.mult))
        import os as _os
        if _os.environ.get("KDBG_NOCOEF"):
            qh = p.cop("vector", lambda e, bi=bi: e.tensor_copy(
                qhb[:, bi:bi + D], qtt[:]))
        else:
            qh = p.cop("vector", lambda e, bi=bi: e.tensor_tensor(
                qhb[:, bi:bi + D].rearrange("p (h d) -> p h d", h=NH),
                qtt[:].rearrange("p (h d) -> p h d", h=NH),
                cfc[:, 2 * NH:3 * NH].rearrange("p (h o) -> p h o", o=1).to_broadcast(
                    [128, NH, HD]), ALU.mult))
        p.cop("vector", lambda e: e.tensor_tensor(
            cfc[:, 3 * NH:4 * NH], cfc[:, 0:NH], cfc[:, 0:NH], ALU.mult))
        t2 = p.cop("vector", lambda e: e.tensor_tensor(
            cfc[:, 3 * NH:4 * NH], cfc[:, 3 * NH:4 * NH],
            sqc[:, 1:2].to_broadcast([128, NH]), ALU.add))
        s1 = p.cop("scalar", lambda e: e.activation(out=cfc[:, 4 * NH:5 * NH],
                                                    in_=cfc[:, 3 * NH:4 * NH],
                                                    func=AF.Ln), waits=[(vch, t2)])
        s2 = p.cop("scalar", lambda e: e.activation(out=cfc[:, 4 * NH:5 * NH],
                                                    in_=cfc[:, 4 * NH:5 * NH],
                                                    func=AF.Exp, scale=0.5))
        qt_ = p.cop("vector", lambda e, kind=kind: e.tensor_copy(
            qtk[:, (0 if kind == "q" else NH):(NH if kind == "q" else 2 * NH)],
            cfc[:, 4 * NH:5 * NH]), waits=[(ach, s2)])
        return max(qh, qt_)

    tile_done = [0] * NT
    store_c = [0] * NT
    kind_done = {"q": 0, "k": 0, "v": 0}
    for ti in range(NT):
        pp = ti % 2
        done_c = 0
        pst = store_c[ti - 1] if ti else 0
        for ki_, (wt, kind, dstd, qoff) in enumerate((
                (wq, "q", t["qh_tm"], 0),
                (wkk, "k", t["kh_tm"], D),
                (wvv, "v", t["v_tm"], 2 * D))):
            def em(e, wt=wt, ti=ti, ki_=ki_):
                out = []
                for c in range(HC):
                    for n in range(2):
                        out.append(e.matmul(
                            ps[:, ki_ * D + n * 512: ki_ * D + (n + 1) * 512],
                            htT[:, c * T + ti * 128: c * T + ti * 128 + 128],
                            wt[:, c * D + n * 512: c * D + (n + 1) * 512],
                            start=(c == 0), stop=(c == HC - 1)))
                return out
            w = [(dsy, hl), (dgp, wl3), (ach, bb_c), (vch, kind_done[kind])]
            mmq = p.mmg(em, w)
            if kind == "v":
                done_c = p.cop("vector", lambda e, ki_=ki_: e.tensor_tensor(
                    qhb[:, 2 * D:3 * D], ps[:, ki_ * D:(ki_ + 1) * D],
                    biasB[:, 2 * D:3 * D], ALU.add), waits=[(mms, mmq), (dsy, pst)])
            else:
                p.cop("vector", lambda e: e.engine_nop(), waits=[(mms, mmq), (dsy, pst)])
                done_c = hyp_map(kind, ps[:, ki_ * D:(ki_ + 1) * D])
            kind_done[kind] = done_c
            store_c[ti] = p.dma("sync", lambda e, ti=ti, dstd=dstd, qoff=qoff: e.dma_start(
                out=dstd[ti * 128:(ti + 1) * 128, :], in_=qhb[:, qoff:qoff + D]),
                waits=[(vch, done_c)])
        tile_done[ti] = p.cop("vector", lambda e: e.memset(qtk[:, 2 * NH:128], 0.0),
                              waits=[(dsy, pst)])
        store_c[ti] = p.dma("sync", lambda e, ti=ti: e.dma_start(
            out=t["qtkt"][ti * 128:(ti + 1) * 128, :], in_=qtk[:]),
            waits=[(vch, tile_done[ti])])
    p.barrier()
    stA.close()

    stB = ExitStack()
    alloc = mk_alloc(stB)
    qkT = alloc("pb_qkT", [128, T], BF16)
    Mrow = alloc("pb_M", [32, T], F32)
    Msc = alloc("pb_Msc", [16, T], F32)
    rowsb = alloc("pb_rowsb", [32, T], BF16)
    mneg = alloc("pb_mneg", [16, T], BF16)
    sqc2 = alloc("pb_sqc", [128, 4], F32)
    qtd = t["misc_d"]
    rowd = p.nc.dram_tensor("pb_rowd", [48, T], BF16).ap()
    onesd = p.nc.dram_tensor("pb_onesd", [1, T], BF16).ap()

    qk_l = p.dma("sync", lambda e: e.dma_start(out=qkT[:], in_=t["qtkt"], transpose=True))
    iv2 = p.cop("vector", lambda e: e.tensor_tensor(
        sqc2[:, 0:1], gl["sc32"][:, 0:1], gl["sc32"][:, 0:1], ALU.mult),
        waits=[(vch, gl["sc32_rdy"])])
    iv2 = p.cop("vector", lambda e: e.tensor_scalar_mul(sqc2[:, 0:1], sqc2[:, 0:1],
                                                        1.0 / 1024.0))
    iv2 = p.cop("vector", lambda e: e.reciprocal(sqc2[:, 0:1], sqc2[:, 0:1]))
    m_c = p.cop("vector", lambda e: e.tensor_copy(Mrow[0:32, :], qkT[0:32, :]),
                waits=[(dsy, qk_l)])
    m_c = p.cop("vector", lambda e: e.tensor_tensor(
        Mrow[0:32, :], Mrow[0:32, :], Mrow[0:32, :], ALU.mult))
    m_c = p.cop("vector", lambda e: e.tensor_scalar_sub(
        Mrow[0:32, :], Mrow[0:32, :], sqc2[0:32, 0:1]), waits=[(vch, iv2)])
    m_c = p.cop("vector", lambda e: e.tensor_scalar_max(Mrow[0:32, :], Mrow[0:32, :], 0.0))
    m_sq = p.cop("scalar", lambda e: e.activation(out=Mrow[0:32, :], in_=Mrow[0:32, :],
                                                  func=AF.Ln, bias=p.cst[0:32, 0:1]),
                 waits=[(vch, m_c), (vch, p.cst_rdy)])
    m_sq = p.cop("scalar", lambda e: e.activation(out=Mrow[0:32, :], in_=Mrow[0:32, :],
                                                  func=AF.Exp, scale=0.5))
    mx = p.cop("vector", lambda e: e.reduce_max(sqc2[0:32, 1:2], qkT[0:32, :], axis=AX.X),
               waits=[(ach, m_sq)])
    mx = p.cop("vector", lambda e: e.reduce_max(sqc2[0:32, 2:3], Mrow[0:32, :], axis=AX.X))
    b1 = p.dma("gpsimd", lambda e: e.dma_start(out=t["misc_d"][320:352],
                                               in_=sqc2[0:32, 1]), waits=[(vch, mx)])
    b2 = p.dma("gpsimd", lambda e: e.dma_start(out=t["misc_d"][352:384],
                                               in_=sqc2[0:32, 2]), waits=[(vch, mx)])
    b3 = p.dma("gpsimd", lambda e: e.dma_start(
        out=sqc2[0:16, 1:2], in_=t["misc_d"][336:352].rearrange("(p a) -> p a", a=1)),
        waits=[(dgp, b2)])
    b4 = p.dma("gpsimd", lambda e: e.dma_start(
        out=sqc2[0:16, 2:3], in_=t["misc_d"][368:384].rearrange("(p a) -> p a", a=1)),
        waits=[(dgp, b2)])
    mm_t = p.cop("vector", lambda e: e.tensor_scalar(
        Msc[0:16, :], qkT[0:16, :], sqc2[0:16, 1:2], 0.125, ALU.mult, ALU.mult),
        waits=[(dgp, b4)])
    mm_t = p.cop("vector", lambda e: e.tensor_scalar(
        Mrow[0:16, :], Mrow[0:16, :], sqc2[0:16, 2:3], 0.125, ALU.mult, ALU.mult))
    mm_t = p.cop("vector", lambda e: e.tensor_tensor(
        Mrow[0:16, :], Mrow[0:16, :], Msc[0:16, :], ALU.add))
    mm_t = p.cop("vector", lambda e: e.tensor_scalar_mul(
        rowsb[0:32, :], qkT[0:32, :], 0.125))
    mm_t = p.cop("vector", lambda e: e.tensor_scalar_mul(
        mneg[0:16, :], Mrow[0:16, :], -1.0))
    r1 = p.dma("sync", lambda e: e.dma_start(out=rowd[0:16, :], in_=rowsb[16:32, :]),
               waits=[(vch, mm_t)])
    r1 = p.dma("sync", lambda e: e.dma_start(out=rowd[16:32, :], in_=mneg[0:16, :]),
               waits=[(vch, mm_t)])
    r2 = p.dma("sync", lambda e: e.dma_start(out=rowd[32:48, :], in_=qkT[0:16, :]),
               waits=[(vch, mm_t)])
    oz = p.cop("vector", lambda e: e.memset(rowsb[0:1, :], 1.0))
    r3 = p.dma("sync", lambda e: e.dma_start(out=onesd[:], in_=rowsb[0:1, :]),
               waits=[(vch, oz)])
    p.barrier()
    stB.close()

    stC = ExitStack()
    alloc = mk_alloc(stC)
    QT2 = alloc("pc_qt2", [128, T], BF16)
    KT2 = alloc("pc_kt2", [128, T], BF16)
    Kop = alloc("pc_Kop", [66, T], BF16)
    Qop = alloc("pc_Qop", [66, T], BF16)
    vh = alloc("pc_vh", [128, NT * 72], BF16)
    PT = alloc("pc_PT", [128, 2 * 8 * 2048], BF16)
    rrow = alloc("pc_rrow", [1, 512], F32)
    onum = alloc("pc_onum", [64, 512], F32)
    OT = alloc("pc_OT", [128, HC * T], BF16)
    onesc2 = alloc("pc_ones2", [1, 64], F32)
    psS = alloc("pc_psS", [128, 1536], F32, psum=True)
    psS2 = alloc("pc_psS2", [128, 1024], F32, psum=True)
    psS3 = alloc("pc_psS3", [128, 1024], F32, psum=True)
    psO = alloc("pc_psO", [65, 512], F32, psum=True)
    o2 = p.cop("vector", lambda e: e.memset(onesc2[:], 1.0))
    vz = p.cop("vector", lambda e: e.memset(
        vh[:].rearrange("p (kc x) -> p kc x", kc=NT)[:, :, 64:65], 1.0))
    exs = p.sem("pb_exs")

    exp_c = 0
    av_prev = 0
    av_pp = [0, 0]
    oe_prev = 0
    oe_prev_v = 0
    bm_prev = 0
    on_prev = 0
    pair_s_done = 0
    head_free = 0
    last_qk_read = 0
    for hp in range(HC):
        ql = p.dma("sync", lambda e, hp=hp: e.dma_start(
            out=QT2[:], in_=t["qh_tm"][:, hp * 128:(hp + 1) * 128], transpose=True),
            waits=[(mms, pair_s_done), (vch, last_qk_read)])
        kl = p.dma("sync", lambda e, hp=hp: e.dma_start(
            out=KT2[:], in_=t["kh_tm"][:, hp * 128:(hp + 1) * 128], transpose=True),
            waits=[(mms, pair_s_done), (vch, last_qk_read)])
        for hh in range(2):
            h = hp * 2 + hh
            hb = hh * 64
            vl = p.dma("sync", lambda e, h=h: e.dma_start(
                out=vh[:].rearrange("p (kc x) -> p kc x", kc=NT)[:, :, 0:64],
                in_=t["v_tm"].rearrange("(kc p) f -> p kc f", p=128)[:, :, h * 64:(h + 1) * 64]),
                waits=[(mms, av_prev)])
            w0 = [(dsy, kl), (mms, head_free)]
            kc1 = p.cop("vector", lambda e, hb=hb: e.tensor_scalar_mul(
                Kop[0:64, :], KT2[hb:hb + 64, :], -0.125), waits=w0)
            k64 = p.dma("gpsimd", lambda e, h=h: e.dma_start(
                out=Kop[64:65, :], in_=rowd[h:h + 1, :]), waits=[(mms, head_free)])
            k65 = p.dma("gpsimd", lambda e: e.dma_start(
                out=Kop[65:66, :], in_=onesd[:]), waits=[(mms, head_free)])
            qc1 = p.cop("vector", lambda e, hb=hb: e.tensor_copy(
                Qop[0:64, :], QT2[hb:hb + 64, :]), waits=[(dsy, ql), (mms, head_free)])
            last_qk_read = qc1
            q64 = p.dma("gpsimd", lambda e, h=h: e.dma_start(
                out=Qop[64:65, :], in_=rowd[32 + h:33 + h, :]), waits=[(mms, head_free)])
            q65 = p.dma("gpsimd", lambda e, h=h: e.dma_start(
                out=Qop[65:66, :], in_=rowd[16 + h:17 + h, :]), waits=[(mms, head_free)])
            GRPS = [3, 2, 2, 3, 2, 2, 3, 2, 2, 3, 2, 2, 2, 2]
            AVS = [3, 3, 3, 3, 2, 2, 2, 2, 2, 2, 2, 2, 2, 2]
            SBUFS = [psS, psS2, psS3]
            exp_pp = [exs.cnt, exs.cnt, exs.cnt]

            def emit_norm(nqb):
                nonlocal oe_prev, oe_prev_v, bm_prev, on_prev
                rc = p.cop("vector", lambda e: e.reciprocal(rrow[:], psO[64:65, :]),
                           waits=[(mms, av_prev)])
                oe_prev_v = p.cop("vector", lambda e: e.tensor_copy(onum[:], psO[0:64, :]),
                                  waits=[(vch, rc)])
                oe_prev = oe_prev_v
                bm_prev = p.mmg(lambda e: e.matmul(psO[0:64, :], onesc2[:], rrow[:],
                                                   start=True, stop=True),
                                [(vch, rc), (vch, o2), (vch, oe_prev), (vch, on_prev)])
                on_prev = p.cop("vector", lambda e, hp=hp, hb=hb, nqb=nqb: e.tensor_tensor(
                    OT[hb:hb + 64, hp * T + nqb * 512: hp * T + (nqb + 1) * 512],
                    onum[:], psO[0:64, :], ALU.mult), waits=[(vch, oe_prev), (mms, bm_prev)])

            pend = None
            for qb in range(8):
                bsel = (qb % 2) * 8 * 2048
                kc0 = 0
                avk = 0
                for gi_, gw_ in enumerate(GRPS):
                    buf = SBUFS[gi_ % 3]

                    def em(e, kc0=kc0, gw_=gw_, qb=qb, buf=buf):
                        out = []
                        for j in range(gw_):
                            kc = kc0 + j
                            out.append(e.matmul(
                                buf[:, j * 512:(j + 1) * 512],
                                Kop[:, kc * 128:(kc + 1) * 128],
                                Qop[:, qb * 512:(qb + 1) * 512],
                                start=True, stop=True, skip_group_check=True))
                        return out
                    sm = p.mmg(em, [(vch, kc1), (vch, qc1), (dgp, q65),
                                    (exs, exp_pp[gi_ % 3])])
                    exp_pp[gi_ % 3] = exp_c = p.aop(
                        "scalar", lambda e, kc0=kc0, gw_=gw_, bsel=bsel, buf=buf: e.activation(
                            out=PT[:, bsel + kc0 * 512: bsel + (kc0 + gw_) * 512],
                            in_=buf[:, 0:gw_ * 512], func=AF.Exp),
                        waits=[(mms, sm), (mms, av_pp[qb % 2])], sem=exs)
                    kc0 += gw_
                    if pend is not None:
                        pqb, pexp = pend
                        pbsel = (pqb % 2) * 8 * 2048
                        aw = AVS[gi_]

                        def emav(e, avk=avk, aw=aw, pbsel=pbsel):
                            out = []
                            for kc in range(avk, avk + aw):
                                out.append(e.matmul(
                                    psO[:], vh[:, kc * 72: kc * 72 + 65],
                                    PT[:, pbsel + kc * 512: pbsel + (kc + 1) * 512],
                                    start=(kc == 0), stop=(kc == NT - 1),
                                    skip_group_check=True))
                            return out
                        w_av = ([(exs, pexp), (dsy, vl), (vch, vz),
                                 (vch, oe_prev), (mms, bm_prev), (vch, on_prev)]
                                if avk == 0 else [])
                        av_prev = p.mmg(emav, w_av)
                        avk += aw
                if pend is not None:
                    pqb, pexp = pend
                    av_pp[pqb % 2] = av_prev
                    emit_norm(pqb)
                pend = (qb, exp_c)
            pqb, pexp = pend
            pbsel = (pqb % 2) * 8 * 2048

            def emav_full(e, pbsel=pbsel):
                out = []
                for kc in range(NT):
                    out.append(e.matmul(
                        psO[:], vh[:, kc * 72: kc * 72 + 65],
                        PT[:, pbsel + kc * 512: pbsel + (kc + 1) * 512],
                        start=(kc == 0), stop=(kc == NT - 1)))
                return out
            av_prev = p.mmg(emav_full, [(exs, pexp), (dsy, vl), (vch, vz),
                                        (vch, oe_prev), (mms, bm_prev), (vch, on_prev)])
            av_pp[pqb % 2] = av_prev
            emit_norm(pqb)
            head_free = av_prev
            pair_s_done = av_prev
    p.barrier()
    stC.close()

    stD = ExitStack()
    alloc = mk_alloc(stD)
    wo = alloc("pd_wo", [128, HC * D], BF16)
    bB = alloc("pd_bB", [128, D], F32)
    brow2 = alloc("pd_brow", [1, D], F32)
    ones2 = alloc("pd_ones", [1, 128], F32)
    x1t = alloc("pd_x1", [128, 2 * D], F32)
    f1t = alloc("pd_f1", [128, 2 * D], F32)
    outt = alloc("pd_out", [128, 2 * D], F32)
    pso = alloc("pd_pso", [128, 2 * D], F32, psum=True)
    psb3 = alloc("pd_psb3", [128, 512], F32, psum=True)
    wol = p.dma("gpsimd", lambda e: e.dma_start(
        out=wo[:].rearrange("p (c f) -> p c f", c=HC),
        in_=t["woT"].rearrange("(c p) f -> p c f", p=128)))
    o3 = p.cop("vector", lambda e: e.memset(ones2[:], 1.0))
    bl = p.dma("sync", lambda e: e.dma_start(
        out=brow2[:], in_=t["wob"].rearrange("(a f) -> a f", a=1)))
    bb2 = 0
    for n in range(2):
        mb = p.mmg(lambda e, n=n: e.matmul(
            psb3[:], ones2[:], brow2[:, n * 512:(n + 1) * 512], start=True, stop=True),
            [(dsy, bl), (vch, o3), (ach, bb2)])
        bb2 = p.cop("scalar", lambda e, n=n: e.copy(
            out=bB[:, n * 512:(n + 1) * 512], in_=psb3[:]), waits=[(mms, mb)])
    o_c = [0] * NT
    ost = [0] * NT
    for ti in range(NT):
        pp = ti % 2

        def em(e, ti=ti, pp=pp):
            out = []
            for c in range(HC):
                for n in range(2):
                    out.append(e.matmul(
                        pso[:, pp * D + n * 512: pp * D + (n + 1) * 512],
                        OT[:, c * T + ti * 128: c * T + ti * 128 + 128],
                        wo[:, c * D + n * 512: c * D + (n + 1) * 512],
                        start=(c == 0), stop=(c == HC - 1)))
            return out
        w = [(dgp, wol), (ach, bb2)]
        if ti >= 2:
            w.append((vch, o_c[ti - 2]))
        mo = p.mmg(em, w)
        wre = [(vch, o_c[ti - 2])] if ti >= 2 else []
        xld = p.dma("sync", lambda e, ti=ti, pp=pp: e.dma_start(
            out=x1t[:, pp * D:(pp + 1) * D], in_=t["x1_tm"][ti * 128:(ti + 1) * 128, :]),
            waits=wre)
        fld = p.dma("sync", lambda e, ti=ti, pp=pp: e.dma_start(
            out=f1t[:, pp * D:(pp + 1) * D], in_=t["ffn1"][ti * 128:(ti + 1) * 128, :]),
            waits=wre)
        p.cop("vector", lambda e, pp=pp: e.tensor_tensor(
            outt[:, pp * D:(pp + 1) * D], x1t[:, pp * D:(pp + 1) * D],
            f1t[:, pp * D:(pp + 1) * D], ALU.add),
            waits=[(dsy, fld)] + ([(dsy, ost[ti - 2])] if ti >= 2 else []))
        p.cop("vector", lambda e, pp=pp: e.tensor_tensor(
            outt[:, pp * D:(pp + 1) * D], outt[:, pp * D:(pp + 1) * D], bB[:], ALU.add))
        o_c[ti] = p.cop("vector", lambda e, pp=pp: e.tensor_tensor(
            outt[:, pp * D:(pp + 1) * D], outt[:, pp * D:(pp + 1) * D],
            pso[:, pp * D:(pp + 1) * D], ALU.add), waits=[(mms, mo)])
        ost[ti] = p.dma("sync", lambda e, ti=ti, pp=pp: e.dma_start(
            out=t["out"][ti * 128:(ti + 1) * 128, :], in_=outt[:, pp * D:(pp + 1) * D]),
            waits=[(vch, o_c[ti])])
    p.wait_only("sync", [(dsy, ost[NT - 1])])
    p.barrier()
    stD.close()
